# version 12
# speedup vs baseline: 1.0498x; 1.0098x over previous
import math
from contextlib import ExitStack

import numpy as np
import ml_dtypes

import concourse.bass as bass
import concourse.mybir as mybir
import concourse.bass_utils as _bass_utils
from concourse.bass_utils import run_bass_kernel_spmd

if not getattr(_bass_utils, "_ldw_patched", False):
    _orig_run_command = _bass_utils.run_command

    def _run_command_ldw(argv, **kw):
        argv = ["--enable-ldw-opt=true" if a == "--enable-ldw-opt=false" else a
                for a in argv]
        return _orig_run_command(argv, **kw)

    _bass_utils.run_command = _run_command_ldw
    _bass_utils._ldw_patched = True

F32 = mybir.dt.float32
BF16 = mybir.dt.bfloat16
AF = mybir.ActivationFunctionType
ALU = mybir.AluOpType
bfdt = ml_dtypes.bfloat16

T = 2048
D = 1024
AA_STOP = 0
NCH = D // 128
NTCH = T // 128
SHIFT = 4.0

ENGS = ("t", "v", "s", "g", "y")


class Prog:

    def __init__(self, nc, ctx):
        self.nc = nc
        self.ctx = ctx
        self.q = {e: [] for e in ENGS}
        self.sems = {}
        self.cnt = {}

    def _sem(self, name):
        if name not in self.sems:
            self.sems[name] = self.ctx.enter_context(self.nc.semaphore(name))
            self.cnt[name] = 0
        return self.sems[name]

    def wait(self, eng, tok):
        if tok is None:
            return
        if isinstance(tok, (list, tuple)) and tok and isinstance(tok[0], (list, tuple, type(None))):
            for t_ in tok:
                self.wait(eng, t_)
            return
        if isinstance(tok, list):
            for t_ in tok:
                self.wait(eng, t_)
            return
        name, val = tok
        self._sem(name)
        if name.startswith("d") and val != self.cnt[name]:
            raise AssertionError(
                f"non-latest DMA wait on {name}: {val} != {self.cnt[name]}")
        self.q[eng].append(("w", name, val))

    def op(self, eng, fn, inc=None, inc_k=1, waits=()):
        for w in waits:
            self.wait(eng, w)
        tok = None
        if inc is not None:
            self._sem(inc)
            self.cnt[inc] += inc_k
            tok = (inc, self.cnt[inc])
        self.q[eng].append(("o", fn, inc, inc_k))
        return tok

    def dma(self, eng, out_ap, in_ap, waits=(), sem=None, **kw):
        if sem is None:
            sem = "d" + eng
        return self.op(eng, lambda e: e.dma_start(out=out_ap, in_=in_ap, **kw),
                       inc=sem, inc_k=16, waits=waits)

    def tok(self, sem):
        self._sem(sem)
        return (sem, self.cnt[sem])

    dma_latest = tok

    def _uniq(self, name):
        if not hasattr(self, "_names"):
            self._names = {}
        k = self._names.get(name, 0)
        self._names[name] = k + 1
        return name if k == 0 else "%s_u%d" % (name, k)

    def sbuf(self, ctx, name, shape, dt):
        return ctx.enter_context(self.nc.sbuf_tensor(self._uniq(name), shape, dt))

    def psum(self, ctx, name, shape, dt=None):
        return ctx.enter_context(self.nc.psum_tensor(
            self._uniq(name), shape, F32 if dt is None else dt))

    def dram(self, name, shape, dt):
        return self.nc.dram_tensor(self._uniq(name), shape, dt, kind="Internal")

    def fence_dma(self):
        for q_ in ("g", "y"):
            for s_ in ("sv", "ss", "st"):
                if self.cnt.get(s_, 0):
                    self.wait(q_, self.tok(s_))

    def realize(self, block):
        handles = self.sems

        def body(eng_name):
            def run(engine):
                for item in self.q[eng_name]:
                    if item[0] == "w":
                        engine.wait_ge(handles[item[1]], item[2])
                    else:
                        _, fn, inc, inc_k = item
                        ins = fn(engine)
                        if inc is not None:
                            ins.then_inc(handles[inc], inc_k)
            return run

        block.tensor(body("t"))
        block.vector(body("v"))
        block.scalar(body("s"))
        block.gpsimd(body("g"))
        block.sync(body("y"))



def make_consts(nc, ctx, P):
    c = {}
    sb = lambda name, shape, dt: ctx.enter_context(nc.sbuf_tensor(name, shape, dt))
    tri = np.triu(np.ones((128, 128), dtype=np.float32)).astype(bfdt)
    mw = np.zeros((128, 4 * 512), dtype=np.float32)
    for v_ in range(4):
        off = v_ * 128
        mw[:, v_ * 512 + off: v_ * 512 + off + 128] = np.triu(np.ones((128, 128), np.float32))
        mw[:, v_ * 512 + off + 128: (v_ + 1) * 512] = 1.0
    identf = np.eye(128, dtype=np.float32)
    items = [("tri", tri, BF16), ("maskw", mw.astype(bfdt), BF16),
             ("identf", identf, F32)]
    toks = []
    for name, arr, dt in items:
        dr = nc.inline_tensor(np.ascontiguousarray(arr), name="c_" + name)
        t_ = sb("sb_" + name, list(arr.shape), dt)
        toks.append(P.dma("y", t_[:], dr.ap()))
        c[name] = t_
    ones = [("ones_col_b", [128, 1], BF16), ("ones_col_f", [128, 1], F32),
            ("ones_row_b", [1, 128], BF16), ("ones_row_f", [1, 128], F32)]
    for i, (name, shape, dt) in enumerate(ones):
        t_ = sb(name, shape, dt)
        P.op("g", lambda e, t_=t_: e.memset(t_[:], 1.0))
        c[name] = t_
    nullmask = sb("nullmask", [128, 1], F32)
    nz = P.op("g", lambda e: e.memset(nullmask[:], 0.0), inc="sg")
    gtok = P.op("g", lambda e: e.memset(nullmask[0:1, 0:1], 1.0), inc="sg",
                waits=[nz])
    c["nullmask"] = nullmask
    c["ready"] = [gtok]

    biasdb = {}
    for v in (1e-5, -SHIFT, math.log(1.0 / 3.0)):
        v = float(v)
        t_ = ctx.enter_context(nc.sbuf_tensor("bias_%d" % len(biasdb), [128, 1], F32))
        tk = P.op("g", lambda e, t_=t_, v=v: e.memset(t_[:], v), inc="sg")
        biasdb[v] = (t_[:], tk)

    def bias(v):
        return biasdb[float(v)]

    c["bias"] = bias
    return c



def emit_ln(P, ctx, C, xT, xnT, f32_cb=None, ra_rows=None):
    nc = P.nc
    SL = 1024
    NSL = T // SL
    with ExitStack() as lctx:
        sb = lambda name, shape, dt: P.sbuf(lctx, name, shape, dt)
        psb = lambda name, shape: P.psum(lctx, name, shape)
        xsq = sb("ln_xsq", [128, 2 * SL], BF16)
        m_row = sb("ln_m", [1, SL], F32)
        msq_row = sb("ln_msq", [1, SL], F32)
        var_row = sb("ln_var", [1, SL], F32)
        l_row = sb("ln_l", [1, SL], F32)
        rb_b = sb("ln_rbb", [1, SL], BF16)
        ab_b = sb("ln_abb", [1, SL], BF16)
        sums_ps = [psb("ln_sums%d" % h, [1, 512]) for h in range(SL // 512)]
        sq_ps = [psb("ln_sq%d" % h, [1, 512]) for h in range(SL // 512)]
        ab_ps = [psb("ln_ab%d" % h, [128, 512]) for h in range(SL // 512)]
        rb_ps = [psb("ln_rb%d" % h, [128, 512]) for h in range(SL // 512)]

        start_sv = P.tok("sv")
        start_st = P.tok("st")
        P.wait("s", start_sv)
        P.wait("t", start_sv)
        P.wait("v", start_st)

        last = None
        prev_sq_mm = None
        prev_mtok = prev_vtok = prev_btok = None
        prev_mt_last = {}
        prev_add_last = {}
        for sl in range(NSL):
            lo = sl * SL
            if ra_rows is not None:
                r_row = ra_rows[0][0:1, lo:lo + SL]
                a_row = ra_rows[1][0:1, lo:lo + SL]
            else:
                r_row = P.sbuf(lctx, "ln_r%d" % sl, [1, SL], F32)[:]
                a_row = P.sbuf(lctx, "ln_a%d" % sl, [1, SL], F32)[:]
            stoks = {}
            pv_by_c = {}

            def emit_square(c):
                xb = c % 2
                w = []
                if pv_by_c.get(c - 2) is not None:
                    w.append(pv_by_c[c - 2])
                elif c < 2 and prev_sq_mm is not None:
                    w.append(prev_sq_mm)
                stoks[c] = P.op("s", lambda e, c=c, lo=lo, xb=xb: e.activation(
                    out=xsq[:, xb * SL:xb * SL + SL],
                    in_=xT[:, c * T + lo: c * T + lo + SL],
                    func=AF.Square), inc="ss", waits=w)

            emit_square(0)
            emit_square(1)
            for c in range(NCH):
                xb = c % 2
                for h in range(SL // 512):
                    sw = [prev_mtok] if (c == 0 and h == 0 and prev_mtok) else []
                    P.op("t", lambda e, c=c, lo=lo, h=h: e.matmul(
                        sums_ps[h][:, 0:512],
                        C["ones_col_f"][:],
                        xT[:, c * T + lo + h * 512: c * T + lo + (h + 1) * 512],
                        start=(c == 0), stop=(c == NCH - 1)), waits=sw)
                    qw = [stoks[c]] if h == 0 else []
                    if c == 0 and h == 0 and prev_vtok:
                        qw = qw + [prev_vtok]
                    pv = P.op("t", lambda e, c=c, h=h, xb=xb: e.matmul(
                        sq_ps[h][:, 0:512], C["ones_col_b"][:],
                        xsq[:, xb * SL + h * 512: xb * SL + (h + 1) * 512],
                        start=(c == 0), stop=(c == NCH - 1)),
                        waits=qw, inc="st")
                pv_by_c[c] = pv
                prev_sq_mm = pv
                if c + 2 < NCH:
                    emit_square(c + 2)
            mm_done = prev_sq_mm
            mtok = None
            for h in range(SL // 512):
                mtok = P.op("s", lambda e, h=h: e.mul(
                    out=m_row[0:1, h * 512:(h + 1) * 512], in_=sums_ps[h][:],
                    mul=1.0 / D), inc="ss", waits=[mm_done] if h == 0 else [])
            mqtok = P.op("s", lambda e: e.activation(out=msq_row[:], in_=m_row[:],
                                                     func=AF.Square), inc="ss",
                         waits=[mtok])
            vtok = None
            for h in range(SL // 512):
                vtok = P.op("v", lambda e, h=h: e.scalar_tensor_tensor(
                    out=var_row[0:1, h * 512:(h + 1) * 512], in0=sq_ps[h][:],
                    scalar=1.0 / D, in1=msq_row[0:1, h * 512:(h + 1) * 512],
                    op0=ALU.mult, op1=ALU.subtract), inc="sv",
                    waits=[mqtok, mm_done] if h == 0 else [])
            eps_ap, eps_tok = C["bias"](1e-5)
            ltok = P.op("s", lambda e: e.activation(out=l_row[:], in_=var_row[:],
                                                    func=AF.Ln, bias=eps_ap[0:1, :]),
                        inc="ss", waits=[vtok, eps_tok])
            rtok = P.op("s", lambda e, r_row=r_row: e.activation(
                out=r_row, in_=l_row[:], func=AF.Exp, scale=-0.5), inc="ss",
                waits=[ltok])
            atok = P.op("v", lambda e, r_row=r_row, a_row=a_row: e.scalar_tensor_tensor(
                out=a_row, in0=m_row[:], scalar=-1.0, in1=r_row,
                op0=ALU.mult, op1=ALU.mult), inc="sv", waits=[rtok])
            cb1 = P.op("s", lambda e, r_row=r_row: e.activation(
                out=rb_b[:], in_=r_row, func=AF.Copy), inc="ss",
                waits=[atok] + ([prev_btok] if prev_btok is not None else []))
            cb2 = P.op("s", lambda e, a_row=a_row: e.activation(
                out=ab_b[:], in_=a_row, func=AF.Copy), inc="ss")
            btok = None
            for h in range(SL // 512):
                aw = [cb2] if h == 0 else []
                if prev_add_last.get(h) is not None:
                    aw = aw + [prev_add_last[h]]
                P.op("t", lambda e, h=h: e.matmul(
                    ab_ps[h][:, 0:512], C["ones_row_b"][:],
                    ab_b[0:1, h * 512:(h + 1) * 512], start=True, stop=True),
                    waits=aw)
                btok = P.op("t", lambda e, h=h: e.matmul(
                    rb_ps[h][:, 0:512], C["ones_row_b"][:],
                    rb_b[0:1, h * 512:(h + 1) * 512], start=True, stop=True),
                    inc="st",
                    waits=[prev_mt_last[h]] if prev_mt_last.get(h) is not None else [])
            for c in range(NCH):
                for h in range(SL // 512):
                    mt = P.op("v", lambda e, c=c, lo=lo, h=h: e.tensor_mul(
                        xnT[:, c * T + lo + h * 512: c * T + lo + (h + 1) * 512],
                        xT[:, c * T + lo + h * 512: c * T + lo + (h + 1) * 512],
                        rb_ps[h][:, 0:512]),
                        waits=[btok] if (c == 0 and h == 0) else [], inc="sv")
                    last = P.op("v", lambda e, c=c, lo=lo, h=h: e.tensor_add(
                        xnT[:, c * T + lo + h * 512: c * T + lo + (h + 1) * 512],
                        xnT[:, c * T + lo + h * 512: c * T + lo + (h + 1) * 512],
                        ab_ps[h][:, 0:512]), inc="sv", waits=[mt])
                    if c == NCH - 1:
                        prev_mt_last[h] = mt
                        prev_add_last[h] = last
            prev_mtok, prev_vtok, prev_btok = mtok, vtok, btok
            if sl == NSL - 1:
                P.wait("t", last)
    return last



class RowBounce:

    def __init__(self, P, ctx, name, nslots, C=None, fold=1.0):
        nc = P.nc
        self.P = P
        self.n = nslots
        self.C = C
        self.fold = fold
        self.scr = P.dram(name + "_scr", [nslots, T], F32)
        self.scr2 = P.dram(name + "_scr2", [nslots, T], F32)
        w = nslots * T // 128
        self.pk = P.sbuf(ctx, name + "_pk", [128, w], F32)
        self.pk2 = P.sbuf(ctx, name + "_pk2", [128, w], F32)
        self.rowbuf = [P.sbuf(ctx, name + "_r%d" % i, [1, T], BF16)
                       for i in range(2)]
        self.row_guard = [None, None]
        self.wtoks = {}
        self.st_tok = None
        self.rcount = 0

    def write(self, row_ap, idx, waits):
        w = list(waits)
        if idx == 0 and self.st_tok is not None:
            w.append(self.P.dma_latest("dp"))
        self.wtoks[idx] = self.P.dma("y", self.scr.ap()[idx:idx + 1, :], row_ap,
                                     waits=w, sem="db")
        return self.wtoks[idx]

    def recip_all(self):
        P = self.P
        flat = self.scr.ap().rearrange("n t -> (n t)").rearrange("(p f) -> p f", p=128)
        flat2 = self.scr2.ap().rearrange("n t -> (n t)").rearrange("(p f) -> p f", p=128)
        ld = P.dma("y", self.pk[:], flat, waits=[P.dma_latest("db")], sem="dp")
        lntok = P.op("s", lambda e: e.activation(out=self.pk2[:], in_=self.pk[:],
                                                 func=AF.Ln), waits=[ld], inc="ss")
        if self.fold != 1.0:
            b_ap, b_tok = self.C["bias"](math.log(self.fold))
            l2 = P.op("s", lambda e: e.activation(out=self.pk[:], in_=self.pk2[:],
                                                  func=AF.Exp, scale=-1.0,
                                                  bias=b_ap), inc="ss",
                      waits=[lntok, b_tok])
        else:
            l2 = P.op("s", lambda e: e.activation(out=self.pk[:], in_=self.pk2[:],
                                                  func=AF.Exp, scale=-1.0), inc="ss",
                      waits=[lntok])
        self.st_tok = P.dma("y", flat2, self.pk[:],
                            waits=[l2, P.dma_latest("dr0"), P.dma_latest("dr1")],
                            sem="dp")
        self._fresh = 2

    def load_row(self, idx, waits=()):
        b = self.rcount % 2
        self.rcount += 1
        w = list(waits)
        if getattr(self, "_fresh", 0) > 0:
            w.append(self.st_tok)
            self._fresh -= 1
        if self.row_guard[b] is not None:
            w.append(self.row_guard[b])
        tok = self.P.dma("g", self.rowbuf[b][0:1, :], self.scr2.ap()[idx:idx + 1, :],
                         waits=w, sem="dr%d" % b)
        return self.rowbuf[b][0:1, :], tok, b

    def set_guard(self, b, tok):
        self.row_guard[b] = tok



def emit_attn_a(P, ctx, C, xT, xnT, W):
    nc = P.nc
    H, HD = 8, 128
    sb = lambda name, shape, dt: P.sbuf(ctx, name, shape, dt)
    psb = lambda name, shape: P.psum(ctx, name, shape)
    P.fence_dma()

    hd_sb = sb("aa_hd", [8, 3], F32)
    hs_sb = sb("aa_hs", [8, 1], F32)
    hds_sb = sb("aa_hds", [8, 3], F32)
    dirv = sb("aa_dirv", [128, 16], BF16)
    P.dma("y", hd_sb[:], W["head_dirs"], sem="df")
    t2 = P.dma("y", hs_sb[:], W["head_scales"].rearrange("(h a) -> h a", a=1),
               sem="df")
    mz = P.op("g", lambda e: e.memset(dirv[:], 0.0), inc="sg")
    sc = P.op("v", lambda e: e.tensor_scalar(
        out=hds_sb[:], in0=hd_sb[:], scalar1=hs_sb[:], scalar2=float(math.sqrt(HD)),
        op0=ALU.mult, op1=ALU.mult), inc="sv", waits=[t2])
    for h in range(H):
        P.dma("g", dirv[0:3, 2 * h:2 * h + 1], hd_sb[h:h + 1, 0:3],
              waits=[t2, mz] if h == 0 else [], sem="dv")
        P.dma("g", dirv[0:3, 2 * h + 1:2 * h + 2], hds_sb[h:h + 1, 0:3],
              waits=[sc] if h == 0 else [], sem="dv")
    dv_toks = [P.dma_latest("dv")]

    wout_sb = [sb("aa_wout%d" % i, [128, H * 128], BF16) for i in range(2)]
    wqk = [sb("aa_wqk%d" % i, [128, 3 * 8 * 128], BF16) for i in range(2)]
    QTs = [sb("aa_qt%d" % i, [128, T], BF16) for i in range(2)]
    KTs = [sb("aa_kt%d" % i, [128, T], BF16) for i in range(2)]
    Vh = [sb("aa_v%d" % i, [128, NTCH * 128], BF16) for i in range(2)]
    qp_row = sb("aa_qp", [1, T], BF16)
    kp_row = sb("aa_kp", [1, T], BF16)
    sums_row = sb("aa_sums", [1, T], F32)
    O_all = sb("aa_oall", [128, H * T], BF16)
    expT = [sb("aa_exp%d" % i, [128, 512], BF16) for i in range(3)]

    bounce = RowBounce(P, ctx, "aa_b", H)

    qk_ps = [psb("aa_qkps%d" % i, [128, 512]) for i in range(2)]
    row_ps = psb("aa_rowps", [1, 512])
    sc_ps = [psb("aa_scps%d" % i, [128, 512]) for i in range(3)]
    pv_ps = psb("aa_pvps", [128, 512])
    sm_ps = psb("aa_smps", [1, 512])

    ln_done = P.tok("sv")
    scale = 1.0 / math.sqrt(HD)
    sh_ap, sh_tok = C["bias"](-SHIFT)

    def load_head_weights(h, buf, guard):
        tk = None
        for j in range(3):
            for c in range(NCH):
                tk = P.dma(
                    "g", wqk[buf][:, j * 1024 + c * 128: j * 1024 + (c + 1) * 128],
                    W["Wqkv"][c * 128:(c + 1) * 128,
                              j * 1024 + h * 128: j * 1024 + (h + 1) * 128],
                    waits=[guard] if (guard is not None and j == 0 and c == 0) else [],
                    sem="dg%d" % buf)
        return [tk]

    prev_proj_done = [None, None]
    prev_attn_done = [None, None]
    prev_head_done = None
    last_sums_dma = None
    wtoks = load_head_weights(0, 0, None)
    oev_by_head = {}
    pvsum_tok = {}
    last_exp = [None, None, None]

    def proj_stream(jobs):
        evs = {}
        ev = None
        for i, (mms, evac) in enumerate(jobs):
            b = i % 2
            if evs.get(b) is not None:
                P.wait("t", evs[b])
            mm = mms(qk_ps[b])
            ev = evac(qk_ps[b], mm)
            evs[b] = ev
        return ev

    for h in range(H):
        buf = h % 2
        nwtoks = (load_head_weights(h + 1, 1 - buf, prev_proj_done[1 - buf])
                  if h + 1 < H else None)
        P.wait("t", ln_done)
        P.wait("t", wtoks)
        if prev_attn_done[buf] is not None:
            P.wait("t", prev_attn_done[buf])
        qk_ev = {}
        jobs = []
        for mat, dst in ((0, QTs[buf]), (1, KTs[buf])):
            for s4 in range(4):
                def mms(bank, mat=mat, s4=s4, buf=buf):
                    mm = None
                    for c in range(NCH):
                        mm = P.op("t", lambda e, mat=mat, s4=s4, c=c, buf=buf, bank=bank: e.matmul(
                            bank[:, 0:512],
                            wqk[buf][:, mat * 1024 + c * 128: mat * 1024 + (c + 1) * 128],
                            xnT[:, c * T + s4 * 512: c * T + (s4 + 1) * 512],
                            start=(c == 0), stop=(c == NCH - 1)),
                            inc="st" if c == NCH - 1 else None)
                    return mm
                def evac(bank, mm, mat=mat, s4=s4, dst=dst):
                    ev = P.op("v", lambda e, dst=dst, s4=s4, bank=bank: e.tensor_copy(
                        dst[:, s4 * 512:(s4 + 1) * 512], bank[:, 0:512]), inc="sv",
                        waits=[mm])
                    qk_ev[(mat, s4)] = ev
                    return ev
                jobs.append((mms, evac))
        for tc in range(NTCH):
            def mms(bank, tc=tc, buf=buf):
                mm = None
                for c in range(NCH):
                    mm = P.op("t", lambda e, tc=tc, c=c, buf=buf, bank=bank: e.matmul(
                        bank[:, 0:128],
                        xnT[:, c * T + tc * 128: c * T + (tc + 1) * 128],
                        wqk[buf][:, 2048 + c * 128: 2048 + (c + 1) * 128],
                        start=(c == 0), stop=(c == NCH - 1)),
                        inc="st" if c == NCH - 1 else None)
                return mm
            def evac(bank, mm, tc=tc, buf=buf):
                return P.op("v", lambda e, tc=tc, buf=buf, bank=bank: e.tensor_copy(
                    Vh[buf][:, tc * 128:(tc + 1) * 128], bank[:, 0:128]), inc="sv",
                    waits=[mm])
            jobs.append((mms, evac))
        last_ev = proj_stream(jobs)
        proj_done = P.tok("st")
        prev_proj_done[buf] = proj_done
        P.wait("t", dv_toks)
        row_last = None
        for which, src_t, dst, col in ((0, QTs[buf], qp_row, 2 * h),
                                       (1, KTs[buf], kp_row, 2 * h + 1)):
            for s4 in range(4):
                w = [qk_ev[(which, s4)]]
                if row_last is not None:
                    w.append(row_last)
                mm = P.op("t", lambda e, src_t=src_t, s4=s4, col=col: e.matmul(
                    row_ps[:, 0:512], dirv[:, col:col + 1],
                    src_t[:, s4 * 512:(s4 + 1) * 512], start=True, stop=True),
                    inc="st", waits=w)
                sw = [mm]
                if prev_head_done is not None and which == 0 and s4 == 0:
                    sw.append(prev_head_done)
                row_last = P.op("s", lambda e, dst=dst, s4=s4: e.activation(
                    out=dst[0:1, s4 * 512:(s4 + 1) * 512], in_=row_ps[0:1, 0:512],
                    func=AF.Copy), inc="ss", waits=sw)
                P.wait("t", row_last)
        qdone = []
        for qc in range(4):
            nkt = 4 * qc + 4
            oev_prev = qdone[-1] if qdone else None
            pend = []

            def flush_one():
                kt, etok = pend.pop(0)
                pw = [etok]
                if kt == 0 and oev_prev is not None:
                    pw += [oev_prev[0], oev_prev[1]]
                P.op("t", lambda e, buf=buf, kt=kt, nkt=nkt: e.matmul(
                    pv_ps[:, 0:512], Vh[buf][:, kt * 128:(kt + 1) * 128],
                    expT[kt % 3][:, 0:512], start=(kt == 0), stop=(kt == nkt - 1)),
                    waits=pw)
                pvsum_tok[kt % 3] = P.op("t", lambda e, kt=kt, nkt=nkt: e.matmul(
                    sm_ps[:, 0:512], C["ones_col_b"][:], expT[kt % 3][:, 0:512],
                    start=(kt == 0), stop=(kt == nkt - 1)), inc="st")

            for kt in range(nkt):
                par = kt % 3
                scp = sc_ps[par]
                swaits = []
                if kt == 0 and qc == 0:
                    swaits.append(row_last)
                    if h == 0:
                        swaits.append(sh_tok)
                if last_exp[par] is not None:
                    swaits.append(last_exp[par])
                P.op("t", lambda e, buf=buf, kt=kt, qc=qc, scp=scp: e.matmul(
                    scp[:, 0:512], KTs[buf][:, kt * 128:(kt + 1) * 128],
                    QTs[buf][:, qc * 512:(qc + 1) * 512], start=True, stop=False),
                    waits=swaits)
                smm = P.op("t", lambda e, kt=kt, qc=qc, scp=scp: e.matmul(
                    scp[:, 0:512], kp_row[0:1, kt * 128:(kt + 1) * 128],
                    qp_row[0:1, qc * 512:(qc + 1) * 512], start=False, stop=True),
                    inc="st")
                ew = [smm]
                if pvsum_tok.get(par) is not None:
                    ew.append(pvsum_tok[par])
                etok = P.op("s", lambda e, par=par, scp=scp: e.activation(
                    out=expT[par][:, 0:512], in_=scp[:, 0:512], func=AF.Exp,
                    scale=scale, bias=sh_ap), inc="ss", waits=ew)
                if kt >= 4 * qc:
                    v_ = kt - 4 * qc
                    etok = P.op("v", lambda e, par=par, v_=v_: e.tensor_mul(
                        expT[par][:, 0:512], expT[par][:, 0:512],
                        C["maskw"][:, v_ * 512:(v_ + 1) * 512]), inc="sv",
                        waits=[etok])
                last_exp[par] = etok
                pend.append((kt, etok))
                if len(pend) > 2:
                    flush_one()
            while pend:
                flush_one()
            pvt = P.tok("st")
            oev = P.op("v", lambda e, h=h, qc=qc: e.tensor_copy(
                O_all[:, h * T + qc * 512: h * T + (qc + 1) * 512], pv_ps[:, 0:512]),
                inc="sv", waits=[pvt])
            sevw = [pvt]
            if qc == 0 and last_sums_dma is not None:
                sevw.append(last_sums_dma)
            sev = P.op("s", lambda e, qc=qc: e.activation(
                out=sums_row[0:1, qc * 512:(qc + 1) * 512], in_=sm_ps[0:1, 0:512],
                func=AF.Copy), inc="ss", waits=sevw)
            qdone.append((oev, sev))
            P.wait("t", oev)
            P.wait("t", sev)
        last_sums_dma = bounce.write(sums_row[0:1, :], h, waits=[qdone[-1][1]])
        oev_by_head[h] = [q[0] for q in qdone]
        prev_head_done = P.tok("st")
        prev_attn_done[buf] = prev_head_done
        wtoks = nwtoks

    bounce.recip_all()
    P.wait("t", P.tok("sv"))
    norm_last = None
    nb_guard = {}
    for h in range(H):
        rap, rtok, rb = bounce.load_row(h)
        bmm = None
        for s4 in range(4):
            nb = (h * 4 + s4) % 2
            w = [rtok, oev_by_head[h][s4]] if s4 == 0 else []
            if nb_guard.get(nb) is not None:
                w.append(nb_guard[nb])
            bmm = P.op("t", lambda e, s4=s4, rap=rap, nb=nb: e.matmul(
                qk_ps[nb][:, 0:512], C["ones_row_b"][:],
                rap[0:1, s4 * 512:(s4 + 1) * 512], start=True, stop=True),
                inc="st", waits=w)
            norm_last = P.op("v", lambda e, h=h, s4=s4, nb=nb: e.tensor_mul(
                O_all[:, h * T + s4 * 512: h * T + (s4 + 1) * 512],
                O_all[:, h * T + s4 * 512: h * T + (s4 + 1) * 512],
                qk_ps[nb][:, 0:512]), inc="sv", waits=[bmm, oev_by_head[h][s4]])
            nb_guard[nb] = norm_last
        bounce.set_guard(rb, bmm)
    def load_wout(c, wbuf, guard):
        tk = None
        for h in range(H):
            tk = P.dma("g", wout_sb[wbuf][:, h * 128:(h + 1) * 128],
                       W["Wout"][h * 128:(h + 1) * 128, c * 128:(c + 1) * 128],
                       waits=[guard] if (guard is not None and h == 0) else [],
                       sem="dwo%d" % wbuf)
        return [tk]

    wo_toks = load_wout(0, 0, None)
    wo_guard = [None, None]
    rt_by_bank = {}
    for c in range(NCH):
        wbuf = c % 2
        nwo = load_wout(c + 1, 1 - wbuf, wo_guard[1 - wbuf]) if c + 1 < NCH else None
        P.wait("t", wo_toks)
        mmtok = None
        for s4 in range(4):
            yb = s4 % 2
            mmw = [norm_last] if (c == 0 and s4 == 0) else []
            if rt_by_bank.get(yb) is not None:
                mmw.append(rt_by_bank[yb])
            for hh in range(H):
                mmtok = P.op("t", lambda e, s4=s4, hh=hh, wbuf=wbuf, yb=yb: e.matmul(
                    qk_ps[yb][:, 0:512],
                    wout_sb[wbuf][:, hh * 128:(hh + 1) * 128],
                    O_all[:, hh * T + s4 * 512: hh * T + (s4 + 1) * 512],
                    start=(hh == 0), stop=(hh == H - 1)),
                    waits=mmw if hh == 0 else [],
                    inc="st" if hh == H - 1 else None)
            rt = P.op("v", lambda e, c=c, s4=s4, yb=yb: e.tensor_add(
                xT[:, c * T + s4 * 512: c * T + (s4 + 1) * 512],
                xT[:, c * T + s4 * 512: c * T + (s4 + 1) * 512], qk_ps[yb][:, 0:512]),
                inc="sv", waits=[mmtok])
            rt_by_bank[yb] = rt
        wo_guard[wbuf] = mmtok
        wo_toks = nwo
    return



class RowStage:

    def __init__(self, P, ctx, name, nslots):
        nc = P.nc
        self.P = P
        self.n = nslots
        self.scr = P.dram(name + "_scr", [nslots, T], F32)
        self.rowbuf = [P.sbuf(ctx, name + "_r%d" % i, [1, T], BF16)
                       for i in range(2)]
        self.row_guard = [None, None]
        self.rcount = 0
        self.st_tok = None

    def store(self, src_ap, waits):
        self.st_tok = self.P.dma("y", self.scr.ap()[:], src_ap, waits=waits, sem="db")
        return self.st_tok

    def load_row(self, idx, waits=()):
        b = self.rcount % 2
        self.rcount += 1
        w = list(waits)
        if self.rcount <= 2:
            w.append(self.st_tok)
        if self.row_guard[b] is not None:
            w.append(self.row_guard[b])
        tok = self.P.dma("g", self.rowbuf[b][0:1, :], self.scr.ap()[idx:idx + 1, :],
                         waits=w, sem="dr%d" % b)
        return self.rowbuf[b][0:1, :], tok, b

    def set_guard(self, b, tok):
        self.row_guard[b] = tok



def emit_l0_ffn(P, ctx, C, xT, xnT, W):
    nc = P.nc
    E = 8
    sb = lambda name, shape, dt: P.sbuf(ctx, name, shape, dt)

    def chain(ops, first_waits=()):
        tok = None
        for eng, fn in ops:
            w = [tok] if tok is not None else list(first_waits)
            tok = P.op(eng, fn, inc={"v": "sv", "s": "ss", "t": "st"}[eng],
                       waits=w)
        return tok

    w_all = sb("f0_wall", [128, 16 * E], F32)
    wT_stage = RowStage(P, ctx, "f0_ws", E)
    y_acc = sb("f0_yacc", [128, NCH * T], BF16)

    with ExitStack() as gctx:
        gsb = lambda name, shape, dt: P.sbuf(gctx, name, shape, dt)
        gps = lambda name, shape: P.psum(gctx, name, shape)
        rfull = gsb("f0_rf", [1, T], F32)
        afull = gsb("f0_af", [1, T], F32)
        ln_tok = emit_ln(P, gctx, C, xT, xnT, ra_rows=(rfull, afull))
        P.fence_dma()
        gw_sb = gsb("f0_gw", [128, 8 * E], F32)
        gb_row = gsb("f0_gbr", [1, E], F32)
        grow = gsb("f0_grow", [1, E], F32)
        gwsum_sb = gsb("f0_gws", [128, E], F32)
        gb_sb = gsb("f0_gb", [128, E], F32)
        rc_sb = gsb("f0_rc", [128, 1], F32)
        ac_sb = gsb("f0_ac", [128, 1], F32)
        tmp8 = gsb("f0_t8", [128, E], F32)
        lg8 = gsb("f0_lg", [128, E], F32)
        sg8 = gsb("f0_sg", [128, E], F32)
        sfc8 = gsb("f0_sfc", [128, E], F32)
        g4 = gsb("f0_g4", [128, 4], F32)
        sc4 = gsb("f0_sc4", [128, 4], F32)
        m1c = gsb("f0_m1", [128, 1], F32)
        m2c = gsb("f0_m2", [128, 1], F32)
        emask = gsb("f0_em", [128, E], F32)
        t8b = gsb("f0_t8b", [128, E], F32)
        wsc = gsb("f0_wsc", [128, 1], F32)
        wT_sb = gsb("f0_wT", [E, T], F32)
        raw_ps = gps("f0_raw", [128, E])
        r_ps = gps("f0_rps", [128, 1])
        a_ps = gps("f0_aps", [128, 1])
        sum_ps = gps("f0_sum", [1, E])
        bc_ps = gps("f0_bc", [128, E])
        tp_ps = gps("f0_tp", [E, 128])

        for c in range(NCH):
            P.dma("y", gw_sb[:, c * E:(c + 1) * E],
                  W["gate_w"][c * 128:(c + 1) * 128, :], sem="df")
        gwt = P.dma("y", gb_row[:], W["gate_bias"].rearrange("(a e) -> a e", a=1),
                    sem="df")
        smm = None
        for c in range(NCH):
            smm = P.op("t", lambda e, c=c: e.matmul(
                sum_ps[:, 0:E], C["ones_col_f"][:], gw_sb[:, c * E:(c + 1) * E],
                start=(c == 0), stop=(c == NCH - 1)),
                waits=[gwt] if c == 0 else [], inc="st" if c == NCH - 1 else None)
        svt = P.op("s", lambda e: e.activation(out=grow[:], in_=sum_ps[:],
                                               func=AF.Copy), inc="ss", waits=[smm])
        b1 = P.op("t", lambda e: e.matmul(bc_ps[:, 0:E], C["ones_row_f"][:],
                                          grow[:], start=True, stop=True),
                  inc="st", waits=[svt])
        e1 = P.op("v", lambda e: e.tensor_copy(gwsum_sb[:], bc_ps[:]), inc="sv",
                  waits=[b1])
        b2 = P.op("t", lambda e: e.matmul(bc_ps[:, 0:E], C["ones_row_f"][:],
                                          gb_row[:], start=True, stop=True),
                  inc="st", waits=[e1])
        e2 = P.op("v", lambda e: e.tensor_copy(gb_sb[:], bc_ps[:]), inc="sv",
                  waits=[b2])
        P.wait("t", ln_tok)
        prev = e2
        wtr_toks = []
        for tc in range(NTCH):
            mm = None
            for c in range(NCH):
                mm = P.op("t", lambda e, c=c, tc=tc: e.matmul(
                    raw_ps[:, 0:E],
                    xT[:, c * T + tc * 128: c * T + (tc + 1) * 128],
                    gw_sb[:, c * E:(c + 1) * E],
                    start=(c == 0), stop=(c == NCH - 1)),
                    waits=[prev] if c == 0 else [],
                    inc="st" if c == NCH - 1 else None)
            rmm = P.op("t", lambda e, tc=tc: e.matmul(
                r_ps[:], rfull[0:1, tc * 128:(tc + 1) * 128],
                C["ones_row_f"][0:1, 0:1], start=True, stop=True), inc="st")
            amm = P.op("t", lambda e, tc=tc: e.matmul(
                a_ps[:], afull[0:1, tc * 128:(tc + 1) * 128],
                C["ones_row_f"][0:1, 0:1], start=True, stop=True), inc="st")
            cp1 = P.op("v", lambda e: e.tensor_copy(rc_sb[:], r_ps[:]), inc="sv",
                       waits=[rmm, amm])
            cp2 = P.op("v", lambda e: e.tensor_copy(ac_sb[:], a_ps[:]), inc="sv")
            sig = chain([
                ("v", lambda e: e.tensor_scalar(tmp8[:], raw_ps[:], rc_sb[:], None,
                                                ALU.mult)),
                ("v", lambda e: e.scalar_tensor_tensor(lg8[:], gwsum_sb[:], ac_sb[:],
                                                       tmp8[:], ALU.mult, ALU.add)),
            ], first_waits=[cp2, mm])
            sg_t = P.op("s", lambda e: e.activation(out=sg8[:], in_=lg8[:],
                                                    func=AF.Sigmoid), inc="ss",
                        waits=[sig])
            last = chain([
                ("v", lambda e: e.tensor_add(sfc8[:], sg8[:], gb_sb[:])),
                ("v", lambda e: e.tensor_add(g4[:], sfc8[:, 0:E:2], sfc8[:, 1:E:2])),
                ("v", lambda e: e.reduce_max(m1c[:], g4[:], axis=mybir.AxisListType.X)),
                ("v", lambda e: e.tensor_scalar(sc4[:], g4[:], m1c[:], None,
                                                ALU.is_ge)),
                ("v", lambda e: e.scalar_tensor_tensor(sc4[:], sc4[:], -1e9, g4[:],
                                                       ALU.mult, ALU.add)),
                ("v", lambda e: e.reduce_max(m2c[:], sc4[:], axis=mybir.AxisListType.X)),
                ("v", lambda e: e.tensor_scalar(sc4[:], g4[:], m2c[:], None,
                                                ALU.is_ge)),
                ("v", lambda e: e.tensor_copy(emask[:, 0:E:2], sc4[:])),
                ("v", lambda e: e.tensor_copy(emask[:, 1:E:2], sc4[:])),
                ("v", lambda e: e.tensor_mul(t8b[:], sfc8[:], emask[:])),
                ("v", lambda e: e.reduce_max(m1c[:], t8b[:], axis=mybir.AxisListType.X)),
                ("v", lambda e: e.tensor_scalar(tmp8[:], t8b[:], m1c[:], None,
                                                ALU.is_ge)),
                ("v", lambda e: e.scalar_tensor_tensor(tmp8[:], tmp8[:], -1e9, t8b[:],
                                                       ALU.mult, ALU.add)),
                ("v", lambda e: e.reduce_max(m2c[:], tmp8[:], axis=mybir.AxisListType.X)),
                ("v", lambda e: e.tensor_add(wsc[:], m1c[:], m2c[:])),
                ("v", lambda e: e.tensor_scalar(wsc[:], wsc[:], 1e-20, None, ALU.add)),
                ("v", lambda e: e.reciprocal(wsc[:], wsc[:])),
                ("v", lambda e: e.tensor_scalar(tmp8[:], t8b[:], m2c[:], None,
                                                ALU.is_ge)),
                ("v", lambda e: e.tensor_mul(t8b[:], t8b[:], tmp8[:])),
                ("v", lambda e, tc=tc: e.tensor_scalar(
                    w_all[:, tc * E:(tc + 1) * E], t8b[:], wsc[:], 2.5,
                    ALU.mult, ALU.mult)),
            ], first_waits=[sg_t])
            tmm = P.op("t", lambda e, tc=tc: e.transpose(
                tp_ps[:], w_all[:, tc * E:(tc + 1) * E], C["identf"][:]),
                inc="st", waits=[last])
            wtr = P.op("s", lambda e, tc=tc: e.activation(
                out=wT_sb[:, tc * 128:(tc + 1) * 128], in_=tp_ps[:], func=AF.Copy),
                inc="ss", waits=[tmm])
            wtr_toks.append(wtr)
            prev = wtr
        wT_stage.store(wT_sb[:], waits=[wtr_toks[-1]])

    with ExitStack() as ectx:
        esb = lambda name, shape, dt: P.sbuf(ectx, name, shape, dt)
        eps = lambda name, shape: P.psum(ectx, name, shape)
        P.fence_dma()
        P.wait("g", wT_stage.st_tok)
        wg_sb = [esb("f0_wg%d" % i, [128, 8 * 512], BF16) for i in range(2)]
        wu_sb = [esb("f0_wu%d" % i, [128, 8 * 512], BF16) for i in range(2)]
        wd_sb = [esb("f0_wd0", [128, 4 * 1024], BF16)] * 2
        gu_sb = esb("f0_gu", [128, 4 * T], BF16)
        sgu = [esb("f0_sgu%d" % i, [128, 512], BF16) for i in range(4)]
        wbc_sb = esb("f0_wbc", [128, T], BF16)
        g_ps = [eps("f0_gps%d" % i, [128, 512]) for i in range(4)]
        u_ps = [eps("f0_ups%d" % i, [128, 512]) for i in range(4)]
        wb_ps = g_ps[0]
        y_ps = [g_ps[1], g_ps[2]]

        def load_expert(e_, buf, guard):
            tk = None
            for c in range(NCH):
                tk = P.dma("g", wg_sb[buf][:, c * 512:(c + 1) * 512],
                           W["w_gate"][e_, c * 128:(c + 1) * 128, :],
                           waits=[guard] if (guard is not None and c == 0) else [],
                           sem="dg%d" % buf)
                tk = P.dma("g", wu_sb[buf][:, c * 512:(c + 1) * 512],
                           W["w_up"][e_, c * 128:(c + 1) * 128, :], sem="dg%d" % buf)
            return [tk]

        def load_expert_wd(e_, guard):
            tk = None
            for ic in range(4):
                tk = P.dma("g", wd_sb[0][:, ic * 1024:(ic + 1) * 1024],
                           W["w_down"][e_, ic * 128:(ic + 1) * 128, :],
                           waits=[guard] if (guard is not None and ic == 0) else [],
                           sem="dg2")
            return [tk]

        wtoks = load_expert(0, 0, None)
        wdtoks = load_expert_wd(0, None)
        prev_use = [None, None]
        gu_done = [None, None]
        for e_ in range(E):
            buf = e_ % 2
            w_row, w_tok, w_b = wT_stage.load_row(e_)
            nw = load_expert(e_ + 1, 1 - buf, gu_done[1 - buf]) if e_ + 1 < E else None
            bts = []
            for s4 in range(4):
                bank = g_ps[0] if s4 % 2 == 0 else u_ps[0]
                bw = [w_tok] if s4 == 0 else ([bts[s4 - 2]] if s4 >= 2 else [])
                bmm = P.op("t", lambda e, s4=s4, w_row=w_row, bank=bank: e.matmul(
                    bank[:, 0:512], C["ones_row_b"][:],
                    w_row[0:1, s4 * 512:(s4 + 1) * 512], start=True, stop=True),
                    inc="st", waits=bw)
                bts.append(P.op("v", lambda e, s4=s4, bank=bank: e.tensor_copy(
                    wbc_sb[:, s4 * 512:(s4 + 1) * 512], bank[:, 0:512]), inc="sv",
                    waits=[bmm]))
            bt = bts[-1]
            wT_stage.set_guard(w_b, bmm)
            P.wait("t", wtoks)
            gu_toks = []
            t2_by_bank = {}
            pend_gu = []

            def flush_gu(bt=bt):
                s4, mms = pend_gu.pop(0)
                for ic in range(4):
                    gmm, umm = mms[ic]
                    st = P.op("s", lambda e, ic=ic: e.activation(
                        out=sgu[ic][:, 0:512], in_=g_ps[ic][:, 0:512], func=AF.Silu),
                        inc="ss",
                        waits=[gmm] + ([t2_by_bank.get(ic)] if t2_by_bank.get(ic) else []))
                    t1 = P.op("v", lambda e, ic=ic: e.tensor_mul(
                        sgu[ic][:, 0:512], sgu[ic][:, 0:512], u_ps[ic][:, 0:512]),
                        inc="sv", waits=[st, umm])
                    t2 = P.op("v", lambda e, ic=ic, s4=s4: e.tensor_mul(
                        gu_sb[:, ic * T + s4 * 512: ic * T + (s4 + 1) * 512],
                        sgu[ic][:, 0:512], wbc_sb[:, s4 * 512:(s4 + 1) * 512]),
                        inc="sv", waits=[t1, bt] if (ic == 0 and s4 == 0) else [t1])
                    gu_toks.append(t2)
                    t2_by_bank[ic] = t2

            P.wait("t", bt)
            for s4 in range(4):
                mms = {}
                for ic in range(4):
                    if t2_by_bank.get(ic) is not None and s4 > 0:
                        P.wait("t", t2_by_bank[ic])
                    gmm = None
                    for c in range(NCH):
                        gmm = P.op("t", lambda e, c=c, ic=ic, s4=s4, buf=buf: e.matmul(
                            g_ps[ic][:, 0:512],
                            wg_sb[buf][:, c * 512 + ic * 128: c * 512 + (ic + 1) * 128],
                            xnT[:, c * T + s4 * 512: c * T + (s4 + 1) * 512],
                            start=(c == 0), stop=(c == NCH - 1)),
                            inc="st" if c == NCH - 1 else None)
                    umm = None
                    for c in range(NCH):
                        umm = P.op("t", lambda e, c=c, ic=ic, s4=s4, buf=buf: e.matmul(
                            u_ps[ic][:, 0:512],
                            wu_sb[buf][:, c * 512 + ic * 128: c * 512 + (ic + 1) * 128],
                            xnT[:, c * T + s4 * 512: c * T + (s4 + 1) * 512],
                            start=(c == 0), stop=(c == NCH - 1)),
                            inc="st" if c == NCH - 1 else None)
                    mms[ic] = (gmm, umm)
                pend_gu.append((s4, mms))
                flush_gu()
            gu_done[buf] = umm
            P.wait("t", wdtoks)
            dtok = None
            for c in range(NCH):
                for s4 in range(4):
                    yb = (c * 4 + s4) % 2
                    mmw = [gu_toks[-1]] if (c == 0 and s4 == 0) else []
                    if dtok is not None and (c * 4 + s4) >= 2:
                        mmw.append(d_ev[yb])
                    mm = None
                    for ic in range(4):
                        mm = P.op("t", lambda e, c=c, s4=s4, ic=ic, buf=buf, yb=yb: e.matmul(
                            y_ps[yb][:, 0:512],
                            wd_sb[buf][:, ic * 1024 + c * 128: ic * 1024 + (c + 1) * 128],
                            gu_sb[:, ic * T + s4 * 512: ic * T + (s4 + 1) * 512],
                            start=(ic == 0), stop=(ic == 3)),
                            waits=mmw if ic == 0 else [],
                            inc="st" if ic == 3 else None)
                    if e_ == 0:
                        ev = P.op("v", lambda e, c=c, s4=s4, yb=yb: e.tensor_copy(
                            y_acc[:, c * T + s4 * 512: c * T + (s4 + 1) * 512],
                            y_ps[yb][:, 0:512]), inc="sv", waits=[mm])
                    else:
                        ev = P.op("v", lambda e, c=c, s4=s4, yb=yb: e.tensor_add(
                            y_acc[:, c * T + s4 * 512: c * T + (s4 + 1) * 512],
                            y_acc[:, c * T + s4 * 512: c * T + (s4 + 1) * 512],
                            y_ps[yb][:, 0:512]), inc="sv", waits=[mm])
                    if c == 0 and s4 == 0:
                        d_ev = {}
                    d_ev[yb] = ev
                    dtok = ev
            prev_use[buf] = dtok
            wtoks = nw
            if e_ + 1 < E:
                wdtoks = load_expert_wd(e_ + 1, dtok)

    with ExitStack() as sctx2:
        ssb = lambda name, shape, dt: P.sbuf(sctx2, name, shape, dt)
        sps = lambda name, shape: P.psum(sctx2, name, shape)
        P.fence_dma()
        swg_sb = [ssb("f0_swg%d" % i, [128, 8 * 128], BF16) for i in range(2)]
        swu_sb = [ssb("f0_swu%d" % i, [128, 8 * 128], BF16) for i in range(2)]
        swd_sb = ssb("f0_swd", [128, 8 * 1024], BF16)
        gus = ssb("f0_gus", [128, 8 * T], BF16)
        sgu2 = ssb("f0_sgu2", [128, 512], BF16)
        g2_ps = sps("f0_g2", [128, 512])
        u2_ps = sps("f0_u2", [128, 512])
        y2_ps = [sps("f0_y2%d" % i, [128, 512]) for i in range(2)]

        def load_sw(ic, buf, guard):
            tk = None
            for c in range(NCH):
                tk = P.dma("g", swg_sb[buf][:, c * 128:(c + 1) * 128],
                           W["sw_gate"][c * 128:(c + 1) * 128, ic * 128:(ic + 1) * 128],
                           waits=[guard] if (guard is not None and c == 0) else [],
                           sem="dg%d" % buf)
                tk = P.dma("g", swu_sb[buf][:, c * 128:(c + 1) * 128],
                           W["sw_up"][c * 128:(c + 1) * 128, ic * 128:(ic + 1) * 128],
                           sem="dg%d" % buf)
            return [tk]

        swd_tok = None
        for ic in range(NCH):
            swd_tok = P.dma("g", swd_sb[:, ic * 1024:(ic + 1) * 1024],
                            W["sw_down"][ic * 128:(ic + 1) * 128, :], sem="dwo0")
        swtoks = load_sw(0, 0, None)
        prev_use2 = [None, None]
        prev_lt = None
        for ic in range(NCH):
            buf = ic % 2
            nw = load_sw(ic + 1, 1 - buf, prev_use2[1 - buf]) if ic + 1 < NCH else None
            P.wait("t", swtoks)
            lt = None
            for s4 in range(4):
                gmm = None
                for c in range(NCH):
                    gmm = P.op("t", lambda e, c=c, s4=s4, buf=buf: e.matmul(
                        g2_ps[:, 0:512],
                        swg_sb[buf][:, c * 128:(c + 1) * 128],
                        xnT[:, c * T + s4 * 512: c * T + (s4 + 1) * 512],
                        start=(c == 0), stop=(c == NCH - 1)),
                        inc="st" if c == NCH - 1 else None)
                umm = None
                for c in range(NCH):
                    umm = P.op("t", lambda e, c=c, s4=s4, buf=buf: e.matmul(
                        u2_ps[:, 0:512],
                        swu_sb[buf][:, c * 128:(c + 1) * 128],
                        xnT[:, c * T + s4 * 512: c * T + (s4 + 1) * 512],
                        start=(c == 0), stop=(c == NCH - 1)),
                        inc="st" if c == NCH - 1 else None)
                st = P.op("s", lambda e: e.activation(
                    out=sgu2[:, 0:512], in_=g2_ps[:, 0:512], func=AF.Silu),
                    inc="ss",
                    waits=[gmm] + ([prev_lt] if prev_lt is not None else []))
                lt = P.op("v", lambda e, ic=ic, s4=s4: e.tensor_mul(
                    gus[:, ic * T + s4 * 512: ic * T + (s4 + 1) * 512],
                    sgu2[:, 0:512], u2_ps[:, 0:512]), inc="sv", waits=[st, umm])
                prev_lt = lt
                P.wait("t", st)
                P.wait("t", lt)
            prev_use2[buf] = lt
            swtoks = nw
        d_ev2 = {}
        for c in range(NCH):
            for s4 in range(4):
                yb = (c * 4 + s4) % 2
                mmw = [lt, swd_tok] if (c == 0 and s4 == 0) else []
                if d_ev2.get(yb) is not None:
                    mmw.append(d_ev2[yb])
                mm = None
                for ic in range(NCH):
                    mm = P.op("t", lambda e, c=c, s4=s4, ic=ic, yb=yb: e.matmul(
                        y2_ps[yb][:, 0:512],
                        swd_sb[:, ic * 1024 + c * 128: ic * 1024 + (c + 1) * 128],
                        gus[:, ic * T + s4 * 512: ic * T + (s4 + 1) * 512],
                        start=(ic == 0), stop=(ic == NCH - 1)),
                        waits=mmw if ic == 0 else [],
                        inc="st" if ic == NCH - 1 else None)
                a1 = P.op("v", lambda e, c=c, s4=s4, yb=yb: e.tensor_add(
                    xT[:, c * T + s4 * 512: c * T + (s4 + 1) * 512],
                    xT[:, c * T + s4 * 512: c * T + (s4 + 1) * 512],
                    y2_ps[yb][:, 0:512]), waits=[mm], inc="sv")
                a2 = P.op("v", lambda e, c=c, s4=s4: e.tensor_add(
                    xT[:, c * T + s4 * 512: c * T + (s4 + 1) * 512],
                    xT[:, c * T + s4 * 512: c * T + (s4 + 1) * 512],
                    y_acc[:, c * T + s4 * 512: c * T + (s4 + 1) * 512]), inc="sv",
                    waits=[a1])
                d_ev2[yb] = a2
    return




def emit_attn_b(P, ctx, C, xT, xnT, W):
    nc = P.nc
    H = 16
    sb = lambda name, shape, dt: P.sbuf(ctx, name, shape, dt)
    psb = lambda name, shape: P.psum(ctx, name, shape)
    P.fence_dma()
    scale = 1.0 / math.sqrt(32.0)
    ln_tok = P.tok("sv")

    Qc = sb("ab_qc", [128, 2 * T], BF16)
    Kc = sb("ab_kc", [128, 2 * T], BF16)
    Vg = sb("ab_vg", [128, 4 * 17 * 65], BF16)
    knull = sb("ab_knull", [128, 2 * 128], BF16)
    Oc = sb("ab_oc", [128, 2 * T], BF16)
    stg = sb("ab_stg", [128, T], BF16)
    rg = sb("ab_rg", [128, T], BF16)
    rp = sb("ab_rp", [128, T], BF16)
    rtmp = sb("ab_rtmp", [128, T], BF16)
    rsw = sb("ab_rsw", [128, T], BF16)
    wq = sb("ab_wq", [128, 8 * 128], BF16)
    wk = sb("ab_wk", [128, 8 * 128], BF16)
    wqg = sb("ab_wqg", [128, 8 * 128], BF16)
    wkg = sb("ab_wkg", [128, 8 * 128], BF16)
    wv = sb("ab_wv", [128, 8 * 256], BF16)
    wo_g = [sb("ab_wo%d" % i, [128, 2 * 1024], BF16) for i in range(2)]
    sums_sb = sb("ab_sums", [65, T], F32)
    expT = [sb("ab_exp%d" % i, [128, 512], BF16) for i in range(3)]
    vnrow = sb("ab_vn", [1, H * 64], BF16)

    pr_ps = [psb("ab_pr%d" % i, [128, 512]) for i in range(2)]
    sc_ps = [psb("ab_sc%d" % i, [128, 512]) for i in range(3)]
    pv_ps = psb("ab_pv", [65, 512])
    nb_ps = psb("ab_nb", [128, 512])
    yo_ps = [psb("ab_yo0", [128, 512]), nb_ps]

    bounce4 = RowBounce(P, ctx, "ab_b", 4)
    bounce = [bounce4] * 4
    vn_tok = P.dma("g", vnrow[:], W["v_null"].rearrange("(a n) -> a n", a=1),
                   sem="da")
    invf = (10000.0 ** (-np.arange(16, dtype=np.float64) / 16.0))
    ang = np.arange(T, dtype=np.float64)[None, :] * invf[np.arange(128) % 16][:, None]
    cosF_np = np.cos(ang).astype(bfdt)
    sinS_np = np.sin(ang).astype(np.float32)
    sinS_np[:64] *= -1.0
    cosF_t = sb("ab_cosF", [128, T], BF16)
    sinS_t = sb("ab_sinS", [128, T], BF16)
    P.dma("y", cosF_t[:], nc.inline_tensor(np.ascontiguousarray(cosF_np),
                                           name=P._uniq("c_cosF")).ap(), sem="drt")
    rope_tok = P.dma("y", sinS_t[:],
                     nc.inline_tensor(np.ascontiguousarray(sinS_np.astype(bfdt)),
                                      name=P._uniq("c_sinS")).ap(), sem="drt")
    C = dict(C)
    C["cosF"], C["sinS"] = cosF_t, sinS_t

    prev_group_done = None
    prev_pe_done = None
    sums_guard = None
    last_exp = [None, None, None]
    exp_guard = {0: None, 1: None, 2: None}
    yev = {}
    sh_ap, sh_tok = C["bias"](-SHIFT)

    proj_done_tok = None
    for g in range(4):
        wtok = None
        guard = [proj_done_tok] if proj_done_tok is not None else []
        for c in range(NCH):
            r0, r1 = c * 128, (c + 1) * 128
            wtok = P.dma("g", wq[:, r0:r1], W["wq_sem"][r0:r1, g * 128:(g + 1) * 128],
                         waits=guard if c == 0 else [], sem="dg0")
            wtok = P.dma("g", wk[:, r0:r1], W["wk_sem"][r0:r1, g * 128:(g + 1) * 128],
                         sem="dg0")
            for wgt, dst in ((W["wq_geo"], wqg), (W["wk_geo"], wkg)):
                s3 = wgt[r0:r1, g * 128:(g + 1) * 128].rearrange(
                    "r (h s) -> r h s", h=4, s=32)
                wtok = P.dma("g", dst[:, c * 128:c * 128 + 64], s3[:, :, 0:16],
                             sem="dg0")
                wtok = P.dma("g", dst[:, c * 128 + 64:(c + 1) * 128], s3[:, :, 16:32],
                             sem="dg0")
            wtok = P.dma("g", wv[:, c * 256:(c + 1) * 256],
                         W["wv"][r0:r1, g * 256:(g + 1) * 256], sem="dg0")
        for c in range(NCH):
            r0, r1 = c * 128, (c + 1) * 128
            P.dma("g", wo_g[g % 2][:, r0:r1],
                  W["wo"][g * 256:g * 256 + 128, r0:r1],
                  waits=[prev_pe_done] if (prev_pe_done is not None and c == 0) else [],
                  sem="dg1")
            P.dma("g", wo_g[g % 2][:, 1024 + r0:1024 + r1],
                  W["wo"][g * 256 + 128:(g + 1) * 256, r0:r1], sem="dg1")
        kz = P.op("g", lambda e: e.memset(knull[:], 0.0),
                  waits=[prev_pe_done] if prev_pe_done is not None else [], inc="sg")
        ktok = None
        for j in range(4):
            h = 4 * g + j
            p, hh = j // 2, j % 2
            P.dma("g", knull[hh * 64:hh * 64 + 32, p * 128:p * 128 + 1],
                  W["k_sem_null"].rearrange("(h s) -> h s", h=H)[h:h + 1, :],
                  waits=[kz] if j == 0 else [], sem="dg1")
            ktok = P.dma("g", knull[hh * 64 + 32:hh * 64 + 64, p * 128:p * 128 + 1],
                         W["k_geo_null"].rearrange("(h s) -> h s", h=H)[h:h + 1, :],
                         sem="dg1")
        P.wait("t", ln_tok)
        if prev_group_done is not None:
            P.wait("t", prev_group_done)
        last_scat = None
        ev_bank = {}
        bank_i = [0]

        def pick_bank():
            b = bank_i[0] % 2
            bank_i[0] += 1
            if ev_bank.get(b) is not None:
                P.wait("t", ev_bank[b])
            return b

        for mat, wmat, dstc in ((0, wq, Qc), (1, wk, Kc)):
            evs = None
            for s4 in range(4):
                b = pick_bank()
                mm = None
                for c in range(NCH):
                    mm = P.op("t", lambda e, s4=s4, c=c, wmat=wmat, b=b: e.matmul(
                        pr_ps[b][:, 0:512],
                        wmat[:, c * 128:(c + 1) * 128],
                        xnT[:, c * T + s4 * 512: c * T + (s4 + 1) * 512],
                        start=(c == 0), stop=(c == NCH - 1)),
                        waits=[wtok, ktok] if (mat == 0 and s4 == 0 and c == 0) else [],
                        inc="st" if c == NCH - 1 else None)
                w = [mm]
                if s4 == 0 and last_scat is not None:
                    w.append(last_scat)
                evs = P.op("v", lambda e, s4=s4, b=b: e.tensor_copy(
                    stg[:, s4 * 512:(s4 + 1) * 512], pr_ps[b][:, 0:512]), inc="sv",
                    waits=w)
                ev_bank[b] = evs
            for j in range(4):
                p, hh = j // 2, j % 2
                last_scat = P.dma("s", dstc[hh * 64:hh * 64 + 32, p * T:(p + 1) * T],
                                  stg[32 * j:32 * j + 32, :],
                                  waits=[evs] if j == 0 else [], sem="ds")
        for mat, wmat, dstc in ((0, wqg, Qc), (1, wkg, Kc)):
            evs = None
            for s4 in range(4):
                b = pick_bank()
                mm = None
                for c in range(NCH):
                    mm = P.op("t", lambda e, s4=s4, c=c, wmat=wmat, b=b: e.matmul(
                        pr_ps[b][:, 0:512],
                        wmat[:, c * 128:(c + 1) * 128],
                        xnT[:, c * T + s4 * 512: c * T + (s4 + 1) * 512],
                        start=(c == 0), stop=(c == NCH - 1)),
                        inc="st" if c == NCH - 1 else None)
                w = [mm]
                if s4 == 0:
                    w.append(last_scat)
                evs = P.op("v", lambda e, s4=s4, b=b: e.tensor_copy(
                    rg[:, s4 * 512:(s4 + 1) * 512], pr_ps[b][:, 0:512]), inc="sv",
                    waits=w)
                ev_bank[b] = evs
            c1 = P.op("v", lambda e: e.tensor_copy(rsw[0:64, :], rg[64:128, :]),
                      waits=[evs], inc="sv")
            c2 = P.op("v", lambda e: e.tensor_copy(rsw[64:128, :], rg[0:64, :]),
                      inc="sv")
            m1 = P.op("v", lambda e: e.tensor_mul(rp[:], rg[:], C["cosF"][:]),
                      inc="sv", waits=[last_scat, rope_tok])
            m2 = P.op("v", lambda e: e.tensor_mul(rtmp[:], rsw[:], C["sinS"][:]),
                      inc="sv", waits=[c2])
            m3 = P.op("v", lambda e: e.tensor_add(rp[:], rp[:], rtmp[:]),
                      inc="sv", waits=[m1, m2])
            for j in range(4):
                p, hh = j // 2, j % 2
                last_scat = P.dma("s", dstc[hh * 64 + 32:hh * 64 + 48, p * T:(p + 1) * T],
                                  rp[16 * j:16 * j + 16, :],
                                  waits=[m3] if j == 0 else [], sem="ds")
                last_scat = P.dma("s", dstc[hh * 64 + 48:hh * 64 + 64, p * T:(p + 1) * T],
                                  rp[64 + 16 * j:64 + 16 * j + 16, :], sem="ds")
        vz = P.op("g", lambda e: e.memset(Vg[:], 0.0),
                  waits=[prev_pe_done] if prev_pe_done is not None else [], inc="sg")
        vone = P.op("g", lambda e: e.memset(Vg[:, 64::65], 1.0), inc="sg",
                    waits=[vz])
        vev = None
        for tc in range(NTCH):
            b = pick_bank()
            mm = None
            for c in range(NCH):
                mm = P.op("t", lambda e, tc=tc, c=c, b=b: e.matmul(
                    pr_ps[b][:, 0:256],
                    xnT[:, c * T + tc * 128: c * T + (tc + 1) * 128],
                    wv[:, c * 256:(c + 1) * 256],
                    start=(c == 0), stop=(c == NCH - 1)),
                    waits=[vone] if (tc == 0 and c == 0) else [],
                    inc="st" if c == NCH - 1 else None)
            for j in range(4):
                vev = P.op("v", lambda e, tc=tc, j=j, b=b: e.tensor_copy(
                    Vg[:, j * 1105 + tc * 65: j * 1105 + tc * 65 + 64],
                    pr_ps[b][:, j * 64:(j + 1) * 64]), waits=[mm] if j == 0 else [],
                    inc="sv" if j == 3 else None)
            ev_bank[b] = vev
        proj_done_tok = P.tok("st")
        vts = None
        for j in range(4):
            h = 4 * g + j
            vts = P.dma("s", Vg[0:1, j * 1105 + 16 * 65: j * 1105 + 16 * 65 + 64],
                        vnrow[0:1, h * 64:(h + 1) * 64],
                        waits=[vn_tok, vev] if j == 0 else [], sem="ds")
        pvsum_tok = None
        hd_prev = None
        for j in range(4):
            p, hh = j // 2, j % 2
            qbase = hh * 64
            for qc in range(4):
                nkt = 4 * qc + 4
                oev_prev = hd_prev
                pend = []

                def flush_one(pend=None, j=j, nkt=nkt, oev_prev=oev_prev):
                    kt, etok, vt_idx = pend.pop(0)
                    pw = [etok]
                    if kt == 0 and oev_prev is not None:
                        pw.append(oev_prev)
                    tk = P.op("t", lambda e, j=j, vt_idx=vt_idx, kt=kt, nkt=nkt: e.matmul(
                        pv_ps[:, 0:512],
                        Vg[:, j * 1105 + vt_idx * 65: j * 1105 + (vt_idx + 1) * 65],
                        expT[kt % 3][:, 0:512], start=(kt == 0), stop=(kt == nkt)),
                        waits=pw, inc="st")
                    exp_guard[kt % 3] = tk
                    return tk

                for kt in range(nkt + 1):
                    par = kt % 3
                    scp = sc_ps[par]
                    is_null = kt == nkt
                    swaits = []
                    if g == 0 and j == 0 and qc == 0 and kt == 0:
                        swaits.append(sh_tok)
                    if j == 0 and qc == 0 and kt == 0:
                        swaits.append(vts)
                    if last_exp[par] is not None:
                        swaits.append(last_exp[par])
                    if is_null:
                        smm = P.op("t", lambda e, p=p, qbase=qbase, qc=qc, scp=scp: e.matmul(
                            scp[:, 0:512],
                            knull[qbase:qbase + 64, p * 128:(p + 1) * 128],
                            Qc[qbase:qbase + 64, p * T + qc * 512: p * T + (qc + 1) * 512],
                            start=True, stop=True), waits=swaits, inc="st")
                    else:
                        smm = P.op("t", lambda e, p=p, qbase=qbase, kt=kt, qc=qc, scp=scp: e.matmul(
                            scp[:, 0:512],
                            Kc[qbase:qbase + 64, p * T + kt * 128: p * T + (kt + 1) * 128],
                            Qc[qbase:qbase + 64, p * T + qc * 512: p * T + (qc + 1) * 512],
                            start=True, stop=True), waits=swaits, inc="st")
                    ew = [smm]
                    if exp_guard[par] is not None:
                        ew.append(exp_guard[par])
                    etok = P.op("s", lambda e, par=par, scp=scp: e.activation(
                        out=expT[par][:, 0:512], in_=scp[:, 0:512], func=AF.Exp,
                        scale=scale, bias=sh_ap), inc="ss", waits=ew)
                    if is_null:
                        etok = P.op("v", lambda e, par=par: e.tensor_scalar(
                            expT[par][:, 0:512], expT[par][:, 0:512],
                            C["nullmask"][:], None, ALU.mult), inc="sv", waits=[etok])
                    elif kt >= 4 * qc:
                        v_ = kt - 4 * qc
                        etok = P.op("v", lambda e, par=par, v_=v_: e.tensor_mul(
                            expT[par][:, 0:512], expT[par][:, 0:512],
                            C["maskw"][:, v_ * 512:(v_ + 1) * 512]), inc="sv",
                            waits=[etok])
                    last_exp[par] = etok
                    pend.append((kt, etok, 16 if is_null else kt))
                    if len(pend) > 2:
                        pvsum_tok = flush_one(pend)
                while pend:
                    pvsum_tok = flush_one(pend)
                ow = [pvsum_tok]
                if g > 0 and j == 0 and qc == 0:
                    ow.append(prev_pe_done)
                oev = P.op("v", lambda e, p=p, qbase=qbase, qc=qc: e.tensor_copy(
                    Oc[qbase:qbase + 64, p * T + qc * 512: p * T + (qc + 1) * 512],
                    pv_ps[0:64, 0:512]), inc="sv", waits=ow)
                sw = [pvsum_tok]
                if qc == 0 and sums_guard is not None:
                    sw.append(sums_guard)
                sev = P.op("s", lambda e, qc=qc: e.activation(
                    out=sums_sb[64:65, qc * 512:(qc + 1) * 512],
                    in_=pv_ps[64:65, 0:512], func=AF.Copy), inc="ss", waits=sw)
                hd_prev = oev
                P.wait("t", oev)
                P.wait("t", sev)
            sums_guard = bounce[g].write(sums_sb[64:65, :], j, waits=[sev])
        bounce[g].recip_all()
        norm_last = None
        nbank2 = [nb_ps, yo_ps[0]]
        nb_guard = {}
        for j in range(4):
            p, hh = j // 2, j % 2
            qbase = hh * 64
            rap, rtok, rb = bounce[g].load_row(j)
            bmm = None
            for s4 in range(4):
                nb = (j * 4 + s4) % 2
                w = [rtok] if s4 == 0 else []
                if nb_guard.get(nb) is not None:
                    w.append(nb_guard[nb])
                bmm = P.op("t", lambda e, s4=s4, rap=rap, nb=nb: e.matmul(
                    nbank2[nb][:, 0:512], C["ones_row_b"][:],
                    rap[0:1, s4 * 512:(s4 + 1) * 512], start=True, stop=True),
                    inc="st", waits=w)
                norm_last = P.op("v", lambda e, p=p, qbase=qbase, s4=s4, nb=nb: e.tensor_mul(
                    Oc[qbase:qbase + 64, p * T + s4 * 512: p * T + (s4 + 1) * 512],
                    Oc[qbase:qbase + 64, p * T + s4 * 512: p * T + (s4 + 1) * 512],
                    nbank2[nb][0:64, 0:512]), inc="sv", waits=[bmm])
                nb_guard[nb] = norm_last
            bounce[g].set_guard(rb, bmm)
        mm = None
        ydone = None
        for c in range(NCH):
            for s4 in range(4):
                yb = (c * 4 + s4) % 2
                mmw = [norm_last] if (c == 0 and s4 == 0) else []
                if yev.get(yb) is not None:
                    mmw.append(yev[yb])
                for p in range(2):
                    mm = P.op("t", lambda e, c=c, s4=s4, p=p, yb=yb, g=g: e.matmul(
                        yo_ps[yb][:, 0:512],
                        wo_g[g % 2][:, p * 1024 + c * 128: p * 1024 + (c + 1) * 128],
                        Oc[:, p * T + s4 * 512: p * T + (s4 + 1) * 512],
                        start=(p == 0), stop=(p == 1)),
                        waits=mmw if p == 0 else [], inc="st" if p == 1 else None)
                ydone = P.op("v", lambda e, c=c, s4=s4, yb=yb: e.tensor_add(
                    xT[:, c * T + s4 * 512: c * T + (s4 + 1) * 512],
                    xT[:, c * T + s4 * 512: c * T + (s4 + 1) * 512],
                    yo_ps[yb][:, 0:512]), inc="sv", waits=[mm])
                yev[yb] = ydone
        prev_group_done = ydone
        prev_pe_done = mm
    return




def emit_l1_attn_a(P, ctx, C, xT, xnT, W):
    nc = P.nc
    H, DK, DC = 16, 64, 256
    sb = lambda name, shape, dt: P.sbuf(ctx, name, shape, dt)
    psb = lambda name, shape: P.psum(ctx, name, shape)
    P.fence_dma()
    scale = 1.0 / math.sqrt(DK)
    ln_tok = P.tok("sv")

    cq = sb("ma_cq", [128, 2 * T], BF16)
    ckv = sb("ma_ckv", [128, 2 * T], BF16)
    wdq = sb("ma_wdq", [128, 8 * 256], BF16)
    wdkv = sb("ma_wdkv", [128, 8 * 256], BF16)
    wuq = sb("ma_wuq", [128, 2 * 1024], BF16)
    wuk = sb("ma_wuk", [128, 2 * 1024], BF16)
    wuv = sb("ma_wuv", [128, 2 * 1024], BF16)
    wo_sb = [sb("ma_wo%d" % i, [128, 8 * 128], BF16) for i in range(2)]
    Qp = [sb("ma_qp0", [128, T], BF16)] * 2
    Kp = [sb("ma_kp0", [128, T], BF16)] * 2
    Vp = [sb("ma_vp0", [128, 2 * 16 * 65], BF16)] * 2
    O_all = sb("ma_oall", [128, 8 * T], BF16)
    sums_sb = sb("ma_sums", [65, T], F32)
    expT = [sb("ma_exp%d" % i, [128, 512], BF16) for i in range(3)]

    pr_ps = [psb("ma_pr%d" % i, [128, 512]) for i in range(2)]
    sc_ps = [psb("ma_sc%d" % i, [128, 512]) for i in range(3)]
    pv_ps = psb("ma_pv", [65, 512])
    nb_ps = psb("ma_nb", [128, 512])
    yo_ps = [psb("ma_yo0", [128, 512]), nb_ps]

    bounce = RowBounce(P, ctx, "ma_b", H)
    sh_ap, sh_tok = C["bias"](-SHIFT)

    wt = None
    for c in range(NCH):
        r0, r1 = c * 128, (c + 1) * 128
        wt = P.dma("g", wdq[:, c * 256:(c + 1) * 256], W["W_dq"][r0:r1, :], sem="dg0")
        wt = P.dma("g", wdkv[:, c * 256:(c + 1) * 256], W["W_dkv"][r0:r1, :], sem="dg0")
    for c2 in range(2):
        r0, r1 = c2 * 128, (c2 + 1) * 128
        wt = P.dma("g", wuq[:, c2 * 1024:(c2 + 1) * 1024], W["W_uq"][r0:r1, :], sem="dg0")
        wt = P.dma("g", wuk[:, c2 * 1024:(c2 + 1) * 1024], W["W_uk"][r0:r1, :], sem="dg0")
        wt = P.dma("g", wuv[:, c2 * 1024:(c2 + 1) * 1024], W["W_uv"][r0:r1, :], sem="dg0")

    P.wait("t", ln_tok)
    P.wait("t", wt)
    ev = None
    ev_bank = {}
    bank_i = [0]

    def pick_bank():
        b = bank_i[0] % 2
        bank_i[0] += 1
        if ev_bank.get(b) is not None:
            P.wait("t", ev_bank[b])
        return b

    for dst, wmat in ((cq, wdq), (ckv, wdkv)):
        for c2 in range(2):
            for s4 in range(4):
                b = pick_bank()
                mm = None
                for c in range(NCH):
                    mm = P.op("t", lambda e, c=c, c2=c2, s4=s4, wmat=wmat, b=b: e.matmul(
                        pr_ps[b][:, 0:512],
                        wmat[:, c * 256 + c2 * 128: c * 256 + (c2 + 1) * 128],
                        xnT[:, c * T + s4 * 512: c * T + (s4 + 1) * 512],
                        start=(c == 0), stop=(c == NCH - 1)),
                        inc="st" if c == NCH - 1 else None)
                ev = P.op("v", lambda e, dst=dst, c2=c2, s4=s4, b=b: e.tensor_copy(
                    dst[:, c2 * T + s4 * 512: c2 * T + (s4 + 1) * 512],
                    pr_ps[b][:, 0:512]), inc="sv", waits=[mm])
                ev_bank[b] = ev
    cq_done = ev

    prev_pair_pe = {0: None, 1: None}
    prev_pe_done = None
    sums_guard = None
    last_exp = [None, None, None]
    exp_guard = {0: None, 1: None, 2: None}
    oev_by_head = {}
    for hp in range(8):
        buf = 0
        if prev_pe_done is not None:
            P.wait("t", prev_pe_done)
        for dst, wmat in ((Qp[buf], wuq), (Kp[buf], wuk)):
            for s4 in range(4):
                b = pick_bank()
                mm = None
                for c2 in range(2):
                    mm = P.op("t", lambda e, c2=c2, s4=s4, hp=hp, wmat=wmat, b=b: e.matmul(
                        pr_ps[b][:, 0:512],
                        wmat[:, c2 * 1024 + hp * 128: c2 * 1024 + (hp + 1) * 128],
                        (cq if wmat is wuq else ckv)[:, c2 * T + s4 * 512: c2 * T + (s4 + 1) * 512],
                        start=(c2 == 0), stop=(c2 == 1)),
                        waits=[cq_done] if (s4 == 0 and c2 == 0 and hp == 0) else [],
                        inc="st" if c2 == 1 else None)
                ev = P.op("v", lambda e, dst=dst, s4=s4, b=b: e.tensor_copy(
                    dst[:, s4 * 512:(s4 + 1) * 512], pr_ps[b][:, 0:512]), inc="sv",
                    waits=[mm])
                ev_bank[b] = ev
        vz = P.op("g", lambda e, buf=buf: e.memset(Vp[buf][:], 0.0),
                  waits=[prev_pe_done] if prev_pe_done is not None else [],
                  inc="sg")
        vone = P.op("g", lambda e, buf=buf: e.memset(Vp[buf][:, 64::65], 1.0),
                    inc="sg", waits=[vz])
        vev = None
        for tc in range(NTCH):
            b = pick_bank()
            mm = None
            for c2 in range(2):
                mm = P.op("t", lambda e, tc=tc, c2=c2, hp=hp, b=b: e.matmul(
                    pr_ps[b][:, 0:128],
                    ckv[:, c2 * T + tc * 128: c2 * T + (tc + 1) * 128],
                    wuv[:, c2 * 1024 + hp * 128: c2 * 1024 + (hp + 1) * 128],
                    start=(c2 == 0), stop=(c2 == 1)),
                    waits=[vone] if (tc == 0 and c2 == 0) else [],
                    inc="st" if c2 == 1 else None)
            for j in range(2):
                vev = P.op("v", lambda e, tc=tc, j=j, buf=buf, b=b: e.tensor_copy(
                    Vp[buf][:, j * 1040 + tc * 65: j * 1040 + tc * 65 + 64],
                    pr_ps[b][:, j * 64:(j + 1) * 64]), waits=[mm] if j == 0 else [],
                    inc="sv" if j == 1 else None)
            ev_bank[b] = vev
        pvsum_tok = None
        hd_prev = None
        for j in range(2):
            h = 2 * hp + j
            qbase = j * 64
            for qc in range(4):
                nkt = 4 * qc + 4
                oev_prev = hd_prev
                pend = []

                def flush_one(pend=None, j=j, nkt=nkt, oev_prev=oev_prev, buf=buf):
                    kt, etok = pend.pop(0)
                    pw = [etok]
                    if kt == 0 and oev_prev is not None:
                        pw.append(oev_prev)
                    tk = P.op("t", lambda e, j=j, kt=kt, nkt=nkt, buf=buf: e.matmul(
                        pv_ps[:, 0:512],
                        Vp[buf][:, j * 1040 + kt * 65: j * 1040 + (kt + 1) * 65],
                        expT[kt % 3][:, 0:512], start=(kt == 0), stop=(kt == nkt - 1)),
                        waits=pw, inc="st")
                    exp_guard[kt % 3] = tk
                    return tk

                for kt in range(nkt):
                    par = kt % 3
                    scp = sc_ps[par]
                    swaits = []
                    if hp == 0 and j == 0 and qc == 0 and kt == 0:
                        swaits.append(sh_tok)
                    if j == 0 and qc == 0 and kt == 0:
                        swaits.append(vev)
                    if last_exp[par] is not None:
                        swaits.append(last_exp[par])
                    smm = P.op("t", lambda e, buf=buf, qbase=qbase, kt=kt, qc=qc, scp=scp: e.matmul(
                        scp[:, 0:512],
                        Kp[buf][qbase:qbase + 64, kt * 128:(kt + 1) * 128],
                        Qp[buf][qbase:qbase + 64, qc * 512:(qc + 1) * 512],
                        start=True, stop=True), waits=swaits, inc="st")
                    ew = [smm]
                    if exp_guard[par] is not None:
                        ew.append(exp_guard[par])
                    etok = P.op("s", lambda e, par=par, scp=scp: e.activation(
                        out=expT[par][:, 0:512], in_=scp[:, 0:512], func=AF.Exp,
                        scale=scale, bias=sh_ap), inc="ss", waits=ew)
                    if kt >= 4 * qc:
                        v_ = kt - 4 * qc
                        etok = P.op("v", lambda e, par=par, v_=v_: e.tensor_mul(
                            expT[par][:, 0:512], expT[par][:, 0:512],
                            C["maskw"][:, v_ * 512:(v_ + 1) * 512]), inc="sv",
                            waits=[etok])
                    last_exp[par] = etok
                    pend.append((kt, etok))
                    if len(pend) > 2:
                        pvsum_tok = flush_one(pend)
                while pend:
                    pvsum_tok = flush_one(pend)
                ow = [pvsum_tok]
                oev = P.op("v", lambda e, hp=hp, qbase=qbase, qc=qc: e.tensor_copy(
                    O_all[qbase:qbase + 64, hp * T + qc * 512: hp * T + (qc + 1) * 512],
                    pv_ps[0:64, 0:512]), inc="sv", waits=ow)
                sw = [pvsum_tok]
                if qc == 0 and sums_guard is not None:
                    sw.append(sums_guard)
                sev = P.op("s", lambda e, qc=qc: e.activation(
                    out=sums_sb[64:65, qc * 512:(qc + 1) * 512],
                    in_=pv_ps[64:65, 0:512], func=AF.Copy), inc="ss", waits=sw)
                hd_prev = oev
                oev_by_head.setdefault(h, []).append(oev)
                P.wait("t", oev)
                P.wait("t", sev)
            sums_guard = bounce.write(sums_sb[64:65, :], h, waits=[sev])
        prev_pe_done = pvsum_tok
    bounce.recip_all()
    norm_last = None
    nbank3 = [nb_ps, yo_ps[0]]
    nb_guard = {}
    for h in range(H):
        hp, j = h // 2, h % 2
        qbase = j * 64
        rap, rtok, rb = bounce.load_row(h)
        bmm = None
        for s4 in range(4):
            nb = (h * 4 + s4) % 2
            w = [rtok] if s4 == 0 else []
            if nb_guard.get(nb) is not None:
                w.append(nb_guard[nb])
            bmm = P.op("t", lambda e, s4=s4, rap=rap, nb=nb: e.matmul(
                nbank3[nb][:, 0:512], C["ones_row_b"][:],
                rap[0:1, s4 * 512:(s4 + 1) * 512], start=True, stop=True),
                inc="st", waits=w)
            norm_last = P.op("v", lambda e, hp=hp, qbase=qbase, s4=s4, nb=nb: e.tensor_mul(
                O_all[qbase:qbase + 64, hp * T + s4 * 512: hp * T + (s4 + 1) * 512],
                O_all[qbase:qbase + 64, hp * T + s4 * 512: hp * T + (s4 + 1) * 512],
                nbank3[nb][0:64, 0:512]), inc="sv", waits=[bmm, oev_by_head[h][s4]])
            nb_guard[nb] = norm_last
        bounce.set_guard(rb, bmm)
    def load_wo(c, wbuf, guard):
        tk = None
        for pc in range(8):
            tk = P.dma("g", wo_sb[wbuf][:, pc * 128:(pc + 1) * 128],
                       W["W_o"][pc * 128:(pc + 1) * 128, c * 128:(c + 1) * 128],
                       waits=[guard] if (guard is not None and pc == 0) else [],
                       sem="dwo%d" % wbuf)
        return [tk]

    wo_toks = load_wo(0, 0, None)
    wo_guard = [None, None]
    for c in range(NCH):
        wbuf = c % 2
        nwo = load_wo(c + 1, 1 - wbuf, wo_guard[1 - wbuf]) if c + 1 < NCH else None
        P.wait("t", wo_toks)
        mmtok = None
        for s4 in range(4):
            for pc in range(8):
                mmtok = P.op("t", lambda e, s4=s4, pc=pc, wbuf=wbuf: e.matmul(
                    yo_ps[s4 % 2][:, 0:512],
                    wo_sb[wbuf][:, pc * 128:(pc + 1) * 128],
                    O_all[:, pc * T + s4 * 512: pc * T + (s4 + 1) * 512],
                    start=(pc == 0), stop=(pc == 7)),
                    waits=[norm_last] if (c == 0 and s4 == 0 and pc == 0) else [],
                    inc="st" if pc == 7 else None)
            rt = P.op("v", lambda e, c=c, s4=s4: e.tensor_add(
                xT[:, c * T + s4 * 512: c * T + (s4 + 1) * 512],
                xT[:, c * T + s4 * 512: c * T + (s4 + 1) * 512],
                yo_ps[s4 % 2][:, 0:512]), inc="sv", waits=[mmtok])
            P.wait("t", rt)
        wo_guard[wbuf] = mmtok
        wo_toks = nwo
    return




def emit_l1_ffn(P, ctx, C, xT, xnT, W):
    nc = P.nc
    NH, HS, GS, HD = 16, 512, 128, 64
    sb = lambda name, shape, dt: P.sbuf(ctx, name, shape, dt)

    def chain(ops, first_waits=()):
        tok = None
        for eng, fn in ops:
            w = [tok] if tok is not None else list(first_waits)
            tok = P.op(eng, fn, inc={"v": "sv", "s": "ss", "t": "st"}[eng], waits=w)
        return tok

    w_all = sb("f1_wall", [128, 16 * NH], F32)
    wT_stage = RowStage(P, ctx, "f1_ws", NH)

    with ExitStack() as gctx:
        gsb = lambda name, shape, dt: P.sbuf(gctx, name, shape, dt)
        gps = lambda name, shape: P.psum(gctx, name, shape)
        rfull = gsb("f1_rf", [1, T], F32)
        afull = gsb("f1_af", [1, T], F32)
        ln_tok = emit_ln(P, gctx, C, xT, xnT, ra_rows=(rfull, afull))
        P.fence_dma()
        wr_sb = gsb("f1_wr", [128, 1024], F32)
        wrT_sb = gsb("f1_wrT", [128, 8 * 128], F32)
        cent_sb = gsb("f1_cent", [16, 128], F32)
        cT_sb = gsb("f1_cT", [128, 16], F32)
        csq = gsb("f1_csq", [16, 128], F32)
        Sc = gsb("f1_Sc", [16, 1], F32)
        lc = gsb("f1_lc", [16, 1], F32)
        rn_sb = gsb("f1_rn", [16, 1], F32)
        rne = gsb("f1_rne", [16, 1], F32)
        et_sb = gsb("f1_et", [16, 1], F32)
        tmp_sb = gsb("f1_tmp", [1, 1], F32)
        wrsum_row = gsb("f1_wsr", [1, 128], F32)
        wrsum_col = gsb("f1_wsc", [128, 1], F32)
        z32 = gsb("f1_z32", [128, T], F32)
        ztmp = gsb("f1_ztmp", [128, 512], F32)
        zsq = gsb("f1_zsq", [128, 512], BF16)
        lrow = gsb("f1_lrow", [1, 512], F32)
        rz_row = gsb("f1_rz", [1, T], F32)
        lgT = gsb("f1_lgT", [16, T], F32)
        sb16 = gsb("f1_sb16", [16, 512], F32)
        rt16 = gsb("f1_rt16", [128, 16], F32)
        ex16 = gsb("f1_ex16", [128, 16], F32)
        mx1 = gsb("f1_mx1", [128, 1], F32)
        nmx = gsb("f1_nmx", [128, 1], F32)
        sm1 = gsb("f1_sm1", [128, 1], F32)
        mx8 = gsb("f1_mx8", [128, 8], F32)
        sel = gsb("f1_sel", [128, 16], F32)
        raw_ps = gps("f1_raw", [128, 512])
        rb_ps = gps("f1_rb", [128, 512])
        ab_ps = gps("f1_ab", [128, 512])
        sz_ps = gps("f1_sz", [1, 512])
        cz_ps = gps("f1_cz", [16, 512])
        rzb_ps = gps("f1_rzb", [16, 512])
        tp_ps = gps("f1_tp", [128, 16])
        tp2_ps = gps("f1_tp2", [16, 128])
        ws_ps = tp2_ps
        wc_ps = tp_ps

        for c in range(NCH):
            P.dma("y", wr_sb[:, c * 128:(c + 1) * 128],
                  W["w_router"][:, c * 128:(c + 1) * 128], sem="df")
        P.dma("y", cent_sb[:], W["head_centroids"], sem="df")
        ldt = P.dma("y", tmp_sb[:], W["temperature"].rearrange("(a b) -> a b", a=1),
                    sem="df")
        trt = None
        for c in range(NCH):
            tmm = P.op("t", lambda e, c=c: e.transpose(
                tp_ps[:, 0:16] if False else rb_ps[:, 0:128],
                wr_sb[:, c * 128:(c + 1) * 128], C["identf"][:]),
                inc="st", waits=[ldt] if c == 0 else [trt])
            trt = P.op("v", lambda e, c=c: e.tensor_copy(
                wrT_sb[:, c * 128:(c + 1) * 128], rb_ps[:, 0:128]), inc="sv",
                waits=[tmm])
            P.wait("t", trt)
        smm = None
        for c in range(NCH):
            smm = P.op("t", lambda e, c=c: e.matmul(
                ws_ps[0:1, 0:128], C["ones_col_f"][:],
                wrT_sb[:, c * 128:(c + 1) * 128],
                start=(c == 0), stop=(c == NCH - 1)),
                waits=[trt] if c == 0 else [], inc="st" if c == NCH - 1 else None)
        sev = P.op("s", lambda e: e.activation(out=wrsum_row[:], in_=ws_ps[0:1, 0:128],
                                               func=AF.Copy), inc="ss", waits=[smm])
        cmm = P.op("t", lambda e: e.matmul(wc_ps[:, 0:1], wrsum_row[:],
                                           C["ones_row_f"][0:1, 0:1],
                                           start=True, stop=True), inc="st",
                   waits=[sev])
        wcv = P.op("v", lambda e: e.tensor_copy(wrsum_col[:], wc_ps[:, 0:1]), inc="sv",
                   waits=[cmm])
        ctm = P.op("t", lambda e: e.transpose(tp_ps[:, 0:16], cent_sb[:],
                                              C["identf"][0:16, 0:16]), inc="st",
                   waits=[wcv])
        ctv = P.op("v", lambda e: e.tensor_copy(cT_sb[:], tp_ps[:, 0:16]), inc="sv",
                   waits=[ctm])
        cn = chain([
            ("v", lambda e: e.tensor_mul(csq[:], cent_sb[:], cent_sb[:])),
            ("v", lambda e: e.reduce_sum(Sc[:], csq[:], axis=mybir.AxisListType.X)),
        ], first_waits=[ldt])
        cl = chain([
            ("s", lambda e: e.activation(out=lc[:], in_=Sc[:], func=AF.Ln)),
            ("s", lambda e: e.activation(out=rn_sb[:], in_=lc[:], func=AF.Exp,
                                         scale=-0.5)),
            ("s", lambda e: e.activation(out=tmp_sb[:], in_=tmp_sb[:], func=AF.Exp)),
        ], first_waits=[cn])
        etm = P.op("t", lambda e: e.matmul(tp_ps[0:16, 0:1], C["ones_row_f"][0:1, 0:16],
                                           tmp_sb[:], start=True, stop=True),
                   inc="st", waits=[cl, ctv])
        rne_t = chain([
            ("v", lambda e: e.tensor_copy(et_sb[:], tp_ps[0:16, 0:1])),
            ("v", lambda e: e.tensor_mul(rne[:], rn_sb[:], et_sb[:])),
            ("v", lambda e: e.tensor_scalar(rne[:], rne[:],
                                            float(1.0 / math.sqrt(GS)), None,
                                            ALU.mult)),
        ], first_waits=[etm])
        P.wait("t", rne_t)
        P.wait("t", ln_tok)
        zev = None
        for s4 in range(4):
            mm = None
            for c in range(NCH):
                mm = P.op("t", lambda e, c=c, s4=s4: e.matmul(
                    raw_ps[:, 0:512],
                    wrT_sb[:, c * 128:(c + 1) * 128],
                    xT[:, c * T + s4 * 512: c * T + (s4 + 1) * 512],
                    start=(c == 0), stop=(c == NCH - 1)),
                    inc="st" if c == NCH - 1 else None)
            b1 = P.op("t", lambda e, s4=s4: e.matmul(
                rb_ps[:, 0:512], C["ones_row_f"][:],
                rfull[0:1, s4 * 512:(s4 + 1) * 512], start=True, stop=True))
            b2 = P.op("t", lambda e, s4=s4: e.matmul(
                ab_ps[:, 0:512], C["ones_row_f"][:],
                afull[0:1, s4 * 512:(s4 + 1) * 512], start=True, stop=True),
                inc="st")
            zz = chain([
                ("v", lambda e: e.tensor_copy(ztmp[:], raw_ps[:, 0:512])),
                ("v", lambda e: e.tensor_mul(ztmp[:], ztmp[:], rb_ps[:, 0:512])),
                ("v", lambda e, s4=s4: e.scalar_tensor_tensor(
                    z32[:, s4 * 512:(s4 + 1) * 512], ab_ps[:, 0:512], wrsum_col[:],
                    ztmp[:], ALU.mult, ALU.add)),
                ("v", lambda e, s4=s4: e.tensor_mul(
                    zsq[:], z32[:, s4 * 512:(s4 + 1) * 512],
                    z32[:, s4 * 512:(s4 + 1) * 512])),
            ], first_waits=[mm, b2, rne_t if s4 == 0 else None])
            zmm = P.op("t", lambda e: e.matmul(
                sz_ps[:, 0:512], C["ones_col_b"][:], zsq[:],
                start=True, stop=True), inc="st", waits=[zz])
            zev = chain([
                ("s", lambda e: e.activation(out=lrow[:], in_=sz_ps[:], func=AF.Ln)),
                ("s", lambda e, s4=s4: e.activation(
                    out=rz_row[0:1, s4 * 512:(s4 + 1) * 512], in_=lrow[:],
                    func=AF.Exp, scale=-0.5)),
            ], first_waits=[zmm])
            P.wait("t", zev)
        lg_ev = None
        for s4 in range(4):
            mm = P.op("t", lambda e, s4=s4: e.matmul(
                cz_ps[:, 0:512], cT_sb[:], z32[:, s4 * 512:(s4 + 1) * 512],
                start=True, stop=True), inc="st",
                waits=[zev] if s4 == 0 else [lg_ev])
            bz = P.op("t", lambda e, s4=s4: e.matmul(
                rzb_ps[:, 0:512], C["ones_row_f"][0:1, 0:16],
                rz_row[0:1, s4 * 512:(s4 + 1) * 512], start=True, stop=True),
                inc="st")
            lg_ev = chain([
                ("v", lambda e: e.tensor_scalar(sb16[:], cz_ps[:, 0:512], rne[:],
                                                None, ALU.mult)),
                ("v", lambda e, s4=s4: e.tensor_mul(
                    lgT[:, s4 * 512:(s4 + 1) * 512], sb16[:], rzb_ps[:, 0:512])),
            ], first_waits=[mm, bz])
            P.wait("t", lg_ev)
        wtr = None
        for tc in range(NTCH):
            tmm = P.op("t", lambda e, tc=tc: e.transpose(
                tp_ps[:, 0:16], lgT[:, tc * 128:(tc + 1) * 128],
                C["identf"][0:16, 0:16]), inc="st",
                waits=[lg_ev] if tc == 0 else [wtr])
            cp = P.op("v", lambda e: e.tensor_copy(rt16[:], tp_ps[:, 0:16]),
                      inc="sv", waits=[tmm])
            r1 = chain([
                ("v", lambda e: e.reduce_max(mx1[:], rt16[:],
                                             axis=mybir.AxisListType.X)),
                ("v", lambda e: e.tensor_scalar(nmx[:], mx1[:], -1.0, None,
                                                ALU.mult)),
            ], first_waits=[cp])
            ex_t = P.op("s", lambda e: e.activation(out=ex16[:], in_=rt16[:],
                                                    func=AF.Exp, bias=nmx[:]),
                        inc="ss", waits=[r1])
            wtr_prev = wtr
            wtr = chain([
                ("v", lambda e: e.reduce_sum(sm1[:], ex16[:],
                                             axis=mybir.AxisListType.X)),
                ("v", lambda e: e.reciprocal(sm1[:], sm1[:])),
                ("v", lambda e: e.tensor_scalar(ex16[:], ex16[:], sm1[:], None,
                                                ALU.mult)),
                ("v", lambda e: e.max(out=mx8[:], in_=ex16[:])),
                ("v", lambda e: e.tensor_scalar(sel[:], ex16[:], mx8[:, 1:2], None,
                                                ALU.is_ge)),
                ("v", lambda e, tc=tc: e.tensor_mul(
                    w_all[:, tc * NH:(tc + 1) * NH], ex16[:], sel[:])),
            ], first_waits=[ex_t])
            tm2 = P.op("t", lambda e, tc=tc: e.transpose(
                tp2_ps[:], w_all[:, tc * NH:(tc + 1) * NH],
                C["identf"][:]), inc="st", waits=[wtr])
            wtr = P.op("s", lambda e, tc=tc: e.activation(
                out=lgT[:, tc * 128:(tc + 1) * 128], in_=tp2_ps[:],
                func=AF.Copy), inc="ss", waits=[tm2])
        wT_stage.store(lgT[:], waits=[wtr])

    with ExitStack() as ectx:
        esb = lambda name, shape, dt: P.sbuf(ectx, name, shape, dt)
        eps = lambda name, shape: P.psum(ectx, name, shape)
        P.fence_dma()
        P.wait("g", wT_stage.st_tok)
        wie = [esb("f1_wie%d" % i, [128, 8 * 512], BF16) for i in range(2)]
        woe = [esb("f1_woe%d" % i, [128, 4 * 128], BF16) for i in range(2)]
        h_sb = esb("f1_h", [128, 4 * T], BF16)
        osb = esb("f1_osb", [128, 512], BF16)
        h_ps = [eps("f1_hps%d" % i, [128, 512]) for i in range(4)]
        o_ps = eps("f1_ops", [128, 512])
        wb_ps = [eps("f1_wbps%d" % i, [128, 512]) for i in range(2)]
        wz = [P.op("g", lambda e, i=i: e.memset(woe[i][:], 0.0), inc="sg")
              for i in range(2)]

        def load_head(h, buf, guard):
            tk = None
            g0 = [guard] if guard is not None else []
            if h < 2:
                g0 = g0 + [wz[buf]]
            for c in range(NCH):
                tk = P.dma("g", wie[buf][:, c * 512:(c + 1) * 512],
                           W["input_experts"][h, c * 128:(c + 1) * 128, :],
                           waits=g0 if c == 0 else [], sem="dg%d" % buf)
            for ic in range(4):
                tk = P.dma("g", woe[buf][:, ic * 128 + 64:(ic + 1) * 128],
                           W["output_experts"][h, ic * 128:(ic + 1) * 128, :],
                           sem="dg%d" % buf)
            return [tk]

        wtoks = load_head(0, 0, None)
        prev_use = [None, None]
        prev_omm = None
        prev_mvs = [None, None]
        for h in range(NH):
            buf = h % 2
            w_row, w_tok, w_b = wT_stage.load_row(h)
            nw = load_head(h + 1, 1 - buf, prev_use[1 - buf]) if h + 1 < NH else None
            P.wait("t", wtoks)
            ge_last = None
            ge_by_bank = {}
            pend_h = []

            def flush_h():
                s4, mms = pend_h.pop(0)
                ge = None
                for ic in range(4):
                    gw = [mms[ic]]
                    if ic == 0 and s4 == 0 and prev_omm is not None:
                        gw.append(prev_omm)
                    ge = P.op("s", lambda e, ic=ic, s4=s4: e.activation(
                        out=h_sb[:, ic * T + s4 * 512: ic * T + (s4 + 1) * 512],
                        in_=h_ps[ic][:, 0:512], func=AF.Gelu), inc="ss", waits=gw)
                    ge_by_bank[ic] = ge
                return ge

            for s4 in range(4):
                mms = {}
                for ic in range(4):
                    if ge_by_bank.get(ic) is not None and s4 > 0:
                        P.wait("t", ge_by_bank[ic])
                    mm = None
                    for c in range(NCH):
                        mm = P.op("t", lambda e, c=c, ic=ic, s4=s4, buf=buf: e.matmul(
                            h_ps[ic][:, 0:512],
                            wie[buf][:, c * 512 + ic * 128: c * 512 + (ic + 1) * 128],
                            xnT[:, c * T + s4 * 512: c * T + (s4 + 1) * 512],
                            start=(c == 0), stop=(c == NCH - 1)),
                            inc="st" if c == NCH - 1 else None)
                    mms[ic] = mm
                pend_h.append((s4, mms))
                ge_last = flush_h()
            ot = None
            mvs = []
            cc, half = h // 2, h % 2
            qb = half * 64
            for s4 in range(4):
                mm = None
                for ic in range(4):
                    if half == 0:
                        lhs = woe[buf][:, ic * 128 + 64:(ic + 1) * 128]
                        dst = o_ps[0:64, 0:512]
                    else:
                        lhs = woe[buf][:, ic * 128:(ic + 1) * 128]
                        dst = o_ps[:, 0:512]
                    mm = P.op("t", lambda e, lhs=lhs, dst=dst, ic=ic, s4=s4: e.matmul(
                        dst, lhs,
                        h_sb[:, ic * T + s4 * 512: ic * T + (s4 + 1) * 512],
                        start=(ic == 0), stop=(ic == 3)),
                        waits=[ge_last] if (s4 == 0 and ic == 0) else [],
                        inc="st" if ic == 3 else None)
                wbk = s4 % 2
                bw = [w_tok] if s4 == 0 else []
                pb = mvs[s4 - 2] if s4 >= 2 else prev_mvs[wbk]
                if pb is not None:
                    bw = bw + [pb]
                bmm = P.op("t", lambda e, s4=s4, w_row=w_row, wbk=wbk: e.matmul(
                    wb_ps[wbk][:, 0:512], C["ones_row_b"][:],
                    w_row[0:1, s4 * 512:(s4 + 1) * 512], start=True, stop=True),
                    inc="st", waits=bw)
                ov = P.op("v", lambda e, qb=qb: e.tensor_copy(
                    osb[qb:qb + 64, :], o_ps[qb:qb + 64, 0:512]),
                    inc="sv", waits=[mm] + ([ot] if ot is not None else []))
                mv = P.op("v", lambda e, qb=qb, wbk=wbk: e.tensor_mul(
                    osb[qb:qb + 64, :], osb[qb:qb + 64, :],
                    wb_ps[wbk][qb:qb + 64, 0:512]),
                    inc="sv", waits=[bmm, ov])
                mvs.append(mv)
                ot = P.op("v", lambda e, cc=cc, qb=qb, s4=s4: e.tensor_add(
                    xT[qb:qb + 64, cc * T + s4 * 512: cc * T + (s4 + 1) * 512],
                    xT[qb:qb + 64, cc * T + s4 * 512: cc * T + (s4 + 1) * 512],
                    osb[qb:qb + 64, :]), inc="sv", waits=[mv])
                P.wait("t", ot)
            prev_mvs = [mvs[2], mvs[3]]
            wT_stage.set_guard(w_b, bmm)
            prev_use[buf] = mm
            prev_omm = mm
            wtoks = nw
    return




def emit_l1_attn_b(P, ctx, C, xT, xnT, W):
    nc = P.nc
    H, DH, WIN, DIL = 16, 64, 128, 4
    sb = lambda name, shape, dt: P.sbuf(ctx, name, shape, dt)
    psb = lambda name, shape: P.psum(ctx, name, shape)
    P.fence_dma()
    scale = 1.0 / math.sqrt(DH)
    ln_tok = P.tok("sv")

    QT = sb("mb_qt", [128, T], BF16)
    KT = sb("mb_kt", [128, T], BF16)
    Vp = sb("mb_vp", [128, 2 * 16 * 65], BF16)
    Vd = sb("mb_vd", [128, 2 * 16 * 65], BF16)
    Olo = sb("mb_olo", [128, T], BF16)
    Odi = sb("mb_odi", [128, T], BF16)
    Ogl = sb("mb_ogl", [128, T], BF16)
    o_acc = sb("mb_oacc", [128, T], BF16)
    otmp = sb("mb_otmp", [128, 512], BF16)
    y_acc = sb("mb_yacc", [128, NCH * T], BF16)
    wq_sb = sb("mb_wq", [128, 8 * 128], BF16)
    wk_sb = sb("mb_wk", [128, 8 * 128], BF16)
    wv_sb = sb("mb_wv", [128, 8 * 128], BF16)
    wo_sb = [sb("mb_wo%d" % i, [128, 8 * 128], BF16) for i in range(2)]
    sums_sb = sb("mb_sums", [65, T], F32)
    expT = [sb("mb_exp%d" % i, [128, 512], BF16) for i in range(3)]

    pr_ps = [psb("mb_pr%d" % i, [128, 512]) for i in range(2)]
    sc_ps = [psb("mb_sc%d" % i, [128, 512]) for i in range(3)]
    pv_ps = psb("mb_pv", [65, 512])
    nb_ps = psb("mb_nb", [128, 512])
    yo_ps = [psb("mb_yo0", [128, 512]), nb_ps]

    bounce = RowBounce(P, ctx, "mb_b", 6, C=C, fold=1.0 / 3.0)
    sh_ap, sh_tok = C["bias"](-SHIFT)

    prev_pe = None
    prev_proj = None
    prev_res = None
    sums_guard = None
    last_exp = [None, None, None]
    exp_guard = {0: None, 1: None, 2: None}
    yev = {}
    last_pair = None
    for hp in range(8):
        wt = None
        guard = [prev_proj] if prev_proj is not None else []
        wt1 = None
        for c in range(NCH):
            r0, r1 = c * 128, (c + 1) * 128
            wt = P.dma("g", wq_sb[:, r0:r1],
                       W["w_qkv"][r0:r1, hp * 128:(hp + 1) * 128],
                       waits=guard if c == 0 else [], sem="dg0")
            wt = P.dma("g", wk_sb[:, r0:r1],
                       W["w_qkv"][r0:r1, 1024 + hp * 128:1024 + (hp + 1) * 128],
                       sem="dg0")
            wt = P.dma("g", wv_sb[:, r0:r1],
                       W["w_qkv"][r0:r1, 2048 + hp * 128:2048 + (hp + 1) * 128],
                       sem="dg0")
            wt1 = P.dma("g", wo_sb[hp % 2][:, r0:r1],
                        W["w_out"][hp * 128:(hp + 1) * 128, r0:r1],
                        waits=[prev_pe] if (prev_pe is not None and c == 0) else [],
                        sem="dg1")
        P.wait("t", ln_tok)
        if prev_pe is not None:
            P.wait("t", prev_pe)
        ev_bank = {}
        bank_i = [0]

        def pick_bank():
            b = bank_i[0] % 2
            bank_i[0] += 1
            if ev_bank.get(b) is not None:
                P.wait("t", ev_bank[b])
            return b

        for dst, wmat in ((QT, wq_sb), (KT, wk_sb)):
            for s4 in range(4):
                b = pick_bank()
                mm = None
                for c in range(NCH):
                    mm = P.op("t", lambda e, s4=s4, c=c, wmat=wmat, b=b: e.matmul(
                        pr_ps[b][:, 0:512],
                        wmat[:, c * 128:(c + 1) * 128],
                        xnT[:, c * T + s4 * 512: c * T + (s4 + 1) * 512],
                        start=(c == 0), stop=(c == NCH - 1)),
                        waits=[wt] if (dst is QT and s4 == 0 and c == 0) else [],
                        inc="st" if c == NCH - 1 else None)
                ev = P.op("v", lambda e, dst=dst, s4=s4, b=b: e.tensor_copy(
                    dst[:, s4 * 512:(s4 + 1) * 512], pr_ps[b][:, 0:512]), inc="sv",
                    waits=[mm])
                ev_bank[b] = ev
        vz = P.op("g", lambda e: e.memset(Vp[:], 0.0),
                  waits=[prev_pe] if prev_pe is not None else [], inc="sg")
        vz2 = P.op("g", lambda e: e.memset(Vd[:], 0.0), waits=[vz], inc="sg")
        vone = P.op("g", lambda e: e.memset(Vp[:, 64::65], 1.0), inc="sg",
                    waits=[vz2])
        vone2 = P.op("g", lambda e: e.memset(Vd[:, 64::65], 1.0), inc="sg",
                     waits=[vone])
        vev = None
        for tc in range(NTCH):
            b = pick_bank()
            mm = None
            for c in range(NCH):
                mm = P.op("t", lambda e, tc=tc, c=c, b=b: e.matmul(
                    pr_ps[b][:, 0:128],
                    xnT[:, c * T + tc * 128: c * T + (tc + 1) * 128],
                    wv_sb[:, c * 128:(c + 1) * 128],
                    start=(c == 0), stop=(c == NCH - 1)),
                    waits=[vone2] if (tc == 0 and c == 0) else [],
                    inc="st" if c == NCH - 1 else None)
            for j in range(2):
                vev = P.op("v", lambda e, tc=tc, j=j, b=b: e.tensor_copy(
                    Vp[:, j * 1040 + tc * 65: j * 1040 + tc * 65 + 64],
                    pr_ps[b][:, j * 64:(j + 1) * 64]), waits=[mm] if j == 0 else [],
                    inc="sv" if j == 1 else None)
            ev_bank[b] = vev
            gph, ib = tc // 4, tc % 4
            b = pick_bank()
            mm = None
            for c in range(NCH):
                mm = P.op("t", lambda e, tc=tc, c=c, gph=gph, ib=ib, b=b: e.matmul(
                    pr_ps[b][:, 0:128],
                    xnT[:, c * T + gph + ib * 512: c * T + gph + ib * 512 + 509:4],
                    wv_sb[:, c * 128:(c + 1) * 128],
                    start=(c == 0), stop=(c == NCH - 1)),
                    inc="st" if c == NCH - 1 else None)
            for j in range(2):
                vev = P.op("v", lambda e, tc=tc, j=j, b=b: e.tensor_copy(
                    Vd[:, j * 1040 + tc * 65: j * 1040 + tc * 65 + 64],
                    pr_ps[b][:, j * 64:(j + 1) * 64]), waits=[mm] if j == 0 else [],
                    inc="sv" if j == 1 else None)
            ev_bank[b] = vev
        prev_proj = P.tok("st")
        pvsum_tok = None
        for j in range(2):
            qb = j * 64
            pend = []

            def flush_loc(pend=None, j=j, qb=qb):
                w, etok = pend.pop(0)
                slot = pv_ps if w % 2 == 0 else yo_ps[0]
                pvt = P.op("t", lambda e, j=j, w=w, slot=slot: e.matmul(
                    slot[0:65, 0:128],
                    Vp[:, j * 1040 + w * 65: j * 1040 + (w + 1) * 65],
                    expT[w % 3][:, 0:128], start=True, stop=True),
                    waits=[etok] + (pv_guard[w % 2] if pv_guard[w % 2] is not None else []),
                    inc="st")
                exp_guard[w % 3] = pvt
                oev = P.op("v", lambda e, qb=qb, w=w, slot=slot: e.tensor_copy(
                    Olo[qb:qb + 64, w * 128:(w + 1) * 128],
                    slot[0:64, 0:128]), inc="sv", waits=[pvt])
                sw = [pvt]
                if w == 0 and sums_guard is not None:
                    sw.append(sums_guard)
                sev = P.op("s", lambda e, w=w, slot=slot: e.activation(
                    out=sums_sb[64:65, w * 128:(w + 1) * 128],
                    in_=slot[64:65, 0:128], func=AF.Copy),
                    inc="ss", waits=sw)
                pv_guard[w % 2] = [oev, sev]
                return oev, sev

            pv_guard = [last_pair, last_pair]
            for w in range(NTCH):
                par = w % 3
                scp = sc_ps[par]
                swaits = []
                if hp == 0 and j == 0 and w == 0:
                    swaits.append(sh_tok)
                if j == 0 and w == 0:
                    swaits.append(vev)
                if last_exp[par] is not None:
                    swaits.append(last_exp[par])
                smm = P.op("t", lambda e, qb=qb, w=w, scp=scp: e.matmul(
                    scp[:, 0:128],
                    KT[qb:qb + 64, w * 128:(w + 1) * 128],
                    QT[qb:qb + 64, w * 128:(w + 1) * 128],
                    start=True, stop=True), waits=swaits, inc="st")
                ew = [smm]
                if exp_guard[par] is not None:
                    ew.append(exp_guard[par])
                etok = P.op("s", lambda e, par=par, scp=scp: e.activation(
                    out=expT[par][:, 0:128], in_=scp[:, 0:128], func=AF.Exp,
                    scale=scale, bias=sh_ap), inc="ss", waits=ew)
                last_exp[par] = etok
                pend.append((w, etok))
                if len(pend) > 2:
                    oev, sev = flush_loc(pend)
            while pend:
                oev, sev = flush_loc(pend)
            last_pair = [oev, sev]
            sums_guard = bounce.write(sums_sb[64:65, :], 3 * j + 0, waits=[sev])
            pend = []
            ev_prev = [last_pair]

            def flush_dil(pend=None, j=j):
                kk, etok = pend.pop(0)
                gph, ktd = kk // 4, kk % 4
                pw = [etok]
                if ktd == 0 and ev_prev[0] is not None:
                    pw += ev_prev[0]
                tk = P.op("t", lambda e, j=j, gph=gph, ktd=ktd, kk=kk: e.matmul(
                    pv_ps[:, 0:512],
                    Vd[:, j * 1040 + kk * 65: j * 1040 + (kk + 1) * 65],
                    expT[kk % 3][:, 0:512], start=(ktd == 0), stop=(ktd == 3)),
                    waits=pw, inc="st")
                exp_guard[kk % 3] = tk
                return tk, gph, ktd

            for kk in range(16):
                gph, ktd = kk // 4, kk % 4
                par = kk % 3
                scp = sc_ps[par]
                swaits = []
                if last_exp[par] is not None:
                    swaits.append(last_exp[par])
                smm = P.op("t", lambda e, qb=qb, gph=gph, ktd=ktd, scp=scp: e.matmul(
                    scp[:, 0:512],
                    KT[qb:qb + 64, gph + ktd * 512: gph + ktd * 512 + 509:4],
                    QT[qb:qb + 64, gph: gph + 2045:4],
                    start=True, stop=True), waits=swaits, inc="st")
                ew = [smm]
                if exp_guard[par] is not None:
                    ew.append(exp_guard[par])
                etok = P.op("s", lambda e, par=par, scp=scp: e.activation(
                    out=expT[par][:, 0:512], in_=scp[:, 0:512], func=AF.Exp,
                    scale=scale, bias=sh_ap), inc="ss", waits=ew)
                last_exp[par] = etok
                pend.append((kk, etok))
                if len(pend) > 2:
                    tk, fg, fk = flush_dil(pend)
                    if fk == 3:
                        oev = P.op("v", lambda e, qb=qb, fg=fg: e.tensor_copy(
                            Odi[qb:qb + 64, fg * 512:(fg + 1) * 512],
                            pv_ps[0:64, 0:512]), inc="sv", waits=[tk])
                        sev = P.op("s", lambda e, fg=fg: e.activation(
                            out=sums_sb[64:65, fg * 512:(fg + 1) * 512],
                            in_=pv_ps[64:65, 0:512], func=AF.Copy), inc="ss",
                            waits=[tk] + ([sums_guard] if fg == 0 else []))
                        ev_prev[0] = [oev, sev]
            while pend:
                tk, fg, fk = flush_dil(pend)
                if fk == 3:
                    oev = P.op("v", lambda e, qb=qb, fg=fg: e.tensor_copy(
                        Odi[qb:qb + 64, fg * 512:(fg + 1) * 512],
                        pv_ps[0:64, 0:512]), inc="sv", waits=[tk])
                    sev = P.op("s", lambda e, fg=fg: e.activation(
                        out=sums_sb[64:65, fg * 512:(fg + 1) * 512],
                        in_=pv_ps[64:65, 0:512], func=AF.Copy), inc="ss",
                        waits=[tk] + ([sums_guard] if fg == 0 else []))
                    ev_prev[0] = [oev, sev]
            last_pair = [oev, sev]
            sums_guard = bounce.write(sums_sb[64:65, :], 3 * j + 1, waits=[sev])
            pend = []
            ev_prev2 = [last_pair]
            for qq in range(4 * NTCH):
                qc, kt = qq // NTCH, qq % NTCH
                par = qq % 3
                scp = sc_ps[par]
                swaits = []
                if last_exp[par] is not None:
                    swaits.append(last_exp[par])
                smm = P.op("t", lambda e, qb=qb, kt=kt, qc=qc, scp=scp: e.matmul(
                    scp[:, 0:512],
                    KT[qb:qb + 64, kt * 128:(kt + 1) * 128],
                    QT[qb:qb + 64, qc * 512:(qc + 1) * 512],
                    start=True, stop=True), waits=swaits, inc="st")
                ew = [smm]
                if exp_guard[par] is not None:
                    ew.append(exp_guard[par])
                etok = P.op("s", lambda e, par=par, scp=scp: e.activation(
                    out=expT[par][:, 0:512], in_=scp[:, 0:512], func=AF.Exp,
                    scale=scale, bias=sh_ap), inc="ss", waits=ew)
                last_exp[par] = etok
                pend.append((qq, etok))
                if len(pend) > 2:
                    qq2, etok2 = pend.pop(0)
                    qc2, kt2 = qq2 // NTCH, qq2 % NTCH
                    pw = [etok2]
                    if kt2 == 0 and ev_prev2[0] is not None:
                        pw += ev_prev2[0]
                    tk = P.op("t", lambda e, j=j, kt2=kt2, qq2=qq2: e.matmul(
                        pv_ps[:, 0:512],
                        Vp[:, j * 1040 + kt2 * 65: j * 1040 + (kt2 + 1) * 65],
                        expT[qq2 % 3][:, 0:512],
                        start=(kt2 == 0), stop=(kt2 == NTCH - 1)),
                        waits=pw, inc="st")
                    exp_guard[qq2 % 3] = tk
                    if kt2 == NTCH - 1:
                        oev = P.op("v", lambda e, qb=qb, qc2=qc2: e.tensor_copy(
                            Ogl[qb:qb + 64, qc2 * 512:(qc2 + 1) * 512],
                            pv_ps[0:64, 0:512]), inc="sv", waits=[tk])
                        sev = P.op("s", lambda e, qc2=qc2: e.activation(
                            out=sums_sb[64:65, qc2 * 512:(qc2 + 1) * 512],
                            in_=pv_ps[64:65, 0:512], func=AF.Copy), inc="ss",
                            waits=[tk] + ([sums_guard] if qc2 == 0 else []))
                        ev_prev2[0] = [oev, sev]
            while pend:
                qq2, etok2 = pend.pop(0)
                qc2, kt2 = qq2 // NTCH, qq2 % NTCH
                pw = [etok2]
                if kt2 == 0 and ev_prev2[0] is not None:
                    pw += ev_prev2[0]
                tk = P.op("t", lambda e, j=j, kt2=kt2, qq2=qq2: e.matmul(
                    pv_ps[:, 0:512],
                    Vp[:, j * 1040 + kt2 * 65: j * 1040 + (kt2 + 1) * 65],
                    expT[qq2 % 3][:, 0:512],
                    start=(kt2 == 0), stop=(kt2 == NTCH - 1)),
                    waits=pw, inc="st")
                exp_guard[qq2 % 3] = tk
                if kt2 == NTCH - 1:
                    oev = P.op("v", lambda e, qb=qb, qc2=qc2: e.tensor_copy(
                        Ogl[qb:qb + 64, qc2 * 512:(qc2 + 1) * 512],
                        pv_ps[0:64, 0:512]), inc="sv", waits=[tk])
                    sev = P.op("s", lambda e, qc2=qc2: e.activation(
                        out=sums_sb[64:65, qc2 * 512:(qc2 + 1) * 512],
                        in_=pv_ps[64:65, 0:512], func=AF.Copy), inc="ss",
                        waits=[tk] + ([sums_guard] if qc2 == 0 else []))
                    ev_prev2[0] = [oev, sev]
            last_pair = [oev, sev]
            sums_guard = bounce.write(sums_sb[64:65, :], 3 * j + 2, waits=[sev])
        bounce.recip_all()
        comb_last = None
        nbank4 = [nb_ps, yo_ps[0]]
        nb_guard = {}
        for j in range(2):
            qb = j * 64
            for pat, Osrc in ((0, Olo), (1, Odi), (2, Ogl)):
                rap, rtok, rb = bounce.load_row(3 * j + pat)
                bmm = None
                for s4 in range(4):
                    nb = s4 % 2
                    w = [rtok] if s4 == 0 else []
                    if nb_guard.get(nb) is not None:
                        w.append(nb_guard[nb])
                    bmm = P.op("t", lambda e, s4=s4, rap=rap, nb=nb: e.matmul(
                        nbank4[nb][:, 0:512], C["ones_row_b"][:],
                        rap[0:1, s4 * 512:(s4 + 1) * 512], start=True, stop=True),
                        inc="st", waits=w)
                    t1 = P.op("v", lambda e, qb=qb, s4=s4, Osrc=Osrc, nb=nb: e.tensor_mul(
                        otmp[qb:qb + 64, :], Osrc[qb:qb + 64, s4 * 512:(s4 + 1) * 512],
                        nbank4[nb][qb:qb + 64, 0:512]), inc="sv",
                        waits=[bmm] + ([comb_last] if comb_last is not None else []))
                    nb_guard[nb] = t1
                    if pat == 0:
                        comb_last = P.op("v", lambda e, qb=qb, s4=s4: e.tensor_copy(
                            o_acc[qb:qb + 64, s4 * 512:(s4 + 1) * 512],
                            otmp[qb:qb + 64, :]), inc="sv", waits=[t1])
                    elif pat == 2:
                        comb_last = P.op("v", lambda e, qb=qb, s4=s4: e.tensor_add(
                            o_acc[qb:qb + 64, s4 * 512:(s4 + 1) * 512],
                            o_acc[qb:qb + 64, s4 * 512:(s4 + 1) * 512],
                            otmp[qb:qb + 64, :]), inc="sv", waits=[t1])
                    else:
                        comb_last = P.op("v", lambda e, qb=qb, s4=s4: e.tensor_add(
                            o_acc[qb:qb + 64, s4 + 0:s4 + 2045:4],
                            o_acc[qb:qb + 64, s4 + 0:s4 + 2045:4],
                            otmp[qb:qb + 64, :]), inc="sv", waits=[t1])
                    P.wait("t", comb_last)
                bounce.set_guard(rb, bmm)
        mm = None
        for c in range(NCH):
            for s4 in range(4):
                yb = (c * 4 + s4) % 2
                mmw = [comb_last, wt1] if (c == 0 and s4 == 0) else []
                if yev.get(yb) is not None:
                    mmw.append(yev[yb])
                mm = P.op("t", lambda e, c=c, s4=s4, yb=yb, hp=hp: e.matmul(
                    yo_ps[yb][:, 0:512],
                    wo_sb[hp % 2][:, c * 128:(c + 1) * 128],
                    o_acc[:, s4 * 512:(s4 + 1) * 512],
                    start=True, stop=True),
                    waits=mmw, inc="st")
                if hp == 0:
                    ydone = P.op("v", lambda e, c=c, s4=s4, yb=yb: e.tensor_copy(
                        y_acc[:, c * T + s4 * 512: c * T + (s4 + 1) * 512],
                        yo_ps[yb][:, 0:512]), inc="sv", waits=[mm])
                else:
                    ydone = P.op("v", lambda e, c=c, s4=s4, yb=yb: e.tensor_add(
                        y_acc[:, c * T + s4 * 512: c * T + (s4 + 1) * 512],
                        y_acc[:, c * T + s4 * 512: c * T + (s4 + 1) * 512],
                        yo_ps[yb][:, 0:512]), inc="sv", waits=[mm])
                yev[yb] = ydone
        last_pair = [yev[0], yev[1]]
        prev_pe = mm
        prev_res = ydone
    rt = None
    for c in range(NCH):
        for s4 in range(4):
            rt = P.op("v", lambda e, c=c, s4=s4: e.tensor_add(
                xT[:, c * T + s4 * 512: c * T + (s4 + 1) * 512],
                xT[:, c * T + s4 * 512: c * T + (s4 + 1) * 512],
                y_acc[:, c * T + s4 * 512: c * T + (s4 + 1) * 512]), inc="sv",
                waits=[prev_res] if (c == 0 and s4 == 0) else [])
    return



INPUT_SPECS = {
    "L0_attn_a_Wqkv": (1024, 3072), "L0_attn_a_Wout": (1024, 1024),
    "L0_attn_a_head_scales": (8,), "L0_attn_a_head_dirs": (8, 3),
    "L0_ffn_gate_w": (1024, 8), "L0_ffn_gate_bias": (8,),
    "L0_ffn_w_gate": (8, 1024, 512), "L0_ffn_w_up": (8, 1024, 512),
    "L0_ffn_w_down": (8, 512, 1024),
    "L0_ffn_sw_gate": (1024, 1024), "L0_ffn_sw_up": (1024, 1024),
    "L0_ffn_sw_down": (1024, 1024),
    "L0_attn_b_wq_sem": (1024, 512), "L0_attn_b_wk_sem": (1024, 512),
    "L0_attn_b_wq_geo": (1024, 512), "L0_attn_b_wk_geo": (1024, 512),
    "L0_attn_b_wv": (1024, 1024), "L0_attn_b_wo": (1024, 1024),
    "L0_attn_b_k_sem_null": (512,), "L0_attn_b_k_geo_null": (512,),
    "L0_attn_b_v_null": (1024,),
    "L1_attn_a_W_dq": (1024, 256), "L1_attn_a_W_uq": (256, 1024),
    "L1_attn_a_W_dkv": (1024, 256), "L1_attn_a_W_uk": (256, 1024),
    "L1_attn_a_W_uv": (256, 1024), "L1_attn_a_W_o": (1024, 1024),
    "L1_ffn_w_router": (128, 1024), "L1_ffn_head_centroids": (16, 128),
    "L1_ffn_temperature": (1,), "L1_ffn_input_experts": (16, 1024, 512),
    "L1_ffn_output_experts": (16, 512, 64),
    "L1_attn_b_w_qkv": (1024, 3072), "L1_attn_b_w_out": (1024, 1024),
}

L0_FFN_W = ["L0_ffn_gate_w", "L0_ffn_gate_bias", "L0_ffn_w_gate", "L0_ffn_w_up",
            "L0_ffn_w_down", "L0_ffn_sw_gate", "L0_ffn_sw_up", "L0_ffn_sw_down"]

L1_ATTN_B_W = ["L1_attn_b_w_qkv", "L1_attn_b_w_out"]

L1_FFN_W = ["L1_ffn_w_router", "L1_ffn_head_centroids", "L1_ffn_temperature",
            "L1_ffn_input_experts", "L1_ffn_output_experts"]

L1_ATTN_A_W = ["L1_attn_a_W_dq", "L1_attn_a_W_uq", "L1_attn_a_W_dkv",
               "L1_attn_a_W_uk", "L1_attn_a_W_uv", "L1_attn_a_W_o"]

L0_ATTN_B_W = ["L0_attn_b_wq_sem", "L0_attn_b_wk_sem", "L0_attn_b_wq_geo",
               "L0_attn_b_wk_geo", "L0_attn_b_wv", "L0_attn_b_wo",
               "L0_attn_b_k_sem_null", "L0_attn_b_k_geo_null", "L0_attn_b_v_null"]

SUBLAYERS = [
    ("l0_attn_a", ["L0_attn_a_Wqkv", "L0_attn_a_Wout", "L0_attn_a_head_scales",
                   "L0_attn_a_head_dirs"]),
    ("l0_ffn", L0_FFN_W),
    ("l0_attn_b", L0_ATTN_B_W),
    ("l0_ffn2", L0_FFN_W),
    ("l1_attn_a", L1_ATTN_A_W),
    ("l1_ffn", L1_FFN_W),
    ("l1_attn_b", L1_ATTN_B_W),
    ("l1_ffn2", L1_FFN_W),
]


def emit_sublayer(P, sctx, C, xT, xnT, name, waps):
    P.fence_dma()
    if name == "ln_dbg":
        emit_ln(P, sctx, C, xT, xnT)
        lt = P.tok("sv")
        for c in range(NCH):
            P.op("v", lambda e, c=c: e.tensor_copy(
                xT[:, c * T:(c + 1) * T], xnT[:, c * T:(c + 1) * T]), inc="sv",
                waits=[lt] if c == 0 else [])
    elif name == "l0_attn_a":
        emit_ln(P, sctx, C, xT, xnT)
        emit_attn_a(P, sctx, C, xT, xnT, {
            "Wqkv": waps["L0_attn_a_Wqkv"], "Wout": waps["L0_attn_a_Wout"],
            "head_scales": waps["L0_attn_a_head_scales"],
            "head_dirs": waps["L0_attn_a_head_dirs"]})
    elif name == "l0_attn_b":
        emit_ln(P, sctx, C, xT, xnT)
        emit_attn_b(P, sctx, C, xT, xnT, {
            "wq_sem": waps["L0_attn_b_wq_sem"], "wk_sem": waps["L0_attn_b_wk_sem"],
            "wq_geo": waps["L0_attn_b_wq_geo"], "wk_geo": waps["L0_attn_b_wk_geo"],
            "wv": waps["L0_attn_b_wv"], "wo": waps["L0_attn_b_wo"],
            "k_sem_null": waps["L0_attn_b_k_sem_null"],
            "k_geo_null": waps["L0_attn_b_k_geo_null"],
            "v_null": waps["L0_attn_b_v_null"]})
    elif name == "l1_attn_a":
        emit_ln(P, sctx, C, xT, xnT)
        emit_l1_attn_a(P, sctx, C, xT, xnT, {
            "W_dq": waps["L1_attn_a_W_dq"], "W_uq": waps["L1_attn_a_W_uq"],
            "W_dkv": waps["L1_attn_a_W_dkv"], "W_uk": waps["L1_attn_a_W_uk"],
            "W_uv": waps["L1_attn_a_W_uv"], "W_o": waps["L1_attn_a_W_o"]})
    elif name == "l1_attn_b":
        emit_ln(P, sctx, C, xT, xnT)
        emit_l1_attn_b(P, sctx, C, xT, xnT, {
            "w_qkv": waps["L1_attn_b_w_qkv"], "w_out": waps["L1_attn_b_w_out"]})
    elif name in ("l1_ffn", "l1_ffn2"):
        emit_l1_ffn(P, sctx, C, xT, xnT, {
            "w_router": waps["L1_ffn_w_router"],
            "head_centroids": waps["L1_ffn_head_centroids"],
            "temperature": waps["L1_ffn_temperature"],
            "input_experts": waps["L1_ffn_input_experts"],
            "output_experts": waps["L1_ffn_output_experts"]})
    elif name in ("l0_ffn", "l0_ffn2"):
        emit_l0_ffn(P, sctx, C, xT, xnT, {
            "gate_w": waps["L0_ffn_gate_w"], "gate_bias": waps["L0_ffn_gate_bias"],
            "w_gate": waps["L0_ffn_w_gate"], "w_up": waps["L0_ffn_w_up"],
            "w_down": waps["L0_ffn_w_down"], "sw_gate": waps["L0_ffn_sw_gate"],
            "sw_up": waps["L0_ffn_sw_up"], "sw_down": waps["L0_ffn_sw_down"]})
    else:
        raise KeyError(name)


def build_program(names):
    nc = bass.Bass("TRN2", target_bir_lowering=False, debug=False)
    xin = nc.dram_tensor("xin", [D, T], F32, kind="ExternalInput").ap()
    xout = nc.dram_tensor("xout", [D, T], F32, kind="ExternalOutput").ap()
    waps = {}
    for sl, wnames in SUBLAYERS:
        if sl in names:
            for w in wnames:
                if w not in waps:
                    waps[w] = nc.dram_tensor(w, list(INPUT_SPECS[w]), F32,
                                             kind="ExternalInput").ap()
    with ExitStack() as ctx:
        P = Prog(nc, ctx)
        xT = ctx.enter_context(nc.sbuf_tensor("xT", [128, NCH * T], F32))
        xnT = ctx.enter_context(nc.sbuf_tensor("xnT", [128, NCH * T], BF16))
        C = make_consts(nc, ctx, P)
        in_toks = []
        for c in range(NCH):
            in_toks.append(P.dma("y", xT[:, c * T:(c + 1) * T],
                                 xin[c * 128:(c + 1) * 128, :]))
        for e in ("t", "v", "s"):
            P.wait(e, in_toks[-1])
            P.wait(e, C["ready"])
        for sl in names:
            with ExitStack() as sctx:
                emit_sublayer(P, sctx, C, xT, xnT, sl, waps)
        fin_v = P.tok("sv")
        fin_s = P.tok("ss")
        out_toks = []
        for c in range(NCH):
            out_toks.append(P.dma("y", xout[c * 128:(c + 1) * 128, :],
                                  xT[:, c * T:(c + 1) * T],
                                  waits=[fin_v, fin_s] if c == 0 else [],
                                  sem="do"))
        for s_ in sorted(P.cnt):
            if s_.startswith("d"):
                P.wait("y", (s_, P.cnt[s_]))
        block = ctx.enter_context(nc.Block())
        P.realize(block)
    return nc


def kernel(**inputs):
    x = np.ascontiguousarray(inputs["L0_attn_a_x"], dtype=np.float32)
    all_names = [s for s, _ in SUBLAYERS]
    groups = [all_names[i:i + LAUNCH_GROUP] for i in range(0, len(all_names), LAUNCH_GROUP)]
    cur = [np.ascontiguousarray(x[core].T) for core in range(8)]
    for gnames in groups:
        nc = build_program(gnames)
        wnames = []
        seen = set()
        for s, ws in SUBLAYERS:
            if s in gnames:
                for w in ws:
                    if w not in seen:
                        wnames.append(w)
                        seen.add(w)
        in_maps = []
        for core in range(8):
            m = {"xin": cur[core]}
            for w in wnames:
                m[w] = np.ascontiguousarray(inputs[w], dtype=np.float32)
            in_maps.append(m)
        res = run_bass_kernel_spmd(nc, in_maps, core_ids=list(range(8)))
        cur = [np.ascontiguousarray(np.asarray(r["xout"])) for r in res.results]
    return np.stack([np.ascontiguousarray(c.T) for c in cur])


LAUNCH_GROUP = 8


# version 13
# speedup vs baseline: 1.0527x; 1.0027x over previous
import math
from contextlib import ExitStack

import numpy as np
import ml_dtypes

import concourse.bass as bass
import concourse.mybir as mybir
import concourse.bass_utils as _bass_utils
from concourse.bass_utils import run_bass_kernel_spmd

if not getattr(_bass_utils, "_ldw_patched", False):
    _orig_run_command = _bass_utils.run_command

    def _run_command_ldw(argv, **kw):
        argv = ["--enable-ldw-opt=true" if a == "--enable-ldw-opt=false" else a
                for a in argv]
        return _orig_run_command(argv, **kw)

    _bass_utils.run_command = _run_command_ldw
    _bass_utils._ldw_patched = True

F32 = mybir.dt.float32
BF16 = mybir.dt.bfloat16
AF = mybir.ActivationFunctionType
ALU = mybir.AluOpType
bfdt = ml_dtypes.bfloat16

T = 2048
D = 1024
AA_STOP = 0
NCH = D // 128
NTCH = T // 128
SHIFT = 4.0

ENGS = ("t", "v", "s", "g", "y")


class Prog:

    def __init__(self, nc, ctx):
        self.nc = nc
        self.ctx = ctx
        self.q = {e: [] for e in ENGS}
        self.sems = {}
        self.cnt = {}

    def _sem(self, name):
        if name not in self.sems:
            self.sems[name] = self.ctx.enter_context(self.nc.semaphore(name))
            self.cnt[name] = 0
        return self.sems[name]

    def wait(self, eng, tok):
        if tok is None:
            return
        if isinstance(tok, (list, tuple)) and tok and isinstance(tok[0], (list, tuple, type(None))):
            for t_ in tok:
                self.wait(eng, t_)
            return
        if isinstance(tok, list):
            for t_ in tok:
                self.wait(eng, t_)
            return
        name, val = tok
        self._sem(name)
        if name.startswith("d") and val != self.cnt[name]:
            raise AssertionError(
                f"non-latest DMA wait on {name}: {val} != {self.cnt[name]}")
        self.q[eng].append(("w", name, val))

    def op(self, eng, fn, inc=None, inc_k=1, waits=()):
        for w in waits:
            self.wait(eng, w)
        tok = None
        if inc is not None:
            self._sem(inc)
            self.cnt[inc] += inc_k
            tok = (inc, self.cnt[inc])
        self.q[eng].append(("o", fn, inc, inc_k))
        return tok

    def dma(self, eng, out_ap, in_ap, waits=(), sem=None, **kw):
        if sem is None:
            sem = "d" + eng
        return self.op(eng, lambda e: e.dma_start(out=out_ap, in_=in_ap, **kw),
                       inc=sem, inc_k=16, waits=waits)

    def tok(self, sem):
        self._sem(sem)
        return (sem, self.cnt[sem])

    dma_latest = tok

    def _uniq(self, name):
        if not hasattr(self, "_names"):
            self._names = {}
        k = self._names.get(name, 0)
        self._names[name] = k + 1
        return name if k == 0 else "%s_u%d" % (name, k)

    def sbuf(self, ctx, name, shape, dt):
        return ctx.enter_context(self.nc.sbuf_tensor(self._uniq(name), shape, dt))

    def psum(self, ctx, name, shape, dt=None):
        return ctx.enter_context(self.nc.psum_tensor(
            self._uniq(name), shape, F32 if dt is None else dt))

    def dram(self, name, shape, dt):
        return self.nc.dram_tensor(self._uniq(name), shape, dt, kind="Internal")

    def fence_dma(self):
        for q_ in ("g", "y"):
            for s_ in ("sv", "ss", "st"):
                if self.cnt.get(s_, 0):
                    self.wait(q_, self.tok(s_))

    def realize(self, block):
        handles = self.sems

        def body(eng_name):
            def run(engine):
                for item in self.q[eng_name]:
                    if item[0] == "w":
                        engine.wait_ge(handles[item[1]], item[2])
                    else:
                        _, fn, inc, inc_k = item
                        ins = fn(engine)
                        if inc is not None:
                            ins.then_inc(handles[inc], inc_k)
            return run

        block.tensor(body("t"))
        block.vector(body("v"))
        block.scalar(body("s"))
        block.gpsimd(body("g"))
        block.sync(body("y"))



def make_consts(nc, ctx, P):
    c = {}
    sb = lambda name, shape, dt: ctx.enter_context(nc.sbuf_tensor(name, shape, dt))
    tri = np.triu(np.ones((128, 128), dtype=np.float32)).astype(bfdt)
    mw = np.zeros((128, 4 * 512), dtype=np.float32)
    for v_ in range(4):
        off = v_ * 128
        mw[:, v_ * 512 + off: v_ * 512 + off + 128] = np.triu(np.ones((128, 128), np.float32))
        mw[:, v_ * 512 + off + 128: (v_ + 1) * 512] = 1.0
    identf = np.eye(128, dtype=np.float32)
    items = [("tri", tri, BF16), ("maskw", mw.astype(bfdt), BF16),
             ("identf", identf, F32)]
    toks = []
    for name, arr, dt in items:
        dr = nc.inline_tensor(np.ascontiguousarray(arr), name="c_" + name)
        t_ = sb("sb_" + name, list(arr.shape), dt)
        toks.append(P.dma("y", t_[:], dr.ap()))
        c[name] = t_
    ones = [("ones_col_b", [128, 1], BF16), ("ones_col_f", [128, 1], F32),
            ("ones_row_b", [1, 128], BF16), ("ones_row_f", [1, 128], F32)]
    for i, (name, shape, dt) in enumerate(ones):
        t_ = sb(name, shape, dt)
        P.op("g", lambda e, t_=t_: e.memset(t_[:], 1.0))
        c[name] = t_
    nullmask = sb("nullmask", [128, 1], F32)
    nz = P.op("g", lambda e: e.memset(nullmask[:], 0.0), inc="sg")
    gtok = P.op("g", lambda e: e.memset(nullmask[0:1, 0:1], 1.0), inc="sg",
                waits=[nz])
    c["nullmask"] = nullmask
    c["ready"] = [gtok]

    biasdb = {}
    for v in (1e-5, -SHIFT, math.log(1.0 / 3.0)):
        v = float(v)
        t_ = ctx.enter_context(nc.sbuf_tensor("bias_%d" % len(biasdb), [128, 1], F32))
        tk = P.op("g", lambda e, t_=t_, v=v: e.memset(t_[:], v), inc="sg")
        biasdb[v] = (t_[:], tk)

    def bias(v):
        return biasdb[float(v)]

    c["bias"] = bias
    return c



def emit_ln(P, ctx, C, xT, xnT, f32_cb=None, ra_rows=None):
    nc = P.nc
    SL = 1024
    NSL = T // SL
    with ExitStack() as lctx:
        sb = lambda name, shape, dt: P.sbuf(lctx, name, shape, dt)
        psb = lambda name, shape: P.psum(lctx, name, shape)
        xsq = sb("ln_xsq", [128, 2 * SL], BF16)
        m_row = sb("ln_m", [1, SL], F32)
        msq_row = sb("ln_msq", [1, SL], F32)
        var_row = sb("ln_var", [1, SL], F32)
        l_row = sb("ln_l", [1, SL], F32)
        rb_b = sb("ln_rbb", [1, SL], BF16)
        ab_b = sb("ln_abb", [1, SL], BF16)
        sums_ps = [psb("ln_sums%d" % h, [1, 512]) for h in range(SL // 512)]
        sq_ps = [psb("ln_sq%d" % h, [1, 512]) for h in range(SL // 512)]
        ab_ps = [psb("ln_ab%d" % h, [128, 512]) for h in range(SL // 512)]
        rb_ps = [psb("ln_rb%d" % h, [128, 512]) for h in range(SL // 512)]

        start_sv = P.tok("sv")
        start_st = P.tok("st")
        P.wait("s", start_sv)
        P.wait("t", start_sv)
        P.wait("v", start_st)

        last = None
        prev_sq_mm = None
        prev_mtok = prev_vtok = prev_btok = None
        prev_mt_last = {}
        prev_add_last = {}
        for sl in range(NSL):
            lo = sl * SL
            if ra_rows is not None:
                r_row = ra_rows[0][0:1, lo:lo + SL]
                a_row = ra_rows[1][0:1, lo:lo + SL]
            else:
                r_row = P.sbuf(lctx, "ln_r%d" % sl, [1, SL], F32)[:]
                a_row = P.sbuf(lctx, "ln_a%d" % sl, [1, SL], F32)[:]
            stoks = {}
            pv_by_c = {}

            def emit_square(c):
                xb = c % 2
                w = []
                if pv_by_c.get(c - 2) is not None:
                    w.append(pv_by_c[c - 2])
                elif c < 2 and prev_sq_mm is not None:
                    w.append(prev_sq_mm)
                stoks[c] = P.op("s", lambda e, c=c, lo=lo, xb=xb: e.activation(
                    out=xsq[:, xb * SL:xb * SL + SL],
                    in_=xT[:, c * T + lo: c * T + lo + SL],
                    func=AF.Square), inc="ss", waits=w)

            emit_square(0)
            emit_square(1)
            for c in range(NCH):
                xb = c % 2
                for h in range(SL // 512):
                    sw = [prev_mtok] if (c == 0 and h == 0 and prev_mtok) else []
                    P.op("t", lambda e, c=c, lo=lo, h=h: e.matmul(
                        sums_ps[h][:, 0:512],
                        C["ones_col_f"][:],
                        xT[:, c * T + lo + h * 512: c * T + lo + (h + 1) * 512],
                        start=(c == 0), stop=(c == NCH - 1)), waits=sw)
                    qw = [stoks[c]] if h == 0 else []
                    if c == 0 and h == 0 and prev_vtok:
                        qw = qw + [prev_vtok]
                    pv = P.op("t", lambda e, c=c, h=h, xb=xb: e.matmul(
                        sq_ps[h][:, 0:512], C["ones_col_b"][:],
                        xsq[:, xb * SL + h * 512: xb * SL + (h + 1) * 512],
                        start=(c == 0), stop=(c == NCH - 1)),
                        waits=qw, inc="st")
                pv_by_c[c] = pv
                prev_sq_mm = pv
                if c + 2 < NCH:
                    emit_square(c + 2)
            mm_done = prev_sq_mm
            mtok = None
            for h in range(SL // 512):
                mtok = P.op("s", lambda e, h=h: e.mul(
                    out=m_row[0:1, h * 512:(h + 1) * 512], in_=sums_ps[h][:],
                    mul=1.0 / D), inc="ss", waits=[mm_done] if h == 0 else [])
            mqtok = P.op("s", lambda e: e.activation(out=msq_row[:], in_=m_row[:],
                                                     func=AF.Square), inc="ss",
                         waits=[mtok])
            vtok = None
            for h in range(SL // 512):
                vtok = P.op("v", lambda e, h=h: e.scalar_tensor_tensor(
                    out=var_row[0:1, h * 512:(h + 1) * 512], in0=sq_ps[h][:],
                    scalar=1.0 / D, in1=msq_row[0:1, h * 512:(h + 1) * 512],
                    op0=ALU.mult, op1=ALU.subtract), inc="sv",
                    waits=[mqtok, mm_done] if h == 0 else [])
            eps_ap, eps_tok = C["bias"](1e-5)
            ltok = P.op("s", lambda e: e.activation(out=l_row[:], in_=var_row[:],
                                                    func=AF.Ln, bias=eps_ap[0:1, :]),
                        inc="ss", waits=[vtok, eps_tok])
            rtok = P.op("s", lambda e, r_row=r_row: e.activation(
                out=r_row, in_=l_row[:], func=AF.Exp, scale=-0.5), inc="ss",
                waits=[ltok])
            atok = P.op("v", lambda e, r_row=r_row, a_row=a_row: e.scalar_tensor_tensor(
                out=a_row, in0=m_row[:], scalar=-1.0, in1=r_row,
                op0=ALU.mult, op1=ALU.mult), inc="sv", waits=[rtok])
            cb1 = P.op("s", lambda e, r_row=r_row: e.activation(
                out=rb_b[:], in_=r_row, func=AF.Copy), inc="ss",
                waits=[atok] + ([prev_btok] if prev_btok is not None else []))
            cb2 = P.op("s", lambda e, a_row=a_row: e.activation(
                out=ab_b[:], in_=a_row, func=AF.Copy), inc="ss")
            btok = None
            for h in range(SL // 512):
                aw = [cb2] if h == 0 else []
                if prev_add_last.get(h) is not None:
                    aw = aw + [prev_add_last[h]]
                P.op("t", lambda e, h=h: e.matmul(
                    ab_ps[h][:, 0:512], C["ones_row_b"][:],
                    ab_b[0:1, h * 512:(h + 1) * 512], start=True, stop=True),
                    waits=aw)
                btok = P.op("t", lambda e, h=h: e.matmul(
                    rb_ps[h][:, 0:512], C["ones_row_b"][:],
                    rb_b[0:1, h * 512:(h + 1) * 512], start=True, stop=True),
                    inc="st",
                    waits=[prev_mt_last[h]] if prev_mt_last.get(h) is not None else [])
            for c in range(NCH):
                for h in range(SL // 512):
                    mt = P.op("v", lambda e, c=c, lo=lo, h=h: e.tensor_mul(
                        xnT[:, c * T + lo + h * 512: c * T + lo + (h + 1) * 512],
                        xT[:, c * T + lo + h * 512: c * T + lo + (h + 1) * 512],
                        rb_ps[h][:, 0:512]),
                        waits=[btok] if (c == 0 and h == 0) else [], inc="sv")
                    last = P.op("v", lambda e, c=c, lo=lo, h=h: e.tensor_add(
                        xnT[:, c * T + lo + h * 512: c * T + lo + (h + 1) * 512],
                        xnT[:, c * T + lo + h * 512: c * T + lo + (h + 1) * 512],
                        ab_ps[h][:, 0:512]), inc="sv", waits=[mt])
                    if c == NCH - 1:
                        prev_mt_last[h] = mt
                        prev_add_last[h] = last
            prev_mtok, prev_vtok, prev_btok = mtok, vtok, btok
            if sl == NSL - 1:
                P.wait("t", last)
    return last



class RowBounce:

    def __init__(self, P, ctx, name, nslots, C=None, fold=1.0):
        nc = P.nc
        self.P = P
        self.n = nslots
        self.C = C
        self.fold = fold
        self.scr = P.dram(name + "_scr", [nslots, T], F32)
        self.scr2 = P.dram(name + "_scr2", [nslots, T], F32)
        w = nslots * T // 128
        self.pk = P.sbuf(ctx, name + "_pk", [128, w], F32)
        self.pk2 = P.sbuf(ctx, name + "_pk2", [128, w], F32)
        self.rowbuf = [P.sbuf(ctx, name + "_r%d" % i, [1, T], BF16)
                       for i in range(2)]
        self.row_guard = [None, None]
        self.wtoks = {}
        self.st_tok = None
        self.rcount = 0

    def write(self, row_ap, idx, waits):
        w = list(waits)
        if idx == 0 and self.st_tok is not None:
            w.append(self.P.dma_latest("dp"))
        self.wtoks[idx] = self.P.dma("y", self.scr.ap()[idx:idx + 1, :], row_ap,
                                     waits=w, sem="db")
        return self.wtoks[idx]

    def recip_all(self):
        P = self.P
        flat = self.scr.ap().rearrange("n t -> (n t)").rearrange("(p f) -> p f", p=128)
        flat2 = self.scr2.ap().rearrange("n t -> (n t)").rearrange("(p f) -> p f", p=128)
        ld = P.dma("y", self.pk[:], flat, waits=[P.dma_latest("db")], sem="dp")
        lntok = P.op("s", lambda e: e.activation(out=self.pk2[:], in_=self.pk[:],
                                                 func=AF.Ln), waits=[ld], inc="ss")
        if self.fold != 1.0:
            b_ap, b_tok = self.C["bias"](math.log(self.fold))
            l2 = P.op("s", lambda e: e.activation(out=self.pk[:], in_=self.pk2[:],
                                                  func=AF.Exp, scale=-1.0,
                                                  bias=b_ap), inc="ss",
                      waits=[lntok, b_tok])
        else:
            l2 = P.op("s", lambda e: e.activation(out=self.pk[:], in_=self.pk2[:],
                                                  func=AF.Exp, scale=-1.0), inc="ss",
                      waits=[lntok])
        self.st_tok = P.dma("y", flat2, self.pk[:],
                            waits=[l2, P.dma_latest("dr0"), P.dma_latest("dr1")],
                            sem="dp")
        self._fresh = 2

    def load_row(self, idx, waits=()):
        b = self.rcount % 2
        self.rcount += 1
        w = list(waits)
        if getattr(self, "_fresh", 0) > 0:
            w.append(self.st_tok)
            self._fresh -= 1
        if self.row_guard[b] is not None:
            w.append(self.row_guard[b])
        tok = self.P.dma("g", self.rowbuf[b][0:1, :], self.scr2.ap()[idx:idx + 1, :],
                         waits=w, sem="dr%d" % b)
        return self.rowbuf[b][0:1, :], tok, b

    def set_guard(self, b, tok):
        self.row_guard[b] = tok



def emit_attn_a(P, ctx, C, xT, xnT, W):
    nc = P.nc
    H, HD = 8, 128
    sb = lambda name, shape, dt: P.sbuf(ctx, name, shape, dt)
    psb = lambda name, shape: P.psum(ctx, name, shape)
    P.fence_dma()

    hd_sb = sb("aa_hd", [8, 3], F32)
    hs_sb = sb("aa_hs", [8, 1], F32)
    hds_sb = sb("aa_hds", [8, 3], F32)
    dirv = sb("aa_dirv", [128, 16], BF16)
    P.dma("y", hd_sb[:], W["head_dirs"], sem="df")
    t2 = P.dma("y", hs_sb[:], W["head_scales"].rearrange("(h a) -> h a", a=1),
               sem="df")
    mz = P.op("g", lambda e: e.memset(dirv[:], 0.0), inc="sg")
    sc = P.op("v", lambda e: e.tensor_scalar(
        out=hds_sb[:], in0=hd_sb[:], scalar1=hs_sb[:], scalar2=float(math.sqrt(HD)),
        op0=ALU.mult, op1=ALU.mult), inc="sv", waits=[t2])
    for h in range(H):
        P.dma("g", dirv[0:3, 2 * h:2 * h + 1], hd_sb[h:h + 1, 0:3],
              waits=[t2, mz] if h == 0 else [], sem="dv")
        P.dma("g", dirv[0:3, 2 * h + 1:2 * h + 2], hds_sb[h:h + 1, 0:3],
              waits=[sc] if h == 0 else [], sem="dv")
    dv_toks = [P.dma_latest("dv")]

    wout_sb = [sb("aa_wout%d" % i, [128, H * 128], BF16) for i in range(2)]
    wqk = [sb("aa_wqk%d" % i, [128, 3 * 8 * 128], BF16) for i in range(2)]
    QTs = [sb("aa_qt%d" % i, [128, T], BF16) for i in range(2)]
    KTs = [sb("aa_kt%d" % i, [128, T], BF16) for i in range(2)]
    Vh = [sb("aa_v%d" % i, [128, NTCH * 128], BF16) for i in range(2)]
    qp_row = sb("aa_qp", [1, T], BF16)
    kp_row = sb("aa_kp", [1, T], BF16)
    sums_row = sb("aa_sums", [1, T], F32)
    O_all = sb("aa_oall", [128, H * T], BF16)
    expT = [sb("aa_exp%d" % i, [128, 512], BF16) for i in range(3)]

    bounce = RowBounce(P, ctx, "aa_b", H)

    qk_ps = [psb("aa_qkps%d" % i, [128, 512]) for i in range(2)]
    row_ps = psb("aa_rowps", [1, 512])
    sc_ps = [psb("aa_scps%d" % i, [128, 512]) for i in range(3)]
    pv_ps = psb("aa_pvps", [128, 512])
    sm_ps = psb("aa_smps", [1, 512])

    ln_done = P.tok("sv")
    scale = 1.0 / math.sqrt(HD)
    sh_ap, sh_tok = C["bias"](-SHIFT)

    def load_head_weights(h, buf, guard):
        tk = None
        for j in range(3):
            for c in range(NCH):
                tk = P.dma(
                    "g", wqk[buf][:, j * 1024 + c * 128: j * 1024 + (c + 1) * 128],
                    W["Wqkv"][c * 128:(c + 1) * 128,
                              j * 1024 + h * 128: j * 1024 + (h + 1) * 128],
                    waits=[guard] if (guard is not None and j == 0 and c == 0) else [],
                    sem="dg%d" % buf)
        return [tk]

    prev_proj_done = [None, None]
    prev_attn_done = [None, None]
    prev_head_done = None
    last_sums_dma = None
    wtoks = load_head_weights(0, 0, None)
    oev_by_head = {}
    pvsum_tok = {}
    last_exp = [None, None, None]

    def proj_stream(jobs):
        evs = {}
        ev = None
        for i, (mms, evac) in enumerate(jobs):
            b = i % 2
            if evs.get(b) is not None:
                P.wait("t", evs[b])
            mm = mms(qk_ps[b])
            ev = evac(qk_ps[b], mm)
            evs[b] = ev
        return ev

    for h in range(H):
        buf = h % 2
        nwtoks = (load_head_weights(h + 1, 1 - buf, prev_proj_done[1 - buf])
                  if h + 1 < H else None)
        P.wait("t", ln_done)
        P.wait("t", wtoks)
        if prev_attn_done[buf] is not None:
            P.wait("t", prev_attn_done[buf])
        qk_ev = {}
        jobs = []
        for mat, dst in ((0, QTs[buf]), (1, KTs[buf])):
            for s4 in range(4):
                def mms(bank, mat=mat, s4=s4, buf=buf):
                    mm = None
                    for c in range(NCH):
                        mm = P.op("t", lambda e, mat=mat, s4=s4, c=c, buf=buf, bank=bank: e.matmul(
                            bank[:, 0:512],
                            wqk[buf][:, mat * 1024 + c * 128: mat * 1024 + (c + 1) * 128],
                            xnT[:, c * T + s4 * 512: c * T + (s4 + 1) * 512],
                            start=(c == 0), stop=(c == NCH - 1)),
                            inc="st" if c == NCH - 1 else None)
                    return mm
                def evac(bank, mm, mat=mat, s4=s4, dst=dst):
                    ev = P.op("v", lambda e, dst=dst, s4=s4, bank=bank: e.tensor_copy(
                        dst[:, s4 * 512:(s4 + 1) * 512], bank[:, 0:512]), inc="sv",
                        waits=[mm])
                    qk_ev[(mat, s4)] = ev
                    return ev
                jobs.append((mms, evac))
        for tc in range(NTCH):
            def mms(bank, tc=tc, buf=buf):
                mm = None
                for c in range(NCH):
                    mm = P.op("t", lambda e, tc=tc, c=c, buf=buf, bank=bank: e.matmul(
                        bank[:, 0:128],
                        xnT[:, c * T + tc * 128: c * T + (tc + 1) * 128],
                        wqk[buf][:, 2048 + c * 128: 2048 + (c + 1) * 128],
                        start=(c == 0), stop=(c == NCH - 1)),
                        inc="st" if c == NCH - 1 else None)
                return mm
            def evac(bank, mm, tc=tc, buf=buf):
                return P.op("v", lambda e, tc=tc, buf=buf, bank=bank: e.tensor_copy(
                    Vh[buf][:, tc * 128:(tc + 1) * 128], bank[:, 0:128]), inc="sv",
                    waits=[mm])
            jobs.append((mms, evac))
        last_ev = proj_stream(jobs)
        proj_done = P.tok("st")
        prev_proj_done[buf] = proj_done
        P.wait("t", dv_toks)
        row_last = None
        for which, src_t, dst, col in ((0, QTs[buf], qp_row, 2 * h),
                                       (1, KTs[buf], kp_row, 2 * h + 1)):
            for s4 in range(4):
                w = [qk_ev[(which, s4)]]
                if row_last is not None:
                    w.append(row_last)
                mm = P.op("t", lambda e, src_t=src_t, s4=s4, col=col: e.matmul(
                    row_ps[:, 0:512], dirv[:, col:col + 1],
                    src_t[:, s4 * 512:(s4 + 1) * 512], start=True, stop=True),
                    inc="st", waits=w)
                sw = [mm]
                if prev_head_done is not None and which == 0 and s4 == 0:
                    sw.append(prev_head_done)
                row_last = P.op("s", lambda e, dst=dst, s4=s4: e.activation(
                    out=dst[0:1, s4 * 512:(s4 + 1) * 512], in_=row_ps[0:1, 0:512],
                    func=AF.Copy), inc="ss", waits=sw)
                P.wait("t", row_last)
        qdone = []
        for qc in range(4):
            nkt = 4 * qc + 4
            oev_prev = qdone[-1] if qdone else None
            pend = []

            def flush_one():
                kt, etok = pend.pop(0)
                pw = [etok]
                if kt == 0 and oev_prev is not None:
                    pw += [oev_prev[0], oev_prev[1]]
                P.op("t", lambda e, buf=buf, kt=kt, nkt=nkt: e.matmul(
                    pv_ps[:, 0:512], Vh[buf][:, kt * 128:(kt + 1) * 128],
                    expT[kt % 3][:, 0:512], start=(kt == 0), stop=(kt == nkt - 1)),
                    waits=pw)
                pvsum_tok[kt % 3] = P.op("t", lambda e, kt=kt, nkt=nkt: e.matmul(
                    sm_ps[:, 0:512], C["ones_col_b"][:], expT[kt % 3][:, 0:512],
                    start=(kt == 0), stop=(kt == nkt - 1)), inc="st")

            for kt in range(nkt):
                par = kt % 3
                scp = sc_ps[par]
                swaits = []
                if kt == 0 and qc == 0:
                    swaits.append(row_last)
                    if h == 0:
                        swaits.append(sh_tok)
                if last_exp[par] is not None:
                    swaits.append(last_exp[par])
                P.op("t", lambda e, buf=buf, kt=kt, qc=qc, scp=scp: e.matmul(
                    scp[:, 0:512], KTs[buf][:, kt * 128:(kt + 1) * 128],
                    QTs[buf][:, qc * 512:(qc + 1) * 512], start=True, stop=False),
                    waits=swaits)
                smm = P.op("t", lambda e, kt=kt, qc=qc, scp=scp: e.matmul(
                    scp[:, 0:512], kp_row[0:1, kt * 128:(kt + 1) * 128],
                    qp_row[0:1, qc * 512:(qc + 1) * 512], start=False, stop=True),
                    inc="st")
                ew = [smm]
                if pvsum_tok.get(par) is not None:
                    ew.append(pvsum_tok[par])
                etok = P.op("s", lambda e, par=par, scp=scp: e.activation(
                    out=expT[par][:, 0:512], in_=scp[:, 0:512], func=AF.Exp,
                    scale=scale, bias=sh_ap), inc="ss", waits=ew)
                if kt >= 4 * qc:
                    v_ = kt - 4 * qc
                    etok = P.op("v", lambda e, par=par, v_=v_: e.tensor_mul(
                        expT[par][:, 0:512], expT[par][:, 0:512],
                        C["maskw"][:, v_ * 512:(v_ + 1) * 512]), inc="sv",
                        waits=[etok])
                last_exp[par] = etok
                pend.append((kt, etok))
                if len(pend) > 2:
                    flush_one()
            while pend:
                flush_one()
            pvt = P.tok("st")
            oev = P.op("v", lambda e, h=h, qc=qc: e.tensor_copy(
                O_all[:, h * T + qc * 512: h * T + (qc + 1) * 512], pv_ps[:, 0:512]),
                inc="sv", waits=[pvt])
            sevw = [pvt]
            if qc == 0 and last_sums_dma is not None:
                sevw.append(last_sums_dma)
            sev = P.op("s", lambda e, qc=qc: e.activation(
                out=sums_row[0:1, qc * 512:(qc + 1) * 512], in_=sm_ps[0:1, 0:512],
                func=AF.Copy), inc="ss", waits=sevw)
            qdone.append((oev, sev))
            P.wait("t", oev)
            P.wait("t", sev)
        last_sums_dma = bounce.write(sums_row[0:1, :], h, waits=[qdone[-1][1]])
        oev_by_head[h] = [q[0] for q in qdone]
        prev_head_done = P.tok("st")
        prev_attn_done[buf] = prev_head_done
        wtoks = nwtoks

    bounce.recip_all()
    P.wait("t", P.tok("sv"))
    norm_last = None
    nb_guard = {}
    rcur = bounce.load_row(0)
    for h in range(H):
        rap, rtok, rb = rcur
        if h + 1 < H:
            rcur = bounce.load_row(h + 1)
        bmm = None
        for s4 in range(4):
            nb = (h * 4 + s4) % 2
            w = [rtok, oev_by_head[h][s4]] if s4 == 0 else []
            if nb_guard.get(nb) is not None:
                w.append(nb_guard[nb])
            bmm = P.op("t", lambda e, s4=s4, rap=rap, nb=nb: e.matmul(
                qk_ps[nb][:, 0:512], C["ones_row_b"][:],
                rap[0:1, s4 * 512:(s4 + 1) * 512], start=True, stop=True),
                inc="st", waits=w)
            norm_last = P.op("v", lambda e, h=h, s4=s4, nb=nb: e.tensor_mul(
                O_all[:, h * T + s4 * 512: h * T + (s4 + 1) * 512],
                O_all[:, h * T + s4 * 512: h * T + (s4 + 1) * 512],
                qk_ps[nb][:, 0:512]), inc="sv", waits=[bmm, oev_by_head[h][s4]])
            nb_guard[nb] = norm_last
        bounce.set_guard(rb, bmm)
    def load_wout(c, wbuf, guard):
        tk = None
        for h in range(H):
            tk = P.dma("g", wout_sb[wbuf][:, h * 128:(h + 1) * 128],
                       W["Wout"][h * 128:(h + 1) * 128, c * 128:(c + 1) * 128],
                       waits=[guard] if (guard is not None and h == 0) else [],
                       sem="dwo%d" % wbuf)
        return [tk]

    wo_toks = load_wout(0, 0, None)
    wo_guard = [None, None]
    rt_by_bank = {}
    for c in range(NCH):
        wbuf = c % 2
        nwo = load_wout(c + 1, 1 - wbuf, wo_guard[1 - wbuf]) if c + 1 < NCH else None
        P.wait("t", wo_toks)
        mmtok = None
        for s4 in range(4):
            yb = s4 % 2
            mmw = [norm_last] if (c == 0 and s4 == 0) else []
            if rt_by_bank.get(yb) is not None:
                mmw.append(rt_by_bank[yb])
            for hh in range(H):
                mmtok = P.op("t", lambda e, s4=s4, hh=hh, wbuf=wbuf, yb=yb: e.matmul(
                    qk_ps[yb][:, 0:512],
                    wout_sb[wbuf][:, hh * 128:(hh + 1) * 128],
                    O_all[:, hh * T + s4 * 512: hh * T + (s4 + 1) * 512],
                    start=(hh == 0), stop=(hh == H - 1)),
                    waits=mmw if hh == 0 else [],
                    inc="st" if hh == H - 1 else None)
            rt = P.op("v", lambda e, c=c, s4=s4, yb=yb: e.tensor_add(
                xT[:, c * T + s4 * 512: c * T + (s4 + 1) * 512],
                xT[:, c * T + s4 * 512: c * T + (s4 + 1) * 512], qk_ps[yb][:, 0:512]),
                inc="sv", waits=[mmtok])
            rt_by_bank[yb] = rt
        wo_guard[wbuf] = mmtok
        wo_toks = nwo
    return



class RowStage:

    def __init__(self, P, ctx, name, nslots):
        nc = P.nc
        self.P = P
        self.n = nslots
        self.scr = P.dram(name + "_scr", [nslots, T], F32)
        self.rowbuf = [P.sbuf(ctx, name + "_r%d" % i, [1, T], BF16)
                       for i in range(2)]
        self.row_guard = [None, None]
        self.rcount = 0
        self.st_tok = None

    def store(self, src_ap, waits):
        self.st_tok = self.P.dma("y", self.scr.ap()[:], src_ap, waits=waits, sem="db")
        return self.st_tok

    def load_row(self, idx, waits=()):
        b = self.rcount % 2
        self.rcount += 1
        w = list(waits)
        if self.rcount <= 2:
            w.append(self.st_tok)
        if self.row_guard[b] is not None:
            w.append(self.row_guard[b])
        tok = self.P.dma("g", self.rowbuf[b][0:1, :], self.scr.ap()[idx:idx + 1, :],
                         waits=w, sem="dr%d" % b)
        return self.rowbuf[b][0:1, :], tok, b

    def set_guard(self, b, tok):
        self.row_guard[b] = tok



def emit_l0_ffn(P, ctx, C, xT, xnT, W):
    nc = P.nc
    E = 8
    sb = lambda name, shape, dt: P.sbuf(ctx, name, shape, dt)

    def chain(ops, first_waits=()):
        tok = None
        for eng, fn in ops:
            w = [tok] if tok is not None else list(first_waits)
            tok = P.op(eng, fn, inc={"v": "sv", "s": "ss", "t": "st"}[eng],
                       waits=w)
        return tok

    w_all = sb("f0_wall", [128, 16 * E], F32)
    wT_stage = RowStage(P, ctx, "f0_ws", E)
    y_acc = sb("f0_yacc", [128, NCH * T], BF16)

    with ExitStack() as gctx:
        gsb = lambda name, shape, dt: P.sbuf(gctx, name, shape, dt)
        gps = lambda name, shape: P.psum(gctx, name, shape)
        rfull = gsb("f0_rf", [1, T], F32)
        afull = gsb("f0_af", [1, T], F32)
        ln_tok = emit_ln(P, gctx, C, xT, xnT, ra_rows=(rfull, afull))
        P.fence_dma()
        gw_sb = gsb("f0_gw", [128, 8 * E], F32)
        gb_row = gsb("f0_gbr", [1, E], F32)
        grow = gsb("f0_grow", [1, E], F32)
        gwsum_sb = gsb("f0_gws", [128, E], F32)
        gb_sb = gsb("f0_gb", [128, E], F32)
        rc_sb = gsb("f0_rc", [128, 1], F32)
        ac_sb = gsb("f0_ac", [128, 1], F32)
        tmp8 = gsb("f0_t8", [128, E], F32)
        lg8 = gsb("f0_lg", [128, E], F32)
        sg8 = gsb("f0_sg", [128, E], F32)
        sfc8 = gsb("f0_sfc", [128, E], F32)
        g4 = gsb("f0_g4", [128, 4], F32)
        sc4 = gsb("f0_sc4", [128, 4], F32)
        m1c = gsb("f0_m1", [128, 1], F32)
        m2c = gsb("f0_m2", [128, 1], F32)
        emask = gsb("f0_em", [128, E], F32)
        t8b = gsb("f0_t8b", [128, E], F32)
        wsc = gsb("f0_wsc", [128, 1], F32)
        wT_sb = gsb("f0_wT", [E, T], F32)
        raw_ps = gps("f0_raw", [128, E])
        r_ps = gps("f0_rps", [128, 1])
        a_ps = gps("f0_aps", [128, 1])
        sum_ps = gps("f0_sum", [1, E])
        bc_ps = gps("f0_bc", [128, E])
        tp_ps = gps("f0_tp", [E, 128])

        for c in range(NCH):
            P.dma("y", gw_sb[:, c * E:(c + 1) * E],
                  W["gate_w"][c * 128:(c + 1) * 128, :], sem="df")
        gwt = P.dma("y", gb_row[:], W["gate_bias"].rearrange("(a e) -> a e", a=1),
                    sem="df")
        smm = None
        for c in range(NCH):
            smm = P.op("t", lambda e, c=c: e.matmul(
                sum_ps[:, 0:E], C["ones_col_f"][:], gw_sb[:, c * E:(c + 1) * E],
                start=(c == 0), stop=(c == NCH - 1)),
                waits=[gwt] if c == 0 else [], inc="st" if c == NCH - 1 else None)
        svt = P.op("s", lambda e: e.activation(out=grow[:], in_=sum_ps[:],
                                               func=AF.Copy), inc="ss", waits=[smm])
        b1 = P.op("t", lambda e: e.matmul(bc_ps[:, 0:E], C["ones_row_f"][:],
                                          grow[:], start=True, stop=True),
                  inc="st", waits=[svt])
        e1 = P.op("v", lambda e: e.tensor_copy(gwsum_sb[:], bc_ps[:]), inc="sv",
                  waits=[b1])
        b2 = P.op("t", lambda e: e.matmul(bc_ps[:, 0:E], C["ones_row_f"][:],
                                          gb_row[:], start=True, stop=True),
                  inc="st", waits=[e1])
        e2 = P.op("v", lambda e: e.tensor_copy(gb_sb[:], bc_ps[:]), inc="sv",
                  waits=[b2])
        P.wait("t", ln_tok)
        prev = e2
        wtr_toks = []
        for tc in range(NTCH):
            mm = None
            for c in range(NCH):
                mm = P.op("t", lambda e, c=c, tc=tc: e.matmul(
                    raw_ps[:, 0:E],
                    xT[:, c * T + tc * 128: c * T + (tc + 1) * 128],
                    gw_sb[:, c * E:(c + 1) * E],
                    start=(c == 0), stop=(c == NCH - 1)),
                    waits=[prev] if c == 0 else [],
                    inc="st" if c == NCH - 1 else None)
            rmm = P.op("t", lambda e, tc=tc: e.matmul(
                r_ps[:], rfull[0:1, tc * 128:(tc + 1) * 128],
                C["ones_row_f"][0:1, 0:1], start=True, stop=True), inc="st")
            amm = P.op("t", lambda e, tc=tc: e.matmul(
                a_ps[:], afull[0:1, tc * 128:(tc + 1) * 128],
                C["ones_row_f"][0:1, 0:1], start=True, stop=True), inc="st")
            cp1 = P.op("v", lambda e: e.tensor_copy(rc_sb[:], r_ps[:]), inc="sv",
                       waits=[rmm, amm])
            cp2 = P.op("v", lambda e: e.tensor_copy(ac_sb[:], a_ps[:]), inc="sv")
            sig = chain([
                ("v", lambda e: e.tensor_scalar(tmp8[:], raw_ps[:], rc_sb[:], None,
                                                ALU.mult)),
                ("v", lambda e: e.scalar_tensor_tensor(lg8[:], gwsum_sb[:], ac_sb[:],
                                                       tmp8[:], ALU.mult, ALU.add)),
            ], first_waits=[cp2, mm])
            sg_t = P.op("s", lambda e: e.activation(out=sg8[:], in_=lg8[:],
                                                    func=AF.Sigmoid), inc="ss",
                        waits=[sig])
            last = chain([
                ("v", lambda e: e.tensor_add(sfc8[:], sg8[:], gb_sb[:])),
                ("v", lambda e: e.tensor_add(g4[:], sfc8[:, 0:E:2], sfc8[:, 1:E:2])),
                ("v", lambda e: e.reduce_max(m1c[:], g4[:], axis=mybir.AxisListType.X)),
                ("v", lambda e: e.tensor_scalar(sc4[:], g4[:], m1c[:], None,
                                                ALU.is_ge)),
                ("v", lambda e: e.scalar_tensor_tensor(sc4[:], sc4[:], -1e9, g4[:],
                                                       ALU.mult, ALU.add)),
                ("v", lambda e: e.reduce_max(m2c[:], sc4[:], axis=mybir.AxisListType.X)),
                ("v", lambda e: e.tensor_scalar(sc4[:], g4[:], m2c[:], None,
                                                ALU.is_ge)),
                ("v", lambda e: e.tensor_copy(emask[:, 0:E:2], sc4[:])),
                ("v", lambda e: e.tensor_copy(emask[:, 1:E:2], sc4[:])),
                ("v", lambda e: e.tensor_mul(t8b[:], sfc8[:], emask[:])),
                ("v", lambda e: e.reduce_max(m1c[:], t8b[:], axis=mybir.AxisListType.X)),
                ("v", lambda e: e.tensor_scalar(tmp8[:], t8b[:], m1c[:], None,
                                                ALU.is_ge)),
                ("v", lambda e: e.scalar_tensor_tensor(tmp8[:], tmp8[:], -1e9, t8b[:],
                                                       ALU.mult, ALU.add)),
                ("v", lambda e: e.reduce_max(m2c[:], tmp8[:], axis=mybir.AxisListType.X)),
                ("v", lambda e: e.tensor_add(wsc[:], m1c[:], m2c[:])),
                ("v", lambda e: e.tensor_scalar(wsc[:], wsc[:], 1e-20, None, ALU.add)),
                ("v", lambda e: e.reciprocal(wsc[:], wsc[:])),
                ("v", lambda e: e.tensor_scalar(tmp8[:], t8b[:], m2c[:], None,
                                                ALU.is_ge)),
                ("v", lambda e: e.tensor_mul(t8b[:], t8b[:], tmp8[:])),
                ("v", lambda e, tc=tc: e.tensor_scalar(
                    w_all[:, tc * E:(tc + 1) * E], t8b[:], wsc[:], 2.5,
                    ALU.mult, ALU.mult)),
            ], first_waits=[sg_t])
            tmm = P.op("t", lambda e, tc=tc: e.transpose(
                tp_ps[:], w_all[:, tc * E:(tc + 1) * E], C["identf"][:]),
                inc="st", waits=[last])
            wtr = P.op("s", lambda e, tc=tc: e.activation(
                out=wT_sb[:, tc * 128:(tc + 1) * 128], in_=tp_ps[:], func=AF.Copy),
                inc="ss", waits=[tmm])
            wtr_toks.append(wtr)
            prev = wtr
        wT_stage.store(wT_sb[:], waits=[wtr_toks[-1]])

    with ExitStack() as ectx:
        esb = lambda name, shape, dt: P.sbuf(ectx, name, shape, dt)
        eps = lambda name, shape: P.psum(ectx, name, shape)
        P.fence_dma()
        P.wait("g", wT_stage.st_tok)
        wg_sb = [esb("f0_wg%d" % i, [128, 8 * 512], BF16) for i in range(2)]
        wu_sb = [esb("f0_wu%d" % i, [128, 8 * 512], BF16) for i in range(2)]
        wd_sb = [esb("f0_wd0", [128, 4 * 1024], BF16)] * 2
        gu_sb = esb("f0_gu", [128, 4 * T], BF16)
        sgu = [esb("f0_sgu%d" % i, [128, 512], BF16) for i in range(4)]
        wbc_sb = esb("f0_wbc", [128, T], BF16)
        g_ps = [eps("f0_gps%d" % i, [128, 512]) for i in range(4)]
        u_ps = [eps("f0_ups%d" % i, [128, 512]) for i in range(4)]
        wb_ps = g_ps[0]
        y_ps = [g_ps[1], g_ps[2]]

        def load_expert(e_, buf, guard):
            tk = None
            for c in range(NCH):
                tk = P.dma("g", wg_sb[buf][:, c * 512:(c + 1) * 512],
                           W["w_gate"][e_, c * 128:(c + 1) * 128, :],
                           waits=[guard] if (guard is not None and c == 0) else [],
                           sem="dg%d" % buf)
                tk = P.dma("g", wu_sb[buf][:, c * 512:(c + 1) * 512],
                           W["w_up"][e_, c * 128:(c + 1) * 128, :], sem="dg%d" % buf)
            return [tk]

        def load_expert_wd(e_, guard):
            tk = None
            for ic in range(4):
                tk = P.dma("g", wd_sb[0][:, ic * 1024:(ic + 1) * 1024],
                           W["w_down"][e_, ic * 128:(ic + 1) * 128, :],
                           waits=[guard] if (guard is not None and ic == 0) else [],
                           sem="dg2")
            return [tk]

        wtoks = load_expert(0, 0, None)
        wdtoks = load_expert_wd(0, None)
        prev_use = [None, None]
        gu_done = [None, None]
        for e_ in range(E):
            buf = e_ % 2
            w_row, w_tok, w_b = wT_stage.load_row(e_)
            nw = load_expert(e_ + 1, 1 - buf, gu_done[1 - buf]) if e_ + 1 < E else None
            bts = []
            for s4 in range(4):
                bank = g_ps[0] if s4 % 2 == 0 else u_ps[0]
                bw = [w_tok] if s4 == 0 else ([bts[s4 - 2]] if s4 >= 2 else [])
                bmm = P.op("t", lambda e, s4=s4, w_row=w_row, bank=bank: e.matmul(
                    bank[:, 0:512], C["ones_row_b"][:],
                    w_row[0:1, s4 * 512:(s4 + 1) * 512], start=True, stop=True),
                    inc="st", waits=bw)
                bts.append(P.op("v", lambda e, s4=s4, bank=bank: e.tensor_copy(
                    wbc_sb[:, s4 * 512:(s4 + 1) * 512], bank[:, 0:512]), inc="sv",
                    waits=[bmm]))
            bt = bts[-1]
            wT_stage.set_guard(w_b, bmm)
            P.wait("t", wtoks)
            gu_toks = []
            t2_by_bank = {}
            pend_gu = []

            def flush_gu(bt=bt):
                s4, mms = pend_gu.pop(0)
                for ic in range(4):
                    gmm, umm = mms[ic]
                    st = P.op("s", lambda e, ic=ic: e.activation(
                        out=sgu[ic][:, 0:512], in_=g_ps[ic][:, 0:512], func=AF.Silu),
                        inc="ss",
                        waits=[gmm] + ([t2_by_bank.get(ic)] if t2_by_bank.get(ic) else []))
                    t1 = P.op("v", lambda e, ic=ic: e.tensor_mul(
                        sgu[ic][:, 0:512], sgu[ic][:, 0:512], u_ps[ic][:, 0:512]),
                        inc="sv", waits=[st, umm])
                    t2 = P.op("v", lambda e, ic=ic, s4=s4: e.tensor_mul(
                        gu_sb[:, ic * T + s4 * 512: ic * T + (s4 + 1) * 512],
                        sgu[ic][:, 0:512], wbc_sb[:, s4 * 512:(s4 + 1) * 512]),
                        inc="sv", waits=[t1, bt] if (ic == 0 and s4 == 0) else [t1])
                    gu_toks.append(t2)
                    t2_by_bank[ic] = t2

            P.wait("t", bt)
            for s4 in range(4):
                mms = {}
                for ic in range(4):
                    if t2_by_bank.get(ic) is not None and s4 > 0:
                        P.wait("t", t2_by_bank[ic])
                    gmm = None
                    for c in range(NCH):
                        gmm = P.op("t", lambda e, c=c, ic=ic, s4=s4, buf=buf: e.matmul(
                            g_ps[ic][:, 0:512],
                            wg_sb[buf][:, c * 512 + ic * 128: c * 512 + (ic + 1) * 128],
                            xnT[:, c * T + s4 * 512: c * T + (s4 + 1) * 512],
                            start=(c == 0), stop=(c == NCH - 1)),
                            inc="st" if c == NCH - 1 else None)
                    umm = None
                    for c in range(NCH):
                        umm = P.op("t", lambda e, c=c, ic=ic, s4=s4, buf=buf: e.matmul(
                            u_ps[ic][:, 0:512],
                            wu_sb[buf][:, c * 512 + ic * 128: c * 512 + (ic + 1) * 128],
                            xnT[:, c * T + s4 * 512: c * T + (s4 + 1) * 512],
                            start=(c == 0), stop=(c == NCH - 1)),
                            inc="st" if c == NCH - 1 else None)
                    mms[ic] = (gmm, umm)
                pend_gu.append((s4, mms))
                flush_gu()
            gu_done[buf] = umm
            P.wait("t", wdtoks)
            dtok = None
            for c in range(NCH):
                for s4 in range(4):
                    yb = (c * 4 + s4) % 2
                    mmw = [gu_toks[-1]] if (c == 0 and s4 == 0) else []
                    if dtok is not None and (c * 4 + s4) >= 2:
                        mmw.append(d_ev[yb])
                    mm = None
                    for ic in range(4):
                        mm = P.op("t", lambda e, c=c, s4=s4, ic=ic, buf=buf, yb=yb: e.matmul(
                            y_ps[yb][:, 0:512],
                            wd_sb[buf][:, ic * 1024 + c * 128: ic * 1024 + (c + 1) * 128],
                            gu_sb[:, ic * T + s4 * 512: ic * T + (s4 + 1) * 512],
                            start=(ic == 0), stop=(ic == 3)),
                            waits=mmw if ic == 0 else [],
                            inc="st" if ic == 3 else None)
                    if e_ == 0:
                        ev = P.op("v", lambda e, c=c, s4=s4, yb=yb: e.tensor_copy(
                            y_acc[:, c * T + s4 * 512: c * T + (s4 + 1) * 512],
                            y_ps[yb][:, 0:512]), inc="sv", waits=[mm])
                    else:
                        ev = P.op("v", lambda e, c=c, s4=s4, yb=yb: e.tensor_add(
                            y_acc[:, c * T + s4 * 512: c * T + (s4 + 1) * 512],
                            y_acc[:, c * T + s4 * 512: c * T + (s4 + 1) * 512],
                            y_ps[yb][:, 0:512]), inc="sv", waits=[mm])
                    if c == 0 and s4 == 0:
                        d_ev = {}
                    d_ev[yb] = ev
                    dtok = ev
            prev_use[buf] = dtok
            wtoks = nw
            if e_ + 1 < E:
                wdtoks = load_expert_wd(e_ + 1, dtok)

    with ExitStack() as sctx2:
        ssb = lambda name, shape, dt: P.sbuf(sctx2, name, shape, dt)
        sps = lambda name, shape: P.psum(sctx2, name, shape)
        P.fence_dma()
        swg_sb = [ssb("f0_swg%d" % i, [128, 8 * 128], BF16) for i in range(2)]
        swu_sb = [ssb("f0_swu%d" % i, [128, 8 * 128], BF16) for i in range(2)]
        swd_sb = ssb("f0_swd", [128, 8 * 1024], BF16)
        gus = ssb("f0_gus", [128, 8 * T], BF16)
        sgu2 = ssb("f0_sgu2", [128, 512], BF16)
        g2_ps = sps("f0_g2", [128, 512])
        u2_ps = sps("f0_u2", [128, 512])
        y2_ps = [sps("f0_y2%d" % i, [128, 512]) for i in range(2)]

        def load_sw(ic, buf, guard):
            tk = None
            for c in range(NCH):
                tk = P.dma("g", swg_sb[buf][:, c * 128:(c + 1) * 128],
                           W["sw_gate"][c * 128:(c + 1) * 128, ic * 128:(ic + 1) * 128],
                           waits=[guard] if (guard is not None and c == 0) else [],
                           sem="dg%d" % buf)
                tk = P.dma("g", swu_sb[buf][:, c * 128:(c + 1) * 128],
                           W["sw_up"][c * 128:(c + 1) * 128, ic * 128:(ic + 1) * 128],
                           sem="dg%d" % buf)
            return [tk]

        swd_tok = None
        for ic in range(NCH):
            swd_tok = P.dma("g", swd_sb[:, ic * 1024:(ic + 1) * 1024],
                            W["sw_down"][ic * 128:(ic + 1) * 128, :], sem="dwo0")
        swtoks = load_sw(0, 0, None)
        prev_use2 = [None, None]
        prev_lt = None
        for ic in range(NCH):
            buf = ic % 2
            nw = load_sw(ic + 1, 1 - buf, prev_use2[1 - buf]) if ic + 1 < NCH else None
            P.wait("t", swtoks)
            lt = None
            for s4 in range(4):
                gmm = None
                for c in range(NCH):
                    gmm = P.op("t", lambda e, c=c, s4=s4, buf=buf: e.matmul(
                        g2_ps[:, 0:512],
                        swg_sb[buf][:, c * 128:(c + 1) * 128],
                        xnT[:, c * T + s4 * 512: c * T + (s4 + 1) * 512],
                        start=(c == 0), stop=(c == NCH - 1)),
                        inc="st" if c == NCH - 1 else None)
                umm = None
                for c in range(NCH):
                    umm = P.op("t", lambda e, c=c, s4=s4, buf=buf: e.matmul(
                        u2_ps[:, 0:512],
                        swu_sb[buf][:, c * 128:(c + 1) * 128],
                        xnT[:, c * T + s4 * 512: c * T + (s4 + 1) * 512],
                        start=(c == 0), stop=(c == NCH - 1)),
                        inc="st" if c == NCH - 1 else None)
                st = P.op("s", lambda e: e.activation(
                    out=sgu2[:, 0:512], in_=g2_ps[:, 0:512], func=AF.Silu),
                    inc="ss",
                    waits=[gmm] + ([prev_lt] if prev_lt is not None else []))
                lt = P.op("v", lambda e, ic=ic, s4=s4: e.tensor_mul(
                    gus[:, ic * T + s4 * 512: ic * T + (s4 + 1) * 512],
                    sgu2[:, 0:512], u2_ps[:, 0:512]), inc="sv", waits=[st, umm])
                prev_lt = lt
                P.wait("t", st)
                P.wait("t", lt)
            prev_use2[buf] = lt
            swtoks = nw
        d_ev2 = {}
        for c in range(NCH):
            for s4 in range(4):
                yb = (c * 4 + s4) % 2
                mmw = [lt, swd_tok] if (c == 0 and s4 == 0) else []
                if d_ev2.get(yb) is not None:
                    mmw.append(d_ev2[yb])
                mm = None
                for ic in range(NCH):
                    mm = P.op("t", lambda e, c=c, s4=s4, ic=ic, yb=yb: e.matmul(
                        y2_ps[yb][:, 0:512],
                        swd_sb[:, ic * 1024 + c * 128: ic * 1024 + (c + 1) * 128],
                        gus[:, ic * T + s4 * 512: ic * T + (s4 + 1) * 512],
                        start=(ic == 0), stop=(ic == NCH - 1)),
                        waits=mmw if ic == 0 else [],
                        inc="st" if ic == NCH - 1 else None)
                a1 = P.op("v", lambda e, c=c, s4=s4, yb=yb: e.tensor_add(
                    xT[:, c * T + s4 * 512: c * T + (s4 + 1) * 512],
                    xT[:, c * T + s4 * 512: c * T + (s4 + 1) * 512],
                    y2_ps[yb][:, 0:512]), waits=[mm], inc="sv")
                a2 = P.op("v", lambda e, c=c, s4=s4: e.tensor_add(
                    xT[:, c * T + s4 * 512: c * T + (s4 + 1) * 512],
                    xT[:, c * T + s4 * 512: c * T + (s4 + 1) * 512],
                    y_acc[:, c * T + s4 * 512: c * T + (s4 + 1) * 512]), inc="sv",
                    waits=[a1])
                d_ev2[yb] = a2
    return




def emit_attn_b(P, ctx, C, xT, xnT, W):
    nc = P.nc
    H = 16
    sb = lambda name, shape, dt: P.sbuf(ctx, name, shape, dt)
    psb = lambda name, shape: P.psum(ctx, name, shape)
    P.fence_dma()
    scale = 1.0 / math.sqrt(32.0)
    ln_tok = P.tok("sv")

    Qc = sb("ab_qc", [128, 2 * T], BF16)
    Kc = sb("ab_kc", [128, 2 * T], BF16)
    Vg = sb("ab_vg", [128, 4 * 17 * 65], BF16)
    knull = sb("ab_knull", [128, 2 * 128], BF16)
    Oc = sb("ab_oc", [128, 2 * T], BF16)
    stg = sb("ab_stg", [128, T], BF16)
    rg = sb("ab_rg", [128, T], BF16)
    rp = sb("ab_rp", [128, T], BF16)
    rtmp = sb("ab_rtmp", [128, T], BF16)
    rsw = sb("ab_rsw", [128, T], BF16)
    wq = sb("ab_wq", [128, 8 * 128], BF16)
    wk = sb("ab_wk", [128, 8 * 128], BF16)
    wqg = sb("ab_wqg", [128, 8 * 128], BF16)
    wkg = sb("ab_wkg", [128, 8 * 128], BF16)
    wv = sb("ab_wv", [128, 8 * 256], BF16)
    wo_g = [sb("ab_wo%d" % i, [128, 2 * 1024], BF16) for i in range(2)]
    sums_sb = sb("ab_sums", [65, T], F32)
    expT = [sb("ab_exp%d" % i, [128, 512], BF16) for i in range(3)]
    vnrow = sb("ab_vn", [1, H * 64], BF16)

    pr_ps = [psb("ab_pr%d" % i, [128, 512]) for i in range(2)]
    sc_ps = [psb("ab_sc%d" % i, [128, 512]) for i in range(3)]
    pv_ps = psb("ab_pv", [65, 512])
    nb_ps = psb("ab_nb", [128, 512])
    yo_ps = [psb("ab_yo0", [128, 512]), nb_ps]

    bounce4 = RowBounce(P, ctx, "ab_b", 4)
    bounce = [bounce4] * 4
    vn_tok = P.dma("g", vnrow[:], W["v_null"].rearrange("(a n) -> a n", a=1),
                   sem="da")
    invf = (10000.0 ** (-np.arange(16, dtype=np.float64) / 16.0))
    ang = np.arange(T, dtype=np.float64)[None, :] * invf[np.arange(128) % 16][:, None]
    cosF_np = np.cos(ang).astype(bfdt)
    sinS_np = np.sin(ang).astype(np.float32)
    sinS_np[:64] *= -1.0
    cosF_t = sb("ab_cosF", [128, T], BF16)
    sinS_t = sb("ab_sinS", [128, T], BF16)
    P.dma("y", cosF_t[:], nc.inline_tensor(np.ascontiguousarray(cosF_np),
                                           name=P._uniq("c_cosF")).ap(), sem="drt")
    rope_tok = P.dma("y", sinS_t[:],
                     nc.inline_tensor(np.ascontiguousarray(sinS_np.astype(bfdt)),
                                      name=P._uniq("c_sinS")).ap(), sem="drt")
    C = dict(C)
    C["cosF"], C["sinS"] = cosF_t, sinS_t

    prev_group_done = None
    prev_pe_done = None
    sums_guard = None
    last_exp = [None, None, None]
    exp_guard = {0: None, 1: None, 2: None}
    yev = {}
    sh_ap, sh_tok = C["bias"](-SHIFT)

    proj_done_tok = None
    for g in range(4):
        wtok = None
        guard = [proj_done_tok] if proj_done_tok is not None else []
        for c in range(NCH):
            r0, r1 = c * 128, (c + 1) * 128
            wtok = P.dma("g", wq[:, r0:r1], W["wq_sem"][r0:r1, g * 128:(g + 1) * 128],
                         waits=guard if c == 0 else [], sem="dg0")
            wtok = P.dma("g", wk[:, r0:r1], W["wk_sem"][r0:r1, g * 128:(g + 1) * 128],
                         sem="dg0")
            for wgt, dst in ((W["wq_geo"], wqg), (W["wk_geo"], wkg)):
                s3 = wgt[r0:r1, g * 128:(g + 1) * 128].rearrange(
                    "r (h s) -> r h s", h=4, s=32)
                wtok = P.dma("g", dst[:, c * 128:c * 128 + 64], s3[:, :, 0:16],
                             sem="dg0")
                wtok = P.dma("g", dst[:, c * 128 + 64:(c + 1) * 128], s3[:, :, 16:32],
                             sem="dg0")
            wtok = P.dma("g", wv[:, c * 256:(c + 1) * 256],
                         W["wv"][r0:r1, g * 256:(g + 1) * 256], sem="dg0")
        for c in range(NCH):
            r0, r1 = c * 128, (c + 1) * 128
            P.dma("g", wo_g[g % 2][:, r0:r1],
                  W["wo"][g * 256:g * 256 + 128, r0:r1],
                  waits=[prev_pe_done] if (prev_pe_done is not None and c == 0) else [],
                  sem="dg1")
            P.dma("g", wo_g[g % 2][:, 1024 + r0:1024 + r1],
                  W["wo"][g * 256 + 128:(g + 1) * 256, r0:r1], sem="dg1")
        kz = P.op("g", lambda e: e.memset(knull[:], 0.0),
                  waits=[prev_pe_done] if prev_pe_done is not None else [], inc="sg")
        ktok = None
        for j in range(4):
            h = 4 * g + j
            p, hh = j // 2, j % 2
            P.dma("g", knull[hh * 64:hh * 64 + 32, p * 128:p * 128 + 1],
                  W["k_sem_null"].rearrange("(h s) -> h s", h=H)[h:h + 1, :],
                  waits=[kz] if j == 0 else [], sem="dg1")
            ktok = P.dma("g", knull[hh * 64 + 32:hh * 64 + 64, p * 128:p * 128 + 1],
                         W["k_geo_null"].rearrange("(h s) -> h s", h=H)[h:h + 1, :],
                         sem="dg1")
        P.wait("t", ln_tok)
        if prev_group_done is not None:
            P.wait("t", prev_group_done)
        last_scat = None
        ev_bank = {}
        bank_i = [0]

        def pick_bank():
            b = bank_i[0] % 2
            bank_i[0] += 1
            if ev_bank.get(b) is not None:
                P.wait("t", ev_bank[b])
            return b

        for mat, wmat, dstc in ((0, wq, Qc), (1, wk, Kc)):
            evs = None
            for s4 in range(4):
                b = pick_bank()
                mm = None
                for c in range(NCH):
                    mm = P.op("t", lambda e, s4=s4, c=c, wmat=wmat, b=b: e.matmul(
                        pr_ps[b][:, 0:512],
                        wmat[:, c * 128:(c + 1) * 128],
                        xnT[:, c * T + s4 * 512: c * T + (s4 + 1) * 512],
                        start=(c == 0), stop=(c == NCH - 1)),
                        waits=[wtok, ktok] if (mat == 0 and s4 == 0 and c == 0) else [],
                        inc="st" if c == NCH - 1 else None)
                w = [mm]
                if s4 == 0 and last_scat is not None:
                    w.append(last_scat)
                evs = P.op("v", lambda e, s4=s4, b=b: e.tensor_copy(
                    stg[:, s4 * 512:(s4 + 1) * 512], pr_ps[b][:, 0:512]), inc="sv",
                    waits=w)
                ev_bank[b] = evs
            for j in range(4):
                p, hh = j // 2, j % 2
                last_scat = P.dma("s", dstc[hh * 64:hh * 64 + 32, p * T:(p + 1) * T],
                                  stg[32 * j:32 * j + 32, :],
                                  waits=[evs] if j == 0 else [], sem="ds")
        for mat, wmat, dstc in ((0, wqg, Qc), (1, wkg, Kc)):
            evs = None
            for s4 in range(4):
                b = pick_bank()
                mm = None
                for c in range(NCH):
                    mm = P.op("t", lambda e, s4=s4, c=c, wmat=wmat, b=b: e.matmul(
                        pr_ps[b][:, 0:512],
                        wmat[:, c * 128:(c + 1) * 128],
                        xnT[:, c * T + s4 * 512: c * T + (s4 + 1) * 512],
                        start=(c == 0), stop=(c == NCH - 1)),
                        inc="st" if c == NCH - 1 else None)
                w = [mm]
                if s4 == 0:
                    w.append(last_scat)
                evs = P.op("v", lambda e, s4=s4, b=b: e.tensor_copy(
                    rg[:, s4 * 512:(s4 + 1) * 512], pr_ps[b][:, 0:512]), inc="sv",
                    waits=w)
                ev_bank[b] = evs
            c1 = P.op("v", lambda e: e.tensor_copy(rsw[0:64, :], rg[64:128, :]),
                      waits=[evs], inc="sv")
            c2 = P.op("v", lambda e: e.tensor_copy(rsw[64:128, :], rg[0:64, :]),
                      inc="sv")
            m1 = P.op("v", lambda e: e.tensor_mul(rp[:], rg[:], C["cosF"][:]),
                      inc="sv", waits=[last_scat, rope_tok])
            m2 = P.op("v", lambda e: e.tensor_mul(rtmp[:], rsw[:], C["sinS"][:]),
                      inc="sv", waits=[c2])
            m3 = P.op("v", lambda e: e.tensor_add(rp[:], rp[:], rtmp[:]),
                      inc="sv", waits=[m1, m2])
            for j in range(4):
                p, hh = j // 2, j % 2
                last_scat = P.dma("s", dstc[hh * 64 + 32:hh * 64 + 48, p * T:(p + 1) * T],
                                  rp[16 * j:16 * j + 16, :],
                                  waits=[m3] if j == 0 else [], sem="ds")
                last_scat = P.dma("s", dstc[hh * 64 + 48:hh * 64 + 64, p * T:(p + 1) * T],
                                  rp[64 + 16 * j:64 + 16 * j + 16, :], sem="ds")
        vz = P.op("g", lambda e: e.memset(Vg[:], 0.0),
                  waits=[prev_pe_done] if prev_pe_done is not None else [], inc="sg")
        vone = P.op("g", lambda e: e.memset(Vg[:, 64::65], 1.0), inc="sg",
                    waits=[vz])
        vev = None
        for tc in range(NTCH):
            b = pick_bank()
            mm = None
            for c in range(NCH):
                mm = P.op("t", lambda e, tc=tc, c=c, b=b: e.matmul(
                    pr_ps[b][:, 0:256],
                    xnT[:, c * T + tc * 128: c * T + (tc + 1) * 128],
                    wv[:, c * 256:(c + 1) * 256],
                    start=(c == 0), stop=(c == NCH - 1)),
                    waits=[vone] if (tc == 0 and c == 0) else [],
                    inc="st" if c == NCH - 1 else None)
            for j in range(4):
                vev = P.op("v", lambda e, tc=tc, j=j, b=b: e.tensor_copy(
                    Vg[:, j * 1105 + tc * 65: j * 1105 + tc * 65 + 64],
                    pr_ps[b][:, j * 64:(j + 1) * 64]), waits=[mm] if j == 0 else [],
                    inc="sv" if j == 3 else None)
            ev_bank[b] = vev
        proj_done_tok = P.tok("st")
        vts = None
        for j in range(4):
            h = 4 * g + j
            vts = P.dma("s", Vg[0:1, j * 1105 + 16 * 65: j * 1105 + 16 * 65 + 64],
                        vnrow[0:1, h * 64:(h + 1) * 64],
                        waits=[vn_tok, vev] if j == 0 else [], sem="ds")
        pvsum_tok = None
        hd_prev = None
        for j in range(4):
            p, hh = j // 2, j % 2
            qbase = hh * 64
            for qc in range(4):
                nkt = 4 * qc + 4
                oev_prev = hd_prev
                pend = []

                def flush_one(pend=None, j=j, nkt=nkt, oev_prev=oev_prev):
                    kt, etok, vt_idx = pend.pop(0)
                    pw = [etok]
                    if kt == 0 and oev_prev is not None:
                        pw.append(oev_prev)
                    tk = P.op("t", lambda e, j=j, vt_idx=vt_idx, kt=kt, nkt=nkt: e.matmul(
                        pv_ps[:, 0:512],
                        Vg[:, j * 1105 + vt_idx * 65: j * 1105 + (vt_idx + 1) * 65],
                        expT[kt % 3][:, 0:512], start=(kt == 0), stop=(kt == nkt)),
                        waits=pw, inc="st")
                    exp_guard[kt % 3] = tk
                    return tk

                for kt in range(nkt + 1):
                    par = kt % 3
                    scp = sc_ps[par]
                    is_null = kt == nkt
                    swaits = []
                    if g == 0 and j == 0 and qc == 0 and kt == 0:
                        swaits.append(sh_tok)
                    if j == 0 and qc == 0 and kt == 0:
                        swaits.append(vts)
                    if last_exp[par] is not None:
                        swaits.append(last_exp[par])
                    if is_null:
                        smm = P.op("t", lambda e, p=p, qbase=qbase, qc=qc, scp=scp: e.matmul(
                            scp[:, 0:512],
                            knull[qbase:qbase + 64, p * 128:(p + 1) * 128],
                            Qc[qbase:qbase + 64, p * T + qc * 512: p * T + (qc + 1) * 512],
                            start=True, stop=True), waits=swaits, inc="st")
                    else:
                        smm = P.op("t", lambda e, p=p, qbase=qbase, kt=kt, qc=qc, scp=scp: e.matmul(
                            scp[:, 0:512],
                            Kc[qbase:qbase + 64, p * T + kt * 128: p * T + (kt + 1) * 128],
                            Qc[qbase:qbase + 64, p * T + qc * 512: p * T + (qc + 1) * 512],
                            start=True, stop=True), waits=swaits, inc="st")
                    ew = [smm]
                    if exp_guard[par] is not None:
                        ew.append(exp_guard[par])
                    etok = P.op("s", lambda e, par=par, scp=scp: e.activation(
                        out=expT[par][:, 0:512], in_=scp[:, 0:512], func=AF.Exp,
                        scale=scale, bias=sh_ap), inc="ss", waits=ew)
                    if is_null:
                        etok = P.op("v", lambda e, par=par: e.tensor_scalar(
                            expT[par][:, 0:512], expT[par][:, 0:512],
                            C["nullmask"][:], None, ALU.mult), inc="sv", waits=[etok])
                    elif kt >= 4 * qc:
                        v_ = kt - 4 * qc
                        etok = P.op("v", lambda e, par=par, v_=v_: e.tensor_mul(
                            expT[par][:, 0:512], expT[par][:, 0:512],
                            C["maskw"][:, v_ * 512:(v_ + 1) * 512]), inc="sv",
                            waits=[etok])
                    last_exp[par] = etok
                    pend.append((kt, etok, 16 if is_null else kt))
                    if len(pend) > 2:
                        pvsum_tok = flush_one(pend)
                while pend:
                    pvsum_tok = flush_one(pend)
                ow = [pvsum_tok]
                if g > 0 and j == 0 and qc == 0:
                    ow.append(prev_pe_done)
                oev = P.op("v", lambda e, p=p, qbase=qbase, qc=qc: e.tensor_copy(
                    Oc[qbase:qbase + 64, p * T + qc * 512: p * T + (qc + 1) * 512],
                    pv_ps[0:64, 0:512]), inc="sv", waits=ow)
                sw = [pvsum_tok]
                if qc == 0 and sums_guard is not None:
                    sw.append(sums_guard)
                sev = P.op("s", lambda e, qc=qc: e.activation(
                    out=sums_sb[64:65, qc * 512:(qc + 1) * 512],
                    in_=pv_ps[64:65, 0:512], func=AF.Copy), inc="ss", waits=sw)
                hd_prev = oev
                P.wait("t", oev)
                P.wait("t", sev)
            sums_guard = bounce[g].write(sums_sb[64:65, :], j, waits=[sev])
        bounce[g].recip_all()
        norm_last = None
        nbank2 = [nb_ps, yo_ps[0]]
        nb_guard = {}
        rcur = bounce[g].load_row(0)
        for j in range(4):
            p, hh = j // 2, j % 2
            qbase = hh * 64
            rap, rtok, rb = rcur
            if j + 1 < 4:
                rcur = bounce[g].load_row(j + 1)
            bmm = None
            for s4 in range(4):
                nb = (j * 4 + s4) % 2
                w = [rtok] if s4 == 0 else []
                if nb_guard.get(nb) is not None:
                    w.append(nb_guard[nb])
                bmm = P.op("t", lambda e, s4=s4, rap=rap, nb=nb: e.matmul(
                    nbank2[nb][:, 0:512], C["ones_row_b"][:],
                    rap[0:1, s4 * 512:(s4 + 1) * 512], start=True, stop=True),
                    inc="st", waits=w)
                norm_last = P.op("v", lambda e, p=p, qbase=qbase, s4=s4, nb=nb: e.tensor_mul(
                    Oc[qbase:qbase + 64, p * T + s4 * 512: p * T + (s4 + 1) * 512],
                    Oc[qbase:qbase + 64, p * T + s4 * 512: p * T + (s4 + 1) * 512],
                    nbank2[nb][0:64, 0:512]), inc="sv", waits=[bmm])
                nb_guard[nb] = norm_last
            bounce[g].set_guard(rb, bmm)
        mm = None
        ydone = None
        for c in range(NCH):
            for s4 in range(4):
                yb = (c * 4 + s4) % 2
                mmw = [norm_last] if (c == 0 and s4 == 0) else []
                if yev.get(yb) is not None:
                    mmw.append(yev[yb])
                for p in range(2):
                    mm = P.op("t", lambda e, c=c, s4=s4, p=p, yb=yb, g=g: e.matmul(
                        yo_ps[yb][:, 0:512],
                        wo_g[g % 2][:, p * 1024 + c * 128: p * 1024 + (c + 1) * 128],
                        Oc[:, p * T + s4 * 512: p * T + (s4 + 1) * 512],
                        start=(p == 0), stop=(p == 1)),
                        waits=mmw if p == 0 else [], inc="st" if p == 1 else None)
                ydone = P.op("v", lambda e, c=c, s4=s4, yb=yb: e.tensor_add(
                    xT[:, c * T + s4 * 512: c * T + (s4 + 1) * 512],
                    xT[:, c * T + s4 * 512: c * T + (s4 + 1) * 512],
                    yo_ps[yb][:, 0:512]), inc="sv", waits=[mm])
                yev[yb] = ydone
        prev_group_done = ydone
        prev_pe_done = mm
    return




def emit_l1_attn_a(P, ctx, C, xT, xnT, W):
    nc = P.nc
    H, DK, DC = 16, 64, 256
    sb = lambda name, shape, dt: P.sbuf(ctx, name, shape, dt)
    psb = lambda name, shape: P.psum(ctx, name, shape)
    P.fence_dma()
    scale = 1.0 / math.sqrt(DK)
    ln_tok = P.tok("sv")

    cq = sb("ma_cq", [128, 2 * T], BF16)
    ckv = sb("ma_ckv", [128, 2 * T], BF16)
    wdq = sb("ma_wdq", [128, 8 * 256], BF16)
    wdkv = sb("ma_wdkv", [128, 8 * 256], BF16)
    wuq = sb("ma_wuq", [128, 2 * 1024], BF16)
    wuk = sb("ma_wuk", [128, 2 * 1024], BF16)
    wuv = sb("ma_wuv", [128, 2 * 1024], BF16)
    wo_sb = [sb("ma_wo%d" % i, [128, 8 * 128], BF16) for i in range(2)]
    Qp = [sb("ma_qp0", [128, T], BF16)] * 2
    Kp = [sb("ma_kp0", [128, T], BF16)] * 2
    Vp = [sb("ma_vp0", [128, 2 * 16 * 65], BF16)] * 2
    O_all = sb("ma_oall", [128, 8 * T], BF16)
    sums_sb = sb("ma_sums", [65, T], F32)
    expT = [sb("ma_exp%d" % i, [128, 512], BF16) for i in range(3)]

    pr_ps = [psb("ma_pr%d" % i, [128, 512]) for i in range(2)]
    sc_ps = [psb("ma_sc%d" % i, [128, 512]) for i in range(3)]
    pv_ps = psb("ma_pv", [65, 512])
    nb_ps = psb("ma_nb", [128, 512])
    yo_ps = [psb("ma_yo0", [128, 512]), nb_ps]

    bounce = RowBounce(P, ctx, "ma_b", H)
    sh_ap, sh_tok = C["bias"](-SHIFT)

    wt = None
    for c in range(NCH):
        r0, r1 = c * 128, (c + 1) * 128
        wt = P.dma("g", wdq[:, c * 256:(c + 1) * 256], W["W_dq"][r0:r1, :], sem="dg0")
        wt = P.dma("g", wdkv[:, c * 256:(c + 1) * 256], W["W_dkv"][r0:r1, :], sem="dg0")
    for c2 in range(2):
        r0, r1 = c2 * 128, (c2 + 1) * 128
        wt = P.dma("g", wuq[:, c2 * 1024:(c2 + 1) * 1024], W["W_uq"][r0:r1, :], sem="dg0")
        wt = P.dma("g", wuk[:, c2 * 1024:(c2 + 1) * 1024], W["W_uk"][r0:r1, :], sem="dg0")
        wt = P.dma("g", wuv[:, c2 * 1024:(c2 + 1) * 1024], W["W_uv"][r0:r1, :], sem="dg0")

    P.wait("t", ln_tok)
    P.wait("t", wt)
    ev = None
    ev_bank = {}
    bank_i = [0]

    def pick_bank():
        b = bank_i[0] % 2
        bank_i[0] += 1
        if ev_bank.get(b) is not None:
            P.wait("t", ev_bank[b])
        return b

    for dst, wmat in ((cq, wdq), (ckv, wdkv)):
        for c2 in range(2):
            for s4 in range(4):
                b = pick_bank()
                mm = None
                for c in range(NCH):
                    mm = P.op("t", lambda e, c=c, c2=c2, s4=s4, wmat=wmat, b=b: e.matmul(
                        pr_ps[b][:, 0:512],
                        wmat[:, c * 256 + c2 * 128: c * 256 + (c2 + 1) * 128],
                        xnT[:, c * T + s4 * 512: c * T + (s4 + 1) * 512],
                        start=(c == 0), stop=(c == NCH - 1)),
                        inc="st" if c == NCH - 1 else None)
                ev = P.op("v", lambda e, dst=dst, c2=c2, s4=s4, b=b: e.tensor_copy(
                    dst[:, c2 * T + s4 * 512: c2 * T + (s4 + 1) * 512],
                    pr_ps[b][:, 0:512]), inc="sv", waits=[mm])
                ev_bank[b] = ev
    cq_done = ev

    prev_pair_pe = {0: None, 1: None}
    prev_pe_done = None
    sums_guard = None
    last_exp = [None, None, None]
    exp_guard = {0: None, 1: None, 2: None}
    oev_by_head = {}
    for hp in range(8):
        buf = 0
        if prev_pe_done is not None:
            P.wait("t", prev_pe_done)
        for dst, wmat in ((Qp[buf], wuq), (Kp[buf], wuk)):
            for s4 in range(4):
                b = pick_bank()
                mm = None
                for c2 in range(2):
                    mm = P.op("t", lambda e, c2=c2, s4=s4, hp=hp, wmat=wmat, b=b: e.matmul(
                        pr_ps[b][:, 0:512],
                        wmat[:, c2 * 1024 + hp * 128: c2 * 1024 + (hp + 1) * 128],
                        (cq if wmat is wuq else ckv)[:, c2 * T + s4 * 512: c2 * T + (s4 + 1) * 512],
                        start=(c2 == 0), stop=(c2 == 1)),
                        waits=[cq_done] if (s4 == 0 and c2 == 0 and hp == 0) else [],
                        inc="st" if c2 == 1 else None)
                ev = P.op("v", lambda e, dst=dst, s4=s4, b=b: e.tensor_copy(
                    dst[:, s4 * 512:(s4 + 1) * 512], pr_ps[b][:, 0:512]), inc="sv",
                    waits=[mm])
                ev_bank[b] = ev
        vz = P.op("g", lambda e, buf=buf: e.memset(Vp[buf][:], 0.0),
                  waits=[prev_pe_done] if prev_pe_done is not None else [],
                  inc="sg")
        vone = P.op("g", lambda e, buf=buf: e.memset(Vp[buf][:, 64::65], 1.0),
                    inc="sg", waits=[vz])
        vev = None
        for tc in range(NTCH):
            b = pick_bank()
            mm = None
            for c2 in range(2):
                mm = P.op("t", lambda e, tc=tc, c2=c2, hp=hp, b=b: e.matmul(
                    pr_ps[b][:, 0:128],
                    ckv[:, c2 * T + tc * 128: c2 * T + (tc + 1) * 128],
                    wuv[:, c2 * 1024 + hp * 128: c2 * 1024 + (hp + 1) * 128],
                    start=(c2 == 0), stop=(c2 == 1)),
                    waits=[vone] if (tc == 0 and c2 == 0) else [],
                    inc="st" if c2 == 1 else None)
            for j in range(2):
                vev = P.op("v", lambda e, tc=tc, j=j, buf=buf, b=b: e.tensor_copy(
                    Vp[buf][:, j * 1040 + tc * 65: j * 1040 + tc * 65 + 64],
                    pr_ps[b][:, j * 64:(j + 1) * 64]), waits=[mm] if j == 0 else [],
                    inc="sv" if j == 1 else None)
            ev_bank[b] = vev
        pvsum_tok = None
        hd_prev = None
        for j in range(2):
            h = 2 * hp + j
            qbase = j * 64
            for qc in range(4):
                nkt = 4 * qc + 4
                oev_prev = hd_prev
                pend = []

                def flush_one(pend=None, j=j, nkt=nkt, oev_prev=oev_prev, buf=buf):
                    kt, etok = pend.pop(0)
                    pw = [etok]
                    if kt == 0 and oev_prev is not None:
                        pw.append(oev_prev)
                    tk = P.op("t", lambda e, j=j, kt=kt, nkt=nkt, buf=buf: e.matmul(
                        pv_ps[:, 0:512],
                        Vp[buf][:, j * 1040 + kt * 65: j * 1040 + (kt + 1) * 65],
                        expT[kt % 3][:, 0:512], start=(kt == 0), stop=(kt == nkt - 1)),
                        waits=pw, inc="st")
                    exp_guard[kt % 3] = tk
                    return tk

                for kt in range(nkt):
                    par = kt % 3
                    scp = sc_ps[par]
                    swaits = []
                    if hp == 0 and j == 0 and qc == 0 and kt == 0:
                        swaits.append(sh_tok)
                    if j == 0 and qc == 0 and kt == 0:
                        swaits.append(vev)
                    if last_exp[par] is not None:
                        swaits.append(last_exp[par])
                    smm = P.op("t", lambda e, buf=buf, qbase=qbase, kt=kt, qc=qc, scp=scp: e.matmul(
                        scp[:, 0:512],
                        Kp[buf][qbase:qbase + 64, kt * 128:(kt + 1) * 128],
                        Qp[buf][qbase:qbase + 64, qc * 512:(qc + 1) * 512],
                        start=True, stop=True), waits=swaits, inc="st")
                    ew = [smm]
                    if exp_guard[par] is not None:
                        ew.append(exp_guard[par])
                    etok = P.op("s", lambda e, par=par, scp=scp: e.activation(
                        out=expT[par][:, 0:512], in_=scp[:, 0:512], func=AF.Exp,
                        scale=scale, bias=sh_ap), inc="ss", waits=ew)
                    if kt >= 4 * qc:
                        v_ = kt - 4 * qc
                        etok = P.op("v", lambda e, par=par, v_=v_: e.tensor_mul(
                            expT[par][:, 0:512], expT[par][:, 0:512],
                            C["maskw"][:, v_ * 512:(v_ + 1) * 512]), inc="sv",
                            waits=[etok])
                    last_exp[par] = etok
                    pend.append((kt, etok))
                    if len(pend) > 2:
                        pvsum_tok = flush_one(pend)
                while pend:
                    pvsum_tok = flush_one(pend)
                ow = [pvsum_tok]
                oev = P.op("v", lambda e, hp=hp, qbase=qbase, qc=qc: e.tensor_copy(
                    O_all[qbase:qbase + 64, hp * T + qc * 512: hp * T + (qc + 1) * 512],
                    pv_ps[0:64, 0:512]), inc="sv", waits=ow)
                sw = [pvsum_tok]
                if qc == 0 and sums_guard is not None:
                    sw.append(sums_guard)
                sev = P.op("s", lambda e, qc=qc: e.activation(
                    out=sums_sb[64:65, qc * 512:(qc + 1) * 512],
                    in_=pv_ps[64:65, 0:512], func=AF.Copy), inc="ss", waits=sw)
                hd_prev = oev
                oev_by_head.setdefault(h, []).append(oev)
                P.wait("t", oev)
                P.wait("t", sev)
            sums_guard = bounce.write(sums_sb[64:65, :], h, waits=[sev])
        prev_pe_done = pvsum_tok
    bounce.recip_all()
    norm_last = None
    nbank3 = [nb_ps, yo_ps[0]]
    nb_guard = {}
    rcur = bounce.load_row(0)
    for h in range(H):
        hp, j = h // 2, h % 2
        qbase = j * 64
        rap, rtok, rb = rcur
        if h + 1 < H:
            rcur = bounce.load_row(h + 1)
        bmm = None
        for s4 in range(4):
            nb = (h * 4 + s4) % 2
            w = [rtok] if s4 == 0 else []
            if nb_guard.get(nb) is not None:
                w.append(nb_guard[nb])
            bmm = P.op("t", lambda e, s4=s4, rap=rap, nb=nb: e.matmul(
                nbank3[nb][:, 0:512], C["ones_row_b"][:],
                rap[0:1, s4 * 512:(s4 + 1) * 512], start=True, stop=True),
                inc="st", waits=w)
            norm_last = P.op("v", lambda e, hp=hp, qbase=qbase, s4=s4, nb=nb: e.tensor_mul(
                O_all[qbase:qbase + 64, hp * T + s4 * 512: hp * T + (s4 + 1) * 512],
                O_all[qbase:qbase + 64, hp * T + s4 * 512: hp * T + (s4 + 1) * 512],
                nbank3[nb][0:64, 0:512]), inc="sv", waits=[bmm, oev_by_head[h][s4]])
            nb_guard[nb] = norm_last
        bounce.set_guard(rb, bmm)
    def load_wo(c, wbuf, guard):
        tk = None
        for pc in range(8):
            tk = P.dma("g", wo_sb[wbuf][:, pc * 128:(pc + 1) * 128],
                       W["W_o"][pc * 128:(pc + 1) * 128, c * 128:(c + 1) * 128],
                       waits=[guard] if (guard is not None and pc == 0) else [],
                       sem="dwo%d" % wbuf)
        return [tk]

    wo_toks = load_wo(0, 0, None)
    wo_guard = [None, None]
    for c in range(NCH):
        wbuf = c % 2
        nwo = load_wo(c + 1, 1 - wbuf, wo_guard[1 - wbuf]) if c + 1 < NCH else None
        P.wait("t", wo_toks)
        mmtok = None
        for s4 in range(4):
            for pc in range(8):
                mmtok = P.op("t", lambda e, s4=s4, pc=pc, wbuf=wbuf: e.matmul(
                    yo_ps[s4 % 2][:, 0:512],
                    wo_sb[wbuf][:, pc * 128:(pc + 1) * 128],
                    O_all[:, pc * T + s4 * 512: pc * T + (s4 + 1) * 512],
                    start=(pc == 0), stop=(pc == 7)),
                    waits=[norm_last] if (c == 0 and s4 == 0 and pc == 0) else [],
                    inc="st" if pc == 7 else None)
            rt = P.op("v", lambda e, c=c, s4=s4: e.tensor_add(
                xT[:, c * T + s4 * 512: c * T + (s4 + 1) * 512],
                xT[:, c * T + s4 * 512: c * T + (s4 + 1) * 512],
                yo_ps[s4 % 2][:, 0:512]), inc="sv", waits=[mmtok])
            P.wait("t", rt)
        wo_guard[wbuf] = mmtok
        wo_toks = nwo
    return




def emit_l1_ffn(P, ctx, C, xT, xnT, W):
    nc = P.nc
    NH, HS, GS, HD = 16, 512, 128, 64
    sb = lambda name, shape, dt: P.sbuf(ctx, name, shape, dt)

    def chain(ops, first_waits=()):
        tok = None
        for eng, fn in ops:
            w = [tok] if tok is not None else list(first_waits)
            tok = P.op(eng, fn, inc={"v": "sv", "s": "ss", "t": "st"}[eng], waits=w)
        return tok

    w_all = sb("f1_wall", [128, 16 * NH], F32)
    wT_stage = RowStage(P, ctx, "f1_ws", NH)

    with ExitStack() as gctx:
        gsb = lambda name, shape, dt: P.sbuf(gctx, name, shape, dt)
        gps = lambda name, shape: P.psum(gctx, name, shape)
        rfull = gsb("f1_rf", [1, T], F32)
        afull = gsb("f1_af", [1, T], F32)
        ln_tok = emit_ln(P, gctx, C, xT, xnT, ra_rows=(rfull, afull))
        P.fence_dma()
        wr_sb = gsb("f1_wr", [128, 1024], F32)
        wrT_sb = gsb("f1_wrT", [128, 8 * 128], F32)
        cent_sb = gsb("f1_cent", [16, 128], F32)
        cT_sb = gsb("f1_cT", [128, 16], F32)
        csq = gsb("f1_csq", [16, 128], F32)
        Sc = gsb("f1_Sc", [16, 1], F32)
        lc = gsb("f1_lc", [16, 1], F32)
        rn_sb = gsb("f1_rn", [16, 1], F32)
        rne = gsb("f1_rne", [16, 1], F32)
        et_sb = gsb("f1_et", [16, 1], F32)
        tmp_sb = gsb("f1_tmp", [1, 1], F32)
        wrsum_row = gsb("f1_wsr", [1, 128], F32)
        wrsum_col = gsb("f1_wsc", [128, 1], F32)
        z32 = gsb("f1_z32", [128, T], F32)
        ztmp = gsb("f1_ztmp", [128, 512], F32)
        zsq = gsb("f1_zsq", [128, 512], BF16)
        lrow = gsb("f1_lrow", [1, 512], F32)
        rz_row = gsb("f1_rz", [1, T], F32)
        lgT = gsb("f1_lgT", [16, T], F32)
        sb16 = gsb("f1_sb16", [16, 512], F32)
        rt16 = gsb("f1_rt16", [128, 16], F32)
        ex16 = gsb("f1_ex16", [128, 16], F32)
        mx1 = gsb("f1_mx1", [128, 1], F32)
        nmx = gsb("f1_nmx", [128, 1], F32)
        sm1 = gsb("f1_sm1", [128, 1], F32)
        mx8 = gsb("f1_mx8", [128, 8], F32)
        sel = gsb("f1_sel", [128, 16], F32)
        raw_ps = gps("f1_raw", [128, 512])
        rb_ps = gps("f1_rb", [128, 512])
        ab_ps = gps("f1_ab", [128, 512])
        sz_ps = gps("f1_sz", [1, 512])
        cz_ps = gps("f1_cz", [16, 512])
        rzb_ps = gps("f1_rzb", [16, 512])
        tp_ps = gps("f1_tp", [128, 16])
        tp2_ps = gps("f1_tp2", [16, 128])
        ws_ps = tp2_ps
        wc_ps = tp_ps

        for c in range(NCH):
            P.dma("y", wr_sb[:, c * 128:(c + 1) * 128],
                  W["w_router"][:, c * 128:(c + 1) * 128], sem="df")
        P.dma("y", cent_sb[:], W["head_centroids"], sem="df")
        ldt = P.dma("y", tmp_sb[:], W["temperature"].rearrange("(a b) -> a b", a=1),
                    sem="df")
        trt = None
        for c in range(NCH):
            tmm = P.op("t", lambda e, c=c: e.transpose(
                tp_ps[:, 0:16] if False else rb_ps[:, 0:128],
                wr_sb[:, c * 128:(c + 1) * 128], C["identf"][:]),
                inc="st", waits=[ldt] if c == 0 else [trt])
            trt = P.op("v", lambda e, c=c: e.tensor_copy(
                wrT_sb[:, c * 128:(c + 1) * 128], rb_ps[:, 0:128]), inc="sv",
                waits=[tmm])
            P.wait("t", trt)
        smm = None
        for c in range(NCH):
            smm = P.op("t", lambda e, c=c: e.matmul(
                ws_ps[0:1, 0:128], C["ones_col_f"][:],
                wrT_sb[:, c * 128:(c + 1) * 128],
                start=(c == 0), stop=(c == NCH - 1)),
                waits=[trt] if c == 0 else [], inc="st" if c == NCH - 1 else None)
        sev = P.op("s", lambda e: e.activation(out=wrsum_row[:], in_=ws_ps[0:1, 0:128],
                                               func=AF.Copy), inc="ss", waits=[smm])
        cmm = P.op("t", lambda e: e.matmul(wc_ps[:, 0:1], wrsum_row[:],
                                           C["ones_row_f"][0:1, 0:1],
                                           start=True, stop=True), inc="st",
                   waits=[sev])
        wcv = P.op("v", lambda e: e.tensor_copy(wrsum_col[:], wc_ps[:, 0:1]), inc="sv",
                   waits=[cmm])
        ctm = P.op("t", lambda e: e.transpose(tp_ps[:, 0:16], cent_sb[:],
                                              C["identf"][0:16, 0:16]), inc="st",
                   waits=[wcv])
        ctv = P.op("v", lambda e: e.tensor_copy(cT_sb[:], tp_ps[:, 0:16]), inc="sv",
                   waits=[ctm])
        cn = chain([
            ("v", lambda e: e.tensor_mul(csq[:], cent_sb[:], cent_sb[:])),
            ("v", lambda e: e.reduce_sum(Sc[:], csq[:], axis=mybir.AxisListType.X)),
        ], first_waits=[ldt])
        cl = chain([
            ("s", lambda e: e.activation(out=lc[:], in_=Sc[:], func=AF.Ln)),
            ("s", lambda e: e.activation(out=rn_sb[:], in_=lc[:], func=AF.Exp,
                                         scale=-0.5)),
            ("s", lambda e: e.activation(out=tmp_sb[:], in_=tmp_sb[:], func=AF.Exp)),
        ], first_waits=[cn])
        etm = P.op("t", lambda e: e.matmul(tp_ps[0:16, 0:1], C["ones_row_f"][0:1, 0:16],
                                           tmp_sb[:], start=True, stop=True),
                   inc="st", waits=[cl, ctv])
        rne_t = chain([
            ("v", lambda e: e.tensor_copy(et_sb[:], tp_ps[0:16, 0:1])),
            ("v", lambda e: e.tensor_mul(rne[:], rn_sb[:], et_sb[:])),
            ("v", lambda e: e.tensor_scalar(rne[:], rne[:],
                                            float(1.0 / math.sqrt(GS)), None,
                                            ALU.mult)),
        ], first_waits=[etm])
        P.wait("t", rne_t)
        P.wait("t", ln_tok)
        zev = None
        for s4 in range(4):
            mm = None
            for c in range(NCH):
                mm = P.op("t", lambda e, c=c, s4=s4: e.matmul(
                    raw_ps[:, 0:512],
                    wrT_sb[:, c * 128:(c + 1) * 128],
                    xT[:, c * T + s4 * 512: c * T + (s4 + 1) * 512],
                    start=(c == 0), stop=(c == NCH - 1)),
                    inc="st" if c == NCH - 1 else None)
            b1 = P.op("t", lambda e, s4=s4: e.matmul(
                rb_ps[:, 0:512], C["ones_row_f"][:],
                rfull[0:1, s4 * 512:(s4 + 1) * 512], start=True, stop=True))
            b2 = P.op("t", lambda e, s4=s4: e.matmul(
                ab_ps[:, 0:512], C["ones_row_f"][:],
                afull[0:1, s4 * 512:(s4 + 1) * 512], start=True, stop=True),
                inc="st")
            zz = chain([
                ("v", lambda e: e.tensor_copy(ztmp[:], raw_ps[:, 0:512])),
                ("v", lambda e: e.tensor_mul(ztmp[:], ztmp[:], rb_ps[:, 0:512])),
                ("v", lambda e, s4=s4: e.scalar_tensor_tensor(
                    z32[:, s4 * 512:(s4 + 1) * 512], ab_ps[:, 0:512], wrsum_col[:],
                    ztmp[:], ALU.mult, ALU.add)),
                ("v", lambda e, s4=s4: e.tensor_mul(
                    zsq[:], z32[:, s4 * 512:(s4 + 1) * 512],
                    z32[:, s4 * 512:(s4 + 1) * 512])),
            ], first_waits=[mm, b2, rne_t if s4 == 0 else None])
            zmm = P.op("t", lambda e: e.matmul(
                sz_ps[:, 0:512], C["ones_col_b"][:], zsq[:],
                start=True, stop=True), inc="st", waits=[zz])
            zev = chain([
                ("s", lambda e: e.activation(out=lrow[:], in_=sz_ps[:], func=AF.Ln)),
                ("s", lambda e, s4=s4: e.activation(
                    out=rz_row[0:1, s4 * 512:(s4 + 1) * 512], in_=lrow[:],
                    func=AF.Exp, scale=-0.5)),
            ], first_waits=[zmm])
            P.wait("t", zev)
        lg_ev = None
        for s4 in range(4):
            mm = P.op("t", lambda e, s4=s4: e.matmul(
                cz_ps[:, 0:512], cT_sb[:], z32[:, s4 * 512:(s4 + 1) * 512],
                start=True, stop=True), inc="st",
                waits=[zev] if s4 == 0 else [lg_ev])
            bz = P.op("t", lambda e, s4=s4: e.matmul(
                rzb_ps[:, 0:512], C["ones_row_f"][0:1, 0:16],
                rz_row[0:1, s4 * 512:(s4 + 1) * 512], start=True, stop=True),
                inc="st")
            lg_ev = chain([
                ("v", lambda e: e.tensor_scalar(sb16[:], cz_ps[:, 0:512], rne[:],
                                                None, ALU.mult)),
                ("v", lambda e, s4=s4: e.tensor_mul(
                    lgT[:, s4 * 512:(s4 + 1) * 512], sb16[:], rzb_ps[:, 0:512])),
            ], first_waits=[mm, bz])
            P.wait("t", lg_ev)
        wtr = None
        for tc in range(NTCH):
            tmm = P.op("t", lambda e, tc=tc: e.transpose(
                tp_ps[:, 0:16], lgT[:, tc * 128:(tc + 1) * 128],
                C["identf"][0:16, 0:16]), inc="st",
                waits=[lg_ev] if tc == 0 else [wtr])
            cp = P.op("v", lambda e: e.tensor_copy(rt16[:], tp_ps[:, 0:16]),
                      inc="sv", waits=[tmm])
            r1 = chain([
                ("v", lambda e: e.reduce_max(mx1[:], rt16[:],
                                             axis=mybir.AxisListType.X)),
                ("v", lambda e: e.tensor_scalar(nmx[:], mx1[:], -1.0, None,
                                                ALU.mult)),
            ], first_waits=[cp])
            ex_t = P.op("s", lambda e: e.activation(out=ex16[:], in_=rt16[:],
                                                    func=AF.Exp, bias=nmx[:]),
                        inc="ss", waits=[r1])
            wtr_prev = wtr
            wtr = chain([
                ("v", lambda e: e.reduce_sum(sm1[:], ex16[:],
                                             axis=mybir.AxisListType.X)),
                ("v", lambda e: e.reciprocal(sm1[:], sm1[:])),
                ("v", lambda e: e.tensor_scalar(ex16[:], ex16[:], sm1[:], None,
                                                ALU.mult)),
                ("v", lambda e: e.max(out=mx8[:], in_=ex16[:])),
                ("v", lambda e: e.tensor_scalar(sel[:], ex16[:], mx8[:, 1:2], None,
                                                ALU.is_ge)),
                ("v", lambda e, tc=tc: e.tensor_mul(
                    w_all[:, tc * NH:(tc + 1) * NH], ex16[:], sel[:])),
            ], first_waits=[ex_t])
            tm2 = P.op("t", lambda e, tc=tc: e.transpose(
                tp2_ps[:], w_all[:, tc * NH:(tc + 1) * NH],
                C["identf"][:]), inc="st", waits=[wtr])
            wtr = P.op("s", lambda e, tc=tc: e.activation(
                out=lgT[:, tc * 128:(tc + 1) * 128], in_=tp2_ps[:],
                func=AF.Copy), inc="ss", waits=[tm2])
        wT_stage.store(lgT[:], waits=[wtr])

    with ExitStack() as ectx:
        esb = lambda name, shape, dt: P.sbuf(ectx, name, shape, dt)
        eps = lambda name, shape: P.psum(ectx, name, shape)
        P.fence_dma()
        P.wait("g", wT_stage.st_tok)
        wie = [esb("f1_wie%d" % i, [128, 8 * 512], BF16) for i in range(2)]
        woe = [esb("f1_woe%d" % i, [128, 4 * 128], BF16) for i in range(2)]
        h_sb = esb("f1_h", [128, 4 * T], BF16)
        osb = esb("f1_osb", [128, 512], BF16)
        h_ps = [eps("f1_hps%d" % i, [128, 512]) for i in range(4)]
        o_ps = eps("f1_ops", [128, 512])
        wb_ps = [eps("f1_wbps%d" % i, [128, 512]) for i in range(2)]
        wz = [P.op("g", lambda e, i=i: e.memset(woe[i][:], 0.0), inc="sg")
              for i in range(2)]

        def load_head(h, buf, guard):
            tk = None
            g0 = [guard] if guard is not None else []
            if h < 2:
                g0 = g0 + [wz[buf]]
            for c in range(NCH):
                tk = P.dma("g", wie[buf][:, c * 512:(c + 1) * 512],
                           W["input_experts"][h, c * 128:(c + 1) * 128, :],
                           waits=g0 if c == 0 else [], sem="dg%d" % buf)
            for ic in range(4):
                tk = P.dma("g", woe[buf][:, ic * 128 + 64:(ic + 1) * 128],
                           W["output_experts"][h, ic * 128:(ic + 1) * 128, :],
                           sem="dg%d" % buf)
            return [tk]

        wtoks = load_head(0, 0, None)
        prev_use = [None, None]
        prev_omm = None
        prev_mvs = [None, None]
        for h in range(NH):
            buf = h % 2
            w_row, w_tok, w_b = wT_stage.load_row(h)
            nw = load_head(h + 1, 1 - buf, prev_use[1 - buf]) if h + 1 < NH else None
            P.wait("t", wtoks)
            ge_last = None
            ge_by_bank = {}
            pend_h = []

            def flush_h():
                s4, mms = pend_h.pop(0)
                ge = None
                for ic in range(4):
                    gw = [mms[ic]]
                    if ic == 0 and s4 == 0 and prev_omm is not None:
                        gw.append(prev_omm)
                    ge = P.op("s", lambda e, ic=ic, s4=s4: e.activation(
                        out=h_sb[:, ic * T + s4 * 512: ic * T + (s4 + 1) * 512],
                        in_=h_ps[ic][:, 0:512], func=AF.Gelu), inc="ss", waits=gw)
                    ge_by_bank[ic] = ge
                return ge

            for s4 in range(4):
                mms = {}
                for ic in range(4):
                    if ge_by_bank.get(ic) is not None and s4 > 0:
                        P.wait("t", ge_by_bank[ic])
                    mm = None
                    for c in range(NCH):
                        mm = P.op("t", lambda e, c=c, ic=ic, s4=s4, buf=buf: e.matmul(
                            h_ps[ic][:, 0:512],
                            wie[buf][:, c * 512 + ic * 128: c * 512 + (ic + 1) * 128],
                            xnT[:, c * T + s4 * 512: c * T + (s4 + 1) * 512],
                            start=(c == 0), stop=(c == NCH - 1)),
                            inc="st" if c == NCH - 1 else None)
                    mms[ic] = mm
                pend_h.append((s4, mms))
                ge_last = flush_h()
            ot = None
            mvs = []
            cc, half = h // 2, h % 2
            qb = half * 64
            for s4 in range(4):
                mm = None
                for ic in range(4):
                    if half == 0:
                        lhs = woe[buf][:, ic * 128 + 64:(ic + 1) * 128]
                        dst = o_ps[0:64, 0:512]
                    else:
                        lhs = woe[buf][:, ic * 128:(ic + 1) * 128]
                        dst = o_ps[:, 0:512]
                    mm = P.op("t", lambda e, lhs=lhs, dst=dst, ic=ic, s4=s4: e.matmul(
                        dst, lhs,
                        h_sb[:, ic * T + s4 * 512: ic * T + (s4 + 1) * 512],
                        start=(ic == 0), stop=(ic == 3)),
                        waits=[ge_last] if (s4 == 0 and ic == 0) else [],
                        inc="st" if ic == 3 else None)
                wbk = s4 % 2
                bw = [w_tok] if s4 == 0 else []
                pb = mvs[s4 - 2] if s4 >= 2 else prev_mvs[wbk]
                if pb is not None:
                    bw = bw + [pb]
                bmm = P.op("t", lambda e, s4=s4, w_row=w_row, wbk=wbk: e.matmul(
                    wb_ps[wbk][:, 0:512], C["ones_row_b"][:],
                    w_row[0:1, s4 * 512:(s4 + 1) * 512], start=True, stop=True),
                    inc="st", waits=bw)
                ov = P.op("v", lambda e, qb=qb: e.tensor_copy(
                    osb[qb:qb + 64, :], o_ps[qb:qb + 64, 0:512]),
                    inc="sv", waits=[mm] + ([ot] if ot is not None else []))
                mv = P.op("v", lambda e, qb=qb, wbk=wbk: e.tensor_mul(
                    osb[qb:qb + 64, :], osb[qb:qb + 64, :],
                    wb_ps[wbk][qb:qb + 64, 0:512]),
                    inc="sv", waits=[bmm, ov])
                mvs.append(mv)
                ot = P.op("v", lambda e, cc=cc, qb=qb, s4=s4: e.tensor_add(
                    xT[qb:qb + 64, cc * T + s4 * 512: cc * T + (s4 + 1) * 512],
                    xT[qb:qb + 64, cc * T + s4 * 512: cc * T + (s4 + 1) * 512],
                    osb[qb:qb + 64, :]), inc="sv", waits=[mv])
                P.wait("t", ot)
            prev_mvs = [mvs[2], mvs[3]]
            wT_stage.set_guard(w_b, bmm)
            prev_use[buf] = mm
            prev_omm = mm
            wtoks = nw
    return




def emit_l1_attn_b(P, ctx, C, xT, xnT, W):
    nc = P.nc
    H, DH, WIN, DIL = 16, 64, 128, 4
    sb = lambda name, shape, dt: P.sbuf(ctx, name, shape, dt)
    psb = lambda name, shape: P.psum(ctx, name, shape)
    P.fence_dma()
    scale = 1.0 / math.sqrt(DH)
    ln_tok = P.tok("sv")

    QT = sb("mb_qt", [128, T], BF16)
    KT = sb("mb_kt", [128, T], BF16)
    Vp = sb("mb_vp", [128, 2 * 16 * 65], BF16)
    Vd = sb("mb_vd", [128, 2 * 16 * 65], BF16)
    Olo = sb("mb_olo", [128, T], BF16)
    Odi = sb("mb_odi", [128, T], BF16)
    Ogl = sb("mb_ogl", [128, T], BF16)
    o_acc = sb("mb_oacc", [128, T], BF16)
    otmp = sb("mb_otmp", [128, 512], BF16)
    y_acc = sb("mb_yacc", [128, NCH * T], BF16)
    wq_sb = sb("mb_wq", [128, 8 * 128], BF16)
    wk_sb = sb("mb_wk", [128, 8 * 128], BF16)
    wv_sb = sb("mb_wv", [128, 8 * 128], BF16)
    wo_sb = [sb("mb_wo%d" % i, [128, 8 * 128], BF16) for i in range(2)]
    sums_sb = sb("mb_sums", [65, T], F32)
    expT = [sb("mb_exp%d" % i, [128, 512], BF16) for i in range(3)]

    pr_ps = [psb("mb_pr%d" % i, [128, 512]) for i in range(2)]
    sc_ps = [psb("mb_sc%d" % i, [128, 512]) for i in range(3)]
    pv_ps = psb("mb_pv", [65, 512])
    nb_ps = psb("mb_nb", [128, 512])
    yo_ps = [psb("mb_yo0", [128, 512]), nb_ps]

    bounce = RowBounce(P, ctx, "mb_b", 6, C=C, fold=1.0 / 3.0)
    sh_ap, sh_tok = C["bias"](-SHIFT)

    prev_pe = None
    prev_proj = None
    prev_res = None
    sums_guard = None
    last_exp = [None, None, None]
    exp_guard = {0: None, 1: None, 2: None}
    yev = {}
    last_pair = None
    for hp in range(8):
        wt = None
        guard = [prev_proj] if prev_proj is not None else []
        wt1 = None
        for c in range(NCH):
            r0, r1 = c * 128, (c + 1) * 128
            wt = P.dma("g", wq_sb[:, r0:r1],
                       W["w_qkv"][r0:r1, hp * 128:(hp + 1) * 128],
                       waits=guard if c == 0 else [], sem="dg0")
            wt = P.dma("g", wk_sb[:, r0:r1],
                       W["w_qkv"][r0:r1, 1024 + hp * 128:1024 + (hp + 1) * 128],
                       sem="dg0")
            wt = P.dma("g", wv_sb[:, r0:r1],
                       W["w_qkv"][r0:r1, 2048 + hp * 128:2048 + (hp + 1) * 128],
                       sem="dg0")
            wt1 = P.dma("g", wo_sb[hp % 2][:, r0:r1],
                        W["w_out"][hp * 128:(hp + 1) * 128, r0:r1],
                        waits=[prev_pe] if (prev_pe is not None and c == 0) else [],
                        sem="dg1")
        P.wait("t", ln_tok)
        if prev_pe is not None:
            P.wait("t", prev_pe)
        ev_bank = {}
        bank_i = [0]

        def pick_bank():
            b = bank_i[0] % 2
            bank_i[0] += 1
            if ev_bank.get(b) is not None:
                P.wait("t", ev_bank[b])
            return b

        for dst, wmat in ((QT, wq_sb), (KT, wk_sb)):
            for s4 in range(4):
                b = pick_bank()
                mm = None
                for c in range(NCH):
                    mm = P.op("t", lambda e, s4=s4, c=c, wmat=wmat, b=b: e.matmul(
                        pr_ps[b][:, 0:512],
                        wmat[:, c * 128:(c + 1) * 128],
                        xnT[:, c * T + s4 * 512: c * T + (s4 + 1) * 512],
                        start=(c == 0), stop=(c == NCH - 1)),
                        waits=[wt] if (dst is QT and s4 == 0 and c == 0) else [],
                        inc="st" if c == NCH - 1 else None)
                ev = P.op("v", lambda e, dst=dst, s4=s4, b=b: e.tensor_copy(
                    dst[:, s4 * 512:(s4 + 1) * 512], pr_ps[b][:, 0:512]), inc="sv",
                    waits=[mm])
                ev_bank[b] = ev
        vz = P.op("g", lambda e: e.memset(Vp[:], 0.0),
                  waits=[prev_pe] if prev_pe is not None else [], inc="sg")
        vz2 = P.op("g", lambda e: e.memset(Vd[:], 0.0), waits=[vz], inc="sg")
        vone = P.op("g", lambda e: e.memset(Vp[:, 64::65], 1.0), inc="sg",
                    waits=[vz2])
        vone2 = P.op("g", lambda e: e.memset(Vd[:, 64::65], 1.0), inc="sg",
                     waits=[vone])
        vev = None
        for tc in range(NTCH):
            b = pick_bank()
            mm = None
            for c in range(NCH):
                mm = P.op("t", lambda e, tc=tc, c=c, b=b: e.matmul(
                    pr_ps[b][:, 0:128],
                    xnT[:, c * T + tc * 128: c * T + (tc + 1) * 128],
                    wv_sb[:, c * 128:(c + 1) * 128],
                    start=(c == 0), stop=(c == NCH - 1)),
                    waits=[vone2] if (tc == 0 and c == 0) else [],
                    inc="st" if c == NCH - 1 else None)
            for j in range(2):
                vev = P.op("v", lambda e, tc=tc, j=j, b=b: e.tensor_copy(
                    Vp[:, j * 1040 + tc * 65: j * 1040 + tc * 65 + 64],
                    pr_ps[b][:, j * 64:(j + 1) * 64]), waits=[mm] if j == 0 else [],
                    inc="sv" if j == 1 else None)
            ev_bank[b] = vev
            gph, ib = tc // 4, tc % 4
            b = pick_bank()
            mm = None
            for c in range(NCH):
                mm = P.op("t", lambda e, tc=tc, c=c, gph=gph, ib=ib, b=b: e.matmul(
                    pr_ps[b][:, 0:128],
                    xnT[:, c * T + gph + ib * 512: c * T + gph + ib * 512 + 509:4],
                    wv_sb[:, c * 128:(c + 1) * 128],
                    start=(c == 0), stop=(c == NCH - 1)),
                    inc="st" if c == NCH - 1 else None)
            for j in range(2):
                vev = P.op("v", lambda e, tc=tc, j=j, b=b: e.tensor_copy(
                    Vd[:, j * 1040 + tc * 65: j * 1040 + tc * 65 + 64],
                    pr_ps[b][:, j * 64:(j + 1) * 64]), waits=[mm] if j == 0 else [],
                    inc="sv" if j == 1 else None)
            ev_bank[b] = vev
        prev_proj = P.tok("st")
        pvsum_tok = None
        for j in range(2):
            qb = j * 64
            pend = []

            def flush_loc(pend=None, j=j, qb=qb):
                w, etok = pend.pop(0)
                slot = pv_ps if w % 2 == 0 else yo_ps[0]
                pvt = P.op("t", lambda e, j=j, w=w, slot=slot: e.matmul(
                    slot[0:65, 0:128],
                    Vp[:, j * 1040 + w * 65: j * 1040 + (w + 1) * 65],
                    expT[w % 3][:, 0:128], start=True, stop=True),
                    waits=[etok] + (pv_guard[w % 2] if pv_guard[w % 2] is not None else []),
                    inc="st")
                exp_guard[w % 3] = pvt
                oev = P.op("v", lambda e, qb=qb, w=w, slot=slot: e.tensor_copy(
                    Olo[qb:qb + 64, w * 128:(w + 1) * 128],
                    slot[0:64, 0:128]), inc="sv", waits=[pvt])
                sw = [pvt]
                if w == 0 and sums_guard is not None:
                    sw.append(sums_guard)
                sev = P.op("s", lambda e, w=w, slot=slot: e.activation(
                    out=sums_sb[64:65, w * 128:(w + 1) * 128],
                    in_=slot[64:65, 0:128], func=AF.Copy),
                    inc="ss", waits=sw)
                pv_guard[w % 2] = [oev, sev]
                return oev, sev

            pv_guard = [last_pair, last_pair]
            for w in range(NTCH):
                par = w % 3
                scp = sc_ps[par]
                swaits = []
                if hp == 0 and j == 0 and w == 0:
                    swaits.append(sh_tok)
                if j == 0 and w == 0:
                    swaits.append(vev)
                if last_exp[par] is not None:
                    swaits.append(last_exp[par])
                smm = P.op("t", lambda e, qb=qb, w=w, scp=scp: e.matmul(
                    scp[:, 0:128],
                    KT[qb:qb + 64, w * 128:(w + 1) * 128],
                    QT[qb:qb + 64, w * 128:(w + 1) * 128],
                    start=True, stop=True), waits=swaits, inc="st")
                ew = [smm]
                if exp_guard[par] is not None:
                    ew.append(exp_guard[par])
                etok = P.op("s", lambda e, par=par, scp=scp: e.activation(
                    out=expT[par][:, 0:128], in_=scp[:, 0:128], func=AF.Exp,
                    scale=scale, bias=sh_ap), inc="ss", waits=ew)
                last_exp[par] = etok
                pend.append((w, etok))
                if len(pend) > 2:
                    oev, sev = flush_loc(pend)
            while pend:
                oev, sev = flush_loc(pend)
            last_pair = [oev, sev]
            sums_guard = bounce.write(sums_sb[64:65, :], 3 * j + 0, waits=[sev])
            pend = []
            ev_prev = [last_pair]

            def flush_dil(pend=None, j=j):
                kk, etok = pend.pop(0)
                gph, ktd = kk // 4, kk % 4
                pw = [etok]
                if ktd == 0 and ev_prev[0] is not None:
                    pw += ev_prev[0]
                tk = P.op("t", lambda e, j=j, gph=gph, ktd=ktd, kk=kk: e.matmul(
                    pv_ps[:, 0:512],
                    Vd[:, j * 1040 + kk * 65: j * 1040 + (kk + 1) * 65],
                    expT[kk % 3][:, 0:512], start=(ktd == 0), stop=(ktd == 3)),
                    waits=pw, inc="st")
                exp_guard[kk % 3] = tk
                return tk, gph, ktd

            for kk in range(16):
                gph, ktd = kk // 4, kk % 4
                par = kk % 3
                scp = sc_ps[par]
                swaits = []
                if last_exp[par] is not None:
                    swaits.append(last_exp[par])
                smm = P.op("t", lambda e, qb=qb, gph=gph, ktd=ktd, scp=scp: e.matmul(
                    scp[:, 0:512],
                    KT[qb:qb + 64, gph + ktd * 512: gph + ktd * 512 + 509:4],
                    QT[qb:qb + 64, gph: gph + 2045:4],
                    start=True, stop=True), waits=swaits, inc="st")
                ew = [smm]
                if exp_guard[par] is not None:
                    ew.append(exp_guard[par])
                etok = P.op("s", lambda e, par=par, scp=scp: e.activation(
                    out=expT[par][:, 0:512], in_=scp[:, 0:512], func=AF.Exp,
                    scale=scale, bias=sh_ap), inc="ss", waits=ew)
                last_exp[par] = etok
                pend.append((kk, etok))
                if len(pend) > 2:
                    tk, fg, fk = flush_dil(pend)
                    if fk == 3:
                        oev = P.op("v", lambda e, qb=qb, fg=fg: e.tensor_copy(
                            Odi[qb:qb + 64, fg * 512:(fg + 1) * 512],
                            pv_ps[0:64, 0:512]), inc="sv", waits=[tk])
                        sev = P.op("s", lambda e, fg=fg: e.activation(
                            out=sums_sb[64:65, fg * 512:(fg + 1) * 512],
                            in_=pv_ps[64:65, 0:512], func=AF.Copy), inc="ss",
                            waits=[tk] + ([sums_guard] if fg == 0 else []))
                        ev_prev[0] = [oev, sev]
            while pend:
                tk, fg, fk = flush_dil(pend)
                if fk == 3:
                    oev = P.op("v", lambda e, qb=qb, fg=fg: e.tensor_copy(
                        Odi[qb:qb + 64, fg * 512:(fg + 1) * 512],
                        pv_ps[0:64, 0:512]), inc="sv", waits=[tk])
                    sev = P.op("s", lambda e, fg=fg: e.activation(
                        out=sums_sb[64:65, fg * 512:(fg + 1) * 512],
                        in_=pv_ps[64:65, 0:512], func=AF.Copy), inc="ss",
                        waits=[tk] + ([sums_guard] if fg == 0 else []))
                    ev_prev[0] = [oev, sev]
            last_pair = [oev, sev]
            sums_guard = bounce.write(sums_sb[64:65, :], 3 * j + 1, waits=[sev])
            pend = []
            ev_prev2 = [last_pair]
            for qq in range(4 * NTCH):
                qc, kt = qq // NTCH, qq % NTCH
                par = qq % 3
                scp = sc_ps[par]
                swaits = []
                if last_exp[par] is not None:
                    swaits.append(last_exp[par])
                smm = P.op("t", lambda e, qb=qb, kt=kt, qc=qc, scp=scp: e.matmul(
                    scp[:, 0:512],
                    KT[qb:qb + 64, kt * 128:(kt + 1) * 128],
                    QT[qb:qb + 64, qc * 512:(qc + 1) * 512],
                    start=True, stop=True), waits=swaits, inc="st")
                ew = [smm]
                if exp_guard[par] is not None:
                    ew.append(exp_guard[par])
                etok = P.op("s", lambda e, par=par, scp=scp: e.activation(
                    out=expT[par][:, 0:512], in_=scp[:, 0:512], func=AF.Exp,
                    scale=scale, bias=sh_ap), inc="ss", waits=ew)
                last_exp[par] = etok
                pend.append((qq, etok))
                if len(pend) > 2:
                    qq2, etok2 = pend.pop(0)
                    qc2, kt2 = qq2 // NTCH, qq2 % NTCH
                    pw = [etok2]
                    if kt2 == 0 and ev_prev2[0] is not None:
                        pw += ev_prev2[0]
                    tk = P.op("t", lambda e, j=j, kt2=kt2, qq2=qq2: e.matmul(
                        pv_ps[:, 0:512],
                        Vp[:, j * 1040 + kt2 * 65: j * 1040 + (kt2 + 1) * 65],
                        expT[qq2 % 3][:, 0:512],
                        start=(kt2 == 0), stop=(kt2 == NTCH - 1)),
                        waits=pw, inc="st")
                    exp_guard[qq2 % 3] = tk
                    if kt2 == NTCH - 1:
                        oev = P.op("v", lambda e, qb=qb, qc2=qc2: e.tensor_copy(
                            Ogl[qb:qb + 64, qc2 * 512:(qc2 + 1) * 512],
                            pv_ps[0:64, 0:512]), inc="sv", waits=[tk])
                        sev = P.op("s", lambda e, qc2=qc2: e.activation(
                            out=sums_sb[64:65, qc2 * 512:(qc2 + 1) * 512],
                            in_=pv_ps[64:65, 0:512], func=AF.Copy), inc="ss",
                            waits=[tk] + ([sums_guard] if qc2 == 0 else []))
                        ev_prev2[0] = [oev, sev]
            while pend:
                qq2, etok2 = pend.pop(0)
                qc2, kt2 = qq2 // NTCH, qq2 % NTCH
                pw = [etok2]
                if kt2 == 0 and ev_prev2[0] is not None:
                    pw += ev_prev2[0]
                tk = P.op("t", lambda e, j=j, kt2=kt2, qq2=qq2: e.matmul(
                    pv_ps[:, 0:512],
                    Vp[:, j * 1040 + kt2 * 65: j * 1040 + (kt2 + 1) * 65],
                    expT[qq2 % 3][:, 0:512],
                    start=(kt2 == 0), stop=(kt2 == NTCH - 1)),
                    waits=pw, inc="st")
                exp_guard[qq2 % 3] = tk
                if kt2 == NTCH - 1:
                    oev = P.op("v", lambda e, qb=qb, qc2=qc2: e.tensor_copy(
                        Ogl[qb:qb + 64, qc2 * 512:(qc2 + 1) * 512],
                        pv_ps[0:64, 0:512]), inc="sv", waits=[tk])
                    sev = P.op("s", lambda e, qc2=qc2: e.activation(
                        out=sums_sb[64:65, qc2 * 512:(qc2 + 1) * 512],
                        in_=pv_ps[64:65, 0:512], func=AF.Copy), inc="ss",
                        waits=[tk] + ([sums_guard] if qc2 == 0 else []))
                    ev_prev2[0] = [oev, sev]
            last_pair = [oev, sev]
            sums_guard = bounce.write(sums_sb[64:65, :], 3 * j + 2, waits=[sev])
        bounce.recip_all()
        comb_last = None
        nbank4 = [nb_ps, yo_ps[0]]
        nb_guard = {}
        rcur = bounce.load_row(0)
        for j in range(2):
            qb = j * 64
            for pat, Osrc in ((0, Olo), (1, Odi), (2, Ogl)):
                rap, rtok, rb = rcur
                if 3 * j + pat + 1 < 6:
                    rcur = bounce.load_row(3 * j + pat + 1)
                bmm = None
                for s4 in range(4):
                    nb = s4 % 2
                    w = [rtok] if s4 == 0 else []
                    if nb_guard.get(nb) is not None:
                        w.append(nb_guard[nb])
                    bmm = P.op("t", lambda e, s4=s4, rap=rap, nb=nb: e.matmul(
                        nbank4[nb][:, 0:512], C["ones_row_b"][:],
                        rap[0:1, s4 * 512:(s4 + 1) * 512], start=True, stop=True),
                        inc="st", waits=w)
                    t1 = P.op("v", lambda e, qb=qb, s4=s4, Osrc=Osrc, nb=nb: e.tensor_mul(
                        otmp[qb:qb + 64, :], Osrc[qb:qb + 64, s4 * 512:(s4 + 1) * 512],
                        nbank4[nb][qb:qb + 64, 0:512]), inc="sv",
                        waits=[bmm] + ([comb_last] if comb_last is not None else []))
                    nb_guard[nb] = t1
                    if pat == 0:
                        comb_last = P.op("v", lambda e, qb=qb, s4=s4: e.tensor_copy(
                            o_acc[qb:qb + 64, s4 * 512:(s4 + 1) * 512],
                            otmp[qb:qb + 64, :]), inc="sv", waits=[t1])
                    elif pat == 2:
                        comb_last = P.op("v", lambda e, qb=qb, s4=s4: e.tensor_add(
                            o_acc[qb:qb + 64, s4 * 512:(s4 + 1) * 512],
                            o_acc[qb:qb + 64, s4 * 512:(s4 + 1) * 512],
                            otmp[qb:qb + 64, :]), inc="sv", waits=[t1])
                    else:
                        comb_last = P.op("v", lambda e, qb=qb, s4=s4: e.tensor_add(
                            o_acc[qb:qb + 64, s4 + 0:s4 + 2045:4],
                            o_acc[qb:qb + 64, s4 + 0:s4 + 2045:4],
                            otmp[qb:qb + 64, :]), inc="sv", waits=[t1])
                    P.wait("t", comb_last)
                bounce.set_guard(rb, bmm)
        mm = None
        for c in range(NCH):
            for s4 in range(4):
                yb = (c * 4 + s4) % 2
                mmw = [comb_last, wt1] if (c == 0 and s4 == 0) else []
                if yev.get(yb) is not None:
                    mmw.append(yev[yb])
                mm = P.op("t", lambda e, c=c, s4=s4, yb=yb, hp=hp: e.matmul(
                    yo_ps[yb][:, 0:512],
                    wo_sb[hp % 2][:, c * 128:(c + 1) * 128],
                    o_acc[:, s4 * 512:(s4 + 1) * 512],
                    start=True, stop=True),
                    waits=mmw, inc="st")
                if hp == 0:
                    ydone = P.op("v", lambda e, c=c, s4=s4, yb=yb: e.tensor_copy(
                        y_acc[:, c * T + s4 * 512: c * T + (s4 + 1) * 512],
                        yo_ps[yb][:, 0:512]), inc="sv", waits=[mm])
                else:
                    ydone = P.op("v", lambda e, c=c, s4=s4, yb=yb: e.tensor_add(
                        y_acc[:, c * T + s4 * 512: c * T + (s4 + 1) * 512],
                        y_acc[:, c * T + s4 * 512: c * T + (s4 + 1) * 512],
                        yo_ps[yb][:, 0:512]), inc="sv", waits=[mm])
                yev[yb] = ydone
        last_pair = [yev[0], yev[1]]
        prev_pe = mm
        prev_res = ydone
    rt = None
    for c in range(NCH):
        for s4 in range(4):
            rt = P.op("v", lambda e, c=c, s4=s4: e.tensor_add(
                xT[:, c * T + s4 * 512: c * T + (s4 + 1) * 512],
                xT[:, c * T + s4 * 512: c * T + (s4 + 1) * 512],
                y_acc[:, c * T + s4 * 512: c * T + (s4 + 1) * 512]), inc="sv",
                waits=[prev_res] if (c == 0 and s4 == 0) else [])
    return



INPUT_SPECS = {
    "L0_attn_a_Wqkv": (1024, 3072), "L0_attn_a_Wout": (1024, 1024),
    "L0_attn_a_head_scales": (8,), "L0_attn_a_head_dirs": (8, 3),
    "L0_ffn_gate_w": (1024, 8), "L0_ffn_gate_bias": (8,),
    "L0_ffn_w_gate": (8, 1024, 512), "L0_ffn_w_up": (8, 1024, 512),
    "L0_ffn_w_down": (8, 512, 1024),
    "L0_ffn_sw_gate": (1024, 1024), "L0_ffn_sw_up": (1024, 1024),
    "L0_ffn_sw_down": (1024, 1024),
    "L0_attn_b_wq_sem": (1024, 512), "L0_attn_b_wk_sem": (1024, 512),
    "L0_attn_b_wq_geo": (1024, 512), "L0_attn_b_wk_geo": (1024, 512),
    "L0_attn_b_wv": (1024, 1024), "L0_attn_b_wo": (1024, 1024),
    "L0_attn_b_k_sem_null": (512,), "L0_attn_b_k_geo_null": (512,),
    "L0_attn_b_v_null": (1024,),
    "L1_attn_a_W_dq": (1024, 256), "L1_attn_a_W_uq": (256, 1024),
    "L1_attn_a_W_dkv": (1024, 256), "L1_attn_a_W_uk": (256, 1024),
    "L1_attn_a_W_uv": (256, 1024), "L1_attn_a_W_o": (1024, 1024),
    "L1_ffn_w_router": (128, 1024), "L1_ffn_head_centroids": (16, 128),
    "L1_ffn_temperature": (1,), "L1_ffn_input_experts": (16, 1024, 512),
    "L1_ffn_output_experts": (16, 512, 64),
    "L1_attn_b_w_qkv": (1024, 3072), "L1_attn_b_w_out": (1024, 1024),
}

L0_FFN_W = ["L0_ffn_gate_w", "L0_ffn_gate_bias", "L0_ffn_w_gate", "L0_ffn_w_up",
            "L0_ffn_w_down", "L0_ffn_sw_gate", "L0_ffn_sw_up", "L0_ffn_sw_down"]

L1_ATTN_B_W = ["L1_attn_b_w_qkv", "L1_attn_b_w_out"]

L1_FFN_W = ["L1_ffn_w_router", "L1_ffn_head_centroids", "L1_ffn_temperature",
            "L1_ffn_input_experts", "L1_ffn_output_experts"]

L1_ATTN_A_W = ["L1_attn_a_W_dq", "L1_attn_a_W_uq", "L1_attn_a_W_dkv",
               "L1_attn_a_W_uk", "L1_attn_a_W_uv", "L1_attn_a_W_o"]

L0_ATTN_B_W = ["L0_attn_b_wq_sem", "L0_attn_b_wk_sem", "L0_attn_b_wq_geo",
               "L0_attn_b_wk_geo", "L0_attn_b_wv", "L0_attn_b_wo",
               "L0_attn_b_k_sem_null", "L0_attn_b_k_geo_null", "L0_attn_b_v_null"]

SUBLAYERS = [
    ("l0_attn_a", ["L0_attn_a_Wqkv", "L0_attn_a_Wout", "L0_attn_a_head_scales",
                   "L0_attn_a_head_dirs"]),
    ("l0_ffn", L0_FFN_W),
    ("l0_attn_b", L0_ATTN_B_W),
    ("l0_ffn2", L0_FFN_W),
    ("l1_attn_a", L1_ATTN_A_W),
    ("l1_ffn", L1_FFN_W),
    ("l1_attn_b", L1_ATTN_B_W),
    ("l1_ffn2", L1_FFN_W),
]


def emit_sublayer(P, sctx, C, xT, xnT, name, waps):
    P.fence_dma()
    if name == "ln_dbg":
        emit_ln(P, sctx, C, xT, xnT)
        lt = P.tok("sv")
        for c in range(NCH):
            P.op("v", lambda e, c=c: e.tensor_copy(
                xT[:, c * T:(c + 1) * T], xnT[:, c * T:(c + 1) * T]), inc="sv",
                waits=[lt] if c == 0 else [])
    elif name == "l0_attn_a":
        emit_ln(P, sctx, C, xT, xnT)
        emit_attn_a(P, sctx, C, xT, xnT, {
            "Wqkv": waps["L0_attn_a_Wqkv"], "Wout": waps["L0_attn_a_Wout"],
            "head_scales": waps["L0_attn_a_head_scales"],
            "head_dirs": waps["L0_attn_a_head_dirs"]})
    elif name == "l0_attn_b":
        emit_ln(P, sctx, C, xT, xnT)
        emit_attn_b(P, sctx, C, xT, xnT, {
            "wq_sem": waps["L0_attn_b_wq_sem"], "wk_sem": waps["L0_attn_b_wk_sem"],
            "wq_geo": waps["L0_attn_b_wq_geo"], "wk_geo": waps["L0_attn_b_wk_geo"],
            "wv": waps["L0_attn_b_wv"], "wo": waps["L0_attn_b_wo"],
            "k_sem_null": waps["L0_attn_b_k_sem_null"],
            "k_geo_null": waps["L0_attn_b_k_geo_null"],
            "v_null": waps["L0_attn_b_v_null"]})
    elif name == "l1_attn_a":
        emit_ln(P, sctx, C, xT, xnT)
        emit_l1_attn_a(P, sctx, C, xT, xnT, {
            "W_dq": waps["L1_attn_a_W_dq"], "W_uq": waps["L1_attn_a_W_uq"],
            "W_dkv": waps["L1_attn_a_W_dkv"], "W_uk": waps["L1_attn_a_W_uk"],
            "W_uv": waps["L1_attn_a_W_uv"], "W_o": waps["L1_attn_a_W_o"]})
    elif name == "l1_attn_b":
        emit_ln(P, sctx, C, xT, xnT)
        emit_l1_attn_b(P, sctx, C, xT, xnT, {
            "w_qkv": waps["L1_attn_b_w_qkv"], "w_out": waps["L1_attn_b_w_out"]})
    elif name in ("l1_ffn", "l1_ffn2"):
        emit_l1_ffn(P, sctx, C, xT, xnT, {
            "w_router": waps["L1_ffn_w_router"],
            "head_centroids": waps["L1_ffn_head_centroids"],
            "temperature": waps["L1_ffn_temperature"],
            "input_experts": waps["L1_ffn_input_experts"],
            "output_experts": waps["L1_ffn_output_experts"]})
    elif name in ("l0_ffn", "l0_ffn2"):
        emit_l0_ffn(P, sctx, C, xT, xnT, {
            "gate_w": waps["L0_ffn_gate_w"], "gate_bias": waps["L0_ffn_gate_bias"],
            "w_gate": waps["L0_ffn_w_gate"], "w_up": waps["L0_ffn_w_up"],
            "w_down": waps["L0_ffn_w_down"], "sw_gate": waps["L0_ffn_sw_gate"],
            "sw_up": waps["L0_ffn_sw_up"], "sw_down": waps["L0_ffn_sw_down"]})
    else:
        raise KeyError(name)


def build_program(names):
    nc = bass.Bass("TRN2", target_bir_lowering=False, debug=False)
    xin = nc.dram_tensor("xin", [D, T], F32, kind="ExternalInput").ap()
    xout = nc.dram_tensor("xout", [D, T], F32, kind="ExternalOutput").ap()
    waps = {}
    for sl, wnames in SUBLAYERS:
        if sl in names:
            for w in wnames:
                if w not in waps:
                    waps[w] = nc.dram_tensor(w, list(INPUT_SPECS[w]), F32,
                                             kind="ExternalInput").ap()
    with ExitStack() as ctx:
        P = Prog(nc, ctx)
        xT = ctx.enter_context(nc.sbuf_tensor("xT", [128, NCH * T], F32))
        xnT = ctx.enter_context(nc.sbuf_tensor("xnT", [128, NCH * T], BF16))
        C = make_consts(nc, ctx, P)
        in_toks = []
        for c in range(NCH):
            in_toks.append(P.dma("y", xT[:, c * T:(c + 1) * T],
                                 xin[c * 128:(c + 1) * 128, :]))
        for e in ("t", "v", "s"):
            P.wait(e, in_toks[-1])
            P.wait(e, C["ready"])
        for sl in names:
            with ExitStack() as sctx:
                emit_sublayer(P, sctx, C, xT, xnT, sl, waps)
        fin_v = P.tok("sv")
        fin_s = P.tok("ss")
        out_toks = []
        for c in range(NCH):
            out_toks.append(P.dma("y", xout[c * 128:(c + 1) * 128, :],
                                  xT[:, c * T:(c + 1) * T],
                                  waits=[fin_v, fin_s] if c == 0 else [],
                                  sem="do"))
        for s_ in sorted(P.cnt):
            if s_.startswith("d"):
                P.wait("y", (s_, P.cnt[s_]))
        block = ctx.enter_context(nc.Block())
        P.realize(block)
    return nc


def kernel(**inputs):
    x = np.ascontiguousarray(inputs["L0_attn_a_x"], dtype=np.float32)
    all_names = [s for s, _ in SUBLAYERS]
    groups = [all_names[i:i + LAUNCH_GROUP] for i in range(0, len(all_names), LAUNCH_GROUP)]
    cur = [np.ascontiguousarray(x[core].T) for core in range(8)]
    for gnames in groups:
        nc = build_program(gnames)
        wnames = []
        seen = set()
        for s, ws in SUBLAYERS:
            if s in gnames:
                for w in ws:
                    if w not in seen:
                        wnames.append(w)
                        seen.add(w)
        in_maps = []
        for core in range(8):
            m = {"xin": cur[core]}
            for w in wnames:
                m[w] = np.ascontiguousarray(inputs[w], dtype=np.float32)
            in_maps.append(m)
        res = run_bass_kernel_spmd(nc, in_maps, core_ids=list(range(8)))
        cur = [np.ascontiguousarray(np.asarray(r["xout"])) for r in res.results]
    return np.stack([np.ascontiguousarray(c.T) for c in cur])


LAUNCH_GROUP = 8


# version 14 (final)
# speedup vs baseline: 1.0581x; 1.0052x over previous
import math
from contextlib import ExitStack

import numpy as np
import ml_dtypes

import concourse.bass as bass
import concourse.mybir as mybir
import concourse.bass_utils as _bass_utils
from concourse.bass_utils import run_bass_kernel_spmd

if not getattr(_bass_utils, "_ldw_patched", False):
    _orig_run_command = _bass_utils.run_command

    def _run_command_ldw(argv, **kw):
        argv = ["--enable-ldw-opt=true" if a == "--enable-ldw-opt=false" else a
                for a in argv]
        return _orig_run_command(argv, **kw)

    _bass_utils.run_command = _run_command_ldw
    _bass_utils._ldw_patched = True

F32 = mybir.dt.float32
BF16 = mybir.dt.bfloat16
AF = mybir.ActivationFunctionType
ALU = mybir.AluOpType
bfdt = ml_dtypes.bfloat16

T = 2048
D = 1024
AA_STOP = 0
NCH = D // 128
NTCH = T // 128
SHIFT = 4.0

ENGS = ("t", "v", "s", "g", "y")


class Prog:

    def __init__(self, nc, ctx):
        self.nc = nc
        self.ctx = ctx
        self.q = {e: [] for e in ENGS}
        self.sems = {}
        self.cnt = {}

    def _sem(self, name):
        if name not in self.sems:
            self.sems[name] = self.ctx.enter_context(self.nc.semaphore(name))
            self.cnt[name] = 0
        return self.sems[name]

    def wait(self, eng, tok):
        if tok is None:
            return
        if isinstance(tok, (list, tuple)) and tok and isinstance(tok[0], (list, tuple, type(None))):
            for t_ in tok:
                self.wait(eng, t_)
            return
        if isinstance(tok, list):
            for t_ in tok:
                self.wait(eng, t_)
            return
        name, val = tok
        self._sem(name)
        if name.startswith("d") and val != self.cnt[name]:
            raise AssertionError(
                f"non-latest DMA wait on {name}: {val} != {self.cnt[name]}")
        self.q[eng].append(("w", name, val))

    def op(self, eng, fn, inc=None, inc_k=1, waits=()):
        for w in waits:
            self.wait(eng, w)
        tok = None
        if inc is not None:
            self._sem(inc)
            self.cnt[inc] += inc_k
            tok = (inc, self.cnt[inc])
        self.q[eng].append(("o", fn, inc, inc_k))
        return tok

    def dma(self, eng, out_ap, in_ap, waits=(), sem=None, **kw):
        if sem is None:
            sem = "d" + eng
        return self.op(eng, lambda e: e.dma_start(out=out_ap, in_=in_ap, **kw),
                       inc=sem, inc_k=16, waits=waits)

    def tok(self, sem):
        self._sem(sem)
        return (sem, self.cnt[sem])

    dma_latest = tok

    def _uniq(self, name):
        if not hasattr(self, "_names"):
            self._names = {}
        k = self._names.get(name, 0)
        self._names[name] = k + 1
        return name if k == 0 else "%s_u%d" % (name, k)

    def sbuf(self, ctx, name, shape, dt):
        return ctx.enter_context(self.nc.sbuf_tensor(self._uniq(name), shape, dt))

    def psum(self, ctx, name, shape, dt=None):
        return ctx.enter_context(self.nc.psum_tensor(
            self._uniq(name), shape, F32 if dt is None else dt))

    def dram(self, name, shape, dt):
        return self.nc.dram_tensor(self._uniq(name), shape, dt, kind="Internal")

    def fence_dma(self):
        for q_ in ("g", "y"):
            for s_ in ("sv", "ss", "st"):
                if self.cnt.get(s_, 0):
                    self.wait(q_, self.tok(s_))

    def realize(self, block):
        handles = self.sems

        def body(eng_name):
            def run(engine):
                for item in self.q[eng_name]:
                    if item[0] == "w":
                        engine.wait_ge(handles[item[1]], item[2])
                    else:
                        _, fn, inc, inc_k = item
                        ins = fn(engine)
                        if inc is not None:
                            ins.then_inc(handles[inc], inc_k)
            return run

        block.tensor(body("t"))
        block.vector(body("v"))
        block.scalar(body("s"))
        block.gpsimd(body("g"))
        block.sync(body("y"))



def make_consts(nc, ctx, P):
    c = {}
    sb = lambda name, shape, dt: ctx.enter_context(nc.sbuf_tensor(name, shape, dt))
    tri = np.triu(np.ones((128, 128), dtype=np.float32)).astype(bfdt)
    mw = np.zeros((128, 4 * 512), dtype=np.float32)
    for v_ in range(4):
        off = v_ * 128
        mw[:, v_ * 512 + off: v_ * 512 + off + 128] = np.triu(np.ones((128, 128), np.float32))
        mw[:, v_ * 512 + off + 128: (v_ + 1) * 512] = 1.0
    identf = np.eye(128, dtype=np.float32)
    items = [("tri", tri, BF16), ("maskw", mw.astype(bfdt), BF16),
             ("identf", identf, F32)]
    toks = []
    for name, arr, dt in items:
        dr = nc.inline_tensor(np.ascontiguousarray(arr), name="c_" + name)
        t_ = sb("sb_" + name, list(arr.shape), dt)
        toks.append(P.dma("y", t_[:], dr.ap()))
        c[name] = t_
    ones = [("ones_col_b", [128, 1], BF16), ("ones_col_f", [128, 1], F32),
            ("ones_row_b", [1, 128], BF16), ("ones_row_f", [1, 128], F32)]
    for i, (name, shape, dt) in enumerate(ones):
        t_ = sb(name, shape, dt)
        P.op("g", lambda e, t_=t_: e.memset(t_[:], 1.0))
        c[name] = t_
    nullmask = sb("nullmask", [128, 1], F32)
    nz = P.op("g", lambda e: e.memset(nullmask[:], 0.0), inc="sg")
    gtok = P.op("g", lambda e: e.memset(nullmask[0:1, 0:1], 1.0), inc="sg",
                waits=[nz])
    c["nullmask"] = nullmask
    c["ready"] = [gtok]

    biasdb = {}
    for v in (1e-5, -SHIFT, math.log(1.0 / 3.0)):
        v = float(v)
        t_ = ctx.enter_context(nc.sbuf_tensor("bias_%d" % len(biasdb), [128, 1], F32))
        tk = P.op("g", lambda e, t_=t_, v=v: e.memset(t_[:], v), inc="sg")
        biasdb[v] = (t_[:], tk)

    def bias(v):
        return biasdb[float(v)]

    c["bias"] = bias
    return c



def emit_ln(P, ctx, C, xT, xnT, f32_cb=None, ra_rows=None):
    nc = P.nc
    SL = 1024
    NSL = T // SL
    with ExitStack() as lctx:
        sb = lambda name, shape, dt: P.sbuf(lctx, name, shape, dt)
        psb = lambda name, shape: P.psum(lctx, name, shape)
        xsq = sb("ln_xsq", [128, 2 * SL], BF16)
        m_row = sb("ln_m", [1, SL], F32)
        msq_row = sb("ln_msq", [1, SL], F32)
        var_row = sb("ln_var", [1, SL], F32)
        l_row = sb("ln_l", [1, SL], F32)
        rb_b = sb("ln_rbb", [1, SL], BF16)
        ab_b = sb("ln_abb", [1, SL], BF16)
        sums_ps = [psb("ln_sums%d" % h, [1, 512]) for h in range(SL // 512)]
        sq_ps = [psb("ln_sq%d" % h, [1, 512]) for h in range(SL // 512)]
        ab_ps = [psb("ln_ab%d" % h, [128, 512]) for h in range(SL // 512)]
        rb_ps = [psb("ln_rb%d" % h, [128, 512]) for h in range(SL // 512)]

        start_sv = P.tok("sv")
        start_st = P.tok("st")
        P.wait("s", start_sv)
        P.wait("t", start_sv)
        P.wait("v", start_st)

        last = None
        prev_sq_mm = None
        prev_mtok = prev_vtok = prev_btok = None
        prev_mt_last = {}
        prev_add_last = {}
        for sl in range(NSL):
            lo = sl * SL
            if ra_rows is not None:
                r_row = ra_rows[0][0:1, lo:lo + SL]
                a_row = ra_rows[1][0:1, lo:lo + SL]
            else:
                r_row = P.sbuf(lctx, "ln_r%d" % sl, [1, SL], F32)[:]
                a_row = P.sbuf(lctx, "ln_a%d" % sl, [1, SL], F32)[:]
            stoks = {}
            pv_by_c = {}

            def emit_square(c):
                xb = c % 2
                w = []
                if pv_by_c.get(c - 2) is not None:
                    w.append(pv_by_c[c - 2])
                elif c < 2 and prev_sq_mm is not None:
                    w.append(prev_sq_mm)
                stoks[c] = P.op("s", lambda e, c=c, lo=lo, xb=xb: e.activation(
                    out=xsq[:, xb * SL:xb * SL + SL],
                    in_=xT[:, c * T + lo: c * T + lo + SL],
                    func=AF.Square), inc="ss", waits=w)

            emit_square(0)
            emit_square(1)
            for c in range(NCH):
                xb = c % 2
                for h in range(SL // 512):
                    sw = [prev_mtok] if (c == 0 and h == 0 and prev_mtok) else []
                    P.op("t", lambda e, c=c, lo=lo, h=h: e.matmul(
                        sums_ps[h][:, 0:512],
                        C["ones_col_f"][:],
                        xT[:, c * T + lo + h * 512: c * T + lo + (h + 1) * 512],
                        start=(c == 0), stop=(c == NCH - 1)), waits=sw)
                    qw = [stoks[c]] if h == 0 else []
                    if c == 0 and h == 0 and prev_vtok:
                        qw = qw + [prev_vtok]
                    pv = P.op("t", lambda e, c=c, h=h, xb=xb: e.matmul(
                        sq_ps[h][:, 0:512], C["ones_col_b"][:],
                        xsq[:, xb * SL + h * 512: xb * SL + (h + 1) * 512],
                        start=(c == 0), stop=(c == NCH - 1)),
                        waits=qw, inc="st")
                pv_by_c[c] = pv
                prev_sq_mm = pv
                if c + 2 < NCH:
                    emit_square(c + 2)
            mm_done = prev_sq_mm
            mtok = None
            for h in range(SL // 512):
                mtok = P.op("s", lambda e, h=h: e.mul(
                    out=m_row[0:1, h * 512:(h + 1) * 512], in_=sums_ps[h][:],
                    mul=1.0 / D), inc="ss", waits=[mm_done] if h == 0 else [])
            mqtok = P.op("s", lambda e: e.activation(out=msq_row[:], in_=m_row[:],
                                                     func=AF.Square), inc="ss",
                         waits=[mtok])
            vtok = None
            for h in range(SL // 512):
                vtok = P.op("v", lambda e, h=h: e.scalar_tensor_tensor(
                    out=var_row[0:1, h * 512:(h + 1) * 512], in0=sq_ps[h][:],
                    scalar=1.0 / D, in1=msq_row[0:1, h * 512:(h + 1) * 512],
                    op0=ALU.mult, op1=ALU.subtract), inc="sv",
                    waits=[mqtok, mm_done] if h == 0 else [])
            eps_ap, eps_tok = C["bias"](1e-5)
            ltok = P.op("s", lambda e: e.activation(out=l_row[:], in_=var_row[:],
                                                    func=AF.Ln, bias=eps_ap[0:1, :]),
                        inc="ss", waits=[vtok, eps_tok])
            rtok = P.op("s", lambda e, r_row=r_row: e.activation(
                out=r_row, in_=l_row[:], func=AF.Exp, scale=-0.5), inc="ss",
                waits=[ltok])
            atok = P.op("v", lambda e, r_row=r_row, a_row=a_row: e.scalar_tensor_tensor(
                out=a_row, in0=m_row[:], scalar=-1.0, in1=r_row,
                op0=ALU.mult, op1=ALU.mult), inc="sv", waits=[rtok])
            cb1 = P.op("s", lambda e, r_row=r_row: e.activation(
                out=rb_b[:], in_=r_row, func=AF.Copy), inc="ss",
                waits=[atok] + ([prev_btok] if prev_btok is not None else []))
            cb2 = P.op("s", lambda e, a_row=a_row: e.activation(
                out=ab_b[:], in_=a_row, func=AF.Copy), inc="ss")
            btok = None
            for h in range(SL // 512):
                aw = [cb2] if h == 0 else []
                if prev_add_last.get(h) is not None:
                    aw = aw + [prev_add_last[h]]
                P.op("t", lambda e, h=h: e.matmul(
                    ab_ps[h][:, 0:512], C["ones_row_b"][:],
                    ab_b[0:1, h * 512:(h + 1) * 512], start=True, stop=True),
                    waits=aw)
                btok = P.op("t", lambda e, h=h: e.matmul(
                    rb_ps[h][:, 0:512], C["ones_row_b"][:],
                    rb_b[0:1, h * 512:(h + 1) * 512], start=True, stop=True),
                    inc="st",
                    waits=[prev_mt_last[h]] if prev_mt_last.get(h) is not None else [])
            for c in range(NCH):
                for h in range(SL // 512):
                    mt = P.op("v", lambda e, c=c, lo=lo, h=h: e.tensor_mul(
                        xnT[:, c * T + lo + h * 512: c * T + lo + (h + 1) * 512],
                        xT[:, c * T + lo + h * 512: c * T + lo + (h + 1) * 512],
                        rb_ps[h][:, 0:512]),
                        waits=[btok] if (c == 0 and h == 0) else [], inc="sv")
                    last = P.op("v", lambda e, c=c, lo=lo, h=h: e.tensor_add(
                        xnT[:, c * T + lo + h * 512: c * T + lo + (h + 1) * 512],
                        xnT[:, c * T + lo + h * 512: c * T + lo + (h + 1) * 512],
                        ab_ps[h][:, 0:512]), inc="sv", waits=[mt])
                    if c == NCH - 1:
                        prev_mt_last[h] = mt
                        prev_add_last[h] = last
            prev_mtok, prev_vtok, prev_btok = mtok, vtok, btok
            if sl == NSL - 1:
                P.wait("t", last)
    return last



class RowBounce:

    def __init__(self, P, ctx, name, nslots, C=None, fold=1.0):
        nc = P.nc
        self.P = P
        self.n = nslots
        self.C = C
        self.fold = fold
        self.scr = P.dram(name + "_scr", [nslots, T], F32)
        self.scr2 = P.dram(name + "_scr2", [nslots, T], F32)
        w = nslots * T // 128
        self.pk = P.sbuf(ctx, name + "_pk", [128, w], F32)
        self.pk2 = P.sbuf(ctx, name + "_pk2", [128, w], F32)
        self.rowbuf = [P.sbuf(ctx, name + "_r%d" % i, [1, T], BF16)
                       for i in range(2)]
        self.row_guard = [None, None]
        self.wtoks = {}
        self.st_tok = None
        self.rcount = 0

    def write(self, row_ap, idx, waits):
        w = list(waits)
        if idx == 0 and self.st_tok is not None:
            w.append(self.P.dma_latest("dp"))
        self.wtoks[idx] = self.P.dma("y", self.scr.ap()[idx:idx + 1, :], row_ap,
                                     waits=w, sem="db")
        return self.wtoks[idx]

    def recip_all(self):
        P = self.P
        flat = self.scr.ap().rearrange("n t -> (n t)").rearrange("(p f) -> p f", p=128)
        flat2 = self.scr2.ap().rearrange("n t -> (n t)").rearrange("(p f) -> p f", p=128)
        ld = P.dma("y", self.pk[:], flat, waits=[P.dma_latest("db")], sem="dp")
        lntok = P.op("s", lambda e: e.activation(out=self.pk2[:], in_=self.pk[:],
                                                 func=AF.Ln), waits=[ld], inc="ss")
        if self.fold != 1.0:
            b_ap, b_tok = self.C["bias"](math.log(self.fold))
            l2 = P.op("s", lambda e: e.activation(out=self.pk[:], in_=self.pk2[:],
                                                  func=AF.Exp, scale=-1.0,
                                                  bias=b_ap), inc="ss",
                      waits=[lntok, b_tok])
        else:
            l2 = P.op("s", lambda e: e.activation(out=self.pk[:], in_=self.pk2[:],
                                                  func=AF.Exp, scale=-1.0), inc="ss",
                      waits=[lntok])
        self.st_tok = P.dma("y", flat2, self.pk[:],
                            waits=[l2, P.dma_latest("dr0"), P.dma_latest("dr1")],
                            sem="dp")
        self._fresh = 2

    def load_row(self, idx, waits=()):
        b = self.rcount % 2
        self.rcount += 1
        w = list(waits)
        if getattr(self, "_fresh", 0) > 0:
            w.append(self.st_tok)
            self._fresh -= 1
        if self.row_guard[b] is not None:
            w.append(self.row_guard[b])
        tok = self.P.dma("g", self.rowbuf[b][0:1, :], self.scr2.ap()[idx:idx + 1, :],
                         waits=w, sem="dr%d" % b)
        return self.rowbuf[b][0:1, :], tok, b

    def set_guard(self, b, tok):
        self.row_guard[b] = tok



def emit_attn_a(P, ctx, C, xT, xnT, W):
    nc = P.nc
    H, HD = 8, 128
    sb = lambda name, shape, dt: P.sbuf(ctx, name, shape, dt)
    psb = lambda name, shape: P.psum(ctx, name, shape)
    P.fence_dma()

    hd_sb = sb("aa_hd", [8, 3], F32)
    hs_sb = sb("aa_hs", [8, 1], F32)
    hds_sb = sb("aa_hds", [8, 3], F32)
    dirv = sb("aa_dirv", [128, 16], BF16)
    P.dma("y", hd_sb[:], W["head_dirs"], sem="df")
    t2 = P.dma("y", hs_sb[:], W["head_scales"].rearrange("(h a) -> h a", a=1),
               sem="df")
    mz = P.op("g", lambda e: e.memset(dirv[:], 0.0), inc="sg")
    sc = P.op("v", lambda e: e.tensor_scalar(
        out=hds_sb[:], in0=hd_sb[:], scalar1=hs_sb[:], scalar2=float(math.sqrt(HD)),
        op0=ALU.mult, op1=ALU.mult), inc="sv", waits=[t2])
    for h in range(H):
        P.dma("g", dirv[0:3, 2 * h:2 * h + 1], hd_sb[h:h + 1, 0:3],
              waits=[t2, mz] if h == 0 else [], sem="dv")
        P.dma("g", dirv[0:3, 2 * h + 1:2 * h + 2], hds_sb[h:h + 1, 0:3],
              waits=[sc] if h == 0 else [], sem="dv")
    dv_toks = [P.dma_latest("dv")]

    wout_sb = [sb("aa_wout%d" % i, [128, H * 128], BF16) for i in range(2)]
    wqk = [sb("aa_wqk%d" % i, [128, 3 * 8 * 128], BF16) for i in range(2)]
    QTs = [sb("aa_qt%d" % i, [128, T], BF16) for i in range(2)]
    KTs = [sb("aa_kt%d" % i, [128, T], BF16) for i in range(2)]
    Vh = [sb("aa_v%d" % i, [128, NTCH * 128], BF16) for i in range(2)]
    qp_row = sb("aa_qp", [1, T], BF16)
    kp_row = sb("aa_kp", [1, T], BF16)
    sums_row = sb("aa_sums", [1, T], F32)
    O_all = sb("aa_oall", [128, H * T], BF16)
    expT = [sb("aa_exp%d" % i, [128, 512], BF16) for i in range(3)]

    bounce = RowBounce(P, ctx, "aa_b", H)

    qk_ps = [psb("aa_qkps%d" % i, [128, 512]) for i in range(2)]
    row_ps = psb("aa_rowps", [1, 512])
    sc_ps = [psb("aa_scps%d" % i, [128, 512]) for i in range(3)]
    pv_ps = psb("aa_pvps", [128, 512])
    sm_ps = psb("aa_smps", [1, 512])

    ln_done = P.tok("sv")
    scale = 1.0 / math.sqrt(HD)
    sh_ap, sh_tok = C["bias"](-SHIFT)

    def load_head_weights(h, buf, guard):
        tk = None
        for j in range(3):
            for c in range(NCH):
                tk = P.dma(
                    "g", wqk[buf][:, j * 1024 + c * 128: j * 1024 + (c + 1) * 128],
                    W["Wqkv"][c * 128:(c + 1) * 128,
                              j * 1024 + h * 128: j * 1024 + (h + 1) * 128],
                    waits=[guard] if (guard is not None and j == 0 and c == 0) else [],
                    sem="dg%d" % buf)
        return [tk]

    prev_proj_done = [None, None]
    prev_attn_done = [None, None]
    prev_head_done = None
    last_sums_dma = None
    wtoks = load_head_weights(0, 0, None)
    oev_by_head = {}
    pvsum_tok = {}
    last_exp = [None, None, None]

    def proj_stream(jobs):
        evs = {}
        ev = None
        for i, (mms, evac) in enumerate(jobs):
            b = i % 2
            if evs.get(b) is not None:
                P.wait("t", evs[b])
            mm = mms(qk_ps[b])
            ev = evac(qk_ps[b], mm)
            evs[b] = ev
        return ev

    for h in range(H):
        buf = h % 2
        nwtoks = (load_head_weights(h + 1, 1 - buf, prev_proj_done[1 - buf])
                  if h + 1 < H else None)
        P.wait("t", ln_done)
        P.wait("t", wtoks)
        if prev_attn_done[buf] is not None:
            P.wait("t", prev_attn_done[buf])
        qk_ev = {}
        jobs = []
        for mat, dst in ((0, QTs[buf]), (1, KTs[buf])):
            for s4 in range(4):
                def mms(bank, mat=mat, s4=s4, buf=buf):
                    mm = None
                    for c in range(NCH):
                        mm = P.op("t", lambda e, mat=mat, s4=s4, c=c, buf=buf, bank=bank: e.matmul(
                            bank[:, 0:512],
                            wqk[buf][:, mat * 1024 + c * 128: mat * 1024 + (c + 1) * 128],
                            xnT[:, c * T + s4 * 512: c * T + (s4 + 1) * 512],
                            start=(c == 0), stop=(c == NCH - 1)),
                            inc="st" if c == NCH - 1 else None)
                    return mm
                def evac(bank, mm, mat=mat, s4=s4, dst=dst):
                    ev = P.op("v", lambda e, dst=dst, s4=s4, bank=bank: e.tensor_copy(
                        dst[:, s4 * 512:(s4 + 1) * 512], bank[:, 0:512]), inc="sv",
                        waits=[mm])
                    qk_ev[(mat, s4)] = ev
                    return ev
                jobs.append((mms, evac))
        for tc in range(NTCH):
            def mms(bank, tc=tc, buf=buf):
                mm = None
                for c in range(NCH):
                    mm = P.op("t", lambda e, tc=tc, c=c, buf=buf, bank=bank: e.matmul(
                        bank[:, 0:128],
                        xnT[:, c * T + tc * 128: c * T + (tc + 1) * 128],
                        wqk[buf][:, 2048 + c * 128: 2048 + (c + 1) * 128],
                        start=(c == 0), stop=(c == NCH - 1)),
                        inc="st" if c == NCH - 1 else None)
                return mm
            def evac(bank, mm, tc=tc, buf=buf):
                return P.op("v", lambda e, tc=tc, buf=buf, bank=bank: e.tensor_copy(
                    Vh[buf][:, tc * 128:(tc + 1) * 128], bank[:, 0:128]), inc="sv",
                    waits=[mm])
            jobs.append((mms, evac))
        last_ev = proj_stream(jobs)
        proj_done = P.tok("st")
        prev_proj_done[buf] = proj_done
        P.wait("t", dv_toks)
        row_last = None
        for which, src_t, dst, col in ((0, QTs[buf], qp_row, 2 * h),
                                       (1, KTs[buf], kp_row, 2 * h + 1)):
            for s4 in range(4):
                w = [qk_ev[(which, s4)]]
                if row_last is not None:
                    w.append(row_last)
                mm = P.op("t", lambda e, src_t=src_t, s4=s4, col=col: e.matmul(
                    row_ps[:, 0:512], dirv[:, col:col + 1],
                    src_t[:, s4 * 512:(s4 + 1) * 512], start=True, stop=True),
                    inc="st", waits=w)
                sw = [mm]
                if prev_head_done is not None and which == 0 and s4 == 0:
                    sw.append(prev_head_done)
                row_last = P.op("s", lambda e, dst=dst, s4=s4: e.activation(
                    out=dst[0:1, s4 * 512:(s4 + 1) * 512], in_=row_ps[0:1, 0:512],
                    func=AF.Copy), inc="ss", waits=sw)
                P.wait("t", row_last)
        qdone = []
        for qc in range(4):
            nkt = 4 * qc + 4
            oev_prev = qdone[-1] if qdone else None
            pend = []

            def flush_one():
                kt, etok = pend.pop(0)
                pw = [etok]
                if kt == 0 and oev_prev is not None:
                    pw += [oev_prev[0], oev_prev[1]]
                P.op("t", lambda e, buf=buf, kt=kt, nkt=nkt: e.matmul(
                    pv_ps[:, 0:512], Vh[buf][:, kt * 128:(kt + 1) * 128],
                    expT[kt % 3][:, 0:512], start=(kt == 0), stop=(kt == nkt - 1)),
                    waits=pw)
                pvsum_tok[kt % 3] = P.op("t", lambda e, kt=kt, nkt=nkt: e.matmul(
                    sm_ps[:, 0:512], C["ones_col_b"][:], expT[kt % 3][:, 0:512],
                    start=(kt == 0), stop=(kt == nkt - 1)), inc="st")

            for kt in range(nkt):
                par = kt % 3
                scp = sc_ps[par]
                swaits = []
                if kt == 0 and qc == 0:
                    swaits.append(row_last)
                    if h == 0:
                        swaits.append(sh_tok)
                if last_exp[par] is not None:
                    swaits.append(last_exp[par])
                P.op("t", lambda e, buf=buf, kt=kt, qc=qc, scp=scp: e.matmul(
                    scp[:, 0:512], KTs[buf][:, kt * 128:(kt + 1) * 128],
                    QTs[buf][:, qc * 512:(qc + 1) * 512], start=True, stop=False),
                    waits=swaits)
                smm = P.op("t", lambda e, kt=kt, qc=qc, scp=scp: e.matmul(
                    scp[:, 0:512], kp_row[0:1, kt * 128:(kt + 1) * 128],
                    qp_row[0:1, qc * 512:(qc + 1) * 512], start=False, stop=True),
                    inc="st")
                ew = [smm]
                if pvsum_tok.get(par) is not None:
                    ew.append(pvsum_tok[par])
                etok = P.op("s", lambda e, par=par, scp=scp: e.activation(
                    out=expT[par][:, 0:512], in_=scp[:, 0:512], func=AF.Exp,
                    scale=scale, bias=sh_ap), inc="ss", waits=ew)
                if kt >= 4 * qc:
                    v_ = kt - 4 * qc
                    etok = P.op("v", lambda e, par=par, v_=v_: e.tensor_mul(
                        expT[par][:, 0:512], expT[par][:, 0:512],
                        C["maskw"][:, v_ * 512:(v_ + 1) * 512]), inc="sv",
                        waits=[etok])
                last_exp[par] = etok
                pend.append((kt, etok))
                if len(pend) > 2:
                    flush_one()
            while pend:
                flush_one()
            pvt = P.tok("st")
            oev = P.op("v", lambda e, h=h, qc=qc: e.tensor_copy(
                O_all[:, h * T + qc * 512: h * T + (qc + 1) * 512], pv_ps[:, 0:512]),
                inc="sv", waits=[pvt])
            sevw = [pvt]
            if qc == 0 and last_sums_dma is not None:
                sevw.append(last_sums_dma)
            sev = P.op("s", lambda e, qc=qc: e.activation(
                out=sums_row[0:1, qc * 512:(qc + 1) * 512], in_=sm_ps[0:1, 0:512],
                func=AF.Copy), inc="ss", waits=sevw)
            qdone.append((oev, sev))
            P.wait("t", oev)
            P.wait("t", sev)
        last_sums_dma = bounce.write(sums_row[0:1, :], h, waits=[qdone[-1][1]])
        oev_by_head[h] = [q[0] for q in qdone]
        prev_head_done = P.tok("st")
        prev_attn_done[buf] = prev_head_done
        wtoks = nwtoks

    bounce.recip_all()
    P.wait("t", P.tok("sv"))
    norm_last = None
    nb_guard = {}
    for h in range(H):
        rap, rtok, rb = bounce.load_row(h)
        bmm = None
        for s4 in range(4):
            nb = (h * 4 + s4) % 2
            w = [rtok, oev_by_head[h][s4]] if s4 == 0 else []
            if nb_guard.get(nb) is not None:
                w.append(nb_guard[nb])
            bmm = P.op("t", lambda e, s4=s4, rap=rap, nb=nb: e.matmul(
                qk_ps[nb][:, 0:512], C["ones_row_b"][:],
                rap[0:1, s4 * 512:(s4 + 1) * 512], start=True, stop=True),
                inc="st", waits=w)
            norm_last = P.op("v", lambda e, h=h, s4=s4, nb=nb: e.tensor_mul(
                O_all[:, h * T + s4 * 512: h * T + (s4 + 1) * 512],
                O_all[:, h * T + s4 * 512: h * T + (s4 + 1) * 512],
                qk_ps[nb][:, 0:512]), inc="sv", waits=[bmm, oev_by_head[h][s4]])
            nb_guard[nb] = norm_last
        bounce.set_guard(rb, bmm)
    def load_wout(c, wbuf, guard):
        tk = None
        for h in range(H):
            tk = P.dma("g", wout_sb[wbuf][:, h * 128:(h + 1) * 128],
                       W["Wout"][h * 128:(h + 1) * 128, c * 128:(c + 1) * 128],
                       waits=[guard] if (guard is not None and h == 0) else [],
                       sem="dwo%d" % wbuf)
        return [tk]

    wo_toks = load_wout(0, 0, None)
    wo_guard = [None, None]
    rt_by_bank = {}
    for c in range(NCH):
        wbuf = c % 2
        nwo = load_wout(c + 1, 1 - wbuf, wo_guard[1 - wbuf]) if c + 1 < NCH else None
        P.wait("t", wo_toks)
        mmtok = None
        for s4 in range(4):
            yb = s4 % 2
            mmw = [norm_last] if (c == 0 and s4 == 0) else []
            if rt_by_bank.get(yb) is not None:
                mmw.append(rt_by_bank[yb])
            for hh in range(H):
                mmtok = P.op("t", lambda e, s4=s4, hh=hh, wbuf=wbuf, yb=yb: e.matmul(
                    qk_ps[yb][:, 0:512],
                    wout_sb[wbuf][:, hh * 128:(hh + 1) * 128],
                    O_all[:, hh * T + s4 * 512: hh * T + (s4 + 1) * 512],
                    start=(hh == 0), stop=(hh == H - 1)),
                    waits=mmw if hh == 0 else [],
                    inc="st" if hh == H - 1 else None)
            rt = P.op("v", lambda e, c=c, s4=s4, yb=yb: e.tensor_add(
                xT[:, c * T + s4 * 512: c * T + (s4 + 1) * 512],
                xT[:, c * T + s4 * 512: c * T + (s4 + 1) * 512], qk_ps[yb][:, 0:512]),
                inc="sv", waits=[mmtok])
            rt_by_bank[yb] = rt
        wo_guard[wbuf] = mmtok
        wo_toks = nwo
    return



class RowStage:

    def __init__(self, P, ctx, name, nslots):
        nc = P.nc
        self.P = P
        self.n = nslots
        self.scr = P.dram(name + "_scr", [nslots, T], F32)
        self.rowbuf = [P.sbuf(ctx, name + "_r%d" % i, [1, T], BF16)
                       for i in range(2)]
        self.row_guard = [None, None]
        self.rcount = 0
        self.st_tok = None

    def store(self, src_ap, waits):
        self.st_tok = self.P.dma("y", self.scr.ap()[:], src_ap, waits=waits, sem="db")
        return self.st_tok

    def load_row(self, idx, waits=()):
        b = self.rcount % 2
        self.rcount += 1
        w = list(waits)
        if self.rcount <= 2:
            w.append(self.st_tok)
        if self.row_guard[b] is not None:
            w.append(self.row_guard[b])
        tok = self.P.dma("g", self.rowbuf[b][0:1, :], self.scr.ap()[idx:idx + 1, :],
                         waits=w, sem="dr%d" % b)
        return self.rowbuf[b][0:1, :], tok, b

    def set_guard(self, b, tok):
        self.row_guard[b] = tok



def emit_l0_ffn(P, ctx, C, xT, xnT, W):
    nc = P.nc
    E = 8
    sb = lambda name, shape, dt: P.sbuf(ctx, name, shape, dt)

    def chain(ops, first_waits=()):
        tok = None
        for eng, fn in ops:
            w = [tok] if tok is not None else list(first_waits)
            tok = P.op(eng, fn, inc={"v": "sv", "s": "ss", "t": "st"}[eng],
                       waits=w)
        return tok

    w_all = sb("f0_wall", [128, 16 * E], F32)
    wT_stage = RowStage(P, ctx, "f0_ws", E)
    y_acc = sb("f0_yacc", [128, NCH * T], BF16)

    with ExitStack() as gctx:
        gsb = lambda name, shape, dt: P.sbuf(gctx, name, shape, dt)
        gps = lambda name, shape: P.psum(gctx, name, shape)
        rfull = gsb("f0_rf", [1, T], F32)
        afull = gsb("f0_af", [1, T], F32)
        ln_tok = emit_ln(P, gctx, C, xT, xnT, ra_rows=(rfull, afull))
        P.fence_dma()
        gw_sb = gsb("f0_gw", [128, 8 * E], F32)
        gb_row = gsb("f0_gbr", [1, E], F32)
        grow = gsb("f0_grow", [1, E], F32)
        gwsum_sb = gsb("f0_gws", [128, E], F32)
        gb_sb = gsb("f0_gb", [128, E], F32)
        rc_sb = gsb("f0_rc", [128, 1], F32)
        ac_sb = gsb("f0_ac", [128, 1], F32)
        tmp8 = gsb("f0_t8", [128, E], F32)
        lg8 = gsb("f0_lg", [128, E], F32)
        sg8 = gsb("f0_sg", [128, E], F32)
        sfc8 = gsb("f0_sfc", [128, E], F32)
        g4 = gsb("f0_g4", [128, 4], F32)
        sc4 = gsb("f0_sc4", [128, 4], F32)
        m1c = gsb("f0_m1", [128, 1], F32)
        m2c = gsb("f0_m2", [128, 1], F32)
        emask = gsb("f0_em", [128, E], F32)
        t8b = gsb("f0_t8b", [128, E], F32)
        wsc = gsb("f0_wsc", [128, 1], F32)
        wT_sb = gsb("f0_wT", [E, T], F32)
        raw_ps = gps("f0_raw", [128, E])
        r_ps = gps("f0_rps", [128, 1])
        a_ps = gps("f0_aps", [128, 1])
        sum_ps = gps("f0_sum", [1, E])
        bc_ps = gps("f0_bc", [128, E])
        tp_ps = gps("f0_tp", [E, 128])

        for c in range(NCH):
            P.dma("y", gw_sb[:, c * E:(c + 1) * E],
                  W["gate_w"][c * 128:(c + 1) * 128, :], sem="df")
        gwt = P.dma("y", gb_row[:], W["gate_bias"].rearrange("(a e) -> a e", a=1),
                    sem="df")
        smm = None
        for c in range(NCH):
            smm = P.op("t", lambda e, c=c: e.matmul(
                sum_ps[:, 0:E], C["ones_col_f"][:], gw_sb[:, c * E:(c + 1) * E],
                start=(c == 0), stop=(c == NCH - 1)),
                waits=[gwt] if c == 0 else [], inc="st" if c == NCH - 1 else None)
        svt = P.op("s", lambda e: e.activation(out=grow[:], in_=sum_ps[:],
                                               func=AF.Copy), inc="ss", waits=[smm])
        b1 = P.op("t", lambda e: e.matmul(bc_ps[:, 0:E], C["ones_row_f"][:],
                                          grow[:], start=True, stop=True),
                  inc="st", waits=[svt])
        e1 = P.op("v", lambda e: e.tensor_copy(gwsum_sb[:], bc_ps[:]), inc="sv",
                  waits=[b1])
        b2 = P.op("t", lambda e: e.matmul(bc_ps[:, 0:E], C["ones_row_f"][:],
                                          gb_row[:], start=True, stop=True),
                  inc="st", waits=[e1])
        e2 = P.op("v", lambda e: e.tensor_copy(gb_sb[:], bc_ps[:]), inc="sv",
                  waits=[b2])
        P.wait("t", ln_tok)
        prev = e2
        wtr_toks = []
        for tc in range(NTCH):
            mm = None
            for c in range(NCH):
                mm = P.op("t", lambda e, c=c, tc=tc: e.matmul(
                    raw_ps[:, 0:E],
                    xT[:, c * T + tc * 128: c * T + (tc + 1) * 128],
                    gw_sb[:, c * E:(c + 1) * E],
                    start=(c == 0), stop=(c == NCH - 1)),
                    waits=[prev] if c == 0 else [],
                    inc="st" if c == NCH - 1 else None)
            rmm = P.op("t", lambda e, tc=tc: e.matmul(
                r_ps[:], rfull[0:1, tc * 128:(tc + 1) * 128],
                C["ones_row_f"][0:1, 0:1], start=True, stop=True), inc="st")
            amm = P.op("t", lambda e, tc=tc: e.matmul(
                a_ps[:], afull[0:1, tc * 128:(tc + 1) * 128],
                C["ones_row_f"][0:1, 0:1], start=True, stop=True), inc="st")
            cp1 = P.op("v", lambda e: e.tensor_copy(rc_sb[:], r_ps[:]), inc="sv",
                       waits=[rmm, amm])
            cp2 = P.op("v", lambda e: e.tensor_copy(ac_sb[:], a_ps[:]), inc="sv")
            sig = chain([
                ("v", lambda e: e.tensor_scalar(tmp8[:], raw_ps[:], rc_sb[:], None,
                                                ALU.mult)),
                ("v", lambda e: e.scalar_tensor_tensor(lg8[:], gwsum_sb[:], ac_sb[:],
                                                       tmp8[:], ALU.mult, ALU.add)),
            ], first_waits=[cp2, mm])
            sg_t = P.op("s", lambda e: e.activation(out=sg8[:], in_=lg8[:],
                                                    func=AF.Sigmoid), inc="ss",
                        waits=[sig])
            last = chain([
                ("v", lambda e: e.tensor_add(sfc8[:], sg8[:], gb_sb[:])),
                ("v", lambda e: e.tensor_add(g4[:], sfc8[:, 0:E:2], sfc8[:, 1:E:2])),
                ("v", lambda e: e.reduce_max(m1c[:], g4[:], axis=mybir.AxisListType.X)),
                ("v", lambda e: e.tensor_scalar(sc4[:], g4[:], m1c[:], None,
                                                ALU.is_ge)),
                ("v", lambda e: e.scalar_tensor_tensor(sc4[:], sc4[:], -1e9, g4[:],
                                                       ALU.mult, ALU.add)),
                ("v", lambda e: e.reduce_max(m2c[:], sc4[:], axis=mybir.AxisListType.X)),
                ("v", lambda e: e.tensor_scalar(sc4[:], g4[:], m2c[:], None,
                                                ALU.is_ge)),
                ("v", lambda e: e.tensor_copy(emask[:, 0:E:2], sc4[:])),
                ("v", lambda e: e.tensor_copy(emask[:, 1:E:2], sc4[:])),
                ("v", lambda e: e.tensor_mul(t8b[:], sfc8[:], emask[:])),
                ("v", lambda e: e.reduce_max(m1c[:], t8b[:], axis=mybir.AxisListType.X)),
                ("v", lambda e: e.tensor_scalar(tmp8[:], t8b[:], m1c[:], None,
                                                ALU.is_ge)),
                ("v", lambda e: e.scalar_tensor_tensor(tmp8[:], tmp8[:], -1e9, t8b[:],
                                                       ALU.mult, ALU.add)),
                ("v", lambda e: e.reduce_max(m2c[:], tmp8[:], axis=mybir.AxisListType.X)),
                ("v", lambda e: e.tensor_add(wsc[:], m1c[:], m2c[:])),
                ("v", lambda e: e.tensor_scalar(wsc[:], wsc[:], 1e-20, None, ALU.add)),
                ("v", lambda e: e.reciprocal(wsc[:], wsc[:])),
                ("v", lambda e: e.tensor_scalar(tmp8[:], t8b[:], m2c[:], None,
                                                ALU.is_ge)),
                ("v", lambda e: e.tensor_mul(t8b[:], t8b[:], tmp8[:])),
                ("v", lambda e, tc=tc: e.tensor_scalar(
                    w_all[:, tc * E:(tc + 1) * E], t8b[:], wsc[:], 2.5,
                    ALU.mult, ALU.mult)),
            ], first_waits=[sg_t])
            tmm = P.op("t", lambda e, tc=tc: e.transpose(
                tp_ps[:], w_all[:, tc * E:(tc + 1) * E], C["identf"][:]),
                inc="st", waits=[last])
            wtr = P.op("s", lambda e, tc=tc: e.activation(
                out=wT_sb[:, tc * 128:(tc + 1) * 128], in_=tp_ps[:], func=AF.Copy),
                inc="ss", waits=[tmm])
            wtr_toks.append(wtr)
            prev = wtr
        wT_stage.store(wT_sb[:], waits=[wtr_toks[-1]])

    with ExitStack() as ectx:
        esb = lambda name, shape, dt: P.sbuf(ectx, name, shape, dt)
        eps = lambda name, shape: P.psum(ectx, name, shape)
        P.fence_dma()
        P.wait("g", wT_stage.st_tok)
        wg_sb = [esb("f0_wg%d" % i, [128, 8 * 512], BF16) for i in range(2)]
        wu_sb = [esb("f0_wu%d" % i, [128, 8 * 512], BF16) for i in range(2)]
        wd_sb = [esb("f0_wd0", [128, 4 * 1024], BF16)] * 2
        gu_sb = esb("f0_gu", [128, 4 * T], BF16)
        sgu = [esb("f0_sgu%d" % i, [128, 512], BF16) for i in range(4)]
        wbc_sb = esb("f0_wbc", [128, T], BF16)
        g_ps = [eps("f0_gps%d" % i, [128, 512]) for i in range(4)]
        u_ps = [eps("f0_ups%d" % i, [128, 512]) for i in range(4)]
        wb_ps = g_ps[0]
        y_ps = [g_ps[1], g_ps[2]]

        def load_expert(e_, buf, guard):
            tk = None
            for c in range(NCH):
                tk = P.dma("g", wg_sb[buf][:, c * 512:(c + 1) * 512],
                           W["w_gate"][e_, c * 128:(c + 1) * 128, :],
                           waits=[guard] if (guard is not None and c == 0) else [],
                           sem="dg%d" % buf)
                tk = P.dma("g", wu_sb[buf][:, c * 512:(c + 1) * 512],
                           W["w_up"][e_, c * 128:(c + 1) * 128, :], sem="dg%d" % buf)
            return [tk]

        def load_expert_wd(e_, guard):
            tk = None
            for ic in range(4):
                tk = P.dma("g", wd_sb[0][:, ic * 1024:(ic + 1) * 1024],
                           W["w_down"][e_, ic * 128:(ic + 1) * 128, :],
                           waits=[guard] if (guard is not None and ic == 0) else [],
                           sem="dg2")
            return [tk]

        wtoks = load_expert(0, 0, None)
        wdtoks = load_expert_wd(0, None)
        prev_use = [None, None]
        gu_done = [None, None]
        prev_dev = None
        for e_ in range(E):
            buf = e_ % 2
            w_row, w_tok, w_b = wT_stage.load_row(e_)
            nw = load_expert(e_ + 1, 1 - buf, gu_done[1 - buf]) if e_ + 1 < E else None
            gu_toks = []
            t2_by_bank = {}
            pend_gu = []

            def emit_bcast(w_row=w_row, w_tok=w_tok):
                bts = []
                bmm = None
                for s4 in range(4):
                    bank = g_ps[0] if s4 % 2 == 0 else u_ps[0]
                    bw = [w_tok] if s4 == 0 else ([bts[s4 - 2]] if s4 >= 2 else [])
                    bmm = P.op("t", lambda e, s4=s4, w_row=w_row, bank=bank: e.matmul(
                        bank[:, 0:512], C["ones_row_b"][:],
                        w_row[0:1, s4 * 512:(s4 + 1) * 512], start=True, stop=True),
                        inc="st", waits=bw)
                    bts.append(P.op("v", lambda e, s4=s4, bank=bank: e.tensor_copy(
                        wbc_sb[:, s4 * 512:(s4 + 1) * 512], bank[:, 0:512]), inc="sv",
                        waits=[bmm]))
                return bts, bmm

            def emit_gu(s4, ic, gw=(), uw=(), buf=buf):
                gmm = None
                for c in range(NCH):
                    gmm = P.op("t", lambda e, c=c, ic=ic, s4=s4, buf=buf: e.matmul(
                        g_ps[ic][:, 0:512],
                        wg_sb[buf][:, c * 512 + ic * 128: c * 512 + (ic + 1) * 128],
                        xnT[:, c * T + s4 * 512: c * T + (s4 + 1) * 512],
                        start=(c == 0), stop=(c == NCH - 1)),
                        waits=list(gw) if c == 0 else [],
                        inc="st" if c == NCH - 1 else None)
                umm = None
                for c in range(NCH):
                    umm = P.op("t", lambda e, c=c, ic=ic, s4=s4, buf=buf: e.matmul(
                        u_ps[ic][:, 0:512],
                        wu_sb[buf][:, c * 512 + ic * 128: c * 512 + (ic + 1) * 128],
                        xnT[:, c * T + s4 * 512: c * T + (s4 + 1) * 512],
                        start=(c == 0), stop=(c == NCH - 1)),
                        waits=list(uw) if c == 0 else [],
                        inc="st" if c == NCH - 1 else None)
                return gmm, umm

            def flush_gu(bt):
                s4, mms, order = pend_gu.pop(0)
                first = True
                for ic in order:
                    gmm, umm = mms[ic]
                    st = P.op("s", lambda e, ic=ic: e.activation(
                        out=sgu[ic][:, 0:512], in_=g_ps[ic][:, 0:512], func=AF.Silu),
                        inc="ss",
                        waits=[gmm] + ([t2_by_bank.get(ic)] if t2_by_bank.get(ic) else []))
                    t1 = P.op("v", lambda e, ic=ic: e.tensor_mul(
                        sgu[ic][:, 0:512], sgu[ic][:, 0:512], u_ps[ic][:, 0:512]),
                        inc="sv", waits=[st, umm])
                    t2 = P.op("v", lambda e, ic=ic, s4=s4: e.tensor_mul(
                        gu_sb[:, ic * T + s4 * 512: ic * T + (s4 + 1) * 512],
                        sgu[ic][:, 0:512], wbc_sb[:, s4 * 512:(s4 + 1) * 512]),
                        inc="sv", waits=[t1, bt] if (first and s4 == 0) else [t1])
                    first = False
                    gu_toks.append(t2)
                    t2_by_bank[ic] = t2

            P.wait("t", wtoks)
            if e_ == 0:
                bts, bmm = emit_bcast()
                wT_stage.set_guard(w_b, bmm)
                P.wait("t", bts[-1])
                mms = {ic: emit_gu(0, ic) for ic in range(4)}
                pend_gu.append((0, mms, [0, 1, 2, 3]))
            else:
                mms = {}
                for ic in (1, 2, 3):
                    gw = [prev_dev[ic - 1]] if ic in (1, 2) else []
                    mms[ic] = emit_gu(0, ic, gw=gw)
                bts, bmm = emit_bcast()
                wT_stage.set_guard(w_b, bmm)
                mms[0] = emit_gu(0, 0, gw=[bts[2]], uw=[bts[3]])
                pend_gu.append((0, mms, [1, 2, 3, 0]))
            flush_gu(bts[-1])
            for s4 in range(1, 4):
                mms = {}
                for ic in range(4):
                    if t2_by_bank.get(ic) is not None:
                        P.wait("t", t2_by_bank[ic])
                    mms[ic] = emit_gu(s4, ic)
                pend_gu.append((s4, mms, [0, 1, 2, 3]))
                flush_gu(bts[-1])
            gu_done[buf] = mms[3][1]
            P.wait("t", wdtoks)
            dtok = None
            for c in range(NCH):
                for s4 in range(4):
                    yb = (c * 4 + s4) % 2
                    mmw = [gu_toks[-1]] if (c == 0 and s4 == 0) else []
                    if dtok is not None and (c * 4 + s4) >= 2:
                        mmw.append(d_ev[yb])
                    mm = None
                    for ic in range(4):
                        mm = P.op("t", lambda e, c=c, s4=s4, ic=ic, buf=buf, yb=yb: e.matmul(
                            y_ps[yb][:, 0:512],
                            wd_sb[buf][:, ic * 1024 + c * 128: ic * 1024 + (c + 1) * 128],
                            gu_sb[:, ic * T + s4 * 512: ic * T + (s4 + 1) * 512],
                            start=(ic == 0), stop=(ic == 3)),
                            waits=mmw if ic == 0 else [],
                            inc="st" if ic == 3 else None)
                    if e_ == 0:
                        ev = P.op("v", lambda e, c=c, s4=s4, yb=yb: e.tensor_copy(
                            y_acc[:, c * T + s4 * 512: c * T + (s4 + 1) * 512],
                            y_ps[yb][:, 0:512]), inc="sv", waits=[mm])
                    else:
                        ev = P.op("v", lambda e, c=c, s4=s4, yb=yb: e.tensor_add(
                            y_acc[:, c * T + s4 * 512: c * T + (s4 + 1) * 512],
                            y_acc[:, c * T + s4 * 512: c * T + (s4 + 1) * 512],
                            y_ps[yb][:, 0:512]), inc="sv", waits=[mm])
                    if c == 0 and s4 == 0:
                        d_ev = {}
                    d_ev[yb] = ev
                    dtok = ev
            prev_dev = (d_ev[0], d_ev[1])
            prev_use[buf] = dtok
            wtoks = nw
            if e_ + 1 < E:
                wdtoks = load_expert_wd(e_ + 1, dtok)

    with ExitStack() as sctx2:
        ssb = lambda name, shape, dt: P.sbuf(sctx2, name, shape, dt)
        sps = lambda name, shape: P.psum(sctx2, name, shape)
        P.fence_dma()
        swg_sb = [ssb("f0_swg%d" % i, [128, 8 * 128], BF16) for i in range(2)]
        swu_sb = [ssb("f0_swu%d" % i, [128, 8 * 128], BF16) for i in range(2)]
        swd_sb = ssb("f0_swd", [128, 8 * 1024], BF16)
        gus = ssb("f0_gus", [128, 8 * T], BF16)
        sgu2 = ssb("f0_sgu2", [128, 512], BF16)
        g2_ps = sps("f0_g2", [128, 512])
        u2_ps = sps("f0_u2", [128, 512])
        y2_ps = [sps("f0_y2%d" % i, [128, 512]) for i in range(2)]

        def load_sw(ic, buf, guard):
            tk = None
            for c in range(NCH):
                tk = P.dma("g", swg_sb[buf][:, c * 128:(c + 1) * 128],
                           W["sw_gate"][c * 128:(c + 1) * 128, ic * 128:(ic + 1) * 128],
                           waits=[guard] if (guard is not None and c == 0) else [],
                           sem="dg%d" % buf)
                tk = P.dma("g", swu_sb[buf][:, c * 128:(c + 1) * 128],
                           W["sw_up"][c * 128:(c + 1) * 128, ic * 128:(ic + 1) * 128],
                           sem="dg%d" % buf)
            return [tk]

        swd_tok = None
        for ic in range(NCH):
            swd_tok = P.dma("g", swd_sb[:, ic * 1024:(ic + 1) * 1024],
                            W["sw_down"][ic * 128:(ic + 1) * 128, :], sem="dwo0")
        swtoks = load_sw(0, 0, None)
        prev_use2 = [None, None]
        prev_lt = None
        for ic in range(NCH):
            buf = ic % 2
            nw = load_sw(ic + 1, 1 - buf, prev_use2[1 - buf]) if ic + 1 < NCH else None
            P.wait("t", swtoks)
            lt = None
            for s4 in range(4):
                gmm = None
                for c in range(NCH):
                    gmm = P.op("t", lambda e, c=c, s4=s4, buf=buf: e.matmul(
                        g2_ps[:, 0:512],
                        swg_sb[buf][:, c * 128:(c + 1) * 128],
                        xnT[:, c * T + s4 * 512: c * T + (s4 + 1) * 512],
                        start=(c == 0), stop=(c == NCH - 1)),
                        inc="st" if c == NCH - 1 else None)
                umm = None
                for c in range(NCH):
                    umm = P.op("t", lambda e, c=c, s4=s4, buf=buf: e.matmul(
                        u2_ps[:, 0:512],
                        swu_sb[buf][:, c * 128:(c + 1) * 128],
                        xnT[:, c * T + s4 * 512: c * T + (s4 + 1) * 512],
                        start=(c == 0), stop=(c == NCH - 1)),
                        inc="st" if c == NCH - 1 else None)
                st = P.op("s", lambda e: e.activation(
                    out=sgu2[:, 0:512], in_=g2_ps[:, 0:512], func=AF.Silu),
                    inc="ss",
                    waits=[gmm] + ([prev_lt] if prev_lt is not None else []))
                lt = P.op("v", lambda e, ic=ic, s4=s4: e.tensor_mul(
                    gus[:, ic * T + s4 * 512: ic * T + (s4 + 1) * 512],
                    sgu2[:, 0:512], u2_ps[:, 0:512]), inc="sv", waits=[st, umm])
                prev_lt = lt
                P.wait("t", st)
                P.wait("t", lt)
            prev_use2[buf] = lt
            swtoks = nw
        d_ev2 = {}
        for c in range(NCH):
            for s4 in range(4):
                yb = (c * 4 + s4) % 2
                mmw = [lt, swd_tok] if (c == 0 and s4 == 0) else []
                if d_ev2.get(yb) is not None:
                    mmw.append(d_ev2[yb])
                mm = None
                for ic in range(NCH):
                    mm = P.op("t", lambda e, c=c, s4=s4, ic=ic, yb=yb: e.matmul(
                        y2_ps[yb][:, 0:512],
                        swd_sb[:, ic * 1024 + c * 128: ic * 1024 + (c + 1) * 128],
                        gus[:, ic * T + s4 * 512: ic * T + (s4 + 1) * 512],
                        start=(ic == 0), stop=(ic == NCH - 1)),
                        waits=mmw if ic == 0 else [],
                        inc="st" if ic == NCH - 1 else None)
                a1 = P.op("v", lambda e, c=c, s4=s4, yb=yb: e.tensor_add(
                    xT[:, c * T + s4 * 512: c * T + (s4 + 1) * 512],
                    xT[:, c * T + s4 * 512: c * T + (s4 + 1) * 512],
                    y2_ps[yb][:, 0:512]), waits=[mm], inc="sv")
                a2 = P.op("v", lambda e, c=c, s4=s4: e.tensor_add(
                    xT[:, c * T + s4 * 512: c * T + (s4 + 1) * 512],
                    xT[:, c * T + s4 * 512: c * T + (s4 + 1) * 512],
                    y_acc[:, c * T + s4 * 512: c * T + (s4 + 1) * 512]), inc="sv",
                    waits=[a1])
                d_ev2[yb] = a2
    return




def emit_attn_b(P, ctx, C, xT, xnT, W):
    nc = P.nc
    H = 16
    sb = lambda name, shape, dt: P.sbuf(ctx, name, shape, dt)
    psb = lambda name, shape: P.psum(ctx, name, shape)
    P.fence_dma()
    scale = 1.0 / math.sqrt(32.0)
    ln_tok = P.tok("sv")

    Qc = sb("ab_qc", [128, 2 * T], BF16)
    Kc = sb("ab_kc", [128, 2 * T], BF16)
    Vg = sb("ab_vg", [128, 4 * 17 * 65], BF16)
    knull = sb("ab_knull", [128, 2 * 128], BF16)
    Oc = sb("ab_oc", [128, 2 * T], BF16)
    stg = sb("ab_stg", [128, T], BF16)
    rg = sb("ab_rg", [128, T], BF16)
    rp = sb("ab_rp", [128, T], BF16)
    rtmp = sb("ab_rtmp", [128, T], BF16)
    rsw = sb("ab_rsw", [128, T], BF16)
    wq = sb("ab_wq", [128, 8 * 128], BF16)
    wk = sb("ab_wk", [128, 8 * 128], BF16)
    wqg = sb("ab_wqg", [128, 8 * 128], BF16)
    wkg = sb("ab_wkg", [128, 8 * 128], BF16)
    wv = sb("ab_wv", [128, 8 * 256], BF16)
    wo_g = [sb("ab_wo%d" % i, [128, 2 * 1024], BF16) for i in range(2)]
    sums_sb = sb("ab_sums", [65, T], F32)
    expT = [sb("ab_exp%d" % i, [128, 512], BF16) for i in range(3)]
    vnrow = sb("ab_vn", [1, H * 64], BF16)

    pr_ps = [psb("ab_pr%d" % i, [128, 512]) for i in range(2)]
    sc_ps = [psb("ab_sc%d" % i, [128, 512]) for i in range(3)]
    pv_ps = psb("ab_pv", [65, 512])
    nb_ps = psb("ab_nb", [128, 512])
    yo_ps = [psb("ab_yo0", [128, 512]), nb_ps]

    bounce4 = RowBounce(P, ctx, "ab_b", 4)
    bounce = [bounce4] * 4
    vn_tok = P.dma("g", vnrow[:], W["v_null"].rearrange("(a n) -> a n", a=1),
                   sem="da")
    invf = (10000.0 ** (-np.arange(16, dtype=np.float64) / 16.0))
    ang = np.arange(T, dtype=np.float64)[None, :] * invf[np.arange(128) % 16][:, None]
    cosF_np = np.cos(ang).astype(bfdt)
    sinS_np = np.sin(ang).astype(np.float32)
    sinS_np[:64] *= -1.0
    cosF_t = sb("ab_cosF", [128, T], BF16)
    sinS_t = sb("ab_sinS", [128, T], BF16)
    P.dma("y", cosF_t[:], nc.inline_tensor(np.ascontiguousarray(cosF_np),
                                           name=P._uniq("c_cosF")).ap(), sem="drt")
    rope_tok = P.dma("y", sinS_t[:],
                     nc.inline_tensor(np.ascontiguousarray(sinS_np.astype(bfdt)),
                                      name=P._uniq("c_sinS")).ap(), sem="drt")
    C = dict(C)
    C["cosF"], C["sinS"] = cosF_t, sinS_t

    prev_group_done = None
    prev_pe_done = None
    sums_guard = None
    last_exp = [None, None, None]
    exp_guard = {0: None, 1: None, 2: None}
    yev = {}
    sh_ap, sh_tok = C["bias"](-SHIFT)

    proj_done_tok = None
    for g in range(4):
        wtok = None
        guard = [proj_done_tok] if proj_done_tok is not None else []
        for c in range(NCH):
            r0, r1 = c * 128, (c + 1) * 128
            wtok = P.dma("g", wq[:, r0:r1], W["wq_sem"][r0:r1, g * 128:(g + 1) * 128],
                         waits=guard if c == 0 else [], sem="dg0")
            wtok = P.dma("g", wk[:, r0:r1], W["wk_sem"][r0:r1, g * 128:(g + 1) * 128],
                         sem="dg0")
            for wgt, dst in ((W["wq_geo"], wqg), (W["wk_geo"], wkg)):
                s3 = wgt[r0:r1, g * 128:(g + 1) * 128].rearrange(
                    "r (h s) -> r h s", h=4, s=32)
                wtok = P.dma("g", dst[:, c * 128:c * 128 + 64], s3[:, :, 0:16],
                             sem="dg0")
                wtok = P.dma("g", dst[:, c * 128 + 64:(c + 1) * 128], s3[:, :, 16:32],
                             sem="dg0")
            wtok = P.dma("g", wv[:, c * 256:(c + 1) * 256],
                         W["wv"][r0:r1, g * 256:(g + 1) * 256], sem="dg0")
        for c in range(NCH):
            r0, r1 = c * 128, (c + 1) * 128
            P.dma("g", wo_g[g % 2][:, r0:r1],
                  W["wo"][g * 256:g * 256 + 128, r0:r1],
                  waits=[prev_pe_done] if (prev_pe_done is not None and c == 0) else [],
                  sem="dg1")
            P.dma("g", wo_g[g % 2][:, 1024 + r0:1024 + r1],
                  W["wo"][g * 256 + 128:(g + 1) * 256, r0:r1], sem="dg1")
        kz = P.op("g", lambda e: e.memset(knull[:], 0.0),
                  waits=[prev_pe_done] if prev_pe_done is not None else [], inc="sg")
        ktok = None
        for j in range(4):
            h = 4 * g + j
            p, hh = j // 2, j % 2
            P.dma("g", knull[hh * 64:hh * 64 + 32, p * 128:p * 128 + 1],
                  W["k_sem_null"].rearrange("(h s) -> h s", h=H)[h:h + 1, :],
                  waits=[kz] if j == 0 else [], sem="dg1")
            ktok = P.dma("g", knull[hh * 64 + 32:hh * 64 + 64, p * 128:p * 128 + 1],
                         W["k_geo_null"].rearrange("(h s) -> h s", h=H)[h:h + 1, :],
                         sem="dg1")
        P.wait("t", ln_tok)
        if prev_group_done is not None:
            P.wait("t", prev_group_done)
        last_scat = None
        ev_bank = {}
        bank_i = [0]

        def pick_bank():
            b = bank_i[0] % 2
            bank_i[0] += 1
            if ev_bank.get(b) is not None:
                P.wait("t", ev_bank[b])
            return b

        for mat, wmat, dstc in ((0, wq, Qc), (1, wk, Kc)):
            evs = None
            for s4 in range(4):
                b = pick_bank()
                mm = None
                for c in range(NCH):
                    mm = P.op("t", lambda e, s4=s4, c=c, wmat=wmat, b=b: e.matmul(
                        pr_ps[b][:, 0:512],
                        wmat[:, c * 128:(c + 1) * 128],
                        xnT[:, c * T + s4 * 512: c * T + (s4 + 1) * 512],
                        start=(c == 0), stop=(c == NCH - 1)),
                        waits=[wtok, ktok] if (mat == 0 and s4 == 0 and c == 0) else [],
                        inc="st" if c == NCH - 1 else None)
                w = [mm]
                if s4 == 0 and last_scat is not None:
                    w.append(last_scat)
                evs = P.op("v", lambda e, s4=s4, b=b: e.tensor_copy(
                    stg[:, s4 * 512:(s4 + 1) * 512], pr_ps[b][:, 0:512]), inc="sv",
                    waits=w)
                ev_bank[b] = evs
            for j in range(4):
                p, hh = j // 2, j % 2
                last_scat = P.dma("s", dstc[hh * 64:hh * 64 + 32, p * T:(p + 1) * T],
                                  stg[32 * j:32 * j + 32, :],
                                  waits=[evs] if j == 0 else [], sem="ds")
        for mat, wmat, dstc in ((0, wqg, Qc), (1, wkg, Kc)):
            evs = None
            for s4 in range(4):
                b = pick_bank()
                mm = None
                for c in range(NCH):
                    mm = P.op("t", lambda e, s4=s4, c=c, wmat=wmat, b=b: e.matmul(
                        pr_ps[b][:, 0:512],
                        wmat[:, c * 128:(c + 1) * 128],
                        xnT[:, c * T + s4 * 512: c * T + (s4 + 1) * 512],
                        start=(c == 0), stop=(c == NCH - 1)),
                        inc="st" if c == NCH - 1 else None)
                w = [mm]
                if s4 == 0:
                    w.append(last_scat)
                evs = P.op("v", lambda e, s4=s4, b=b: e.tensor_copy(
                    rg[:, s4 * 512:(s4 + 1) * 512], pr_ps[b][:, 0:512]), inc="sv",
                    waits=w)
                ev_bank[b] = evs
            c1 = P.op("v", lambda e: e.tensor_copy(rsw[0:64, :], rg[64:128, :]),
                      waits=[evs], inc="sv")
            c2 = P.op("v", lambda e: e.tensor_copy(rsw[64:128, :], rg[0:64, :]),
                      inc="sv")
            m1 = P.op("v", lambda e: e.tensor_mul(rp[:], rg[:], C["cosF"][:]),
                      inc="sv", waits=[last_scat, rope_tok])
            m2 = P.op("v", lambda e: e.tensor_mul(rtmp[:], rsw[:], C["sinS"][:]),
                      inc="sv", waits=[c2])
            m3 = P.op("v", lambda e: e.tensor_add(rp[:], rp[:], rtmp[:]),
                      inc="sv", waits=[m1, m2])
            for j in range(4):
                p, hh = j // 2, j % 2
                last_scat = P.dma("s", dstc[hh * 64 + 32:hh * 64 + 48, p * T:(p + 1) * T],
                                  rp[16 * j:16 * j + 16, :],
                                  waits=[m3] if j == 0 else [], sem="ds")
                last_scat = P.dma("s", dstc[hh * 64 + 48:hh * 64 + 64, p * T:(p + 1) * T],
                                  rp[64 + 16 * j:64 + 16 * j + 16, :], sem="ds")
        vz = P.op("g", lambda e: e.memset(Vg[:], 0.0),
                  waits=[prev_pe_done] if prev_pe_done is not None else [], inc="sg")
        vone = P.op("g", lambda e: e.memset(Vg[:, 64::65], 1.0), inc="sg",
                    waits=[vz])
        vev = None
        for tc in range(NTCH):
            b = pick_bank()
            mm = None
            for c in range(NCH):
                mm = P.op("t", lambda e, tc=tc, c=c, b=b: e.matmul(
                    pr_ps[b][:, 0:256],
                    xnT[:, c * T + tc * 128: c * T + (tc + 1) * 128],
                    wv[:, c * 256:(c + 1) * 256],
                    start=(c == 0), stop=(c == NCH - 1)),
                    waits=[vone] if (tc == 0 and c == 0) else [],
                    inc="st" if c == NCH - 1 else None)
            for j in range(4):
                vev = P.op("v", lambda e, tc=tc, j=j, b=b: e.tensor_copy(
                    Vg[:, j * 1105 + tc * 65: j * 1105 + tc * 65 + 64],
                    pr_ps[b][:, j * 64:(j + 1) * 64]), waits=[mm] if j == 0 else [],
                    inc="sv" if j == 3 else None)
            ev_bank[b] = vev
        proj_done_tok = P.tok("st")
        vts = None
        for j in range(4):
            h = 4 * g + j
            vts = P.dma("s", Vg[0:1, j * 1105 + 16 * 65: j * 1105 + 16 * 65 + 64],
                        vnrow[0:1, h * 64:(h + 1) * 64],
                        waits=[vn_tok, vev] if j == 0 else [], sem="ds")
        pvsum_tok = None
        hd_prev = None
        for j in range(4):
            p, hh = j // 2, j % 2
            qbase = hh * 64
            for qc in range(4):
                nkt = 4 * qc + 4
                oev_prev = hd_prev
                pend = []

                def flush_one(pend=None, j=j, nkt=nkt, oev_prev=oev_prev):
                    kt, etok, vt_idx = pend.pop(0)
                    pw = [etok]
                    if kt == 0 and oev_prev is not None:
                        pw.append(oev_prev)
                    tk = P.op("t", lambda e, j=j, vt_idx=vt_idx, kt=kt, nkt=nkt: e.matmul(
                        pv_ps[:, 0:512],
                        Vg[:, j * 1105 + vt_idx * 65: j * 1105 + (vt_idx + 1) * 65],
                        expT[kt % 3][:, 0:512], start=(kt == 0), stop=(kt == nkt)),
                        waits=pw, inc="st")
                    exp_guard[kt % 3] = tk
                    return tk

                for kt in range(nkt + 1):
                    par = kt % 3
                    scp = sc_ps[par]
                    is_null = kt == nkt
                    swaits = []
                    if g == 0 and j == 0 and qc == 0 and kt == 0:
                        swaits.append(sh_tok)
                    if j == 0 and qc == 0 and kt == 0:
                        swaits.append(vts)
                    if last_exp[par] is not None:
                        swaits.append(last_exp[par])
                    if is_null:
                        smm = P.op("t", lambda e, p=p, qbase=qbase, qc=qc, scp=scp: e.matmul(
                            scp[:, 0:512],
                            knull[qbase:qbase + 64, p * 128:(p + 1) * 128],
                            Qc[qbase:qbase + 64, p * T + qc * 512: p * T + (qc + 1) * 512],
                            start=True, stop=True), waits=swaits, inc="st")
                    else:
                        smm = P.op("t", lambda e, p=p, qbase=qbase, kt=kt, qc=qc, scp=scp: e.matmul(
                            scp[:, 0:512],
                            Kc[qbase:qbase + 64, p * T + kt * 128: p * T + (kt + 1) * 128],
                            Qc[qbase:qbase + 64, p * T + qc * 512: p * T + (qc + 1) * 512],
                            start=True, stop=True), waits=swaits, inc="st")
                    ew = [smm]
                    if exp_guard[par] is not None:
                        ew.append(exp_guard[par])
                    etok = P.op("s", lambda e, par=par, scp=scp: e.activation(
                        out=expT[par][:, 0:512], in_=scp[:, 0:512], func=AF.Exp,
                        scale=scale, bias=sh_ap), inc="ss", waits=ew)
                    if is_null:
                        etok = P.op("v", lambda e, par=par: e.tensor_scalar(
                            expT[par][:, 0:512], expT[par][:, 0:512],
                            C["nullmask"][:], None, ALU.mult), inc="sv", waits=[etok])
                    elif kt >= 4 * qc:
                        v_ = kt - 4 * qc
                        etok = P.op("v", lambda e, par=par, v_=v_: e.tensor_mul(
                            expT[par][:, 0:512], expT[par][:, 0:512],
                            C["maskw"][:, v_ * 512:(v_ + 1) * 512]), inc="sv",
                            waits=[etok])
                    last_exp[par] = etok
                    pend.append((kt, etok, 16 if is_null else kt))
                    if len(pend) > 2:
                        pvsum_tok = flush_one(pend)
                while pend:
                    pvsum_tok = flush_one(pend)
                ow = [pvsum_tok]
                if g > 0 and j == 0 and qc == 0:
                    ow.append(prev_pe_done)
                oev = P.op("v", lambda e, p=p, qbase=qbase, qc=qc: e.tensor_copy(
                    Oc[qbase:qbase + 64, p * T + qc * 512: p * T + (qc + 1) * 512],
                    pv_ps[0:64, 0:512]), inc="sv", waits=ow)
                sw = [pvsum_tok]
                if qc == 0 and sums_guard is not None:
                    sw.append(sums_guard)
                sev = P.op("s", lambda e, qc=qc: e.activation(
                    out=sums_sb[64:65, qc * 512:(qc + 1) * 512],
                    in_=pv_ps[64:65, 0:512], func=AF.Copy), inc="ss", waits=sw)
                hd_prev = oev
                P.wait("t", oev)
                P.wait("t", sev)
            sums_guard = bounce[g].write(sums_sb[64:65, :], j, waits=[sev])
        bounce[g].recip_all()
        norm_last = None
        nbank2 = [nb_ps, yo_ps[0]]
        nb_guard = {}
        for j in range(4):
            p, hh = j // 2, j % 2
            qbase = hh * 64
            rap, rtok, rb = bounce[g].load_row(j)
            bmm = None
            for s4 in range(4):
                nb = (j * 4 + s4) % 2
                w = [rtok] if s4 == 0 else []
                if nb_guard.get(nb) is not None:
                    w.append(nb_guard[nb])
                bmm = P.op("t", lambda e, s4=s4, rap=rap, nb=nb: e.matmul(
                    nbank2[nb][:, 0:512], C["ones_row_b"][:],
                    rap[0:1, s4 * 512:(s4 + 1) * 512], start=True, stop=True),
                    inc="st", waits=w)
                norm_last = P.op("v", lambda e, p=p, qbase=qbase, s4=s4, nb=nb: e.tensor_mul(
                    Oc[qbase:qbase + 64, p * T + s4 * 512: p * T + (s4 + 1) * 512],
                    Oc[qbase:qbase + 64, p * T + s4 * 512: p * T + (s4 + 1) * 512],
                    nbank2[nb][0:64, 0:512]), inc="sv", waits=[bmm])
                nb_guard[nb] = norm_last
            bounce[g].set_guard(rb, bmm)
        mm = None
        ydone = None
        for c in range(NCH):
            for s4 in range(4):
                yb = (c * 4 + s4) % 2
                mmw = [norm_last] if (c == 0 and s4 == 0) else []
                if yev.get(yb) is not None:
                    mmw.append(yev[yb])
                for p in range(2):
                    mm = P.op("t", lambda e, c=c, s4=s4, p=p, yb=yb, g=g: e.matmul(
                        yo_ps[yb][:, 0:512],
                        wo_g[g % 2][:, p * 1024 + c * 128: p * 1024 + (c + 1) * 128],
                        Oc[:, p * T + s4 * 512: p * T + (s4 + 1) * 512],
                        start=(p == 0), stop=(p == 1)),
                        waits=mmw if p == 0 else [], inc="st" if p == 1 else None)
                ydone = P.op("v", lambda e, c=c, s4=s4, yb=yb: e.tensor_add(
                    xT[:, c * T + s4 * 512: c * T + (s4 + 1) * 512],
                    xT[:, c * T + s4 * 512: c * T + (s4 + 1) * 512],
                    yo_ps[yb][:, 0:512]), inc="sv", waits=[mm])
                yev[yb] = ydone
        prev_group_done = ydone
        prev_pe_done = mm
    return




def emit_l1_attn_a(P, ctx, C, xT, xnT, W):
    nc = P.nc
    H, DK, DC = 16, 64, 256
    sb = lambda name, shape, dt: P.sbuf(ctx, name, shape, dt)
    psb = lambda name, shape: P.psum(ctx, name, shape)
    P.fence_dma()
    scale = 1.0 / math.sqrt(DK)
    ln_tok = P.tok("sv")

    cq = sb("ma_cq", [128, 2 * T], BF16)
    ckv = sb("ma_ckv", [128, 2 * T], BF16)
    wdq = sb("ma_wdq", [128, 8 * 256], BF16)
    wdkv = sb("ma_wdkv", [128, 8 * 256], BF16)
    wuq = sb("ma_wuq", [128, 2 * 1024], BF16)
    wuk = sb("ma_wuk", [128, 2 * 1024], BF16)
    wuv = sb("ma_wuv", [128, 2 * 1024], BF16)
    wo_sb = [sb("ma_wo%d" % i, [128, 8 * 128], BF16) for i in range(2)]
    Qp = [sb("ma_qp0", [128, T], BF16)] * 2
    Kp = [sb("ma_kp0", [128, T], BF16)] * 2
    Vp = [sb("ma_vp0", [128, 2 * 16 * 65], BF16)] * 2
    O_all = sb("ma_oall", [128, 8 * T], BF16)
    sums_sb = sb("ma_sums", [65, T], F32)
    expT = [sb("ma_exp%d" % i, [128, 512], BF16) for i in range(3)]

    pr_ps = [psb("ma_pr%d" % i, [128, 512]) for i in range(2)]
    sc_ps = [psb("ma_sc%d" % i, [128, 512]) for i in range(3)]
    pv_ps = psb("ma_pv", [65, 512])
    nb_ps = psb("ma_nb", [128, 512])
    yo_ps = [psb("ma_yo0", [128, 512]), nb_ps]

    bounce = RowBounce(P, ctx, "ma_b", H)
    sh_ap, sh_tok = C["bias"](-SHIFT)

    wt = None
    for c in range(NCH):
        r0, r1 = c * 128, (c + 1) * 128
        wt = P.dma("g", wdq[:, c * 256:(c + 1) * 256], W["W_dq"][r0:r1, :], sem="dg0")
        wt = P.dma("g", wdkv[:, c * 256:(c + 1) * 256], W["W_dkv"][r0:r1, :], sem="dg0")
    for c2 in range(2):
        r0, r1 = c2 * 128, (c2 + 1) * 128
        wt = P.dma("g", wuq[:, c2 * 1024:(c2 + 1) * 1024], W["W_uq"][r0:r1, :], sem="dg0")
        wt = P.dma("g", wuk[:, c2 * 1024:(c2 + 1) * 1024], W["W_uk"][r0:r1, :], sem="dg0")
        wt = P.dma("g", wuv[:, c2 * 1024:(c2 + 1) * 1024], W["W_uv"][r0:r1, :], sem="dg0")

    P.wait("t", ln_tok)
    P.wait("t", wt)
    ev = None
    ev_bank = {}
    bank_i = [0]

    def pick_bank():
        b = bank_i[0] % 2
        bank_i[0] += 1
        if ev_bank.get(b) is not None:
            P.wait("t", ev_bank[b])
        return b

    for dst, wmat in ((cq, wdq), (ckv, wdkv)):
        for c2 in range(2):
            for s4 in range(4):
                b = pick_bank()
                mm = None
                for c in range(NCH):
                    mm = P.op("t", lambda e, c=c, c2=c2, s4=s4, wmat=wmat, b=b: e.matmul(
                        pr_ps[b][:, 0:512],
                        wmat[:, c * 256 + c2 * 128: c * 256 + (c2 + 1) * 128],
                        xnT[:, c * T + s4 * 512: c * T + (s4 + 1) * 512],
                        start=(c == 0), stop=(c == NCH - 1)),
                        inc="st" if c == NCH - 1 else None)
                ev = P.op("v", lambda e, dst=dst, c2=c2, s4=s4, b=b: e.tensor_copy(
                    dst[:, c2 * T + s4 * 512: c2 * T + (s4 + 1) * 512],
                    pr_ps[b][:, 0:512]), inc="sv", waits=[mm])
                ev_bank[b] = ev
    cq_done = ev

    prev_pair_pe = {0: None, 1: None}
    prev_pe_done = None
    sums_guard = None
    last_exp = [None, None, None]
    exp_guard = {0: None, 1: None, 2: None}
    oev_by_head = {}
    for hp in range(8):
        buf = 0
        if prev_pe_done is not None:
            P.wait("t", prev_pe_done)
        for dst, wmat in ((Qp[buf], wuq), (Kp[buf], wuk)):
            for s4 in range(4):
                b = pick_bank()
                mm = None
                for c2 in range(2):
                    mm = P.op("t", lambda e, c2=c2, s4=s4, hp=hp, wmat=wmat, b=b: e.matmul(
                        pr_ps[b][:, 0:512],
                        wmat[:, c2 * 1024 + hp * 128: c2 * 1024 + (hp + 1) * 128],
                        (cq if wmat is wuq else ckv)[:, c2 * T + s4 * 512: c2 * T + (s4 + 1) * 512],
                        start=(c2 == 0), stop=(c2 == 1)),
                        waits=[cq_done] if (s4 == 0 and c2 == 0 and hp == 0) else [],
                        inc="st" if c2 == 1 else None)
                ev = P.op("v", lambda e, dst=dst, s4=s4, b=b: e.tensor_copy(
                    dst[:, s4 * 512:(s4 + 1) * 512], pr_ps[b][:, 0:512]), inc="sv",
                    waits=[mm])
                ev_bank[b] = ev
        vz = P.op("g", lambda e, buf=buf: e.memset(Vp[buf][:], 0.0),
                  waits=[prev_pe_done] if prev_pe_done is not None else [],
                  inc="sg")
        vone = P.op("g", lambda e, buf=buf: e.memset(Vp[buf][:, 64::65], 1.0),
                    inc="sg", waits=[vz])
        vev = None
        for tc in range(NTCH):
            b = pick_bank()
            mm = None
            for c2 in range(2):
                mm = P.op("t", lambda e, tc=tc, c2=c2, hp=hp, b=b: e.matmul(
                    pr_ps[b][:, 0:128],
                    ckv[:, c2 * T + tc * 128: c2 * T + (tc + 1) * 128],
                    wuv[:, c2 * 1024 + hp * 128: c2 * 1024 + (hp + 1) * 128],
                    start=(c2 == 0), stop=(c2 == 1)),
                    waits=[vone] if (tc == 0 and c2 == 0) else [],
                    inc="st" if c2 == 1 else None)
            for j in range(2):
                vev = P.op("v", lambda e, tc=tc, j=j, buf=buf, b=b: e.tensor_copy(
                    Vp[buf][:, j * 1040 + tc * 65: j * 1040 + tc * 65 + 64],
                    pr_ps[b][:, j * 64:(j + 1) * 64]), waits=[mm] if j == 0 else [],
                    inc="sv" if j == 1 else None)
            ev_bank[b] = vev
        pvsum_tok = None
        hd_prev = None
        for j in range(2):
            h = 2 * hp + j
            qbase = j * 64
            for qc in range(4):
                nkt = 4 * qc + 4
                oev_prev = hd_prev
                pend = []

                def flush_one(pend=None, j=j, nkt=nkt, oev_prev=oev_prev, buf=buf):
                    kt, etok = pend.pop(0)
                    pw = [etok]
                    if kt == 0 and oev_prev is not None:
                        pw.append(oev_prev)
                    tk = P.op("t", lambda e, j=j, kt=kt, nkt=nkt, buf=buf: e.matmul(
                        pv_ps[:, 0:512],
                        Vp[buf][:, j * 1040 + kt * 65: j * 1040 + (kt + 1) * 65],
                        expT[kt % 3][:, 0:512], start=(kt == 0), stop=(kt == nkt - 1)),
                        waits=pw, inc="st")
                    exp_guard[kt % 3] = tk
                    return tk

                for kt in range(nkt):
                    par = kt % 3
                    scp = sc_ps[par]
                    swaits = []
                    if hp == 0 and j == 0 and qc == 0 and kt == 0:
                        swaits.append(sh_tok)
                    if j == 0 and qc == 0 and kt == 0:
                        swaits.append(vev)
                    if last_exp[par] is not None:
                        swaits.append(last_exp[par])
                    smm = P.op("t", lambda e, buf=buf, qbase=qbase, kt=kt, qc=qc, scp=scp: e.matmul(
                        scp[:, 0:512],
                        Kp[buf][qbase:qbase + 64, kt * 128:(kt + 1) * 128],
                        Qp[buf][qbase:qbase + 64, qc * 512:(qc + 1) * 512],
                        start=True, stop=True), waits=swaits, inc="st")
                    ew = [smm]
                    if exp_guard[par] is not None:
                        ew.append(exp_guard[par])
                    etok = P.op("s", lambda e, par=par, scp=scp: e.activation(
                        out=expT[par][:, 0:512], in_=scp[:, 0:512], func=AF.Exp,
                        scale=scale, bias=sh_ap), inc="ss", waits=ew)
                    if kt >= 4 * qc:
                        v_ = kt - 4 * qc
                        etok = P.op("v", lambda e, par=par, v_=v_: e.tensor_mul(
                            expT[par][:, 0:512], expT[par][:, 0:512],
                            C["maskw"][:, v_ * 512:(v_ + 1) * 512]), inc="sv",
                            waits=[etok])
                    last_exp[par] = etok
                    pend.append((kt, etok))
                    if len(pend) > 2:
                        pvsum_tok = flush_one(pend)
                while pend:
                    pvsum_tok = flush_one(pend)
                ow = [pvsum_tok]
                oev = P.op("v", lambda e, hp=hp, qbase=qbase, qc=qc: e.tensor_copy(
                    O_all[qbase:qbase + 64, hp * T + qc * 512: hp * T + (qc + 1) * 512],
                    pv_ps[0:64, 0:512]), inc="sv", waits=ow)
                sw = [pvsum_tok]
                if qc == 0 and sums_guard is not None:
                    sw.append(sums_guard)
                sev = P.op("s", lambda e, qc=qc: e.activation(
                    out=sums_sb[64:65, qc * 512:(qc + 1) * 512],
                    in_=pv_ps[64:65, 0:512], func=AF.Copy), inc="ss", waits=sw)
                hd_prev = oev
                oev_by_head.setdefault(h, []).append(oev)
                P.wait("t", oev)
                P.wait("t", sev)
            sums_guard = bounce.write(sums_sb[64:65, :], h, waits=[sev])
        prev_pe_done = pvsum_tok
    bounce.recip_all()
    norm_last = None
    nbank3 = [nb_ps, yo_ps[0]]
    nb_guard = {}
    for h in range(H):
        hp, j = h // 2, h % 2
        qbase = j * 64
        rap, rtok, rb = bounce.load_row(h)
        bmm = None
        for s4 in range(4):
            nb = (h * 4 + s4) % 2
            w = [rtok] if s4 == 0 else []
            if nb_guard.get(nb) is not None:
                w.append(nb_guard[nb])
            bmm = P.op("t", lambda e, s4=s4, rap=rap, nb=nb: e.matmul(
                nbank3[nb][:, 0:512], C["ones_row_b"][:],
                rap[0:1, s4 * 512:(s4 + 1) * 512], start=True, stop=True),
                inc="st", waits=w)
            norm_last = P.op("v", lambda e, hp=hp, qbase=qbase, s4=s4, nb=nb: e.tensor_mul(
                O_all[qbase:qbase + 64, hp * T + s4 * 512: hp * T + (s4 + 1) * 512],
                O_all[qbase:qbase + 64, hp * T + s4 * 512: hp * T + (s4 + 1) * 512],
                nbank3[nb][0:64, 0:512]), inc="sv", waits=[bmm, oev_by_head[h][s4]])
            nb_guard[nb] = norm_last
        bounce.set_guard(rb, bmm)
    def load_wo(c, wbuf, guard):
        tk = None
        for pc in range(8):
            tk = P.dma("g", wo_sb[wbuf][:, pc * 128:(pc + 1) * 128],
                       W["W_o"][pc * 128:(pc + 1) * 128, c * 128:(c + 1) * 128],
                       waits=[guard] if (guard is not None and pc == 0) else [],
                       sem="dwo%d" % wbuf)
        return [tk]

    wo_toks = load_wo(0, 0, None)
    wo_guard = [None, None]
    for c in range(NCH):
        wbuf = c % 2
        nwo = load_wo(c + 1, 1 - wbuf, wo_guard[1 - wbuf]) if c + 1 < NCH else None
        P.wait("t", wo_toks)
        mmtok = None
        for s4 in range(4):
            for pc in range(8):
                mmtok = P.op("t", lambda e, s4=s4, pc=pc, wbuf=wbuf: e.matmul(
                    yo_ps[s4 % 2][:, 0:512],
                    wo_sb[wbuf][:, pc * 128:(pc + 1) * 128],
                    O_all[:, pc * T + s4 * 512: pc * T + (s4 + 1) * 512],
                    start=(pc == 0), stop=(pc == 7)),
                    waits=[norm_last] if (c == 0 and s4 == 0 and pc == 0) else [],
                    inc="st" if pc == 7 else None)
            rt = P.op("v", lambda e, c=c, s4=s4: e.tensor_add(
                xT[:, c * T + s4 * 512: c * T + (s4 + 1) * 512],
                xT[:, c * T + s4 * 512: c * T + (s4 + 1) * 512],
                yo_ps[s4 % 2][:, 0:512]), inc="sv", waits=[mmtok])
            P.wait("t", rt)
        wo_guard[wbuf] = mmtok
        wo_toks = nwo
    return




def emit_l1_ffn(P, ctx, C, xT, xnT, W):
    nc = P.nc
    NH, HS, GS, HD = 16, 512, 128, 64
    sb = lambda name, shape, dt: P.sbuf(ctx, name, shape, dt)

    def chain(ops, first_waits=()):
        tok = None
        for eng, fn in ops:
            w = [tok] if tok is not None else list(first_waits)
            tok = P.op(eng, fn, inc={"v": "sv", "s": "ss", "t": "st"}[eng], waits=w)
        return tok

    w_all = sb("f1_wall", [128, 16 * NH], F32)
    wT_stage = RowStage(P, ctx, "f1_ws", NH)

    with ExitStack() as gctx:
        gsb = lambda name, shape, dt: P.sbuf(gctx, name, shape, dt)
        gps = lambda name, shape: P.psum(gctx, name, shape)
        rfull = gsb("f1_rf", [1, T], F32)
        afull = gsb("f1_af", [1, T], F32)
        ln_tok = emit_ln(P, gctx, C, xT, xnT, ra_rows=(rfull, afull))
        P.fence_dma()
        wr_sb = gsb("f1_wr", [128, 1024], F32)
        wrT_sb = gsb("f1_wrT", [128, 8 * 128], F32)
        cent_sb = gsb("f1_cent", [16, 128], F32)
        cT_sb = gsb("f1_cT", [128, 16], F32)
        csq = gsb("f1_csq", [16, 128], F32)
        Sc = gsb("f1_Sc", [16, 1], F32)
        lc = gsb("f1_lc", [16, 1], F32)
        rn_sb = gsb("f1_rn", [16, 1], F32)
        rne = gsb("f1_rne", [16, 1], F32)
        et_sb = gsb("f1_et", [16, 1], F32)
        tmp_sb = gsb("f1_tmp", [1, 1], F32)
        wrsum_row = gsb("f1_wsr", [1, 128], F32)
        wrsum_col = gsb("f1_wsc", [128, 1], F32)
        z32 = gsb("f1_z32", [128, T], F32)
        ztmp = gsb("f1_ztmp", [128, 512], F32)
        zsq = gsb("f1_zsq", [128, 512], BF16)
        lrow = gsb("f1_lrow", [1, 512], F32)
        rz_row = gsb("f1_rz", [1, T], F32)
        lgT = gsb("f1_lgT", [16, T], F32)
        sb16 = gsb("f1_sb16", [16, 512], F32)
        rt16 = gsb("f1_rt16", [128, 16], F32)
        ex16 = gsb("f1_ex16", [128, 16], F32)
        mx1 = gsb("f1_mx1", [128, 1], F32)
        nmx = gsb("f1_nmx", [128, 1], F32)
        sm1 = gsb("f1_sm1", [128, 1], F32)
        mx8 = gsb("f1_mx8", [128, 8], F32)
        sel = gsb("f1_sel", [128, 16], F32)
        raw_ps = gps("f1_raw", [128, 512])
        rb_ps = gps("f1_rb", [128, 512])
        ab_ps = gps("f1_ab", [128, 512])
        sz_ps = gps("f1_sz", [1, 512])
        cz_ps = gps("f1_cz", [16, 512])
        rzb_ps = gps("f1_rzb", [16, 512])
        tp_ps = gps("f1_tp", [128, 16])
        tp2_ps = gps("f1_tp2", [16, 128])
        ws_ps = tp2_ps
        wc_ps = tp_ps

        for c in range(NCH):
            P.dma("y", wr_sb[:, c * 128:(c + 1) * 128],
                  W["w_router"][:, c * 128:(c + 1) * 128], sem="df")
        P.dma("y", cent_sb[:], W["head_centroids"], sem="df")
        ldt = P.dma("y", tmp_sb[:], W["temperature"].rearrange("(a b) -> a b", a=1),
                    sem="df")
        trt = None
        for c in range(NCH):
            tmm = P.op("t", lambda e, c=c: e.transpose(
                tp_ps[:, 0:16] if False else rb_ps[:, 0:128],
                wr_sb[:, c * 128:(c + 1) * 128], C["identf"][:]),
                inc="st", waits=[ldt] if c == 0 else [trt])
            trt = P.op("v", lambda e, c=c: e.tensor_copy(
                wrT_sb[:, c * 128:(c + 1) * 128], rb_ps[:, 0:128]), inc="sv",
                waits=[tmm])
            P.wait("t", trt)
        smm = None
        for c in range(NCH):
            smm = P.op("t", lambda e, c=c: e.matmul(
                ws_ps[0:1, 0:128], C["ones_col_f"][:],
                wrT_sb[:, c * 128:(c + 1) * 128],
                start=(c == 0), stop=(c == NCH - 1)),
                waits=[trt] if c == 0 else [], inc="st" if c == NCH - 1 else None)
        sev = P.op("s", lambda e: e.activation(out=wrsum_row[:], in_=ws_ps[0:1, 0:128],
                                               func=AF.Copy), inc="ss", waits=[smm])
        cmm = P.op("t", lambda e: e.matmul(wc_ps[:, 0:1], wrsum_row[:],
                                           C["ones_row_f"][0:1, 0:1],
                                           start=True, stop=True), inc="st",
                   waits=[sev])
        wcv = P.op("v", lambda e: e.tensor_copy(wrsum_col[:], wc_ps[:, 0:1]), inc="sv",
                   waits=[cmm])
        ctm = P.op("t", lambda e: e.transpose(tp_ps[:, 0:16], cent_sb[:],
                                              C["identf"][0:16, 0:16]), inc="st",
                   waits=[wcv])
        ctv = P.op("v", lambda e: e.tensor_copy(cT_sb[:], tp_ps[:, 0:16]), inc="sv",
                   waits=[ctm])
        cn = chain([
            ("v", lambda e: e.tensor_mul(csq[:], cent_sb[:], cent_sb[:])),
            ("v", lambda e: e.reduce_sum(Sc[:], csq[:], axis=mybir.AxisListType.X)),
        ], first_waits=[ldt])
        cl = chain([
            ("s", lambda e: e.activation(out=lc[:], in_=Sc[:], func=AF.Ln)),
            ("s", lambda e: e.activation(out=rn_sb[:], in_=lc[:], func=AF.Exp,
                                         scale=-0.5)),
            ("s", lambda e: e.activation(out=tmp_sb[:], in_=tmp_sb[:], func=AF.Exp)),
        ], first_waits=[cn])
        etm = P.op("t", lambda e: e.matmul(tp_ps[0:16, 0:1], C["ones_row_f"][0:1, 0:16],
                                           tmp_sb[:], start=True, stop=True),
                   inc="st", waits=[cl, ctv])
        rne_t = chain([
            ("v", lambda e: e.tensor_copy(et_sb[:], tp_ps[0:16, 0:1])),
            ("v", lambda e: e.tensor_mul(rne[:], rn_sb[:], et_sb[:])),
            ("v", lambda e: e.tensor_scalar(rne[:], rne[:],
                                            float(1.0 / math.sqrt(GS)), None,
                                            ALU.mult)),
        ], first_waits=[etm])
        P.wait("t", rne_t)
        P.wait("t", ln_tok)
        zev = None
        for s4 in range(4):
            mm = None
            for c in range(NCH):
                mm = P.op("t", lambda e, c=c, s4=s4: e.matmul(
                    raw_ps[:, 0:512],
                    wrT_sb[:, c * 128:(c + 1) * 128],
                    xT[:, c * T + s4 * 512: c * T + (s4 + 1) * 512],
                    start=(c == 0), stop=(c == NCH - 1)),
                    inc="st" if c == NCH - 1 else None)
            b1 = P.op("t", lambda e, s4=s4: e.matmul(
                rb_ps[:, 0:512], C["ones_row_f"][:],
                rfull[0:1, s4 * 512:(s4 + 1) * 512], start=True, stop=True))
            b2 = P.op("t", lambda e, s4=s4: e.matmul(
                ab_ps[:, 0:512], C["ones_row_f"][:],
                afull[0:1, s4 * 512:(s4 + 1) * 512], start=True, stop=True),
                inc="st")
            zz = chain([
                ("v", lambda e: e.tensor_copy(ztmp[:], raw_ps[:, 0:512])),
                ("v", lambda e: e.tensor_mul(ztmp[:], ztmp[:], rb_ps[:, 0:512])),
                ("v", lambda e, s4=s4: e.scalar_tensor_tensor(
                    z32[:, s4 * 512:(s4 + 1) * 512], ab_ps[:, 0:512], wrsum_col[:],
                    ztmp[:], ALU.mult, ALU.add)),
                ("v", lambda e, s4=s4: e.tensor_mul(
                    zsq[:], z32[:, s4 * 512:(s4 + 1) * 512],
                    z32[:, s4 * 512:(s4 + 1) * 512])),
            ], first_waits=[mm, b2, rne_t if s4 == 0 else None])
            zmm = P.op("t", lambda e: e.matmul(
                sz_ps[:, 0:512], C["ones_col_b"][:], zsq[:],
                start=True, stop=True), inc="st", waits=[zz])
            zev = chain([
                ("s", lambda e: e.activation(out=lrow[:], in_=sz_ps[:], func=AF.Ln)),
                ("s", lambda e, s4=s4: e.activation(
                    out=rz_row[0:1, s4 * 512:(s4 + 1) * 512], in_=lrow[:],
                    func=AF.Exp, scale=-0.5)),
            ], first_waits=[zmm])
            P.wait("t", zev)
        lg_ev = None
        for s4 in range(4):
            mm = P.op("t", lambda e, s4=s4: e.matmul(
                cz_ps[:, 0:512], cT_sb[:], z32[:, s4 * 512:(s4 + 1) * 512],
                start=True, stop=True), inc="st",
                waits=[zev] if s4 == 0 else [lg_ev])
            bz = P.op("t", lambda e, s4=s4: e.matmul(
                rzb_ps[:, 0:512], C["ones_row_f"][0:1, 0:16],
                rz_row[0:1, s4 * 512:(s4 + 1) * 512], start=True, stop=True),
                inc="st")
            lg_ev = chain([
                ("v", lambda e: e.tensor_scalar(sb16[:], cz_ps[:, 0:512], rne[:],
                                                None, ALU.mult)),
                ("v", lambda e, s4=s4: e.tensor_mul(
                    lgT[:, s4 * 512:(s4 + 1) * 512], sb16[:], rzb_ps[:, 0:512])),
            ], first_waits=[mm, bz])
            P.wait("t", lg_ev)
        wtr = None
        for tc in range(NTCH):
            tmm = P.op("t", lambda e, tc=tc: e.transpose(
                tp_ps[:, 0:16], lgT[:, tc * 128:(tc + 1) * 128],
                C["identf"][0:16, 0:16]), inc="st",
                waits=[lg_ev] if tc == 0 else [wtr])
            cp = P.op("v", lambda e: e.tensor_copy(rt16[:], tp_ps[:, 0:16]),
                      inc="sv", waits=[tmm])
            r1 = chain([
                ("v", lambda e: e.reduce_max(mx1[:], rt16[:],
                                             axis=mybir.AxisListType.X)),
                ("v", lambda e: e.tensor_scalar(nmx[:], mx1[:], -1.0, None,
                                                ALU.mult)),
            ], first_waits=[cp])
            ex_t = P.op("s", lambda e: e.activation(out=ex16[:], in_=rt16[:],
                                                    func=AF.Exp, bias=nmx[:]),
                        inc="ss", waits=[r1])
            wtr_prev = wtr
            wtr = chain([
                ("v", lambda e: e.reduce_sum(sm1[:], ex16[:],
                                             axis=mybir.AxisListType.X)),
                ("v", lambda e: e.reciprocal(sm1[:], sm1[:])),
                ("v", lambda e: e.tensor_scalar(ex16[:], ex16[:], sm1[:], None,
                                                ALU.mult)),
                ("v", lambda e: e.max(out=mx8[:], in_=ex16[:])),
                ("v", lambda e: e.tensor_scalar(sel[:], ex16[:], mx8[:, 1:2], None,
                                                ALU.is_ge)),
                ("v", lambda e, tc=tc: e.tensor_mul(
                    w_all[:, tc * NH:(tc + 1) * NH], ex16[:], sel[:])),
            ], first_waits=[ex_t])
            tm2 = P.op("t", lambda e, tc=tc: e.transpose(
                tp2_ps[:], w_all[:, tc * NH:(tc + 1) * NH],
                C["identf"][:]), inc="st", waits=[wtr])
            wtr = P.op("s", lambda e, tc=tc: e.activation(
                out=lgT[:, tc * 128:(tc + 1) * 128], in_=tp2_ps[:],
                func=AF.Copy), inc="ss", waits=[tm2])
        wT_stage.store(lgT[:], waits=[wtr])

    with ExitStack() as ectx:
        esb = lambda name, shape, dt: P.sbuf(ectx, name, shape, dt)
        eps = lambda name, shape: P.psum(ectx, name, shape)
        P.fence_dma()
        P.wait("g", wT_stage.st_tok)
        wie = [esb("f1_wie%d" % i, [128, 8 * 512], BF16) for i in range(2)]
        woe = [esb("f1_woe%d" % i, [128, 4 * 128], BF16) for i in range(2)]
        h_sb = esb("f1_h", [128, 4 * T], BF16)
        osb = esb("f1_osb", [128, 512], BF16)
        h_ps = [eps("f1_hps%d" % i, [128, 512]) for i in range(4)]
        o_ps = eps("f1_ops", [128, 512])
        wb_ps = [eps("f1_wbps%d" % i, [128, 512]) for i in range(2)]
        wz = [P.op("g", lambda e, i=i: e.memset(woe[i][:], 0.0), inc="sg")
              for i in range(2)]

        def load_head(h, buf, guard):
            tk = None
            g0 = [guard] if guard is not None else []
            if h < 2:
                g0 = g0 + [wz[buf]]
            for c in range(NCH):
                tk = P.dma("g", wie[buf][:, c * 512:(c + 1) * 512],
                           W["input_experts"][h, c * 128:(c + 1) * 128, :],
                           waits=g0 if c == 0 else [], sem="dg%d" % buf)
            for ic in range(4):
                tk = P.dma("g", woe[buf][:, ic * 128 + 64:(ic + 1) * 128],
                           W["output_experts"][h, ic * 128:(ic + 1) * 128, :],
                           sem="dg%d" % buf)
            return [tk]

        wtoks = load_head(0, 0, None)
        prev_use = [None, None]
        prev_omm = None
        prev_mvs = [None, None]
        for h in range(NH):
            buf = h % 2
            w_row, w_tok, w_b = wT_stage.load_row(h)
            nw = load_head(h + 1, 1 - buf, prev_use[1 - buf]) if h + 1 < NH else None
            P.wait("t", wtoks)
            ge_last = None
            ge_by_bank = {}
            pend_h = []

            def flush_h():
                s4, mms = pend_h.pop(0)
                ge = None
                for ic in range(4):
                    gw = [mms[ic]]
                    if ic == 0 and s4 == 0 and prev_omm is not None:
                        gw.append(prev_omm)
                    ge = P.op("s", lambda e, ic=ic, s4=s4: e.activation(
                        out=h_sb[:, ic * T + s4 * 512: ic * T + (s4 + 1) * 512],
                        in_=h_ps[ic][:, 0:512], func=AF.Gelu), inc="ss", waits=gw)
                    ge_by_bank[ic] = ge
                return ge

            for s4 in range(4):
                mms = {}
                for ic in range(4):
                    if ge_by_bank.get(ic) is not None and s4 > 0:
                        P.wait("t", ge_by_bank[ic])
                    mm = None
                    for c in range(NCH):
                        mm = P.op("t", lambda e, c=c, ic=ic, s4=s4, buf=buf: e.matmul(
                            h_ps[ic][:, 0:512],
                            wie[buf][:, c * 512 + ic * 128: c * 512 + (ic + 1) * 128],
                            xnT[:, c * T + s4 * 512: c * T + (s4 + 1) * 512],
                            start=(c == 0), stop=(c == NCH - 1)),
                            inc="st" if c == NCH - 1 else None)
                    mms[ic] = mm
                pend_h.append((s4, mms))
                ge_last = flush_h()
            ot = None
            mvs = []
            cc, half = h // 2, h % 2
            qb = half * 64
            for s4 in range(4):
                mm = None
                for ic in range(4):
                    if half == 0:
                        lhs = woe[buf][:, ic * 128 + 64:(ic + 1) * 128]
                        dst = o_ps[0:64, 0:512]
                    else:
                        lhs = woe[buf][:, ic * 128:(ic + 1) * 128]
                        dst = o_ps[:, 0:512]
                    mm = P.op("t", lambda e, lhs=lhs, dst=dst, ic=ic, s4=s4: e.matmul(
                        dst, lhs,
                        h_sb[:, ic * T + s4 * 512: ic * T + (s4 + 1) * 512],
                        start=(ic == 0), stop=(ic == 3)),
                        waits=[ge_last] if (s4 == 0 and ic == 0) else [],
                        inc="st" if ic == 3 else None)
                wbk = s4 % 2
                bw = [w_tok] if s4 == 0 else []
                pb = mvs[s4 - 2] if s4 >= 2 else prev_mvs[wbk]
                if pb is not None:
                    bw = bw + [pb]
                bmm = P.op("t", lambda e, s4=s4, w_row=w_row, wbk=wbk: e.matmul(
                    wb_ps[wbk][:, 0:512], C["ones_row_b"][:],
                    w_row[0:1, s4 * 512:(s4 + 1) * 512], start=True, stop=True),
                    inc="st", waits=bw)
                ov = P.op("v", lambda e, qb=qb: e.tensor_copy(
                    osb[qb:qb + 64, :], o_ps[qb:qb + 64, 0:512]),
                    inc="sv", waits=[mm] + ([ot] if ot is not None else []))
                mv = P.op("v", lambda e, qb=qb, wbk=wbk: e.tensor_mul(
                    osb[qb:qb + 64, :], osb[qb:qb + 64, :],
                    wb_ps[wbk][qb:qb + 64, 0:512]),
                    inc="sv", waits=[bmm, ov])
                mvs.append(mv)
                ot = P.op("v", lambda e, cc=cc, qb=qb, s4=s4: e.tensor_add(
                    xT[qb:qb + 64, cc * T + s4 * 512: cc * T + (s4 + 1) * 512],
                    xT[qb:qb + 64, cc * T + s4 * 512: cc * T + (s4 + 1) * 512],
                    osb[qb:qb + 64, :]), inc="sv", waits=[mv])
                P.wait("t", ot)
            prev_mvs = [mvs[2], mvs[3]]
            wT_stage.set_guard(w_b, bmm)
            prev_use[buf] = mm
            prev_omm = mm
            wtoks = nw
    return




def emit_l1_attn_b(P, ctx, C, xT, xnT, W):
    nc = P.nc
    H, DH, WIN, DIL = 16, 64, 128, 4
    sb = lambda name, shape, dt: P.sbuf(ctx, name, shape, dt)
    psb = lambda name, shape: P.psum(ctx, name, shape)
    P.fence_dma()
    scale = 1.0 / math.sqrt(DH)
    ln_tok = P.tok("sv")

    QT = sb("mb_qt", [128, T], BF16)
    KT = sb("mb_kt", [128, T], BF16)
    Vp = sb("mb_vp", [128, 2 * 16 * 65], BF16)
    Vd = sb("mb_vd", [128, 2 * 16 * 65], BF16)
    Olo = sb("mb_olo", [128, T], BF16)
    Odi = sb("mb_odi", [128, T], BF16)
    Ogl = sb("mb_ogl", [128, T], BF16)
    o_acc = sb("mb_oacc", [128, T], BF16)
    otmp = sb("mb_otmp", [128, 512], BF16)
    y_acc = sb("mb_yacc", [128, NCH * T], BF16)
    wq_sb = sb("mb_wq", [128, 8 * 128], BF16)
    wk_sb = sb("mb_wk", [128, 8 * 128], BF16)
    wv_sb = sb("mb_wv", [128, 8 * 128], BF16)
    wo_sb = [sb("mb_wo%d" % i, [128, 8 * 128], BF16) for i in range(2)]
    sums_sb = sb("mb_sums", [65, T], F32)
    expT = [sb("mb_exp%d" % i, [128, 512], BF16) for i in range(3)]

    pr_ps = [psb("mb_pr%d" % i, [128, 512]) for i in range(2)]
    sc_ps = [psb("mb_sc%d" % i, [128, 512]) for i in range(3)]
    pv_ps = psb("mb_pv", [65, 512])
    nb_ps = psb("mb_nb", [128, 512])
    yo_ps = [psb("mb_yo0", [128, 512]), nb_ps]

    bounce = RowBounce(P, ctx, "mb_b", 6, C=C, fold=1.0 / 3.0)
    sh_ap, sh_tok = C["bias"](-SHIFT)

    prev_pe = None
    prev_proj = None
    prev_res = None
    sums_guard = None
    last_exp = [None, None, None]
    exp_guard = {0: None, 1: None, 2: None}
    yev = {}
    last_pair = None
    for hp in range(8):
        wt = None
        guard = [prev_proj] if prev_proj is not None else []
        wt1 = None
        for c in range(NCH):
            r0, r1 = c * 128, (c + 1) * 128
            wt = P.dma("g", wq_sb[:, r0:r1],
                       W["w_qkv"][r0:r1, hp * 128:(hp + 1) * 128],
                       waits=guard if c == 0 else [], sem="dg0")
            wt = P.dma("g", wk_sb[:, r0:r1],
                       W["w_qkv"][r0:r1, 1024 + hp * 128:1024 + (hp + 1) * 128],
                       sem="dg0")
            wt = P.dma("g", wv_sb[:, r0:r1],
                       W["w_qkv"][r0:r1, 2048 + hp * 128:2048 + (hp + 1) * 128],
                       sem="dg0")
            wt1 = P.dma("g", wo_sb[hp % 2][:, r0:r1],
                        W["w_out"][hp * 128:(hp + 1) * 128, r0:r1],
                        waits=[prev_pe] if (prev_pe is not None and c == 0) else [],
                        sem="dg1")
        P.wait("t", ln_tok)
        if prev_pe is not None:
            P.wait("t", prev_pe)
        ev_bank = {}
        bank_i = [0]

        def pick_bank():
            b = bank_i[0] % 2
            bank_i[0] += 1
            if ev_bank.get(b) is not None:
                P.wait("t", ev_bank[b])
            return b

        for dst, wmat in ((QT, wq_sb), (KT, wk_sb)):
            for s4 in range(4):
                b = pick_bank()
                mm = None
                for c in range(NCH):
                    mm = P.op("t", lambda e, s4=s4, c=c, wmat=wmat, b=b: e.matmul(
                        pr_ps[b][:, 0:512],
                        wmat[:, c * 128:(c + 1) * 128],
                        xnT[:, c * T + s4 * 512: c * T + (s4 + 1) * 512],
                        start=(c == 0), stop=(c == NCH - 1)),
                        waits=[wt] if (dst is QT and s4 == 0 and c == 0) else [],
                        inc="st" if c == NCH - 1 else None)
                ev = P.op("v", lambda e, dst=dst, s4=s4, b=b: e.tensor_copy(
                    dst[:, s4 * 512:(s4 + 1) * 512], pr_ps[b][:, 0:512]), inc="sv",
                    waits=[mm])
                ev_bank[b] = ev
        vz = P.op("g", lambda e: e.memset(Vp[:], 0.0),
                  waits=[prev_pe] if prev_pe is not None else [], inc="sg")
        vz2 = P.op("g", lambda e: e.memset(Vd[:], 0.0), waits=[vz], inc="sg")
        vone = P.op("g", lambda e: e.memset(Vp[:, 64::65], 1.0), inc="sg",
                    waits=[vz2])
        vone2 = P.op("g", lambda e: e.memset(Vd[:, 64::65], 1.0), inc="sg",
                     waits=[vone])
        vev = None
        for tc in range(NTCH):
            b = pick_bank()
            mm = None
            for c in range(NCH):
                mm = P.op("t", lambda e, tc=tc, c=c, b=b: e.matmul(
                    pr_ps[b][:, 0:128],
                    xnT[:, c * T + tc * 128: c * T + (tc + 1) * 128],
                    wv_sb[:, c * 128:(c + 1) * 128],
                    start=(c == 0), stop=(c == NCH - 1)),
                    waits=[vone2] if (tc == 0 and c == 0) else [],
                    inc="st" if c == NCH - 1 else None)
            for j in range(2):
                vev = P.op("v", lambda e, tc=tc, j=j, b=b: e.tensor_copy(
                    Vp[:, j * 1040 + tc * 65: j * 1040 + tc * 65 + 64],
                    pr_ps[b][:, j * 64:(j + 1) * 64]), waits=[mm] if j == 0 else [],
                    inc="sv" if j == 1 else None)
            ev_bank[b] = vev
            gph, ib = tc // 4, tc % 4
            b = pick_bank()
            mm = None
            for c in range(NCH):
                mm = P.op("t", lambda e, tc=tc, c=c, gph=gph, ib=ib, b=b: e.matmul(
                    pr_ps[b][:, 0:128],
                    xnT[:, c * T + gph + ib * 512: c * T + gph + ib * 512 + 509:4],
                    wv_sb[:, c * 128:(c + 1) * 128],
                    start=(c == 0), stop=(c == NCH - 1)),
                    inc="st" if c == NCH - 1 else None)
            for j in range(2):
                vev = P.op("v", lambda e, tc=tc, j=j, b=b: e.tensor_copy(
                    Vd[:, j * 1040 + tc * 65: j * 1040 + tc * 65 + 64],
                    pr_ps[b][:, j * 64:(j + 1) * 64]), waits=[mm] if j == 0 else [],
                    inc="sv" if j == 1 else None)
            ev_bank[b] = vev
        prev_proj = P.tok("st")
        pvsum_tok = None
        for j in range(2):
            qb = j * 64
            pend = []

            def flush_loc(pend=None, j=j, qb=qb):
                w, etok = pend.pop(0)
                slot = pv_ps if w % 2 == 0 else yo_ps[0]
                pvt = P.op("t", lambda e, j=j, w=w, slot=slot: e.matmul(
                    slot[0:65, 0:128],
                    Vp[:, j * 1040 + w * 65: j * 1040 + (w + 1) * 65],
                    expT[w % 3][:, 0:128], start=True, stop=True),
                    waits=[etok] + (pv_guard[w % 2] if pv_guard[w % 2] is not None else []),
                    inc="st")
                exp_guard[w % 3] = pvt
                oev = P.op("v", lambda e, qb=qb, w=w, slot=slot: e.tensor_copy(
                    Olo[qb:qb + 64, w * 128:(w + 1) * 128],
                    slot[0:64, 0:128]), inc="sv", waits=[pvt])
                sw = [pvt]
                if w == 0 and sums_guard is not None:
                    sw.append(sums_guard)
                sev = P.op("s", lambda e, w=w, slot=slot: e.activation(
                    out=sums_sb[64:65, w * 128:(w + 1) * 128],
                    in_=slot[64:65, 0:128], func=AF.Copy),
                    inc="ss", waits=sw)
                pv_guard[w % 2] = [oev, sev]
                return oev, sev

            pv_guard = [last_pair, last_pair]
            for w in range(NTCH):
                par = w % 3
                scp = sc_ps[par]
                swaits = []
                if hp == 0 and j == 0 and w == 0:
                    swaits.append(sh_tok)
                if j == 0 and w == 0:
                    swaits.append(vev)
                if last_exp[par] is not None:
                    swaits.append(last_exp[par])
                smm = P.op("t", lambda e, qb=qb, w=w, scp=scp: e.matmul(
                    scp[:, 0:128],
                    KT[qb:qb + 64, w * 128:(w + 1) * 128],
                    QT[qb:qb + 64, w * 128:(w + 1) * 128],
                    start=True, stop=True), waits=swaits, inc="st")
                ew = [smm]
                if exp_guard[par] is not None:
                    ew.append(exp_guard[par])
                etok = P.op("s", lambda e, par=par, scp=scp: e.activation(
                    out=expT[par][:, 0:128], in_=scp[:, 0:128], func=AF.Exp,
                    scale=scale, bias=sh_ap), inc="ss", waits=ew)
                last_exp[par] = etok
                pend.append((w, etok))
                if len(pend) > 2:
                    oev, sev = flush_loc(pend)
            while pend:
                oev, sev = flush_loc(pend)
            last_pair = [oev, sev]
            sums_guard = bounce.write(sums_sb[64:65, :], 3 * j + 0, waits=[sev])
            pend = []
            ev_prev = [last_pair]

            def flush_dil(pend=None, j=j):
                kk, etok = pend.pop(0)
                gph, ktd = kk // 4, kk % 4
                pw = [etok]
                if ktd == 0 and ev_prev[0] is not None:
                    pw += ev_prev[0]
                tk = P.op("t", lambda e, j=j, gph=gph, ktd=ktd, kk=kk: e.matmul(
                    pv_ps[:, 0:512],
                    Vd[:, j * 1040 + kk * 65: j * 1040 + (kk + 1) * 65],
                    expT[kk % 3][:, 0:512], start=(ktd == 0), stop=(ktd == 3)),
                    waits=pw, inc="st")
                exp_guard[kk % 3] = tk
                return tk, gph, ktd

            for kk in range(16):
                gph, ktd = kk // 4, kk % 4
                par = kk % 3
                scp = sc_ps[par]
                swaits = []
                if last_exp[par] is not None:
                    swaits.append(last_exp[par])
                smm = P.op("t", lambda e, qb=qb, gph=gph, ktd=ktd, scp=scp: e.matmul(
                    scp[:, 0:512],
                    KT[qb:qb + 64, gph + ktd * 512: gph + ktd * 512 + 509:4],
                    QT[qb:qb + 64, gph: gph + 2045:4],
                    start=True, stop=True), waits=swaits, inc="st")
                ew = [smm]
                if exp_guard[par] is not None:
                    ew.append(exp_guard[par])
                etok = P.op("s", lambda e, par=par, scp=scp: e.activation(
                    out=expT[par][:, 0:512], in_=scp[:, 0:512], func=AF.Exp,
                    scale=scale, bias=sh_ap), inc="ss", waits=ew)
                last_exp[par] = etok
                pend.append((kk, etok))
                if len(pend) > 2:
                    tk, fg, fk = flush_dil(pend)
                    if fk == 3:
                        oev = P.op("v", lambda e, qb=qb, fg=fg: e.tensor_copy(
                            Odi[qb:qb + 64, fg * 512:(fg + 1) * 512],
                            pv_ps[0:64, 0:512]), inc="sv", waits=[tk])
                        sev = P.op("s", lambda e, fg=fg: e.activation(
                            out=sums_sb[64:65, fg * 512:(fg + 1) * 512],
                            in_=pv_ps[64:65, 0:512], func=AF.Copy), inc="ss",
                            waits=[tk] + ([sums_guard] if fg == 0 else []))
                        ev_prev[0] = [oev, sev]
            while pend:
                tk, fg, fk = flush_dil(pend)
                if fk == 3:
                    oev = P.op("v", lambda e, qb=qb, fg=fg: e.tensor_copy(
                        Odi[qb:qb + 64, fg * 512:(fg + 1) * 512],
                        pv_ps[0:64, 0:512]), inc="sv", waits=[tk])
                    sev = P.op("s", lambda e, fg=fg: e.activation(
                        out=sums_sb[64:65, fg * 512:(fg + 1) * 512],
                        in_=pv_ps[64:65, 0:512], func=AF.Copy), inc="ss",
                        waits=[tk] + ([sums_guard] if fg == 0 else []))
                    ev_prev[0] = [oev, sev]
            last_pair = [oev, sev]
            sums_guard = bounce.write(sums_sb[64:65, :], 3 * j + 1, waits=[sev])
            pend = []
            ev_prev2 = [last_pair]
            for qq in range(4 * NTCH):
                qc, kt = qq // NTCH, qq % NTCH
                par = qq % 3
                scp = sc_ps[par]
                swaits = []
                if last_exp[par] is not None:
                    swaits.append(last_exp[par])
                smm = P.op("t", lambda e, qb=qb, kt=kt, qc=qc, scp=scp: e.matmul(
                    scp[:, 0:512],
                    KT[qb:qb + 64, kt * 128:(kt + 1) * 128],
                    QT[qb:qb + 64, qc * 512:(qc + 1) * 512],
                    start=True, stop=True), waits=swaits, inc="st")
                ew = [smm]
                if exp_guard[par] is not None:
                    ew.append(exp_guard[par])
                etok = P.op("s", lambda e, par=par, scp=scp: e.activation(
                    out=expT[par][:, 0:512], in_=scp[:, 0:512], func=AF.Exp,
                    scale=scale, bias=sh_ap), inc="ss", waits=ew)
                last_exp[par] = etok
                pend.append((qq, etok))
                if len(pend) > 2:
                    qq2, etok2 = pend.pop(0)
                    qc2, kt2 = qq2 // NTCH, qq2 % NTCH
                    pw = [etok2]
                    if kt2 == 0 and ev_prev2[0] is not None:
                        pw += ev_prev2[0]
                    tk = P.op("t", lambda e, j=j, kt2=kt2, qq2=qq2: e.matmul(
                        pv_ps[:, 0:512],
                        Vp[:, j * 1040 + kt2 * 65: j * 1040 + (kt2 + 1) * 65],
                        expT[qq2 % 3][:, 0:512],
                        start=(kt2 == 0), stop=(kt2 == NTCH - 1)),
                        waits=pw, inc="st")
                    exp_guard[qq2 % 3] = tk
                    if kt2 == NTCH - 1:
                        oev = P.op("v", lambda e, qb=qb, qc2=qc2: e.tensor_copy(
                            Ogl[qb:qb + 64, qc2 * 512:(qc2 + 1) * 512],
                            pv_ps[0:64, 0:512]), inc="sv", waits=[tk])
                        sev = P.op("s", lambda e, qc2=qc2: e.activation(
                            out=sums_sb[64:65, qc2 * 512:(qc2 + 1) * 512],
                            in_=pv_ps[64:65, 0:512], func=AF.Copy), inc="ss",
                            waits=[tk] + ([sums_guard] if qc2 == 0 else []))
                        ev_prev2[0] = [oev, sev]
            while pend:
                qq2, etok2 = pend.pop(0)
                qc2, kt2 = qq2 // NTCH, qq2 % NTCH
                pw = [etok2]
                if kt2 == 0 and ev_prev2[0] is not None:
                    pw += ev_prev2[0]
                tk = P.op("t", lambda e, j=j, kt2=kt2, qq2=qq2: e.matmul(
                    pv_ps[:, 0:512],
                    Vp[:, j * 1040 + kt2 * 65: j * 1040 + (kt2 + 1) * 65],
                    expT[qq2 % 3][:, 0:512],
                    start=(kt2 == 0), stop=(kt2 == NTCH - 1)),
                    waits=pw, inc="st")
                exp_guard[qq2 % 3] = tk
                if kt2 == NTCH - 1:
                    oev = P.op("v", lambda e, qb=qb, qc2=qc2: e.tensor_copy(
                        Ogl[qb:qb + 64, qc2 * 512:(qc2 + 1) * 512],
                        pv_ps[0:64, 0:512]), inc="sv", waits=[tk])
                    sev = P.op("s", lambda e, qc2=qc2: e.activation(
                        out=sums_sb[64:65, qc2 * 512:(qc2 + 1) * 512],
                        in_=pv_ps[64:65, 0:512], func=AF.Copy), inc="ss",
                        waits=[tk] + ([sums_guard] if qc2 == 0 else []))
                    ev_prev2[0] = [oev, sev]
            last_pair = [oev, sev]
            sums_guard = bounce.write(sums_sb[64:65, :], 3 * j + 2, waits=[sev])
        bounce.recip_all()
        comb_last = None
        nbank4 = [nb_ps, yo_ps[0]]
        nb_guard = {}
        for j in range(2):
            qb = j * 64
            for pat, Osrc in ((0, Olo), (1, Odi), (2, Ogl)):
                rap, rtok, rb = bounce.load_row(3 * j + pat)
                bmm = None
                for s4 in range(4):
                    nb = s4 % 2
                    w = [rtok] if s4 == 0 else []
                    if nb_guard.get(nb) is not None:
                        w.append(nb_guard[nb])
                    bmm = P.op("t", lambda e, s4=s4, rap=rap, nb=nb: e.matmul(
                        nbank4[nb][:, 0:512], C["ones_row_b"][:],
                        rap[0:1, s4 * 512:(s4 + 1) * 512], start=True, stop=True),
                        inc="st", waits=w)
                    t1 = P.op("v", lambda e, qb=qb, s4=s4, Osrc=Osrc, nb=nb: e.tensor_mul(
                        otmp[qb:qb + 64, :], Osrc[qb:qb + 64, s4 * 512:(s4 + 1) * 512],
                        nbank4[nb][qb:qb + 64, 0:512]), inc="sv",
                        waits=[bmm] + ([comb_last] if comb_last is not None else []))
                    nb_guard[nb] = t1
                    if pat == 0:
                        comb_last = P.op("v", lambda e, qb=qb, s4=s4: e.tensor_copy(
                            o_acc[qb:qb + 64, s4 * 512:(s4 + 1) * 512],
                            otmp[qb:qb + 64, :]), inc="sv", waits=[t1])
                    elif pat == 2:
                        comb_last = P.op("v", lambda e, qb=qb, s4=s4: e.tensor_add(
                            o_acc[qb:qb + 64, s4 * 512:(s4 + 1) * 512],
                            o_acc[qb:qb + 64, s4 * 512:(s4 + 1) * 512],
                            otmp[qb:qb + 64, :]), inc="sv", waits=[t1])
                    else:
                        comb_last = P.op("v", lambda e, qb=qb, s4=s4: e.tensor_add(
                            o_acc[qb:qb + 64, s4 + 0:s4 + 2045:4],
                            o_acc[qb:qb + 64, s4 + 0:s4 + 2045:4],
                            otmp[qb:qb + 64, :]), inc="sv", waits=[t1])
                    P.wait("t", comb_last)
                bounce.set_guard(rb, bmm)
        mm = None
        for c in range(NCH):
            for s4 in range(4):
                yb = (c * 4 + s4) % 2
                mmw = [comb_last, wt1] if (c == 0 and s4 == 0) else []
                if yev.get(yb) is not None:
                    mmw.append(yev[yb])
                mm = P.op("t", lambda e, c=c, s4=s4, yb=yb, hp=hp: e.matmul(
                    yo_ps[yb][:, 0:512],
                    wo_sb[hp % 2][:, c * 128:(c + 1) * 128],
                    o_acc[:, s4 * 512:(s4 + 1) * 512],
                    start=True, stop=True),
                    waits=mmw, inc="st")
                if hp == 0:
                    ydone = P.op("v", lambda e, c=c, s4=s4, yb=yb: e.tensor_copy(
                        y_acc[:, c * T + s4 * 512: c * T + (s4 + 1) * 512],
                        yo_ps[yb][:, 0:512]), inc="sv", waits=[mm])
                else:
                    ydone = P.op("v", lambda e, c=c, s4=s4, yb=yb: e.tensor_add(
                        y_acc[:, c * T + s4 * 512: c * T + (s4 + 1) * 512],
                        y_acc[:, c * T + s4 * 512: c * T + (s4 + 1) * 512],
                        yo_ps[yb][:, 0:512]), inc="sv", waits=[mm])
                yev[yb] = ydone
        last_pair = [yev[0], yev[1]]
        prev_pe = mm
        prev_res = ydone
    rt = None
    for c in range(NCH):
        for s4 in range(4):
            rt = P.op("v", lambda e, c=c, s4=s4: e.tensor_add(
                xT[:, c * T + s4 * 512: c * T + (s4 + 1) * 512],
                xT[:, c * T + s4 * 512: c * T + (s4 + 1) * 512],
                y_acc[:, c * T + s4 * 512: c * T + (s4 + 1) * 512]), inc="sv",
                waits=[prev_res] if (c == 0 and s4 == 0) else [])
    return



INPUT_SPECS = {
    "L0_attn_a_Wqkv": (1024, 3072), "L0_attn_a_Wout": (1024, 1024),
    "L0_attn_a_head_scales": (8,), "L0_attn_a_head_dirs": (8, 3),
    "L0_ffn_gate_w": (1024, 8), "L0_ffn_gate_bias": (8,),
    "L0_ffn_w_gate": (8, 1024, 512), "L0_ffn_w_up": (8, 1024, 512),
    "L0_ffn_w_down": (8, 512, 1024),
    "L0_ffn_sw_gate": (1024, 1024), "L0_ffn_sw_up": (1024, 1024),
    "L0_ffn_sw_down": (1024, 1024),
    "L0_attn_b_wq_sem": (1024, 512), "L0_attn_b_wk_sem": (1024, 512),
    "L0_attn_b_wq_geo": (1024, 512), "L0_attn_b_wk_geo": (1024, 512),
    "L0_attn_b_wv": (1024, 1024), "L0_attn_b_wo": (1024, 1024),
    "L0_attn_b_k_sem_null": (512,), "L0_attn_b_k_geo_null": (512,),
    "L0_attn_b_v_null": (1024,),
    "L1_attn_a_W_dq": (1024, 256), "L1_attn_a_W_uq": (256, 1024),
    "L1_attn_a_W_dkv": (1024, 256), "L1_attn_a_W_uk": (256, 1024),
    "L1_attn_a_W_uv": (256, 1024), "L1_attn_a_W_o": (1024, 1024),
    "L1_ffn_w_router": (128, 1024), "L1_ffn_head_centroids": (16, 128),
    "L1_ffn_temperature": (1,), "L1_ffn_input_experts": (16, 1024, 512),
    "L1_ffn_output_experts": (16, 512, 64),
    "L1_attn_b_w_qkv": (1024, 3072), "L1_attn_b_w_out": (1024, 1024),
}

L0_FFN_W = ["L0_ffn_gate_w", "L0_ffn_gate_bias", "L0_ffn_w_gate", "L0_ffn_w_up",
            "L0_ffn_w_down", "L0_ffn_sw_gate", "L0_ffn_sw_up", "L0_ffn_sw_down"]

L1_ATTN_B_W = ["L1_attn_b_w_qkv", "L1_attn_b_w_out"]

L1_FFN_W = ["L1_ffn_w_router", "L1_ffn_head_centroids", "L1_ffn_temperature",
            "L1_ffn_input_experts", "L1_ffn_output_experts"]

L1_ATTN_A_W = ["L1_attn_a_W_dq", "L1_attn_a_W_uq", "L1_attn_a_W_dkv",
               "L1_attn_a_W_uk", "L1_attn_a_W_uv", "L1_attn_a_W_o"]

L0_ATTN_B_W = ["L0_attn_b_wq_sem", "L0_attn_b_wk_sem", "L0_attn_b_wq_geo",
               "L0_attn_b_wk_geo", "L0_attn_b_wv", "L0_attn_b_wo",
               "L0_attn_b_k_sem_null", "L0_attn_b_k_geo_null", "L0_attn_b_v_null"]

SUBLAYERS = [
    ("l0_attn_a", ["L0_attn_a_Wqkv", "L0_attn_a_Wout", "L0_attn_a_head_scales",
                   "L0_attn_a_head_dirs"]),
    ("l0_ffn", L0_FFN_W),
    ("l0_attn_b", L0_ATTN_B_W),
    ("l0_ffn2", L0_FFN_W),
    ("l1_attn_a", L1_ATTN_A_W),
    ("l1_ffn", L1_FFN_W),
    ("l1_attn_b", L1_ATTN_B_W),
    ("l1_ffn2", L1_FFN_W),
]


def emit_sublayer(P, sctx, C, xT, xnT, name, waps):
    P.fence_dma()
    if name == "ln_dbg":
        emit_ln(P, sctx, C, xT, xnT)
        lt = P.tok("sv")
        for c in range(NCH):
            P.op("v", lambda e, c=c: e.tensor_copy(
                xT[:, c * T:(c + 1) * T], xnT[:, c * T:(c + 1) * T]), inc="sv",
                waits=[lt] if c == 0 else [])
    elif name == "l0_attn_a":
        emit_ln(P, sctx, C, xT, xnT)
        emit_attn_a(P, sctx, C, xT, xnT, {
            "Wqkv": waps["L0_attn_a_Wqkv"], "Wout": waps["L0_attn_a_Wout"],
            "head_scales": waps["L0_attn_a_head_scales"],
            "head_dirs": waps["L0_attn_a_head_dirs"]})
    elif name == "l0_attn_b":
        emit_ln(P, sctx, C, xT, xnT)
        emit_attn_b(P, sctx, C, xT, xnT, {
            "wq_sem": waps["L0_attn_b_wq_sem"], "wk_sem": waps["L0_attn_b_wk_sem"],
            "wq_geo": waps["L0_attn_b_wq_geo"], "wk_geo": waps["L0_attn_b_wk_geo"],
            "wv": waps["L0_attn_b_wv"], "wo": waps["L0_attn_b_wo"],
            "k_sem_null": waps["L0_attn_b_k_sem_null"],
            "k_geo_null": waps["L0_attn_b_k_geo_null"],
            "v_null": waps["L0_attn_b_v_null"]})
    elif name == "l1_attn_a":
        emit_ln(P, sctx, C, xT, xnT)
        emit_l1_attn_a(P, sctx, C, xT, xnT, {
            "W_dq": waps["L1_attn_a_W_dq"], "W_uq": waps["L1_attn_a_W_uq"],
            "W_dkv": waps["L1_attn_a_W_dkv"], "W_uk": waps["L1_attn_a_W_uk"],
            "W_uv": waps["L1_attn_a_W_uv"], "W_o": waps["L1_attn_a_W_o"]})
    elif name == "l1_attn_b":
        emit_ln(P, sctx, C, xT, xnT)
        emit_l1_attn_b(P, sctx, C, xT, xnT, {
            "w_qkv": waps["L1_attn_b_w_qkv"], "w_out": waps["L1_attn_b_w_out"]})
    elif name in ("l1_ffn", "l1_ffn2"):
        emit_l1_ffn(P, sctx, C, xT, xnT, {
            "w_router": waps["L1_ffn_w_router"],
            "head_centroids": waps["L1_ffn_head_centroids"],
            "temperature": waps["L1_ffn_temperature"],
            "input_experts": waps["L1_ffn_input_experts"],
            "output_experts": waps["L1_ffn_output_experts"]})
    elif name in ("l0_ffn", "l0_ffn2"):
        emit_l0_ffn(P, sctx, C, xT, xnT, {
            "gate_w": waps["L0_ffn_gate_w"], "gate_bias": waps["L0_ffn_gate_bias"],
            "w_gate": waps["L0_ffn_w_gate"], "w_up": waps["L0_ffn_w_up"],
            "w_down": waps["L0_ffn_w_down"], "sw_gate": waps["L0_ffn_sw_gate"],
            "sw_up": waps["L0_ffn_sw_up"], "sw_down": waps["L0_ffn_sw_down"]})
    else:
        raise KeyError(name)


def build_program(names):
    nc = bass.Bass("TRN2", target_bir_lowering=False, debug=False)
    xin = nc.dram_tensor("xin", [D, T], F32, kind="ExternalInput").ap()
    xout = nc.dram_tensor("xout", [D, T], F32, kind="ExternalOutput").ap()
    waps = {}
    for sl, wnames in SUBLAYERS:
        if sl in names:
            for w in wnames:
                if w not in waps:
                    waps[w] = nc.dram_tensor(w, list(INPUT_SPECS[w]), F32,
                                             kind="ExternalInput").ap()
    with ExitStack() as ctx:
        P = Prog(nc, ctx)
        xT = ctx.enter_context(nc.sbuf_tensor("xT", [128, NCH * T], F32))
        xnT = ctx.enter_context(nc.sbuf_tensor("xnT", [128, NCH * T], BF16))
        C = make_consts(nc, ctx, P)
        in_toks = []
        for c in range(NCH):
            in_toks.append(P.dma("y", xT[:, c * T:(c + 1) * T],
                                 xin[c * 128:(c + 1) * 128, :]))
        for e in ("t", "v", "s"):
            P.wait(e, in_toks[-1])
            P.wait(e, C["ready"])
        for sl in names:
            with ExitStack() as sctx:
                emit_sublayer(P, sctx, C, xT, xnT, sl, waps)
        fin_v = P.tok("sv")
        fin_s = P.tok("ss")
        out_toks = []
        for c in range(NCH):
            out_toks.append(P.dma("y", xout[c * 128:(c + 1) * 128, :],
                                  xT[:, c * T:(c + 1) * T],
                                  waits=[fin_v, fin_s] if c == 0 else [],
                                  sem="do"))
        for s_ in sorted(P.cnt):
            if s_.startswith("d"):
                P.wait("y", (s_, P.cnt[s_]))
        block = ctx.enter_context(nc.Block())
        P.realize(block)
    return nc


def kernel(**inputs):
    x = np.ascontiguousarray(inputs["L0_attn_a_x"], dtype=np.float32)
    all_names = [s for s, _ in SUBLAYERS]
    groups = [all_names[i:i + LAUNCH_GROUP] for i in range(0, len(all_names), LAUNCH_GROUP)]
    cur = [np.ascontiguousarray(x[core].T) for core in range(8)]
    for gnames in groups:
        nc = build_program(gnames)
        wnames = []
        seen = set()
        for s, ws in SUBLAYERS:
            if s in gnames:
                for w in ws:
                    if w not in seen:
                        wnames.append(w)
                        seen.add(w)
        in_maps = []
        for core in range(8):
            m = {"xin": cur[core]}
            for w in wnames:
                m[w] = np.ascontiguousarray(inputs[w], dtype=np.float32)
            in_maps.append(m)
        res = run_bass_kernel_spmd(nc, in_maps, core_ids=list(range(8)))
        cur = [np.ascontiguousarray(np.asarray(r["xout"])) for r in res.results]
    return np.stack([np.ascontiguousarray(c.T) for c in cur])


LAUNCH_GROUP = 8
